# version 03
# baseline (speedup 1.0000x reference)
import contextlib
import numpy as np
import ml_dtypes
import concourse.bass as bass
import concourse.mybir as mybir
from concourse.bass_utils import run_bass_kernel_spmd

F32 = mybir.dt.float32
BF16 = mybir.dt.bfloat16
I32 = mybir.dt.int32
AF = mybir.ActivationFunctionType
ALU = mybir.AluOpType
AX = mybir.AxisListType

BF16NP = ml_dtypes.bfloat16

T = 2048
DM = 1024
NDT = DM // 128
TC = 512
NTC = T // TC
ROUND_C = float(np.float32(1.5 * 2.0**23))

ENGINES = ("t", "v", "s", "g", "y")
DEBUG_TAPS = False
ATTN_DEBUG = None


class Op:
    __slots__ = ("eng", "emit", "deps", "inc", "idx", "need_inc", "cum", "is_cc", "cc_sem")

    def __init__(self, eng, emit, deps, inc, idx):
        self.eng, self.emit, self.deps, self.inc, self.idx = eng, emit, deps, inc, idx
        self.need_inc = False
        self.cum = 0
        self.is_cc = False
        self.cc_sem = None


class Prog:

    def __init__(self, nc):
        self.nc = nc
        self.items = {e: [] for e in ENGINES}
        self.last_writer = {}
        self.readers = {}
        self.sems = {}
        self.totals = {}

    def barrier(self):
        lasts = [self.items[e][-1] for e in ENGINES if self.items[e]]
        for e in ENGINES:
            o = Op(e, lambda en: en.nop(), list(lasts), 1, len(self.items[e]))
            self.items[e].append(o)

    def cc(self, emit, reads=(), writes=()):
        o = self.op("g", emit, reads, writes, dma=False)
        o.is_cc = True
        return o

    def op(self, eng, emit, reads=(), writes=(), dma=False):
        deps = []
        for k in reads:
            w = self.last_writer.get(k)
            if w is not None:
                deps.append(w)
        for k in writes:
            w = self.last_writer.get(k)
            if w is not None:
                deps.append(w)
            deps.extend(self.readers.get(k, ()))
        o = Op(eng, emit, deps, 16 if dma else 1, len(self.items[eng]))
        self.items[eng].append(o)
        for k in writes:
            self.last_writer[k] = o
            self.readers[k] = []
        for k in reads:
            self.readers.setdefault(k, []).append(o)
        return o

    def t(self, emit, reads=(), writes=()):
        return self.op("t", emit, reads, writes)

    def v(self, emit, reads=(), writes=()):
        return self.op("v", emit, reads, writes)

    def s(self, emit, reads=(), writes=()):
        return self.op("s", emit, reads, writes)

    def g(self, emit, reads=(), writes=()):
        return self.op("g", emit, reads, writes)

    def dma(self, eng, emit, reads=(), writes=()):
        return self.op(eng, emit, reads, writes, dma=True)

    @staticmethod
    def _pe_pe(d, o):
        return d.eng == "t" and o.eng == "t" and d.inc == 1

    def finalize(self, block, stack):
        for e in ENGINES:
            for o in self.items[e]:
                for d in o.deps:
                    if not self._pe_pe(d, o):
                        d.need_inc = True
                if o.inc == 16:
                    o.need_inc = True
        for e in ENGINES:
            c = 0
            for o in self.items[e]:
                if o.is_cc:
                    o.cc_sem = stack.enter_context(self.nc.semaphore("cc_sem{}".format(id(o) % 100000)))
                    o.need_inc = False
                    o.cum = c
                    continue
                if o.need_inc:
                    c += o.inc
                o.cum = c
            self.totals[e] = c
            if c > 0:
                self.sems[e] = stack.enter_context(self.nc.semaphore(f"sem_{e}"))

        prog = self

        def emit_stream(eng_handle, e):
            watermark = {e2: 0 for e2 in ENGINES}
            ccs_done = set()
            prev_dma_cum = 0
            for o in prog.items[e]:
                if o.inc == 16 and prev_dma_cum > watermark[e]:
                    eng_handle.wait_ge(prog.sems[e], prev_dma_cum)
                    watermark[e] = prev_dma_cum
                for d in o.deps:
                    if prog._pe_pe(d, o):
                        continue
                    if d.is_cc:
                        if id(d) not in ccs_done:
                            eng_handle.wait_ge(d.cc_sem, 1)
                            ccs_done.add(id(d))
                        if d.cum > watermark[d.eng]:
                            eng_handle.wait_ge(prog.sems[d.eng], d.cum)
                            watermark[d.eng] = d.cum
                        continue
                    if d.cum > watermark[d.eng]:
                        eng_handle.wait_ge(prog.sems[d.eng], d.cum)
                        watermark[d.eng] = d.cum
                ins = o.emit(eng_handle)
                if o.is_cc:
                    ins.then_inc(o.cc_sem)
                elif o.need_inc:
                    ins.then_inc(prog.sems[e], o.inc)
                if o.inc == 16:
                    prev_dma_cum = o.cum

        for e, attr in (("g", "gpsimd"), ("t", "tensor"), ("v", "vector"),
                        ("s", "scalar"), ("y", "sync")):
            if not self.items[e]:
                continue
            getattr(block, attr)(lambda eng, _e=e: emit_stream(eng, _e))



def emit_ln_chunk(P, tn, ctx, x_hbm, c, XN, xn_tag, rms_only=False, eps=1e-5):
    XB, XSQ = ctx["XB"], ctx["XSQ"]
    AROW, BROW, TROW = ctx["AROW"], ctx["BROW"], ctx["TROW"]
    ps_s1, ps_s2, ps_a, ps_b = ctx["ps_s1"], ctx["ps_s2"], ctx["ps_a"], ctx["ps_b"]
    ones_colbf, ones32 = ctx["ones_colbf"], ctx["ones32"]
    XC, TMPC = ctx["XC"], ctx["TMPC"]
    rs = slice(0, TC)
    rkey = 0
    if not rms_only:
        for j in range(NDT):
            slot = j % 2
            bcol = slice(slot * TC, (slot + 1) * TC)
            P.dma("g", lambda e, c=c, j=j, bcol=bcol: e.dma_start(
                    out=XB[:, bcol],
                    in_=x_hbm[j * 128:(j + 1) * 128, c * TC:(c + 1) * TC]),
                  writes=("xb{}".format(slot),))
            P.t(lambda e, bcol=bcol, j=j: e.matmul(ps_s1[0:1, :], ones_colbf[:, 0:1], XB[:, bcol],
                                                   start=(j == 0), stop=(j == NDT - 1)),
                reads=("ones_colbf", "xb{}".format(slot)), writes=("ps_s1",))
    for j in range(NDT):
        slot = j % 2
        bcol = slice(slot * TC, (slot + 1) * TC)
        P.dma("g", lambda e, c=c, j=j, bcol=bcol: e.dma_start(
                out=XB[:, bcol],
                in_=x_hbm[j * 128:(j + 1) * 128, c * TC:(c + 1) * TC]),
              writes=("xb{}".format(slot),))
        P.s(lambda e, bcol=bcol: e.activation(XSQ[:, bcol], XB[:, bcol], AF.Square),
            reads=("xb{}".format(slot),), writes=("xsq{}".format(slot),))
        P.t(lambda e, bcol=bcol, j=j: e.matmul(ps_s2[0:1, :], ones_colbf[:, 0:1], XSQ[:, bcol],
                                               start=(j == 0), stop=(j == NDT - 1)),
            reads=("ones_colbf", "xsq{}".format(slot)), writes=("ps_s2",))
    if not rms_only:
        P.v(lambda e, rs=rs: e.tensor_scalar(BROW[0:1, rs], ps_s1[0:1, :], -1.0 / DM, None, ALU.mult),
            reads=("ps_s1",), writes=("brow{}".format(rkey),))
        P.v(lambda e, rs=rs: e.tensor_scalar(AROW[0:1, rs], ps_s2[0:1, :], 1.0 / DM, None, ALU.mult),
            reads=("ps_s2",), writes=("arow{}".format(rkey),))
        P.v(lambda e, rs=rs: e.tensor_tensor(TROW[0:1, 0:TC], BROW[0:1, rs], BROW[0:1, rs], ALU.mult),
            reads=("brow{}".format(rkey),), writes=("trow",))
        P.v(lambda e, rs=rs: e.scalar_tensor_tensor(AROW[0:1, rs], AROW[0:1, rs], float(eps),
                                                    TROW[0:1, 0:TC], ALU.add, ALU.subtract),
            reads=("arow{}".format(rkey), "trow"), writes=("arow{}".format(rkey),))
        var_key = "arow{}".format(rkey)
    else:
        P.v(lambda e, rs=rs: e.tensor_scalar(AROW[0:1, rs], ps_s2[0:1, :], 1.0 / DM, float(eps),
                                             ALU.mult, ALU.add),
            reads=("ps_s2",), writes=("arow{}".format(rkey),))
        var_key = "arow{}".format(rkey)
    P.s(lambda e, rs=rs: e.activation(TROW[0:1, 0:TC], AROW[0:1, rs], AF.Ln),
        reads=(var_key,), writes=("trow",))
    P.s(lambda e, rs=rs: e.activation(AROW[0:1, rs], TROW[0:1, 0:TC], AF.Exp, scale=-0.5),
        reads=("trow",), writes=("arow{}".format(rkey),))
    P.t(lambda e, rs=rs: e.matmul(ps_a[:, :], ones32[0:1, :], AROW[0:1, rs], start=True, stop=True),
        reads=("ones32", "arow{}".format(rkey)), writes=("ps_a",))
    if not rms_only:
        P.t(lambda e, rs=rs: e.matmul(ps_b[:, :], ones32[0:1, :], BROW[0:1, rs], start=True, stop=True),
            reads=("ones32", "brow{}".format(rkey)), writes=("ps_b",))
    for j in range(NDT):
        slot = j % 2
        xsl = slice(slot * TC, (slot + 1) * TC)
        P.dma("y", lambda e, j=j, c=c, xsl=xsl: e.dma_start(
                out=XC[:, xsl], in_=x_hbm[j * 128:(j + 1) * 128, c * TC:(c + 1) * TC]),
              writes=("xc{}".format(slot),))
        dst = slice(j * TC, (j + 1) * TC)
        if rms_only:
            P.v(lambda e, xsl=xsl, dst=dst: e.tensor_tensor(XN[:, dst], XC[:, xsl], ps_a[:, :], ALU.mult),
                reads=("xc{}".format(slot), "ps_a"), writes=(f"{xn_tag}{j}",))
        else:
            P.v(lambda e, xsl=xsl, dst=dst: e.tensor_tensor(TMPC[:, xsl], XC[:, xsl], ps_b[:, :], ALU.add),
                reads=("xc{}".format(slot), "ps_b"), writes=("tmpc{}".format(slot),))
            P.v(lambda e, xsl=xsl, dst=dst: e.tensor_tensor(XN[:, dst], TMPC[:, xsl], ps_a[:, :], ALU.mult),
                reads=("tmpc{}".format(slot), "ps_a"), writes=(f"{xn_tag}{j}",))



def emit_ternarize_to_hbm(P, tn, ctx, WT_hbm, rows, cols, out_hbm, out_tag, final_mult):
    nrt = rows // 128
    WF, RS, SROW = ctx["WF"], ctx["RS"], ctx["SROW"]
    ASUM, STAGE = ctx["ASUM"], ctx["WSTG"]
    ones32, ones32col = ctx["ones32"], ctx["ones32col"]
    ps_w, ps_rs = ctx["ps_w"], ctx["ps_rs"]
    for r in range(nrt):
        slot = r % 2
        wsl = slice(slot * 4096, slot * 4096 + cols)
        P.dma("y", lambda e, r=r, wsl=wsl: e.dma_start(out=WF[:, wsl],
                                                       in_=WT_hbm[r * 128:(r + 1) * 128, :]),
              writes=("wf{}".format(slot),))
        P.v(lambda e, r=r, wsl=wsl: e.tensor_reduce(ASUM[:, r:r + 1], WF[:, wsl], AX.X, ALU.add,
                                                    apply_absolute_value=True),
            reads=("wf{}".format(slot),), writes=("asum",))
    P.t(lambda e: e.matmul(ps_w[0:1, 0:nrt], ones32col[:, 0:1], ASUM[:, 0:nrt],
                           start=True, stop=True),
        reads=("asum", "ones32col"), writes=("ps_a",))
    P.v(lambda e: e.tensor_reduce(SROW[0:1, 0:1], ps_w[0:1, 0:nrt], AX.X, ALU.add),
        reads=("ps_a",), writes=("srow",))
    P.v(lambda e: e.tensor_scalar(SROW[0:1, 0:1], SROW[0:1, 0:1], 1.0 / (rows * cols), 1e-8,
                                  ALU.mult, ALU.max),
        reads=("srow",), writes=("srow",))
    P.v(lambda e: e.reciprocal(SROW[0:1, 1:2], SROW[0:1, 0:1]),
        reads=("srow",), writes=("srow",))
    P.t(lambda e: e.matmul(ps_rs[:, 0:1], ones32[0:1, :], SROW[0:1, 1:2], start=True, stop=True),
        reads=("srow", "ones32"), writes=("ps_a",))
    P.v(lambda e: e.tensor_copy(RS[:, 0:1], ps_rs[:, 0:1]),
        reads=("ps_a",), writes=("rs",))
    for r in range(nrt):
        slot = r % 2
        wsl = slice(slot * 4096, slot * 4096 + cols)
        ssl = slice(slot * cols, (slot + 1) * cols)
        P.dma("y", lambda e, r=r, wsl=wsl: e.dma_start(out=WF[:, wsl],
                                                       in_=WT_hbm[r * 128:(r + 1) * 128, :]),
              writes=("wf{}".format(slot),))
        P.v(lambda e, wsl=wsl: e.tensor_scalar(WF[:, wsl], WF[:, wsl], RS[:, 0:1], ROUND_C,
                                               ALU.mult, ALU.add),
            reads=("wf{}".format(slot), "rs"), writes=("wf{}".format(slot),))
        P.v(lambda e, wsl=wsl: e.tensor_scalar(WF[:, wsl], WF[:, wsl], ROUND_C, 1.0,
                                               ALU.subtract, ALU.min),
            reads=("wf{}".format(slot),), writes=("wf{}".format(slot),))
        P.v(lambda e, wsl=wsl, ssl=ssl: e.tensor_scalar(STAGE[:, ssl], WF[:, wsl],
                                                        -1.0, final_mult, ALU.max, ALU.mult),
            reads=("wf{}".format(slot),), writes=("wstg{}".format(slot),))
        P.dma("y", lambda e, r=r, ssl=ssl: e.dma_start(out=out_hbm[r * 128:(r + 1) * 128, :],
                                                       in_=STAGE[:, ssl]),
              reads=("wstg{}".format(slot),), writes=(f"{out_tag}{r}",))



def plan_l0ffn(P, nc, E, io, with_ternarize):
    x_in = io.get("x_in")
    b1_in = io.get("b1_in")
    b2_in = io.get("b2_in")
    x_out = io.get("x_out")
    w1t_in = io.get("w1t_in")
    w2t_in = io.get("w2t_in")
    w1bf = io.get("w1bf_out") if with_ternarize else io.get("w1bf_in")
    w2bf = io.get("w2bf_out") if with_ternarize else io.get("w2bf_in")
    xn_dbg = io.get("xn_dbg")
    h_dbg = io.get("h_dbg")
    o_dbg = io.get("o_dbg")
    DBGF = E(nc.sbuf_tensor("DBGF", [128, 2 * TC], F32)) if DEBUG_TAPS else None
    XN = E(nc.sbuf_tensor("XN", [128, NDT * TC], BF16))
    XB = E(nc.sbuf_tensor("XB", [128, 2 * TC], BF16))
    XSQ = E(nc.sbuf_tensor("XSQ", [128, 2 * TC], BF16))
    XC = E(nc.sbuf_tensor("XC", [128, 2 * TC], F32))
    XR = E(nc.sbuf_tensor("XR", [128, 2 * TC], F32))
    W1B = E(nc.sbuf_tensor("W1B", [128, NDT * 4096], BF16))
    BIGBF_T = E(nc.sbuf_tensor("BIGBF", [128, 32768], BF16))
    BIGBF = BIGBF_T[:, :]
    BIGB = BIGBF.bitcast(F32)
    HB = E(nc.sbuf_tensor("HB", [128, 32 * TC], BF16))
    TMPC = E(nc.sbuf_tensor("TMPC", [128, 2 * TC], F32))
    AROW = E(nc.sbuf_tensor("AROW", [1, TC], F32))
    BROW = E(nc.sbuf_tensor("BROW", [1, TC], F32))
    TROW = E(nc.sbuf_tensor("TROW", [1, TC], F32))
    SROW = E(nc.sbuf_tensor("SROW", [1, 16], F32))
    ASUM = E(nc.sbuf_tensor("ASUM", [128, 64], F32))
    RS = E(nc.sbuf_tensor("RS", [128, 2], F32))
    b1_sb = E(nc.sbuf_tensor("b1_sb", [128, 32], F32))
    b2_sb = E(nc.sbuf_tensor("b2_sb", [128, 8], F32))
    ones_colbf = E(nc.sbuf_tensor("ones_colbf", [128, 1], BF16))
    ones32 = E(nc.sbuf_tensor("ones32", [1, 128], F32))
    ones32col = E(nc.sbuf_tensor("ones32col", [128, 1], F32))

    ps_s1 = E(nc.psum_tensor("ps_s1", [128, TC], F32))
    ps_s2 = E(nc.psum_tensor("ps_s2", [128, TC], F32))
    ps_a = E(nc.psum_tensor("ps_a", [128, TC], F32))
    ps_b = E(nc.psum_tensor("ps_b", [128, TC], F32))
    ps_h = [E(nc.psum_tensor(f"ps_h{i}", [128, TC], F32)) for i in range(2)]
    ps_o = [E(nc.psum_tensor(f"ps_o{i}", [128, TC], F32)) for i in range(2)]

    ctx = dict(XN=XN, XB=XB, XSQ=XSQ, XC=XC, TMPC=TMPC, AROW=AROW, BROW=BROW, TROW=TROW,
               SROW=SROW, ASUM=ASUM, RS=RS,
               WF=BIGB[:, 0:8192], WSTG=BIGBF[:, 16384:24576],
               ones_colbf=ones_colbf, ones32=ones32, ones32col=ones32col,
               ps_s1=ps_s1, ps_s2=ps_s2, ps_a=ps_a, ps_b=ps_b,
               ps_w=ps_a[0:1, 0:64], ps_rs=ps_a[:, 64:65])

    P.g(lambda e: e.memset(ones_colbf[:, :], 1.0), writes=("ones_colbf",))
    P.g(lambda e: e.memset(ones32[:, :], 1.0), writes=("ones32",))
    P.g(lambda e: e.memset(ones32col[:, :], 1.0), writes=("ones32col",))
    P.dma("g", lambda e: e.dma_start(out=b1_sb[:, :], in_=b1_in[:, :]), writes=("b1_sb",))
    P.dma("g", lambda e: e.dma_start(out=b2_sb[:, :], in_=b2_in[:, :]), writes=("b2_sb",))
    P.v(lambda e: e.tensor_scalar(b2_sb[:, :], b2_sb[:, :], 0.5, None, ALU.mult),
        reads=("b2_sb",), writes=("b2_sb",))

    if with_ternarize:
        emit_ternarize_to_hbm(P, "tz1", ctx, w1t_in, DM, 4096, w1bf, "w1hbm", 1.0)
        emit_ternarize_to_hbm(P, "tz2", ctx, w2t_in, 4096, DM, w2bf, "w2hbm", 0.5)
        w1deps = lambda j: (f"w1hbm{j}",)
        w2deps = lambda r: (f"w2hbm{r}",)
        tz_keys = ("wf0", "wf1", "wstg0", "wstg1")
    else:
        w1deps = lambda j: ()
        w2deps = lambda r: ()
        tz_keys = ()

    P.op("y", lambda e: e.nop(), reads=tz_keys, writes=("bigfree",))

    for j in range(NDT):
        P.dma("y", lambda e, j=j: e.dma_start(out=W1B[:, j * 4096:(j + 1) * 4096],
                                              in_=w1bf[j * 128:(j + 1) * 128, :]),
              reads=w1deps(j), writes=(f"w1res{j}",))
    for ft in range(32):
        P.dma("y", lambda e, ft=ft: e.dma_start(out=BIGBF[:, ft * DM:(ft + 1) * DM],
                                                in_=w2bf[ft * 128:(ft + 1) * 128, :]),
              reads=w2deps(ft) + ("bigfree",), writes=(f"w2res{ft}",))

    NFT = 32
    for c in range(NTC):
        emit_ln_chunk(P, "ln", ctx, x_in, c, XN, "xn", rms_only=False, eps=1e-5)
        for ft in range(NFT):
            pslot = ft % 2
            ps = ps_h[pslot]
            for j in range(NDT):
                P.t(lambda e, ft=ft, j=j, ps=ps: e.matmul(
                        ps[:, :],
                        W1B[:, j * 4096 + ft * 128: j * 4096 + (ft + 1) * 128],
                        XN[:, j * TC:(j + 1) * TC],
                        start=(j == 0), stop=(j == NDT - 1)),
                    reads=(f"w1res{j}", f"xn{j}"), writes=(f"ps_h{pslot}",))
            P.s(lambda e, ft=ft, ps=ps, pslot=pslot: e.activation(
                    HB[:, ft * TC:(ft + 1) * TC], ps[:, :], AF.Gelu, bias=b1_sb[:, ft:ft + 1]),
                reads=(f"ps_h{pslot}", "b1_sb"), writes=(f"HB{ft}",))
        if DEBUG_TAPS and c == 0:
            for j in range(NDT):
                sl = slice((j % 2) * TC, (j % 2 + 1) * TC)
                P.v(lambda e, j=j, sl=sl: e.tensor_copy(DBGF[:, sl], XN[:, j * TC:(j + 1) * TC]),
                    reads=(f"xn{j}",), writes=(f"dbg{j%2}",))
                P.dma("y", lambda e, j=j, sl=sl: e.dma_start(out=xn_dbg[:, j * TC:(j + 1) * TC],
                                                             in_=DBGF[:, sl]),
                      reads=(f"dbg{j%2}",), writes=(f"odbgxn{j}",))
            for ft in range(4):
                sl = slice((ft % 2) * TC, (ft % 2 + 1) * TC)
                P.v(lambda e, ft=ft, sl=sl: e.tensor_copy(DBGF[:, sl], HB[:, ft * TC:(ft + 1) * TC]),
                    reads=(f"HB{ft}",), writes=(f"dbg{ft%2}",))
                P.dma("y", lambda e, ft=ft, sl=sl: e.dma_start(out=h_dbg[:, ft * TC:(ft + 1) * TC],
                                                               in_=DBGF[:, sl]),
                      reads=(f"dbg{ft%2}",), writes=(f"odbgh{ft}",))
        for j in range(NDT):
            pslot = j % 2
            ps = ps_o[pslot]
            for ft in range(NFT):
                P.t(lambda e, j=j, ft=ft, ps=ps: e.matmul(
                        ps[:, :], BIGBF[:, ft * DM + j * 128: ft * DM + (j + 1) * 128],
                        HB[:, ft * TC:(ft + 1) * TC],
                        start=(ft == 0), stop=(ft == NFT - 1)),
                    reads=(f"w2res{ft}", f"HB{ft}"), writes=(f"ps_o{pslot}",))
            xslot = j % 2
            xsl = slice(xslot * TC, (xslot + 1) * TC)
            P.dma("y", lambda e, j=j, c=c, xsl=xsl: e.dma_start(
                    out=XR[:, xsl], in_=x_in[j * 128:(j + 1) * 128, c * TC:(c + 1) * TC]),
                  writes=(f"rx{xslot}",))
            tslot = j % 2
            tc2 = slice(tslot * TC, (tslot + 1) * TC)
            P.v(lambda e, j=j, ps=ps, xsl=xsl, tc2=tc2: e.scalar_tensor_tensor(
                    TMPC[:, tc2], ps[:, :], b2_sb[:, j:j + 1], XR[:, xsl], ALU.add, ALU.add),
                reads=(f"ps_o{pslot}", "b2_sb", f"rx{xslot}"), writes=("tmpc{}".format(tslot),))
            P.dma("y", lambda e, j=j, c=c, tc2=tc2: e.dma_start(
                    out=x_out[j * 128:(j + 1) * 128, c * TC:(c + 1) * TC], in_=TMPC[:, tc2]),
                  reads=("tmpc{}".format(tslot),), writes=(f"out:x{c}_{j}",))

    keys = [f"out:x{c}_{j}" for c in range(NTC) for j in range(NDT)]
    if DEBUG_TAPS:
        keys += [f"odbgxn{j}" for j in range(NDT)] + [f"odbgh{ft}" for ft in range(4)]
    if with_ternarize:
        keys += [f"w1hbm{j}" for j in range(NDT)] + [f"w2hbm{r}" for r in range(32)]
    P.op("y", lambda e: e.nop(), reads=tuple(keys))



def build_l0ffn_launch(with_ternarize):
    nc = bass.Bass("TRN2", target_bir_lowering=False)
    x_in = nc.dram_tensor("x_in", [DM, T], F32, kind="ExternalInput").ap()
    b1_in = nc.dram_tensor("b1_in", [128, 32], F32, kind="ExternalInput").ap()
    b2_in = nc.dram_tensor("b2_in", [128, 8], F32, kind="ExternalInput").ap()
    x_out = nc.dram_tensor("x_out", [DM, T], F32, kind="ExternalOutput").ap()
    if with_ternarize:
        w1t_in = nc.dram_tensor("w1t_in", [DM, 4096], F32, kind="ExternalInput").ap()
        w2t_in = nc.dram_tensor("w2t_in", [4096, DM], F32, kind="ExternalInput").ap()
        w1bf = nc.dram_tensor("w1bf_out", [DM, 4096], BF16, kind="ExternalOutput").ap()
        w2bf = nc.dram_tensor("w2bf_out", [4096, DM], BF16, kind="ExternalOutput").ap()
    else:
        w1bf = nc.dram_tensor("w1bf_in", [DM, 4096], BF16, kind="ExternalInput").ap()
        w2bf = nc.dram_tensor("w2bf_in", [4096, DM], BF16, kind="ExternalInput").ap()

    if DEBUG_TAPS:
        xn_dbg = nc.dram_tensor("xn_dbg", [128, NDT * TC], F32, kind="ExternalOutput").ap()
        h_dbg = nc.dram_tensor("h_dbg", [128, 4 * TC], F32, kind="ExternalOutput").ap()
        o_dbg = nc.dram_tensor("o_dbg", [128, 2 * TC], F32, kind="ExternalOutput").ap()

    io = dict(x_in=x_in, b1_in=b1_in, b2_in=b2_in, x_out=x_out)
    if with_ternarize:
        io.update(w1t_in=w1t_in, w2t_in=w2t_in, w1bf_out=w1bf, w2bf_out=w2bf)
    else:
        io.update(w1bf_in=w1bf, w2bf_in=w2bf)
    if DEBUG_TAPS:
        io.update(xn_dbg=xn_dbg, h_dbg=h_dbg, o_dbg=o_dbg)
    st = contextlib.ExitStack()
    E = st.enter_context
    P = Prog(nc)
    plan_l0ffn(P, nc, E, io, with_ternarize)
    block = E(nc.Block())
    P.finalize(block, st)
    st.close()
    return nc


NH = 16
HD = 64


def attn_host_consts(variant):
    c = {}
    c["minus1_in"] = np.full((1, 128), -1.0, np.float32)
    p2 = np.zeros((128, 128), np.float32)
    p2[0:64, 0] = 1.0
    p2[64:128, 64] = 1.0
    c["psel2_in"] = p2
    bs = np.zeros((128, 128), np.float32)
    bs[0, 0:64] = 1.0
    bs[64, 64:128] = 1.0
    c["bsel_in"] = bs
    c["ident_in"] = np.eye(128, dtype=np.float32)
    if variant == "l0":
        W, umin, width = 129, -511, 1152
    elif variant == "l1attn":
        W, umin, width = 512, -511, 1536
    else:
        W = None
    if W is not None:
        ik = np.arange(128)[:, None]
        u = np.arange(width)[None, :]
        qk = u + umin - ik
        c["hostm_in"] = np.where((qk >= 0) & (qk <= W - 1), 0.0, -1e30).astype(np.float32)
    if variant == "l1mha":
        ik = np.arange(128)[:, None]
        u = np.arange(1024)[None, :]
        c["hostm_in"] = np.where(u - 511 - ik >= 0, 0.0, -1e30).astype(np.float32)
        f = (np.arange(128) % 32).astype(np.float64)
        theta = 10000.0 ** (-2.0 * f / 64.0)
        t = np.arange(T, dtype=np.float64)
        ang = t[None, :] * theta[:, None]
        cos, sin = np.cos(ang), np.sin(ang)
        r_rows = np.zeros(128, bool)
        r_rows[0:32] = True
        r_rows[64:96] = True
        ru = np.where(r_rows[:, None], cos, sin)
        rw = np.where(r_rows[:, None], sin, cos)
        c["ru"] = ru.astype(np.float32)
        c["rw"] = rw.astype(np.float32)
        m1 = np.zeros((128, 128), np.float32)
        m2 = np.zeros((128, 128), np.float32)
        for m in range(128):
            if r_rows[m]:
                m1[m, m] = 1.0
                m1[m + 32, m] = -1.0
            else:
                m2[m - 32, m] = 1.0
                m2[m, m] = 1.0
        c["m1"] = m1
        c["m2"] = m2
    return c


def rope_perm():
    perm = np.zeros(DM, np.int64)
    for h in range(NH):
        base = h * 64
        perm[base + 0:base + 32] = base + np.arange(0, 64, 2)
        perm[base + 32:base + 64] = base + np.arange(1, 64, 2)
    return perm


def kt_range(variant, qc):
    if variant == "l0":
        lo = max(0, 4 * qc - 1)
    elif variant == "l1attn":
        lo = max(0, (qc * 512 - 511) // 128)
    else:
        lo = 0
    return list(range(lo, 4 * qc + 4))


def emit_ln_full(P, ctx, x_hbm, XN, rms_only=False, eps=1e-5, gfold=None):
    XB, XSQ = ctx["XB"], ctx["XSQ"]
    AROW, BROW, TROW = ctx["AROW"], ctx["BROW"], ctx["TROW"]
    ps_s1, ps_s2, ps_a, ps_b = ctx["ps_s1"], ctx["ps_s2"], ctx["ps_a"], ctx["ps_b"]
    ones_colbf, ones32 = ctx["ones_colbf"], ctx["ones32"]
    XC, TMPC = ctx["XC"], ctx["TMPC"]
    for c in range(NTC):
        rs = slice(0, TC)
        rkey = 0
        if not rms_only:
            for j in range(NDT):
                slot = j % 2
                bcol = slice(slot * TC, (slot + 1) * TC)
                P.dma("g", lambda e, c=c, j=j, bcol=bcol: e.dma_start(
                        out=XB[:, bcol],
                        in_=x_hbm[j * 128:(j + 1) * 128, c * TC:(c + 1) * TC]),
                      writes=("xb{}".format(slot),))
                P.t(lambda e, bcol=bcol, j=j: e.matmul(ps_s1[0:1, :], ones_colbf[:, 0:1], XB[:, bcol],
                                                       start=(j == 0), stop=(j == NDT - 1)),
                    reads=("ones_colbf", "xb{}".format(slot)), writes=("ps_s1",))
        for j in range(NDT):
            slot = j % 2
            bcol = slice(slot * TC, (slot + 1) * TC)
            P.dma("g", lambda e, c=c, j=j, bcol=bcol: e.dma_start(
                    out=XB[:, bcol],
                    in_=x_hbm[j * 128:(j + 1) * 128, c * TC:(c + 1) * TC]),
                  writes=("xb{}".format(slot),))
            P.s(lambda e, bcol=bcol: e.activation(XSQ[:, bcol], XB[:, bcol], AF.Square),
                reads=("xb{}".format(slot),), writes=("xsq{}".format(slot),))
            P.t(lambda e, bcol=bcol, j=j: e.matmul(ps_s2[0:1, :], ones_colbf[:, 0:1], XSQ[:, bcol],
                                                   start=(j == 0), stop=(j == NDT - 1)),
                reads=("ones_colbf", "xsq{}".format(slot)), writes=("ps_s2",))
        if not rms_only:
            P.v(lambda e, rs=rs: e.tensor_scalar(BROW[0:1, rs], ps_s1[0:1, :], -1.0 / DM, None, ALU.mult),
                reads=("ps_s1",), writes=("brow{}".format(rkey),))
            P.v(lambda e, rs=rs: e.tensor_scalar(AROW[0:1, rs], ps_s2[0:1, :], 1.0 / DM, None, ALU.mult),
                reads=("ps_s2",), writes=("arow{}".format(rkey),))
            P.v(lambda e, rs=rs: e.tensor_tensor(TROW[0:1, 0:TC], BROW[0:1, rs], BROW[0:1, rs], ALU.mult),
                reads=("brow{}".format(rkey),), writes=("trow",))
            P.v(lambda e, rs=rs: e.scalar_tensor_tensor(AROW[0:1, rs], AROW[0:1, rs], float(eps),
                                                        TROW[0:1, 0:TC], ALU.add, ALU.subtract),
                reads=("arow{}".format(rkey), "trow"), writes=("arow{}".format(rkey),))
        else:
            P.v(lambda e, rs=rs: e.tensor_scalar(AROW[0:1, rs], ps_s2[0:1, :], 1.0 / DM, float(eps),
                                                 ALU.mult, ALU.add),
                reads=("ps_s2",), writes=("arow{}".format(rkey),))
        P.s(lambda e, rs=rs: e.activation(TROW[0:1, 0:TC], AROW[0:1, rs], AF.Ln),
            reads=("arow{}".format(rkey),), writes=("trow",))
        P.s(lambda e, rs=rs: e.activation(AROW[0:1, rs], TROW[0:1, 0:TC], AF.Exp, scale=-0.5),
            reads=("trow",), writes=("arow{}".format(rkey),))
        P.t(lambda e, rs=rs: e.matmul(ps_a[:, :], ones32[0:1, :], AROW[0:1, rs], start=True, stop=True),
            reads=("ones32", "arow{}".format(rkey)), writes=("ps_a",))
        if not rms_only:
            P.t(lambda e, rs=rs: e.matmul(ps_b[:, :], ones32[0:1, :], BROW[0:1, rs], start=True, stop=True),
                reads=("ones32", "brow{}".format(rkey)), writes=("ps_b",))
        for j in range(NDT):
            slot = j % 2
            xsl = slice(slot * TC, (slot + 1) * TC)
            P.dma("y", lambda e, j=j, c=c, xsl=xsl: e.dma_start(
                    out=XC[:, xsl], in_=x_hbm[j * 128:(j + 1) * 128, c * TC:(c + 1) * TC]),
                  writes=("xc{}".format(slot),))
            dst = slice(j * T + c * TC, j * T + (c + 1) * TC)
            if rms_only:
                P.v(lambda e, xsl=xsl, dst=dst: e.tensor_tensor(XN[:, dst], XC[:, xsl], ps_a[:, :], ALU.mult),
                    reads=("xc{}".format(slot), "ps_a"), writes=(f"xn{j}_{c}",))
            else:
                P.v(lambda e, xsl=xsl, dst=dst: e.tensor_tensor(TMPC[:, xsl], XC[:, xsl], ps_b[:, :], ALU.add),
                    reads=("xc{}".format(slot), "ps_b"), writes=("tmpc{}".format(slot),))
                P.v(lambda e, xsl=xsl, dst=dst: e.tensor_tensor(XN[:, dst], TMPC[:, xsl], ps_a[:, :], ALU.mult),
                    reads=("tmpc{}".format(slot), "ps_a"), writes=(f"xn{j}_{c}",))


def emit_rope(P, OUT, otag, RU, RW, M1, M2, TMPE, ps, pskey):
    for jt in range(NDT):
        for c in range(NTC):
            cs = slice(jt * T + c * TC, jt * T + (c + 1) * TC)
            tcs = slice(c * TC, (c + 1) * TC)
            P.v(lambda e, cs=cs, tcs=tcs: e.tensor_tensor(
                    TMPE[:, 0:TC], OUT[:, cs], RU[:, tcs], ALU.mult),
                reads=(otag + str(jt), "ropeu"), writes=("tmpe_a",))
            P.v(lambda e, cs=cs, tcs=tcs: e.tensor_tensor(
                    TMPE[:, TC:2 * TC], OUT[:, cs], RW[:, tcs], ALU.mult),
                reads=(otag + str(jt), "ropew"), writes=("tmpe_b",))
            P.t(lambda e: e.matmul(ps[:, :], M1[:, :], TMPE[:, 0:TC], start=True, stop=False),
                reads=("ropem1", "tmpe_a"), writes=(pskey,))
            P.t(lambda e: e.matmul(ps[:, :], M2[:, :], TMPE[:, TC:2 * TC], start=False, stop=True),
                reads=("ropem2", "tmpe_b"), writes=(pskey,))
            P.v(lambda e, cs=cs: e.tensor_copy(OUT[:, cs], ps[:, :]),
                reads=(pskey,), writes=(otag + str(jt),))


def emit_qkln(P, ctx, OUT, otag, GB, gcol, bcol, psel2, bself, SSC, TMPC2, pp, eps=1e-5):
    ps_q = pp[7]
    ps_r = pp[2]
    for jt in range(NDT):
        for c in range(NTC):
            cs = slice(jt * T + c * TC, jt * T + (c + 1) * TC)
            P.t(lambda e, cs=cs: e.matmul(ps_q[:, :], psel2[:, :], OUT[:, cs], start=True, stop=True),
                reads=("psel2", otag + str(jt)), writes=("ps_x",))
            P.v(lambda e: e.tensor_scalar(SSC[:, 0:TC], ps_q[:, :], 1.0 / HD, None, ALU.mult),
                reads=("ps_x",), writes=("sscA",))
            P.s(lambda e, cs=cs: e.activation(ctx["XSQ"][:, 0:TC], OUT[:, cs], AF.Square),
                reads=(otag + str(jt),), writes=("xsq0",))
            P.t(lambda e: e.matmul(ps_q[:, :], psel2[:, :], ctx["XSQ"][:, 0:TC], start=True, stop=True),
                reads=("psel2", "xsq0"), writes=("ps_x",))
            P.v(lambda e: e.tensor_tensor(SSC[:, TC:2 * TC], SSC[:, 0:TC], SSC[:, 0:TC], ALU.mult),
                reads=("sscA",), writes=("sscB",))
            P.v(lambda e: e.scalar_tensor_tensor(TMPC2[:, 0:TC], ps_q[:, :], 1.0 / HD,
                                                 SSC[:, TC:2 * TC], ALU.mult, ALU.subtract),
                reads=("ps_x", "sscB"), writes=("tmpdA",))
            P.v(lambda e: e.tensor_scalar(TMPC2[:, 0:TC], TMPC2[:, 0:TC], float(eps), 1e-37,
                                          ALU.add, ALU.max),
                reads=("tmpdA",), writes=("tmpdA",))
            P.s(lambda e: e.activation(TMPC2[:, TC:2 * TC], TMPC2[:, 0:TC], AF.Ln),
                reads=("tmpdA",), writes=("tmpdB",))
            P.s(lambda e: e.activation(TMPC2[:, 0:TC], TMPC2[:, TC:2 * TC], AF.Exp, scale=-0.5),
                reads=("tmpdB",), writes=("tmpdA",))
            P.t(lambda e: e.matmul(ps_r[:, :], bself[:, :], SSC[:, 0:TC], start=True, stop=True),
                reads=("bself", "sscA"), writes=("ps_a",))
            P.v(lambda e: e.tensor_copy(SSC[:, 0:TC], ps_r[:, :]),
                reads=("ps_a",), writes=("sscA",))
            P.t(lambda e: e.matmul(ps_r[:, :], bself[:, :], TMPC2[:, 0:TC], start=True, stop=True),
                reads=("bself", "tmpdA"), writes=("ps_a",))
            P.v(lambda e, cs=cs: e.tensor_tensor(TMPC2[:, TC:2 * TC], OUT[:, cs], SSC[:, 0:TC],
                                                 ALU.subtract),
                reads=(otag + str(jt), "sscA"), writes=("tmpdB",))
            P.v(lambda e: e.tensor_tensor(TMPC2[:, TC:2 * TC], TMPC2[:, TC:2 * TC], ps_r[:, :],
                                          ALU.mult),
                reads=("tmpdB", "ps_a"), writes=("tmpdB",))
            P.v(lambda e, cs=cs: e.tensor_scalar(OUT[:, cs], TMPC2[:, TC:2 * TC],
                                                 GB[:, gcol:gcol + 1], GB[:, bcol:bcol + 1],
                                                 ALU.mult, ALU.add),
                reads=("tmpdB", "gb"), writes=(otag + str(jt),))


def emit_ln_to_hbm(P, ctx, x_hbm, z_hbm, eps=1e-5):
    XB, XSQ = ctx["XB"], ctx["XSQ"]
    AROW, BROW, TROW = ctx["AROW"], ctx["BROW"], ctx["TROW"]
    ps_s1, ps_s2, ps_a, ps_b = ctx["ps_s1"], ctx["ps_s2"], ctx["ps_a"], ctx["ps_b"]
    ones_colbf, ones32 = ctx["ones_colbf"], ctx["ones32"]
    XC, XR, TMPC = ctx["XC"], ctx["XR"], ctx["TMPC"]
    for c in range(NTC):
        rs = slice(0, TC)
        rkey = 0
        for j in range(NDT):
            slot = j % 2
            bcol = slice(slot * TC, (slot + 1) * TC)
            P.dma("g", lambda e, c=c, j=j, bcol=bcol: e.dma_start(
                    out=XB[:, bcol], in_=x_hbm[j * 128:(j + 1) * 128, c * TC:(c + 1) * TC]),
                  writes=("xb{}".format(slot),))
            P.t(lambda e, bcol=bcol, j=j: e.matmul(ps_s1[0:1, :], ones_colbf[:, 0:1], XB[:, bcol],
                                                   start=(j == 0), stop=(j == NDT - 1)),
                reads=("ones_colbf", "xb{}".format(slot)), writes=("ps_s1",))
        for j in range(NDT):
            slot = j % 2
            bcol = slice(slot * TC, (slot + 1) * TC)
            P.dma("g", lambda e, c=c, j=j, bcol=bcol: e.dma_start(
                    out=XB[:, bcol], in_=x_hbm[j * 128:(j + 1) * 128, c * TC:(c + 1) * TC]),
                  writes=("xb{}".format(slot),))
            P.s(lambda e, bcol=bcol: e.activation(XSQ[:, bcol], XB[:, bcol], AF.Square),
                reads=("xb{}".format(slot),), writes=("xsq{}".format(slot),))
            P.t(lambda e, bcol=bcol, j=j: e.matmul(ps_s2[0:1, :], ones_colbf[:, 0:1], XSQ[:, bcol],
                                                   start=(j == 0), stop=(j == NDT - 1)),
                reads=("ones_colbf", "xsq{}".format(slot)), writes=("ps_s2",))
        P.v(lambda e, rs=rs: e.tensor_scalar(BROW[0:1, rs], ps_s1[0:1, :], -1.0 / DM, None, ALU.mult),
            reads=("ps_s1",), writes=("brow{}".format(rkey),))
        P.v(lambda e, rs=rs: e.tensor_scalar(AROW[0:1, rs], ps_s2[0:1, :], 1.0 / DM, None, ALU.mult),
            reads=("ps_s2",), writes=("arow{}".format(rkey),))
        P.v(lambda e, rs=rs: e.tensor_tensor(TROW[0:1, 0:TC], BROW[0:1, rs], BROW[0:1, rs], ALU.mult),
            reads=("brow{}".format(rkey),), writes=("trow",))
        P.v(lambda e, rs=rs: e.scalar_tensor_tensor(AROW[0:1, rs], AROW[0:1, rs], float(eps),
                                                    TROW[0:1, 0:TC], ALU.add, ALU.subtract),
            reads=("arow{}".format(rkey), "trow"), writes=("arow{}".format(rkey),))
        P.s(lambda e, rs=rs: e.activation(TROW[0:1, 0:TC], AROW[0:1, rs], AF.Ln),
            reads=("arow{}".format(rkey),), writes=("trow",))
        P.s(lambda e, rs=rs: e.activation(AROW[0:1, rs], TROW[0:1, 0:TC], AF.Exp, scale=-0.5),
            reads=("trow",), writes=("arow{}".format(rkey),))
        P.t(lambda e, rs=rs: e.matmul(ps_a[:, :], ones32[0:1, :], AROW[0:1, rs], start=True, stop=True),
            reads=("ones32", "arow{}".format(rkey)), writes=("ps_a",))
        P.t(lambda e, rs=rs: e.matmul(ps_b[:, :], ones32[0:1, :], BROW[0:1, rs], start=True, stop=True),
            reads=("ones32", "brow{}".format(rkey)), writes=("ps_b",))
        for j in range(NDT):
            slot = j % 2
            xsl = slice(slot * TC, (slot + 1) * TC)
            P.dma("y", lambda e, j=j, c=c, xsl=xsl: e.dma_start(
                    out=XC[:, xsl], in_=x_hbm[j * 128:(j + 1) * 128, c * TC:(c + 1) * TC]),
                  writes=("xc{}".format(slot),))
            P.v(lambda e, xsl=xsl: e.tensor_tensor(TMPC[:, xsl], XC[:, xsl], ps_b[:, :], ALU.add),
                reads=("xc{}".format(slot), "ps_b"), writes=("tmpc{}".format(slot),))
            P.v(lambda e, xsl=xsl: e.tensor_tensor(XR[:, xsl], TMPC[:, xsl], ps_a[:, :], ALU.mult),
                reads=("tmpc{}".format(slot), "ps_a"), writes=("xr{}".format(slot),))
            P.dma("y", lambda e, j=j, c=c, xsl=xsl: e.dma_start(
                    out=z_hbm[j * 128:(j + 1) * 128, c * TC:(c + 1) * TC], in_=XR[:, xsl]),
                  reads=("xr{}".format(slot),), writes=("z{}_{}".format(c, j),))


def plan_attn(P, nc, E, io, variant):
    mwidth = {"l0": 1152, "l1attn": 1536, "l1mha": 1024}[variant]
    if variant == "l1mha":
        rope_in = {nm: io[nm] for nm in ("ru", "rw")}
        ropem_in = {nm: io[nm] for nm in ("m1", "m2")}
        g1_in = io["g1_in"]
        zbuf = io["zbuf"]
    if variant == "l1attn":
        gb_in = io["gb_in"]
    x_in = io.get("x_in")
    x_out = io.get("x_out")
    wpack = io.get("wpack")
    bpack = io.get("bpack")
    bvrow = io.get("bvrow")
    minus1_in = io.get("minus1_in")
    psel2_in = io.get("psel2_in")
    bsel_in = io.get("bsel_in")
    ident_in = io.get("ident_in")
    hostm_in = io.get("hostm_in")
    g1_in = io.get("g1_in")
    zbuf = io.get("zbuf")
    gb_in = io.get("gb_in")
    XN = E(nc.sbuf_tensor("XN", [128, NDT * T], BF16))
    QT = E(nc.sbuf_tensor("QT", [128, NDT * T], BF16))
    KT = E(nc.sbuf_tensor("KT", [128, NDT * T], BF16))
    VS = E(nc.sbuf_tensor("VS", [128, 16 * (NH * 65)], BF16))
    WB = E(nc.sbuf_tensor("WB", [128, NDT * DM], BF16))
    ATT = E(nc.sbuf_tensor("ATT", [128, NDT * TC], BF16))
    PT = E(nc.sbuf_tensor("PT", [128, 2 * TC], BF16))
    RECB = E(nc.sbuf_tensor("RECB", [128, TC], F32))
    SUM2 = E(nc.sbuf_tensor("SUM2", [128, TC], F32))
    NBT = E(nc.sbuf_tensor("NBT", [128, TC], BF16))
    NBTO = E(nc.sbuf_tensor("NBTO", [1, TC], BF16))
    QSQ = E(nc.sbuf_tensor("QSQ", [128, 2 * TC], BF16))
    KMX = E(nc.sbuf_tensor("KMX", [128, 16], F32))
    XB = E(nc.sbuf_tensor("XB", [128, 2 * TC], BF16))
    XSQ = E(nc.sbuf_tensor("XSQ", [128, 2 * TC], BF16))
    XC = E(nc.sbuf_tensor("XC", [128, 2 * TC], F32))
    XR = E(nc.sbuf_tensor("XR", [128, 2 * TC], F32))
    TMPC = E(nc.sbuf_tensor("TMPC", [128, 2 * TC], F32))
    TMPD = E(nc.sbuf_tensor("TMPD", [128, 2 * TC], F32))
    AROW = E(nc.sbuf_tensor("AROW", [1, TC], F32))
    BROW = E(nc.sbuf_tensor("BROW", [1, TC], F32))
    TROW = E(nc.sbuf_tensor("TROW", [1, TC], F32))
    b_sb = E(nc.sbuf_tensor("b_sb", [128, 32], F32))
    bv_sb = E(nc.sbuf_tensor("bv_sb", [1, DM], BF16))
    minus1 = E(nc.sbuf_tensor("minus1", [1, 128], BF16))
    psel2 = E(nc.sbuf_tensor("psel2", [128, 128], BF16))
    bself = E(nc.sbuf_tensor("bself", [128, 128], F32))
    IDENT = E(nc.sbuf_tensor("IDENT", [128, 128], BF16))
    ones_colbf = E(nc.sbuf_tensor("ones_colbf", [128, 1], BF16))
    onesrow = E(nc.sbuf_tensor("onesrow", [1, 128], BF16))
    ones32 = E(nc.sbuf_tensor("ones32", [1, 128], F32))
    HOSTM = E(nc.sbuf_tensor("HOSTM", [128, mwidth], BF16))
    if variant == "l1mha":
        ROPE = {nm: E(nc.sbuf_tensor("R" + nm, [128, T], BF16)) for nm in ("ru", "rw")}
        ROPEM = {nm: E(nc.sbuf_tensor("RM" + nm, [128, 128], BF16)) for nm in ("m1", "m2")}
        TMPE = E(nc.sbuf_tensor("TMPE", [128, 2 * TC], BF16))
        G1 = E(nc.sbuf_tensor("G1", [128, 8], F32))
    if variant == "l1attn":
        GB = E(nc.sbuf_tensor("GB", [128, 4], F32))
        SSC = E(nc.sbuf_tensor("SSC", [128, 2 * TC], F32))

    pp = [E(nc.psum_tensor("pp{}".format(i), [128, TC], F32)) for i in range(8)]
    ps_s1, ps_s2, ps_a, ps_b = pp[0], pp[1], pp[2], pp[3]
    ps_st = [pp[4], pp[5]]
    ps_av = pp[6]
    ps_x = pp[7]

    ctx = dict(XN=XN, XB=XB, XSQ=XSQ, XC=XC, XR=XR, TMPC=TMPC, AROW=AROW, BROW=BROW, TROW=TROW,
               ones_colbf=ones_colbf, ones32=ones32,
               ps_s1=ps_s1, ps_s2=ps_s2, ps_a=ps_a, ps_b=ps_b)

    P.g(lambda e: e.memset(ones_colbf[:, :], 1.0), writes=("ones_colbf",))
    P.g(lambda e: e.memset(ones32[:, :], 1.0), writes=("ones32",))
    P.g(lambda e: e.memset(onesrow[:, :], 1.0), writes=("onesrow",))
    P.dma("g", lambda e: e.dma_start(out=b_sb[:, :], in_=bpack[:, :]), writes=("b_sb",))
    P.dma("g", lambda e: e.dma_start(out=bv_sb[:, :], in_=bvrow[:, :]), writes=("bv_sb",))
    P.dma("g", lambda e: e.dma_start(out=minus1[:, :], in_=minus1_in[:, :]), writes=("minus1",))
    P.dma("g", lambda e: e.dma_start(out=psel2[:, :], in_=psel2_in[:, :]), writes=("psel2",))
    P.dma("y", lambda e: e.dma_start(out=bself[:, :], in_=bsel_in[:, :]), writes=("bself",))
    P.dma("g", lambda e: e.dma_start(out=IDENT[:, :], in_=ident_in[:, :]), writes=("ident",))
    P.dma("g", lambda e: e.dma_start(out=HOSTM[:, :], in_=hostm_in[:, :]), writes=("hostm",))
    if variant == "l1mha":
        for nm in ("ru", "rw"):
            P.dma("g", lambda e, nm=nm: e.dma_start(out=ROPE[nm][:, :], in_=rope_in[nm][:, :]),
                  writes=("rope" + nm[1],))
        for nm in ("m1", "m2"):
            P.dma("g", lambda e, nm=nm: e.dma_start(out=ROPEM[nm][:, :], in_=ropem_in[nm][:, :]),
                  writes=("rope" + nm,))
        P.dma("y", lambda e: e.dma_start(out=G1[:, :], in_=g1_in[:, :]), writes=("g1",))
    if variant == "l1attn":
        P.dma("y", lambda e: e.dma_start(out=GB[:, :], in_=gb_in[:, :]), writes=("gb",))
        P.v(lambda e: e.tensor_scalar(GB[:, 0:2], GB[:, 0:2], 0.125, None, ALU.mult),
            reads=("gb",), writes=("gb",))

    if variant == "l1mha":
        emit_ln_to_hbm(P, ctx, x_in, zbuf)
        base_hbm = zbuf
        base_dep = lambda c, j: ("z{}_{}".format(c, j),)
        emit_ln_full(P, ctx, zbuf, XN, rms_only=True)
    else:
        base_hbm = x_in
        base_dep = lambda c, j: ()
        emit_ln_full(P, ctx, x_in, XN, rms_only=False)

    def load_w(part, fold_g1=False):
        for j in range(NDT):
            P.dma("g", lambda e, j=j, part=part: e.dma_start(
                    out=WB[:, j * DM:(j + 1) * DM],
                    in_=wpack[j * 128:(j + 1) * 128, part * DM:(part + 1) * DM]),
                  writes=("wb{}".format(j),))
            if fold_g1:
                P.v(lambda e, j=j: e.tensor_scalar(WB[:, j * DM:(j + 1) * DM],
                                                   WB[:, j * DM:(j + 1) * DM],
                                                   G1[:, j:j + 1], None, ALU.mult),
                    reads=("wb{}".format(j), "g1"), writes=("wb{}".format(j),))

    fold = variant == "l1mha"

    def qkv_feat(part, OUT, otag, bias_col, scale):
        for qo in range(NDT):
            for c in range(NTC):
                for j in range(NDT):
                    P.t(lambda e, qo=qo, c=c, j=j: e.matmul(
                            ps_x[:, :], WB[:, j * DM + qo * 128:(j * DM + (qo + 1) * 128)],
                            XN[:, j * T + c * TC: j * T + (c + 1) * TC],
                            start=(j == 0), stop=(j == NDT - 1)),
                        reads=("wb{}".format(j), "xn{}_{}".format(j, c)), writes=("ps_x",))
                if bias_col is None:
                    P.v(lambda e, qo=qo, c=c: e.tensor_scalar(
                            OUT[:, qo * T + c * TC: qo * T + (c + 1) * TC], ps_x[:, :],
                            float(scale), None, ALU.mult),
                        reads=("ps_x",), writes=(otag + str(qo),))
                else:
                    P.v(lambda e, qo=qo, c=c: e.tensor_scalar(
                            OUT[:, qo * T + c * TC: qo * T + (c + 1) * TC], ps_x[:, :],
                            b_sb[:, bias_col + qo:bias_col + qo + 1], float(scale),
                            ALU.add, ALU.mult),
                        reads=("ps_x", "b_sb"), writes=(otag + str(qo),))

    has_bias = variant != "l1mha"
    load_w(0, fold)
    qkv_feat(0, QT, "qt", 0 if has_bias else None,
             1.0 if variant == "l1attn" else 0.125)
    load_w(1, fold)
    qkv_feat(1, KT, "kt", 8 if has_bias else None, 1.0)
    load_w(2, fold)
    P.g(lambda e: e.memset(VS[:, :], 1.0), writes=("vsmem",))
    for tt in range(16):
        for nchunk in range(2):
            ps = ps_st[nchunk]
            for j in range(NDT):
                P.t(lambda e, tt=tt, nchunk=nchunk, j=j, ps=ps: e.matmul(
                        ps[:, :], XN[:, j * T + tt * 128: j * T + (tt + 1) * 128],
                        WB[:, j * DM + nchunk * TC: j * DM + (nchunk + 1) * TC],
                        start=(j == 0), stop=False),
                    reads=("wb{}".format(j), "xn{}_{}".format(j, tt // 4)),
                    writes=("ps_st{}".format(nchunk),))
            P.t(lambda e, nchunk=nchunk, ps=ps: e.matmul(
                    ps[:, :], onesrow[0:1, :], bv_sb[0:1, nchunk * TC:(nchunk + 1) * TC],
                    start=False, stop=True),
                reads=("onesrow", "bv_sb"), writes=("ps_st{}".format(nchunk),))
            for hh in range(8):
                h = nchunk * 8 + hh
                P.v(lambda e, tt=tt, hh=hh, h=h, ps=ps: e.tensor_copy(
                        VS[:, tt * (NH * 65) + h * 65: tt * (NH * 65) + h * 65 + 64],
                        ps[:, hh * 64:(hh + 1) * 64]),
                    reads=("ps_st{}".format(nchunk), "vsmem"), writes=("vs{}_{}".format(tt, h),))

    if variant == "l1mha":
        emit_rope(P, QT, "qt", ROPE["ru"], ROPE["rw"], ROPEM["m1"], ROPEM["m2"], TMPE, ps_x, "ps_x")
        emit_rope(P, KT, "kt", ROPE["ru"], ROPE["rw"], ROPEM["m1"], ROPEM["m2"], TMPE, ps_x, "ps_x")
    if variant == "l1attn":
        emit_qkln(P, ctx, QT, "qt", GB, 0, 1, psel2, bself, SSC, TMPD, pp)
        emit_qkln(P, ctx, KT, "kt", GB, 2, 3, psel2, bself, SSC, TMPD, pp)

    for j in range(NDT):
        for c in range(NTC):
            sl = slice((c % 2) * TC, (c % 2 + 1) * TC)
            P.s(lambda e, j=j, c=c, sl=sl: e.activation(
                    QSQ[:, sl], KT[:, j * T + c * TC: j * T + (c + 1) * TC], AF.Square),
                reads=("kt{}".format(j),), writes=("qsq{}".format(c % 2),))
            P.t(lambda e, j=j, c=c, sl=sl: e.matmul(ps_x[:, :], psel2[:, :], QSQ[:, sl],
                                                    start=True, stop=True),
                reads=("psel2", "qsq{}".format(c % 2)), writes=("ps_x",))
            P.v(lambda e: e.tensor_reduce(TMPD[:, 0:1], ps_x[:, :], AX.X, ALU.max),
                reads=("ps_x",), writes=("tmpdA",))
            if c == 0:
                P.v(lambda e, j=j: e.tensor_copy(KMX[:, j:j + 1], TMPD[:, 0:1]),
                    reads=("tmpdA",), writes=("kmx",))
            else:
                P.v(lambda e, j=j: e.tensor_tensor(KMX[:, j:j + 1], KMX[:, j:j + 1],
                                                   TMPD[:, 0:1], ALU.max),
                    reads=("kmx", "tmpdA"), writes=("kmx",))
    P.s(lambda e: e.activation(KMX[:, 8:16], KMX[:, 0:8], AF.Ln),
        reads=("kmx",), writes=("lkb",))
    P.v(lambda e: e.tensor_scalar(KMX[:, 8:16], KMX[:, 8:16], 0.5, None, ALU.mult),
        reads=("lkb",), writes=("lkb",))

    load_w(3, False)
    umin = -511
    for qc in range(NTC):
        for jt in range(NDT):
            sl = slice((jt % 2) * TC, (jt % 2 + 1) * TC)
            P.s(lambda e, jt=jt, qc=qc, sl=sl: e.activation(
                    QSQ[:, sl], QT[:, jt * T + qc * TC: jt * T + (qc + 1) * TC], AF.Square),
                reads=("qt{}".format(jt),), writes=("qsq{}".format(jt % 2),))
            P.t(lambda e, jt=jt, sl=sl: e.matmul(ps_x[:, :], psel2[:, :], QSQ[:, sl],
                                                 start=True, stop=True),
                reads=("psel2", "qsq{}".format(jt % 2)), writes=("ps_x",))
            P.s(lambda e: e.activation(TMPD[:, 0:TC], ps_x[:, :], AF.Ln),
                reads=("ps_x",), writes=("tmpdA",))
            P.s(lambda e, jt=jt: e.activation(NBT[:, 0:TC], TMPD[:, 0:TC], AF.Exp, scale=0.5,
                                              bias=KMX[:, 8 + jt:9 + jt]),
                reads=("tmpdA", "lkb"), writes=("nbt",))
            P.v(lambda e: e.tensor_copy(NBTO[0:1, :], NBT[64:65, 0:TC]),
                reads=("nbt",), writes=("nbto",))
            for hp in range(2):
                h = jt * 2 + hp
                kts = kt_range(variant, qc)
                first = True
                for kt in kts:
                    psl = ps_st[kt % 2]
                    pskey = "ps_st{}".format(kt % 2)
                    P.t(lambda e, jt=jt, hp=hp, qc=qc, kt=kt, psl=psl: e.matmul(
                            psl[:, :],
                            KT[hp * 64:(hp + 1) * 64, jt * T + kt * 128: jt * T + (kt + 1) * 128],
                            QT[hp * 64:(hp + 1) * 64, jt * T + qc * TC: jt * T + (qc + 1) * TC],
                            start=True, stop=False),
                        reads=("kt{}".format(jt), "qt{}".format(jt)), writes=(pskey,))
                    if variant in ("l0", "l1attn") or kt >= 4 * qc:
                        off = qc * 512 - kt * 128 - umin
                        P.t(lambda e, off=off, psl=psl: e.matmul(
                                psl[:, :], IDENT[:, :], HOSTM[:, off:off + TC],
                                start=False, stop=False),
                            reads=("ident", "hostm"), writes=(pskey,))
                    nbr = NBT[0:1, 0:TC] if hp == 0 else NBTO[0:1, :]
                    nbk = "nbt" if hp == 0 else "nbto"
                    P.t(lambda e, psl=psl, nbr=nbr: e.matmul(
                            psl[:, :], minus1[0:1, :], nbr,
                            start=False, stop=True),
                        reads=("minus1", nbk), writes=(pskey,))
                    P.s(lambda e, kt=kt, psl=psl: e.activation(
                            PT[:, (kt % 2) * TC:(kt % 2 + 1) * TC], psl[:, :], AF.Exp),
                        reads=(pskey,), writes=("pt{}".format(kt % 2),))
                    P.t(lambda e, h=h, kt=kt, first=first, last=(kt == kts[-1]): e.matmul(
                            ps_av[0:65, :],
                            VS[:, kt * (NH * 65) + h * 65: kt * (NH * 65) + h * 65 + 65],
                            PT[:, (kt % 2) * TC:(kt % 2 + 1) * TC],
                            start=first, stop=last),
                        reads=("vs{}_{}".format(kt, h), "pt{}".format(kt % 2)), writes=("ps_av",))
                    first = False
                P.v(lambda e, hp=hp: e.tensor_copy(SUM2[hp * 64:hp * 64 + 1, :], ps_av[64:65, :]),
                    reads=("ps_av",), writes=("sum2_{}".format(hp),))
                P.v(lambda e, hp=hp: e.tensor_copy(TMPC[hp * 64:(hp + 1) * 64, 0:TC],
                                                   ps_av[0:64, :]),
                    reads=("ps_av",), writes=("tmpcA",))
            P.v(lambda e: e.tensor_scalar(RECB[:, :], SUM2[:, :], 1e-37, None, ALU.max),
                reads=("sum2_0", "sum2_1"), writes=("recb",))
            P.s(lambda e: e.activation(TMPD[:, TC:2 * TC], RECB[:, :], AF.Ln),
                reads=("recb",), writes=("tmpdB",))
            P.s(lambda e: e.activation(SUM2[:, :], TMPD[:, TC:2 * TC], AF.Exp, scale=-1.0),
                reads=("tmpdB",), writes=("sum2_0", "sum2_1"))
            P.t(lambda e: e.matmul(ps_x[:, :], bself[:, :], SUM2[:, :], start=True, stop=True),
                reads=("bself", "sum2_0", "sum2_1"), writes=("ps_x",))
            P.v(lambda e: e.tensor_copy(RECB[:, :], ps_x[:, :]),
                reads=("ps_x",), writes=("recb",))
            P.v(lambda e, jt=jt: e.tensor_tensor(ATT[:, jt * TC:(jt + 1) * TC],
                                                 TMPC[:, 0:TC], RECB[:, :], ALU.mult),
                reads=("tmpcA", "recb"), writes=("att{}".format(jt),))
        for j in range(NDT):
            for ct in range(NDT):
                P.t(lambda e, j=j, ct=ct: e.matmul(
                        ps_x[:, :], WB[:, ct * DM + j * 128: ct * DM + (j + 1) * 128],
                        ATT[:, ct * TC:(ct + 1) * TC],
                        start=(ct == 0), stop=(ct == NDT - 1)),
                    reads=("wb{}".format(ct), "att{}".format(ct)), writes=("ps_x",))
            xslot = j % 2
            xsl = slice(xslot * TC, (xslot + 1) * TC)
            P.dma("y", lambda e, j=j, qc=qc, xsl=xsl: e.dma_start(
                    out=XR[:, xsl], in_=base_hbm[j * 128:(j + 1) * 128, qc * TC:(qc + 1) * TC]),
                  reads=base_dep(qc, j), writes=("xr{}".format(xslot),))
            P.v(lambda e, j=j, xsl=xsl: e.scalar_tensor_tensor(
                    XC[:, xsl], ps_x[:, :], b_sb[:, 24 + j:25 + j], XR[:, xsl], ALU.add, ALU.add),
                reads=("ps_x", "b_sb", "xr{}".format(xslot)), writes=("xc{}".format(xslot),))
            P.dma("y", lambda e, j=j, qc=qc, xsl=xsl: e.dma_start(
                    out=x_out[j * 128:(j + 1) * 128, qc * TC:(qc + 1) * TC], in_=XC[:, xsl]),
                  reads=("xc{}".format(xslot),), writes=("out:x{}_{}".format(qc, j),))

    keys = ["out:x{}_{}".format(c, j) for c in range(NTC) for j in range(NDT)]
    if ATTN_DEBUG == "avx":
        P.v(lambda e: e.tensor_copy(XR[:, 0:TC], TMPC[:, 0:TC]),
            reads=("tmpcA",) + tuple(keys), writes=("xr0",))
        P.dma("y", lambda e: e.dma_start(out=x_out[0:128, 3 * TC:4 * TC], in_=XR[:, 0:TC]),
              reads=("xr0",), writes=("dbgx:0",))
        P.v(lambda e: e.tensor_copy(XR[:, TC:2 * TC], SUM2[:, :]),
            reads=("sum2_0", "sum2_1") + tuple(keys), writes=("xr1",))
        P.dma("y", lambda e: e.dma_start(out=x_out[128:256, 3 * TC:4 * TC], in_=XR[:, TC:2 * TC]),
              reads=("xr1",), writes=("dbgx:1",))
        P.v(lambda e: e.tensor_copy(XR[:, 0:TC], PT[:, TC:2 * TC]),
            reads=("pt1",) + tuple(keys), writes=("xr0",))
        P.dma("y", lambda e: e.dma_start(out=x_out[256:384, 3 * TC:4 * TC], in_=XR[:, 0:TC]),
              reads=("xr0",), writes=("dbgx:2",))
        P.v(lambda e: e.tensor_copy(XR[:, TC:2 * TC], NBT[:, 0:TC]),
            reads=("nbt",) + tuple(keys), writes=("xr1",))
        P.dma("y", lambda e: e.dma_start(out=x_out[384:512, 3 * TC:4 * TC], in_=XR[:, TC:2 * TC]),
              reads=("xr1",), writes=("dbgx:3",))
        keys = ["dbgx:0", "dbgx:1", "dbgx:2", "dbgx:3"]
    if ATTN_DEBUG == "att":
        for j in range(NDT):
            slot = j % 2
            xsl = slice(slot * TC, (slot + 1) * TC)
            P.v(lambda e, j=j, xsl=xsl: e.tensor_copy(XR[:, xsl], ATT[:, j * TC:(j + 1) * TC]),
                reads=("att{}".format(j),) + tuple(keys), writes=("xr{}".format(slot),))
            P.dma("y", lambda e, j=j, xsl=xsl: e.dma_start(
                    out=x_out[j * 128:(j + 1) * 128, 3 * TC:4 * TC], in_=XR[:, xsl]),
                  reads=("xr{}".format(slot),), writes=("dbga:{}".format(j),))
        keys = ["dbga:{}".format(j) for j in range(NDT)]
    if ATTN_DEBUG == "vs":
        for rb in range(NDT):
            slot = rb % 2
            xsl = slice(slot * TC, (slot + 1) * TC)
            for c4 in range(4):
                P.v(lambda e, rb=rb, c4=c4, xsl=xsl: e.tensor_copy(
                        XR[:, xsl], VS[:, rb * 2048 + c4 * TC: rb * 2048 + (c4 + 1) * TC]),
                    reads=tuple("vs{}_{}".format(tt, h) for tt in range(16) for h in range(16))
                          + tuple(keys), writes=("xr{}".format(slot),))
                P.dma("y", lambda e, rb=rb, c4=c4, xsl=xsl: e.dma_start(
                        out=x_out[rb * 128:(rb + 1) * 128, c4 * TC:(c4 + 1) * TC], in_=XR[:, xsl]),
                      reads=("xr{}".format(slot),), writes=("dbgv:{}_{}".format(rb, c4),))
        keys = ["dbgv:{}_{}".format(rb, c4) for rb in range(NDT) for c4 in range(4)]
    if ATTN_DEBUG in ("qt", "kt", "xn"):
        DT_ = {"qt": QT, "kt": KT, "xn": XN}[ATTN_DEBUG]
        tagd = {"qt": "qt", "kt": "kt", "xn": None}[ATTN_DEBUG]
        for j in range(NDT):
            for c in range(NTC):
                slot = j % 2
                xsl = slice(slot * TC, (slot + 1) * TC)
                rk = (tagd + str(j),) if tagd else ("xn{}_{}".format(j, c),)
                P.v(lambda e, j=j, c=c, xsl=xsl, DT_=DT_: e.tensor_copy(
                        XR[:, xsl], DT_[:, j * T + c * TC: j * T + (c + 1) * TC]),
                    reads=rk + tuple(keys), writes=("xr{}".format(slot),))
                P.dma("y", lambda e, j=j, c=c, xsl=xsl: e.dma_start(
                        out=x_out[j * 128:(j + 1) * 128, c * TC:(c + 1) * TC], in_=XR[:, xsl]),
                      reads=("xr{}".format(slot),), writes=("dbg:{}_{}".format(c, j),))
        keys = ["dbg:{}_{}".format(c, j) for c in range(NTC) for j in range(NDT)]
    P.op("y", lambda e: e.nop(), reads=tuple(keys))


def build_attn_launch(variant):
    nc = bass.Bass("TRN2", target_bir_lowering=False)
    x_in = nc.dram_tensor("x_in", [DM, T], F32, kind="ExternalInput").ap()
    x_out = nc.dram_tensor("x_out", [DM, T], F32, kind="ExternalOutput").ap()
    wpack = nc.dram_tensor("wpack", [DM, 4 * DM], F32, kind="ExternalInput").ap()
    bpack = nc.dram_tensor("bpack", [128, 32], F32, kind="ExternalInput").ap()
    bvrow = nc.dram_tensor("bvrow", [1, DM], F32, kind="ExternalInput").ap()
    minus1_in = nc.dram_tensor("minus1_in", [1, 128], F32, kind="ExternalInput").ap()
    psel2_in = nc.dram_tensor("psel2_in", [128, 128], F32, kind="ExternalInput").ap()
    bsel_in = nc.dram_tensor("bsel_in", [128, 128], F32, kind="ExternalInput").ap()
    ident_in = nc.dram_tensor("ident_in", [128, 128], F32, kind="ExternalInput").ap()
    mwidth = {"l0": 1152, "l1attn": 1536, "l1mha": 1024}[variant]
    hostm_in = nc.dram_tensor("hostm_in", [128, mwidth], F32, kind="ExternalInput").ap()
    if variant == "l1mha":
        rope_in = {nm: nc.dram_tensor(nm, [128, T if nm in ("ru", "rw") else T], F32,
                                      kind="ExternalInput").ap() for nm in ("ru", "rw")}
        ropem_in = {nm: nc.dram_tensor(nm, [128, 128], F32, kind="ExternalInput").ap()
                    for nm in ("m1", "m2")}
        g1_in = nc.dram_tensor("g1_in", [128, 8], F32, kind="ExternalInput").ap()
        zbuf = nc.dram_tensor("zbuf", [DM, T], F32, kind="Internal").ap()
    if variant == "l1attn":
        gb_in = nc.dram_tensor("gb_in", [128, 4], F32, kind="ExternalInput").ap()

    io = dict(x_in=x_in, x_out=x_out, wpack=wpack, bpack=bpack, bvrow=bvrow,
              minus1_in=minus1_in, psel2_in=psel2_in, bsel_in=bsel_in, ident_in=ident_in,
              hostm_in=hostm_in)
    if variant == "l1mha":
        io.update(diagm_in=None, g1_in=g1_in, zbuf=zbuf, ru=rope_in["ru"], rw=rope_in["rw"],
                  m1=ropem_in["m1"], m2=ropem_in["m2"])
    if variant == "l1attn":
        io.update(gb_in=gb_in)
    st = contextlib.ExitStack()
    E = st.enter_context
    P = Prog(nc)
    plan_attn(P, nc, E, io, variant)
    block = E(nc.Block())
    P.finalize(block, st)
    st.close()
    return nc


def run_attn(xt, variant, wq, wk, wv, wo, bq, bk, bv, bo, gb=None, g1=None):
    hc = attn_host_consts(variant)
    if variant == "l1mha":
        perm = rope_perm()
        wq = wq[:, perm]
        wk = wk[:, perm]
    wpackh = np.ascontiguousarray(np.concatenate([wq, wk, wv, wo], axis=1).astype(np.float32))
    z8 = np.zeros(DM, np.float32)
    bq = z8 if bq is None else np.asarray(bq, np.float32)
    bk = z8 if bk is None else np.asarray(bk, np.float32)
    bv = z8 if bv is None else np.asarray(bv, np.float32)
    bo = z8 if bo is None else np.asarray(bo, np.float32)
    bpackh = np.zeros((128, 32), np.float32)
    bpackh[:, 0:8] = bq.reshape(8, 128).T
    bpackh[:, 8:16] = bk.reshape(8, 128).T
    bpackh[:, 24:32] = bo.reshape(8, 128).T
    bvrowh = np.ascontiguousarray(bv.reshape(1, DM))
    nc = build_attn_launch(variant)
    im = {"wpack": wpackh, "bpack": bpackh, "bvrow": bvrowh,
          "minus1_in": hc["minus1_in"], "psel2_in": hc["psel2_in"],
          "bsel_in": hc["bsel_in"], "ident_in": hc["ident_in"]}
    im["hostm_in"] = hc["hostm_in"]
    if variant == "l1mha":
        im.update({"ru": hc["ru"], "rw": hc["rw"], "m1": hc["m1"], "m2": hc["m2"]})
        im["g1_in"] = np.ascontiguousarray(np.asarray(g1, np.float32).reshape(8, 128).T)
    if variant == "l1attn":
        gq, gbb, gk, gkb = [np.asarray(a, np.float32) for a in gb]
        gbh = np.zeros((128, 4), np.float32)
        gbh[:, 0] = np.tile(gq, 2)
        gbh[:, 1] = np.tile(gbb, 2)
        gbh[:, 2] = np.tile(gk, 2)
        gbh[:, 3] = np.tile(gkb, 2)
        im["gb_in"] = gbh
    in_maps = [dict(im, x_in=xt[b]) for b in range(8)]
    res = run_bass_kernel_spmd(nc, in_maps, core_ids=list(range(8)))
    return [res.results[b]["x_out"] for b in range(8)]



def plan_router(P, nc, E, io):
    x_in = io.get("x_in")
    wr_in = io.get("wr_in")
    g2_in = io.get("g2_in")
    zn_out = io.get("zn_out")
    vloc_out = io.get("vloc_out")
    XB = E(nc.sbuf_tensor("XB", [128, 2 * TC], BF16))
    XSQ = E(nc.sbuf_tensor("XSQ", [128, 2 * TC], BF16))
    XR = E(nc.sbuf_tensor("XR", [128, NDT * TC], F32))
    XNC = E(nc.sbuf_tensor("XNC", [128, 2 * TC], BF16))
    XC = E(nc.sbuf_tensor("XC", [128, 2 * TC], F32))
    AROW = E(nc.sbuf_tensor("AROW", [1, TC], F32))
    TROW = E(nc.sbuf_tensor("TROW", [1, TC], F32))
    WR = E(nc.sbuf_tensor("WR", [128, NDT * 8], F32))
    G2 = E(nc.sbuf_tensor("G2", [128, 8], F32))
    LG = E(nc.sbuf_tensor("LG", [128, 8], F32))
    LG2 = E(nc.sbuf_tensor("LG2", [128, 8], F32))
    EXPL = E(nc.sbuf_tensor("EXPL", [128, 8], F32))
    PRB = E(nc.sbuf_tensor("PRB", [128, 8], F32))
    MSK = E(nc.sbuf_tensor("MSK", [128, 8], F32))
    SC = E(nc.sbuf_tensor("SC", [128, 8], F32))
    VLOC = E(nc.sbuf_tensor("VLOC", [128, 128], F32))
    ones_colbf = E(nc.sbuf_tensor("ones_colbf", [128, 1], BF16))
    ones32 = E(nc.sbuf_tensor("ones32", [1, 128], F32))
    ps_s2 = E(nc.psum_tensor("ps_s2", [128, TC], F32))
    ps_a = E(nc.psum_tensor("ps_a", [128, TC], F32))
    ps_r = E(nc.psum_tensor("ps_r", [128, 8], F32))

    P.g(lambda e: e.memset(ones_colbf[:, :], 1.0), writes=("ones_colbf",))
    P.g(lambda e: e.memset(ones32[:, :], 1.0), writes=("ones32",))
    P.dma("g", lambda e: e.dma_start(out=G2[:, :], in_=g2_in[:, :]), writes=("g2",))
    for j in range(NDT):
        P.dma("g", lambda e, j=j: e.dma_start(out=WR[:, j * 8:(j + 1) * 8],
                                              in_=wr_in[j * 128:(j + 1) * 128, :]),
              writes=("wr{}".format(j),))
        P.v(lambda e, j=j: e.tensor_scalar(WR[:, j * 8:(j + 1) * 8], WR[:, j * 8:(j + 1) * 8],
                                           G2[:, j:j + 1], None, ALU.mult),
            reads=("wr{}".format(j), "g2"), writes=("wr{}".format(j),))

    for c in range(NTC):
        for j in range(NDT):
            slot = j % 2
            bcol = slice(slot * TC, (slot + 1) * TC)
            P.dma("g", lambda e, c=c, j=j, bcol=bcol: e.dma_start(
                    out=XB[:, bcol], in_=x_in[j * 128:(j + 1) * 128, c * TC:(c + 1) * TC]),
                  writes=("xb{}".format(slot),))
            P.s(lambda e, bcol=bcol: e.activation(XSQ[:, bcol], XB[:, bcol], AF.Square),
                reads=("xb{}".format(slot),), writes=("xsq{}".format(slot),))
            P.t(lambda e, bcol=bcol, j=j: e.matmul(ps_s2[0:1, :], ones_colbf[:, 0:1], XSQ[:, bcol],
                                                   start=(j == 0), stop=(j == NDT - 1)),
                reads=("ones_colbf", "xsq{}".format(slot)), writes=("ps_s2",))
        P.v(lambda e: e.tensor_scalar(AROW[0:1, :], ps_s2[0:1, :], 1.0 / DM, 1e-6,
                                      ALU.mult, ALU.add),
            reads=("ps_s2",), writes=("arow0",))
        P.s(lambda e: e.activation(TROW[0:1, :], AROW[0:1, :], AF.Ln),
            reads=("arow0",), writes=("trow",))
        P.s(lambda e: e.activation(AROW[0:1, :], TROW[0:1, :], AF.Exp, scale=-0.5),
            reads=("trow",), writes=("arow0",))
        P.t(lambda e: e.matmul(ps_a[:, :], ones32[0:1, :], AROW[0:1, :], start=True, stop=True),
            reads=("ones32", "arow0"), writes=("ps_a",))
        for j in range(NDT):
            slot = j % 2
            xsl = slice(slot * TC, (slot + 1) * TC)
            P.dma("y", lambda e, j=j, c=c, xsl=xsl: e.dma_start(
                    out=XC[:, xsl], in_=x_in[j * 128:(j + 1) * 128, c * TC:(c + 1) * TC]),
                  writes=("xc{}".format(slot),))
            P.v(lambda e, j=j, xsl=xsl: e.tensor_tensor(XR[:, j * TC:(j + 1) * TC], XC[:, xsl],
                                                        ps_a[:, :], ALU.mult),
                reads=("xc{}".format(slot), "ps_a"), writes=("xrj{}".format(j),))
            P.v(lambda e, j=j, xsl=xsl: e.tensor_scalar(XNC[:, xsl], XR[:, j * TC:(j + 1) * TC],
                                                        G2[:, j:j + 1], None, ALU.mult),
                reads=("xrj{}".format(j), "g2"), writes=("xnc{}".format(slot),))
            P.dma("y", lambda e, j=j, c=c, xsl=xsl: e.dma_start(
                    out=zn_out[j * 128:(j + 1) * 128, c * TC:(c + 1) * TC], in_=XNC[:, xsl]),
                  reads=("xnc{}".format(slot),), writes=("out:zn{}_{}".format(c, j),))
        for s4 in range(4):
            tt = c * 4 + s4
            for j in range(NDT):
                P.t(lambda e, j=j, s4=s4: e.matmul(
                        ps_r[:, :], XR[:, j * TC + s4 * 128: j * TC + (s4 + 1) * 128],
                        WR[:, j * 8:(j + 1) * 8], start=(j == 0), stop=(j == NDT - 1)),
                    reads=("xrj{}".format(j), "wr{}".format(j)), writes=("ps_r",))
            P.v(lambda e: e.tensor_copy(LG[:, :], ps_r[:, :]), reads=("ps_r",), writes=("lg",))
            P.v(lambda e: e.reduce_max(SC[:, 0:1], LG[:, :], axis=AX.X, negate=True),
                reads=("lg",), writes=("sc0",))
            P.s(lambda e: e.activation(EXPL[:, :], LG[:, :], AF.Exp, bias=SC[:, 0:1]),
                reads=("lg", "sc0"), writes=("expl",))
            P.v(lambda e: e.reduce_sum(SC[:, 1:2], EXPL[:, :], axis=AX.X),
                reads=("expl",), writes=("sc1",))
            P.s(lambda e: e.activation(SC[:, 2:3], SC[:, 1:2], AF.Ln),
                reads=("sc1",), writes=("sc2",))
            P.s(lambda e: e.activation(SC[:, 3:4], SC[:, 2:3], AF.Exp, scale=-1.0),
                reads=("sc2",), writes=("sc3",))
            P.v(lambda e: e.tensor_scalar(SC[:, 4:5], SC[:, 0:1], -1.0, None, ALU.mult),
                reads=("sc0",), writes=("sc4",))
            P.v(lambda e: e.tensor_scalar(MSK[:, :], LG[:, :], SC[:, 4:5], None, ALU.is_ge),
                reads=("lg", "sc4"), writes=("msk",))
            P.v(lambda e: e.scalar_tensor_tensor(LG2[:, :], MSK[:, :], -1e30, LG[:, :],
                                                 ALU.mult, ALU.add),
                reads=("msk", "lg"), writes=("lg2",))
            P.v(lambda e: e.reduce_max(SC[:, 5:6], LG2[:, :], axis=AX.X),
                reads=("lg2",), writes=("sc5",))
            P.s(lambda e: e.activation(SC[:, 6:7], SC[:, 5:6], AF.Exp, bias=SC[:, 0:1]),
                reads=("sc5", "sc0"), writes=("sc6",))
            P.v(lambda e: e.tensor_scalar(SC[:, 7:8], SC[:, 6:7], SC[:, 3:4], None, ALU.mult),
                reads=("sc6", "sc3"), writes=("sc7",))
            P.v(lambda e: e.tensor_scalar(PRB[:, :], EXPL[:, :], SC[:, 3:4], None, ALU.mult),
                reads=("expl", "sc3"), writes=("prb",))
            P.v(lambda e: e.tensor_scalar(MSK[:, :], PRB[:, :], SC[:, 7:8], None, ALU.is_ge),
                reads=("prb", "sc7"), writes=("msk",))
            P.v(lambda e, tt=tt: e.tensor_tensor(VLOC[:, tt * 8:(tt + 1) * 8], PRB[:, :],
                                                 MSK[:, :], ALU.mult),
                reads=("prb", "msk"), writes=("vloc{}".format(tt),))
    P.dma("y", lambda e: e.dma_start(out=vloc_out[:, :], in_=VLOC[:, :]),
          reads=tuple("vloc{}".format(tt) for tt in range(16)), writes=("out:vloc",))
    keys = ["out:zn{}_{}".format(c, j) for c in range(NTC) for j in range(NDT)] + ["out:vloc"]
    P.op("y", lambda e: e.nop(), reads=tuple(keys))


def build_router_launch():
    nc = bass.Bass("TRN2", target_bir_lowering=False)
    x_in = nc.dram_tensor("x_in", [DM, T], F32, kind="ExternalInput").ap()
    wr_in = nc.dram_tensor("wr_in", [DM, 8], F32, kind="ExternalInput").ap()
    g2_in = nc.dram_tensor("g2_in", [128, 8], F32, kind="ExternalInput").ap()
    zn_out = nc.dram_tensor("zn_out", [DM, T], BF16, kind="ExternalOutput").ap()
    vloc_out = nc.dram_tensor("vloc_out", [128, 128], F32, kind="ExternalOutput").ap()

    io = dict(x_in=x_in, wr_in=wr_in, g2_in=g2_in, zn_out=zn_out, vloc_out=vloc_out)
    st = contextlib.ExitStack()
    E = st.enter_context
    P = Prog(nc)
    plan_router(P, nc, E, io)
    block = E(nc.Block())
    P.finalize(block, st)
    st.close()
    return nc


def run_router(s1t, w_router, g2):
    nc = build_router_launch()
    wrh = np.ascontiguousarray(np.asarray(w_router, np.float32))
    g2h = np.ascontiguousarray(np.asarray(g2, np.float32).reshape(8, 128).T)
    in_maps = [{"x_in": s1t[b], "wr_in": wrh, "g2_in": g2h} for b in range(8)]
    res = run_bass_kernel_spmd(nc, in_maps, core_ids=list(range(8)))
    zn = [res.results[b]["zn_out"] for b in range(8)]
    vloc = [res.results[b]["vloc_out"] for b in range(8)]
    return zn, vloc


def vloc_decode(vl):
    out = np.zeros((T, 8), np.float32)
    for tt in range(16):
        out[tt * 128:(tt + 1) * 128, :] = vl[:, tt * 8:(tt + 1) * 8]
    return out


def moe_host_layouts(vglobal):
    vb = np.zeros((128, 1024), np.float32)
    for e in range(8):
        for s in range(16):
            vb[e * 16 + s, :] = vglobal[s * 1024:(s + 1) * 1024, e]
    return vb



CAP = 1280.0


def plan_moe(P, nc, E, io, ago=None, permsel=None, idf32=None):
    x3_in = io.get("x3_in")
    s1_in = io.get("s1_in")
    zn_in = io.get("zn_in")
    vbis_in = io.get("vbis_in")
    vtok_in = io.get("vtok_in")
    w1_in = io.get("w1_in")
    w2_in = io.get("w2_in")
    gsel_in = io.get("gsel_in")
    gsel2_in = io.get("gsel2_in")
    esel3_in = io.get("esel3_in")
    ident8_in = io.get("ident8_in")
    identf_in = io.get("identf_in")
    x_out = io.get("x_out")
    ZN = E(nc.sbuf_tensor("ZN", [128, NDT * T], BF16))
    W1E = E(nc.sbuf_tensor("W1E", [128, NDT * 2048], BF16))
    W2E = E(nc.sbuf_tensor("W2E", [128, 16 * DM], BF16))
    HB = E(nc.sbuf_tensor("HB", [128, 16 * TC], BF16))
    ACC = E(nc.sbuf_tensor("ACC", [128, NDT * T], F32))
    ZG = E(nc.sbuf_tensor("ZG", [128, NDT * TC], BF16))
    VB = E(nc.sbuf_tensor("VB", [128, 1024], F32))
    VT = E(nc.sbuf_tensor("VT", [128, 128], F32))
    GT = E(nc.sbuf_tensor("GT", [128, 128], F32))
    GA = E(nc.sbuf_tensor("GA", [8, 16 * 128], BF16))
    GTB = E(nc.sbuf_tensor("GTB", [128, 128], BF16))
    CNTB = E(nc.sbuf_tensor("CNTB", [128, 1024], BF16))
    CNTS = E(nc.sbuf_tensor("CNTS", [128, 4], F32))
    LOHI = E(nc.sbuf_tensor("LOHI", [8, 8], F32))
    TAUB = E(nc.sbuf_tensor("TAUB", [128, 8], F32))
    TAUR = E(nc.sbuf_tensor("TAUR", [1, 8], F32))
    XC = E(nc.sbuf_tensor("XC", [128, 2 * TC], F32))
    GSEL = E(nc.sbuf_tensor("GSEL", [128, 8], F32))
    GSEL2 = E(nc.sbuf_tensor("GSEL2", [8, 128], F32))
    ESEL3 = E(nc.sbuf_tensor("ESEL3", [8, 1024], BF16))
    IDENT8 = E(nc.sbuf_tensor("IDENT8", [8, 8], F32))
    IDENTF = E(nc.sbuf_tensor("IDENTF", [128, 128], BF16))
    ones32 = E(nc.sbuf_tensor("ones32", [1, 128], F32))
    MIDB = E(nc.sbuf_tensor("MIDB", [128, 1], F32))
    CNT8 = E(nc.sbuf_tensor("CNT8", [8, 4], F32))
    pp = [E(nc.psum_tensor("pp{}".format(i), [128, TC], F32)) for i in range(8)]
    ps_m = pp[0]
    ps_c = pp[1]
    ps_h = [pp[2], pp[3]]
    ps_o = [pp[4], pp[5]]
    ps_g = pp[6]
    ps_t = pp[7]

    P.g(lambda e: e.memset(ones32[:, :], 1.0), writes=("ones32",))
    if ago is None:
        P.dma("g", lambda e: e.dma_start(out=VB[:, :], in_=vbis_in[:, :]), writes=("vb",))
    else:
        AGT = E(nc.sbuf_tensor("AGT", [128, 128], F32))
        AGT2 = E(nc.sbuf_tensor("AGT2", [128, 128], F32))
        PRM = E(nc.sbuf_tensor("PRM", [128, 128], F32))
        IDF = E(nc.sbuf_tensor("IDF", [128, 128], F32))
        P.dma("g", lambda e: e.dma_start(out=PRM[:, :], in_=permsel[:, :]), writes=("prm",))
        P.dma("g", lambda e: e.dma_start(out=IDF[:, :], in_=idf32[:, :]), writes=("idf",))
        for r in range(8):
            P.dma("g", lambda e, r=r: e.dma_start(out=AGT[:, :], in_=ago[r * 128:(r + 1) * 128, :]),
                  reads=("cc_done",), writes=("agt",))
            P.t(lambda e: e.transpose(pp[0][0:128, 0:128], AGT[:, :], IDF[:, :]),
                reads=("agt", "idf"), writes=("pp0",))
            P.v(lambda e: e.tensor_copy(AGT2[:, :], pp[0][0:128, 0:128]),
                reads=("pp0",), writes=("agt2",))
            P.t(lambda e: e.matmul(pp[0][0:128, 0:128], PRM[:, :], AGT2[:, :], start=True, stop=True),
                reads=("prm", "agt2"), writes=("pp0",))
            P.v(lambda e, r=r: e.tensor_copy(VB[:, r * 128:(r + 1) * 128], pp[0][0:128, 0:128]),
                reads=("pp0",), writes=("vb",))
    P.dma("g", lambda e: e.dma_start(out=VT[:, :], in_=vtok_in[:, :]), writes=("vt",))
    P.dma("g", lambda e: e.dma_start(out=GSEL[:, :], in_=gsel_in[:, :]), writes=("gsel",))
    P.dma("g", lambda e: e.dma_start(out=GSEL2[:, :], in_=gsel2_in[:, :]), writes=("gsel2",))
    P.dma("g", lambda e: e.dma_start(out=ESEL3[:, :], in_=esel3_in[:, :]), writes=("esel3",))
    P.dma("g", lambda e: e.dma_start(out=IDENT8[:, :], in_=ident8_in[:, :]), writes=("ident8",))
    P.dma("g", lambda e: e.dma_start(out=IDENTF[:, :], in_=identf_in[:, :]), writes=("identf",))
    for j in range(NDT):
        P.dma("g", lambda e, j=j: e.dma_start(out=ZN[:, j * T:(j + 1) * T],
                                              in_=zn_in[j * 128:(j + 1) * 128, :]),
              writes=("zn{}".format(j),))

    P.v(lambda e: e.memset(LOHI[:, 0:1], 0.0), writes=("lo",))
    P.v(lambda e: e.memset(LOHI[:, 1:2], 1.01), writes=("hi",))
    for it in range(40):
        P.v(lambda e: e.tensor_tensor(LOHI[:, 2:3], LOHI[:, 0:1], LOHI[:, 1:2], ALU.add),
            reads=("lo", "hi"), writes=("mid",))
        P.v(lambda e: e.tensor_scalar(LOHI[:, 2:3], LOHI[:, 2:3], 0.5, None, ALU.mult),
            reads=("mid",), writes=("mid",))
        P.t(lambda e: e.matmul(ps_m[:, 0:1], GSEL2[:, :], LOHI[0:8, 2:3], start=True, stop=True),
            reads=("gsel2", "mid"), writes=("pp0",))
        P.v(lambda e: e.tensor_copy(MIDB[:, :], ps_m[:, 0:1]), reads=("pp0",), writes=("midb",))
        P.v(lambda e: e.tensor_scalar(CNTB[:, :], VB[:, :], MIDB[:, 0:1], None, ALU.is_ge),
            reads=("vb", "midb"), writes=("cntb",))
        P.v(lambda e: e.reduce_sum(CNTS[:, 0:1], CNTB[:, :], axis=AX.X),
            reads=("cntb",), writes=("cnts",))
        P.t(lambda e: e.matmul(ps_c[0:8, 0:1], GSEL[:, :], CNTS[:, 0:1], start=True, stop=True),
            reads=("gsel", "cnts"), writes=("pp1",))
        P.v(lambda e: e.tensor_copy(CNT8[:, 0:1], ps_c[0:8, 0:1]), reads=("pp1",), writes=("cnt8",))
        P.v(lambda e: e.tensor_scalar(CNT8[:, 1:2], CNT8[:, 0:1], CAP, None, ALU.is_ge),
            reads=("cnt8",), writes=("sel",))
        P.v(lambda e: e.tensor_tensor(LOHI[:, 3:4], LOHI[:, 2:3], LOHI[:, 0:1], ALU.subtract),
            reads=("mid", "lo"), writes=("sc_d",))
        P.v(lambda e: e.tensor_tensor(LOHI[:, 3:4], LOHI[:, 3:4], CNT8[:, 1:2], ALU.mult),
            reads=("sc_d", "sel"), writes=("sc_d",))
        P.v(lambda e: e.tensor_tensor(LOHI[:, 0:1], LOHI[:, 0:1], LOHI[:, 3:4], ALU.add),
            reads=("lo", "sc_d"), writes=("lo",))
        P.v(lambda e: e.tensor_tensor(LOHI[:, 4:5], LOHI[:, 1:2], LOHI[:, 2:3], ALU.subtract),
            reads=("hi", "mid"), writes=("sc_e",))
        P.v(lambda e: e.tensor_tensor(LOHI[:, 4:5], LOHI[:, 4:5], CNT8[:, 1:2], ALU.mult),
            reads=("sc_e", "sel"), writes=("sc_e",))
        P.v(lambda e: e.tensor_tensor(LOHI[:, 1:2], LOHI[:, 2:3], LOHI[:, 4:5], ALU.add),
            reads=("mid", "sc_e"), writes=("hi",))
    P.t(lambda e: e.matmul(ps_m[0:1, 0:8], LOHI[0:8, 0:1], IDENT8[:, :], start=True, stop=True),
        reads=("lo", "ident8"), writes=("pp0",))
    P.v(lambda e: e.tensor_scalar(TAUR[0:1, :], ps_m[0:1, 0:8], 0.0, None, ALU.max),
        reads=("pp0",), writes=("taur",))
    P.t(lambda e: e.matmul(ps_m[:, 0:8], ones32[0:1, :], TAUR[0:1, :], start=True, stop=True),
        reads=("ones32", "taur"), writes=("pp0",))
    P.v(lambda e: e.tensor_copy(TAUB[:, :], ps_m[:, 0:8]), reads=("pp0",), writes=("taub",))
    for tt in range(16):
        sl = slice(tt * 8, (tt + 1) * 8)
        P.v(lambda e, sl=sl: e.tensor_tensor(GT[:, sl], VT[:, sl], TAUB[:, :], ALU.is_ge),
            reads=("vt", "taub"), writes=("gt",))
        P.v(lambda e, sl=sl: e.scalar_tensor_tensor(GT[:, sl], GT[:, sl], 0.5, VT[:, sl],
                                                    ALU.mult, ALU.mult),
            reads=("gt", "vt"), writes=("gt",))
    for tt in range(16):
        P.v(lambda e, tt=tt: e.tensor_copy(GTB[:, tt * 8:(tt + 1) * 8] if False else GTB[:, 0:8],
                                           GT[:, tt * 8:(tt + 1) * 8]),
            reads=("gt",), writes=("gtb",))
        P.t(lambda e, tt=tt: e.matmul(ps_t[0:8, 0:128], GTB[:, 0:8], IDENTF[:, :],
                                      start=True, stop=True),
            reads=("gtb", "identf"), writes=("pp7",))
        P.v(lambda e, tt=tt: e.tensor_copy(GA[0:8, tt * 128:(tt + 1) * 128], ps_t[0:8, 0:128]),
            reads=("pp7",), writes=("ga",))

    for c in range(NTC):
        for j in range(NDT):
            P.dma("y", lambda e, j=j, c=c: e.dma_start(
                    out=XC[:, 0:TC], in_=x3_in[j * 128:(j + 1) * 128, c * TC:(c + 1) * TC]),
                  writes=("xc0",))
            P.dma("y", lambda e, j=j, c=c: e.dma_start(
                    out=XC[:, TC:2 * TC], in_=s1_in[j * 128:(j + 1) * 128, c * TC:(c + 1) * TC]),
                  writes=("xc1",))
            P.v(lambda e, j=j, c=c: e.scalar_tensor_tensor(
                    ACC[:, j * T + c * TC: j * T + (c + 1) * TC], XC[:, TC:2 * TC], 0.5, XC[:, 0:TC],
                    ALU.mult, ALU.add),
                reads=("xc1", "xc0"), writes=("acc{}_{}".format(j, c),))

    for ex in range(8):
        for j in range(NDT):
            P.dma("g", lambda e, ex=ex, j=j: e.dma_start(
                    out=W1E[:, j * 2048:(j + 1) * 2048],
                    in_=w1_in[ex * DM + j * 128: ex * DM + (j + 1) * 128, :]),
                  writes=("w1e{}".format(j),))
        for ft in range(16):
            P.dma("g", lambda e, ex=ex, ft=ft: e.dma_start(
                    out=W2E[:, ft * DM:(ft + 1) * DM],
                    in_=w2_in[ex * 2048 + ft * 128: ex * 2048 + (ft + 1) * 128, :]),
                  writes=("w2e{}".format(ft),))
        for c in range(NTC):
            P.t(lambda e, ex=ex, c=c: e.matmul(
                    ps_g[:, :], ESEL3[:, ex * 128:(ex + 1) * 128], GA[0:8, c * TC:(c + 1) * TC],
                    start=True, stop=True),
                reads=("esel3", "ga"), writes=("pp6",))
            for j in range(NDT):
                P.v(lambda e, j=j, c=c: e.tensor_tensor(
                        ZG[:, j * TC:(j + 1) * TC], ZN[:, j * T + c * TC: j * T + (c + 1) * TC],
                        ps_g[:, :], ALU.mult),
                    reads=("zn{}".format(j), "pp6"), writes=("zg{}".format(j),))
            for ft in range(16):
                ps = ps_h[ft % 2]
                for j in range(NDT):
                    P.t(lambda e, ft=ft, j=j, ps=ps: e.matmul(
                            ps[:, :], W1E[:, j * 2048 + ft * 128: j * 2048 + (ft + 1) * 128],
                            ZG[:, j * TC:(j + 1) * TC],
                            start=(j == 0), stop=(j == NDT - 1)),
                        reads=("w1e{}".format(j), "zg{}".format(j)), writes=("pp{}".format(2 + ft % 2),))
                P.s(lambda e, ft=ft, ps=ps: e.activation(HB[:, ft * TC:(ft + 1) * TC], ps[:, :],
                                                         AF.Relu),
                    reads=("pp{}".format(2 + ft % 2),), writes=("hb{}".format(ft),))
            for j in range(NDT):
                ps = ps_o[j % 2]
                for ft in range(16):
                    P.t(lambda e, j=j, ft=ft, ps=ps: e.matmul(
                            ps[:, :], W2E[:, ft * DM + j * 128: ft * DM + (j + 1) * 128],
                            HB[:, ft * TC:(ft + 1) * TC],
                            start=(ft == 0), stop=(ft == 15)),
                        reads=("w2e{}".format(ft), "hb{}".format(ft)), writes=("pp{}".format(4 + j % 2),))
                P.v(lambda e, j=j, c=c, ps=ps: e.tensor_tensor(
                        ACC[:, j * T + c * TC: j * T + (c + 1) * TC],
                        ACC[:, j * T + c * TC: j * T + (c + 1) * TC], ps[:, :], ALU.add),
                    reads=("acc{}_{}".format(j, c), "pp{}".format(4 + j % 2)),
                    writes=("acc{}_{}".format(j, c),))

    for c in range(NTC):
        for j in range(NDT):
            P.dma("y", lambda e, j=j, c=c: e.dma_start(
                    out=x_out[j * 128:(j + 1) * 128, c * TC:(c + 1) * TC],
                    in_=ACC[:, j * T + c * TC: j * T + (c + 1) * TC]),
                  reads=("acc{}_{}".format(j, c),), writes=("out:x{}_{}".format(c, j),))
    keys = ["out:x{}_{}".format(c, j) for c in range(NTC) for j in range(NDT)]
    P.op("y", lambda e: e.nop(), reads=tuple(keys))


def build_moe_launch():
    nc = bass.Bass("TRN2", target_bir_lowering=False)
    x3_in = nc.dram_tensor("x3_in", [DM, T], F32, kind="ExternalInput").ap()
    s1_in = nc.dram_tensor("s1_in", [DM, T], F32, kind="ExternalInput").ap()
    zn_in = nc.dram_tensor("zn_in", [DM, T], BF16, kind="ExternalInput").ap()
    vbis_in = nc.dram_tensor("vbis_in", [128, 1024], F32, kind="ExternalInput").ap()
    vtok_in = nc.dram_tensor("vtok_in", [128, 128], F32, kind="ExternalInput").ap()
    w1_in = nc.dram_tensor("w1_in", [8 * DM, 2048], F32, kind="ExternalInput").ap()
    w2_in = nc.dram_tensor("w2_in", [8 * 2048, DM], F32, kind="ExternalInput").ap()
    gsel_in = nc.dram_tensor("gsel_in", [128, 8], F32, kind="ExternalInput").ap()
    gsel2_in = nc.dram_tensor("gsel2_in", [8, 128], F32, kind="ExternalInput").ap()
    esel3_in = nc.dram_tensor("esel3_in", [8, 1024], F32, kind="ExternalInput").ap()
    ident8_in = nc.dram_tensor("ident8_in", [8, 8], F32, kind="ExternalInput").ap()
    identf_in = nc.dram_tensor("identf_in", [128, 128], F32, kind="ExternalInput").ap()
    x_out = nc.dram_tensor("x_out", [DM, T], F32, kind="ExternalOutput").ap()

    io = dict(x3_in=x3_in, s1_in=s1_in, zn_in=zn_in, vbis_in=vbis_in, vtok_in=vtok_in,
              w1_in=w1_in, w2_in=w2_in, gsel_in=gsel_in, gsel2_in=gsel2_in, esel3_in=esel3_in,
              ident8_in=ident8_in, identf_in=identf_in, x_out=x_out)
    st = contextlib.ExitStack()
    E = st.enter_context
    P = Prog(nc)
    plan_moe(P, nc, E, io)
    block = E(nc.Block())
    P.finalize(block, st)
    st.close()
    return nc


def moe_consts():
    gsel = np.zeros((128, 8), np.float32)
    gsel2 = np.zeros((8, 128), np.float32)
    for p in range(128):
        gsel[p, p // 16] = 1.0
        gsel2[p // 16, p] = 1.0
    esel3 = np.zeros((8, 1024), np.float32)
    for ex in range(8):
        esel3[ex, ex * 128:(ex + 1) * 128] = 1.0
    return dict(gsel_in=gsel, gsel2_in=gsel2, esel3_in=esel3,
                ident8_in=np.eye(8, dtype=np.float32),
                identf_in=np.eye(128, dtype=np.float32))


def run_moe(x3t, s1t, znt, vlocs, w1, w2):
    vglob = np.concatenate([vloc_decode(vlocs[b]) for b in range(8)], axis=0)
    vbis = moe_host_layouts(vglob)
    consts = moe_consts()
    w1h = np.ascontiguousarray(np.asarray(w1, np.float32).reshape(8 * DM, 2048))
    w2h = np.ascontiguousarray(np.asarray(w2, np.float32).reshape(8 * 2048, DM))
    nc = build_moe_launch()
    in_maps = [dict(consts, x3_in=x3t[b], s1_in=s1t[b], zn_in=znt[b],
                    vbis_in=vbis, vtok_in=vlocs[b], w1_in=w1h, w2_in=w2h)
               for b in range(8)]
    res = run_bass_kernel_spmd(nc, in_maps, core_ids=list(range(8)))
    return [res.results[b]["x_out"] for b in range(8)]



def _np(a):
    return np.ascontiguousarray(np.asarray(a))


def _prep_l0ffn(inp):
    w1t = np.ascontiguousarray(_np(inp["L0_ffn_W1"]).T)
    w2t = np.ascontiguousarray(_np(inp["L0_ffn_W2"]).T)
    b1 = np.ascontiguousarray(_np(inp["L0_ffn_b1"]).reshape(32, 128).T)
    b2 = np.ascontiguousarray(_np(inp["L0_ffn_b2"]).reshape(8, 128).T)
    return w1t, w2t, b1, b2


def run_l0ffn(xt, inp, tern):
    w1t, w2t, b1, b2 = _prep_l0ffn(inp)
    nc = build_l0ffn_launch(with_ternarize=tern is None)
    if tern is None:
        in_maps = [{"x_in": xt[b], "w1t_in": w1t, "w2t_in": w2t, "b1_in": b1, "b2_in": b2}
                   for b in range(8)]
    else:
        in_maps = [{"x_in": xt[b], "w1bf_in": tern[0][b], "w2bf_in": tern[1][b],
                    "b1_in": b1, "b2_in": b2} for b in range(8)]
    res = run_bass_kernel_spmd(nc, in_maps, core_ids=list(range(8)))
    xt2 = [res.results[b]["x_out"] for b in range(8)]
    if tern is None:
        tern = ([res.results[b]["w1bf_out"] for b in range(8)],
                [res.results[b]["w2bf_out"] for b in range(8)])
    return xt2, tern


def run_l1ffn(xt, inp, idx):
    C = DM
    wqkv = _np(inp["L1_ffn_w_qkv"]).astype(np.float32)
    wo = _np(inp["L1_ffn_w_o"]).astype(np.float32)
    g1 = _np(inp["L1_ffn_g1"]).astype(np.float32)
    g2 = _np(inp["L1_ffn_g2"]).astype(np.float32)
    wrt = _np(inp["L1_ffn_w_router"]).astype(np.float32)
    w1 = _np(inp["L1_ffn_w1"]).astype(np.float32)
    w2 = _np(inp["L1_ffn_w2"]).astype(np.float32)
    s1 = run_attn(xt, "l1mha", wqkv[:, 0:C].copy(), wqkv[:, C:2 * C].copy(),
                  wqkv[:, 2 * C:].copy(), wo, None, None, None, None, g1=g1)
    zn, vloc = run_router(s1, wrt, g2)
    x4 = run_moe(xt, s1, zn, vloc, w1, w2)
    return x4



def permsel_const():
    m = np.zeros((128, 128), np.float32)
    for p in range(128):
        e, tt = p // 16, p % 16
        m[tt * 8 + e, p] = 1.0
    return m


class _NcProxy:

    def __init__(self, nc):
        self._nc = nc
        self._n = 0

    def sbuf_tensor(self, name, shape, dtype, **kw):
        self._n += 1
        return self._nc.sbuf_tensor("{}_p{}".format(name, self._n), shape, dtype, **kw)

    def psum_tensor(self, name, shape, dtype=mybir.dt.float32, **kw):
        self._n += 1
        return self._nc.psum_tensor("{}_p{}".format(name, self._n), shape, dtype, **kw)

    def __getattr__(self, k):
        return getattr(self._nc, k)


def build_fused():
    nc0 = bass.Bass("TRN2", target_bir_lowering=False)
    nc = _NcProxy(nc0)
    DT = nc.dram_tensor

    x0 = DT("x0", [DM, T], F32, kind="ExternalInput").ap()
    x_out = DT("x_out", [DM, T], F32, kind="ExternalOutput").ap()

    f_w1t = DT("f_w1t", [DM, 4096], F32, kind="ExternalInput").ap()
    f_w2t = DT("f_w2t", [4096, DM], F32, kind="ExternalInput").ap()
    f_b1 = DT("f_b1", [128, 32], F32, kind="ExternalInput").ap()
    f_b2 = DT("f_b2", [128, 8], F32, kind="ExternalInput").ap()
    w1bf = DT("w1bf", [DM, 4096], BF16, kind="Internal").ap()
    w2bf = DT("w2bf", [4096, DM], BF16, kind="Internal").ap()
    minus1_in = DT("minus1_in", [1, 128], F32, kind="ExternalInput").ap()
    psel2_in = DT("psel2_in", [128, 128], F32, kind="ExternalInput").ap()
    bsel_in = DT("bsel_in", [128, 128], F32, kind="ExternalInput").ap()
    ident_in = DT("ident_in", [128, 128], F32, kind="ExternalInput").ap()
    a0_wpack = DT("a0_wpack", [DM, 4 * DM], F32, kind="ExternalInput").ap()
    a0_bpack = DT("a0_bpack", [128, 32], F32, kind="ExternalInput").ap()
    a0_bvrow = DT("a0_bvrow", [1, DM], F32, kind="ExternalInput").ap()
    a0_hostm = DT("a0_hostm", [128, 1152], F32, kind="ExternalInput").ap()
    m_wpack = DT("m_wpack", [DM, 4 * DM], F32, kind="ExternalInput").ap()
    m_bpack = DT("m_bpack", [128, 32], F32, kind="ExternalInput").ap()
    m_bvrow = DT("m_bvrow", [1, DM], F32, kind="ExternalInput").ap()
    m_hostm = DT("m_hostm", [128, 1024], F32, kind="ExternalInput").ap()
    m_g1 = DT("m_g1", [128, 8], F32, kind="ExternalInput").ap()
    m_ru = DT("m_ru", [64 if False else 128, T], F32, kind="ExternalInput").ap()
    m_rw = DT("m_rw", [128, T], F32, kind="ExternalInput").ap()
    m_m1 = DT("m_m1", [128, 128], F32, kind="ExternalInput").ap()
    m_m2 = DT("m_m2", [128, 128], F32, kind="ExternalInput").ap()
    r_wr = DT("r_wr", [DM, 8], F32, kind="ExternalInput").ap()
    r_g2 = DT("r_g2", [128, 8], F32, kind="ExternalInput").ap()
    o_w1 = DT("o_w1", [8 * DM, 2048], F32, kind="ExternalInput").ap()
    o_w2 = DT("o_w2", [8 * 2048, DM], F32, kind="ExternalInput").ap()
    o_gsel = DT("o_gsel", [128, 8], F32, kind="ExternalInput").ap()
    o_gsel2 = DT("o_gsel2", [8, 128], F32, kind="ExternalInput").ap()
    o_esel3 = DT("o_esel3", [8, 1024], F32, kind="ExternalInput").ap()
    o_ident8 = DT("o_ident8", [8, 8], F32, kind="ExternalInput").ap()
    o_permsel = DT("o_permsel", [128, 128], F32, kind="ExternalInput").ap()
    a1_wpack = DT("a1_wpack", [DM, 4 * DM], F32, kind="ExternalInput").ap()
    a1_bpack = DT("a1_bpack", [128, 32], F32, kind="ExternalInput").ap()
    a1_bvrow = DT("a1_bvrow", [1, DM], F32, kind="ExternalInput").ap()
    a1_hostm = DT("a1_hostm", [128, 1536], F32, kind="ExternalInput").ap()
    a1_gb = DT("a1_gb", [128, 4], F32, kind="ExternalInput").ap()
    x1 = DT("x1", [DM, T], F32, kind="Internal").ap()
    x2 = DT("x2", [DM, T], F32, kind="Internal").ap()
    x3 = DT("x3", [DM, T], F32, kind="Internal").ap()
    x4 = DT("x4", [DM, T], F32, kind="Internal").ap()
    x5 = DT("x5", [DM, T], F32, kind="Internal").ap()
    s1a = DT("s1a", [DM, T], F32, kind="Internal").ap()
    s1b = DT("s1b", [DM, T], F32, kind="Internal").ap()
    zna = DT("zna", [DM, T], BF16, kind="Internal").ap()
    znb = DT("znb", [DM, T], BF16, kind="Internal").ap()
    za = DT("za", [DM, T], F32, kind="Internal").ap()
    zb = DT("zb", [DM, T], F32, kind="Internal").ap()
    agi_a = DT("agi_a", [128, 128], F32, kind="Internal")
    ago_a = DT("ago_a", [1024, 128], F32, kind="Internal")
    agi_b = DT("agi_b", [128, 128], F32, kind="Internal")
    ago_b = DT("ago_b", [1024, 128], F32, kind="Internal")

    master = contextlib.ExitStack()
    P = Prog(nc)

    def phase(fn):
        st = contextlib.ExitStack()
        fn(st.enter_context)
        st.close()
        P.barrier()

    attn_common = dict(minus1_in=minus1_in, psel2_in=psel2_in, bsel_in=bsel_in,
                       ident_in=ident_in)

    phase(lambda E: plan_l0ffn(P, nc, E, dict(
        x_in=x0, x_out=x1, w1t_in=f_w1t, w2t_in=f_w2t, w1bf_out=w1bf, w2bf_out=w2bf,
        b1_in=f_b1, b2_in=f_b2), True))
    phase(lambda E: plan_attn(P, nc, E, dict(
        attn_common, x_in=x1, x_out=x2, wpack=a0_wpack, bpack=a0_bpack, bvrow=a0_bvrow,
        hostm_in=a0_hostm), "l0"))
    phase(lambda E: plan_l0ffn(P, nc, E, dict(
        x_in=x2, x_out=x3, w1bf_in=w1bf, w2bf_in=w2bf, b1_in=f_b1, b2_in=f_b2), False))

    def l1ffn(xin, xout, s1, zn, zz, agi, ago):
        phase(lambda E: plan_attn(P, nc, E, dict(
            attn_common, x_in=xin, x_out=s1, wpack=m_wpack, bpack=m_bpack, bvrow=m_bvrow,
            hostm_in=m_hostm, g1_in=m_g1, zbuf=zz, ru=m_ru, rw=m_rw, m1=m_m1, m2=m_m2),
            "l1mha"))
        phase(lambda E: plan_router(P, nc, E, dict(
            x_in=s1, wr_in=r_wr, g2_in=r_g2, zn_out=zn, vloc_out=agi.ap())))

        def moe_phase(E):
            P.cc(lambda e: e.collective_compute(
                    "AllGather", ALU.bypass, replica_groups=[list(range(8))],
                    ins=[agi.ap().opt()], outs=[ago.ap().opt()]),
                 reads=("out:vloc",), writes=("cc_done",))
            plan_moe(P, nc, E, dict(
                x3_in=xin, s1_in=s1, zn_in=zn, vtok_in=agi.ap(),
                w1_in=o_w1, w2_in=o_w2, gsel_in=o_gsel, gsel2_in=o_gsel2, esel3_in=o_esel3,
                ident8_in=o_ident8, identf_in=ident_in, x_out=xout),
                ago=ago.ap(), permsel=o_permsel, idf32=ident_in)

        phase(moe_phase)

    l1ffn(x3, x4, s1a, zna, za, agi_a, ago_a)
    phase(lambda E: plan_attn(P, nc, E, dict(
        attn_common, x_in=x4, x_out=x5, wpack=a1_wpack, bpack=a1_bpack, bvrow=a1_bvrow,
        hostm_in=a1_hostm, gb_in=a1_gb), "l1attn"))
    l1ffn(x5, x_out, s1b, znb, zb, agi_b, ago_b)

    block = master.enter_context(nc.Block())
    P.finalize(block, master)
    master.close()
    return nc


def kernel(**inp):
    x = _np(inp["L0_attn_x"]).astype(np.float32, copy=False)
    xt = [np.ascontiguousarray(x[b].T) for b in range(8)]
    C = DM

    im = {}
    w1t, w2t, b1p, b2p = _prep_l0ffn(inp)
    im.update(f_w1t=w1t, f_w2t=w2t, f_b1=b1p, f_b2=b2p)

    def attn_pack(wq, wk, wv, wo, bq, bk, bv, bo, permute=False):
        if permute:
            perm = rope_perm()
            wq = wq[:, perm]
            wk = wk[:, perm]
        wpackh = np.ascontiguousarray(np.concatenate([wq, wk, wv, wo], axis=1).astype(np.float32))
        z8 = np.zeros(DM, np.float32)
        bq = z8 if bq is None else np.asarray(bq, np.float32)
        bk = z8 if bk is None else np.asarray(bk, np.float32)
        bv = z8 if bv is None else np.asarray(bv, np.float32)
        bo = z8 if bo is None else np.asarray(bo, np.float32)
        bpackh = np.zeros((128, 32), np.float32)
        bpackh[:, 0:8] = bq.reshape(8, 128).T
        bpackh[:, 8:16] = bk.reshape(8, 128).T
        bpackh[:, 24:32] = bo.reshape(8, 128).T
        return wpackh, bpackh, np.ascontiguousarray(bv.reshape(1, DM))

    hc0 = attn_host_consts("l0")
    im.update(minus1_in=hc0["minus1_in"], psel2_in=hc0["psel2_in"], bsel_in=hc0["bsel_in"],
              ident_in=hc0["ident_in"], a0_hostm=hc0["hostm_in"])
    wp, bp, bv_ = attn_pack(_np(inp["L0_attn_Wq"]).astype(np.float32),
                            _np(inp["L0_attn_Wk"]).astype(np.float32),
                            _np(inp["L0_attn_Wv"]).astype(np.float32),
                            _np(inp["L0_attn_Wo"]).astype(np.float32),
                            _np(inp["L0_attn_bq"]), _np(inp["L0_attn_bk"]),
                            _np(inp["L0_attn_bv"]), _np(inp["L0_attn_bo"]))
    im.update(a0_wpack=wp, a0_bpack=bp, a0_bvrow=bv_)

    hcm = attn_host_consts("l1mha")
    wqkv = _np(inp["L1_ffn_w_qkv"]).astype(np.float32)
    wp, bp, bv_ = attn_pack(wqkv[:, 0:C].copy(), wqkv[:, C:2 * C].copy(), wqkv[:, 2 * C:].copy(),
                            _np(inp["L1_ffn_w_o"]).astype(np.float32),
                            None, None, None, None, permute=True)
    im.update(m_wpack=wp, m_bpack=bp, m_bvrow=bv_, m_hostm=hcm["hostm_in"],
              m_ru=hcm["ru"], m_rw=hcm["rw"], m_m1=hcm["m1"], m_m2=hcm["m2"],
              m_g1=np.ascontiguousarray(_np(inp["L1_ffn_g1"]).astype(np.float32).reshape(8, 128).T))

    im.update(r_wr=np.ascontiguousarray(_np(inp["L1_ffn_w_router"]).astype(np.float32)),
              r_g2=np.ascontiguousarray(_np(inp["L1_ffn_g2"]).astype(np.float32).reshape(8, 128).T))
    mc = moe_consts()
    im.update(o_w1=np.ascontiguousarray(_np(inp["L1_ffn_w1"]).astype(np.float32).reshape(8 * DM, 2048)),
              o_w2=np.ascontiguousarray(_np(inp["L1_ffn_w2"]).astype(np.float32).reshape(8 * 2048, DM)),
              o_gsel=mc["gsel_in"], o_gsel2=mc["gsel2_in"], o_esel3=mc["esel3_in"],
              o_ident8=mc["ident8_in"], o_permsel=permsel_const())

    hc1 = attn_host_consts("l1attn")
    wqkv1 = _np(inp["L1_attn_W_qkv"]).astype(np.float32)
    bqkv1 = _np(inp["L1_attn_b_qkv"]).astype(np.float32)
    wp, bp, bv_ = attn_pack(wqkv1[:, 0:C].copy(), wqkv1[:, C:2 * C].copy(), wqkv1[:, 2 * C:].copy(),
                            _np(inp["L1_attn_W_out"]).astype(np.float32),
                            bqkv1[0:C], bqkv1[C:2 * C], bqkv1[2 * C:],
                            _np(inp["L1_attn_b_out"]).astype(np.float32))
    gq = _np(inp["L1_attn_q_gamma"]).astype(np.float32)
    gqb = _np(inp["L1_attn_q_beta"]).astype(np.float32)
    gk = _np(inp["L1_attn_k_gamma"]).astype(np.float32)
    gkb = _np(inp["L1_attn_k_beta"]).astype(np.float32)
    gbh = np.zeros((128, 4), np.float32)
    gbh[:, 0] = np.tile(gq, 2)
    gbh[:, 1] = np.tile(gqb, 2)
    gbh[:, 2] = np.tile(gk, 2)
    gbh[:, 3] = np.tile(gkb, 2)
    im.update(a1_wpack=wp, a1_bpack=bp, a1_bvrow=bv_, a1_hostm=hc1["hostm_in"], a1_gb=gbh)

    nc = build_fused()
    in_maps = [dict(im, x0=xt[b]) for b in range(8)]
    res = run_bass_kernel_spmd(nc, in_maps, core_ids=list(range(8)))
    out = np.stack([np.ascontiguousarray(res.results[b]["x_out"].T) for b in range(8)])
    return out


# version 04
# speedup vs baseline: 1.1865x; 1.1865x over previous
import contextlib
import numpy as np
import ml_dtypes
import concourse.bass as bass
import concourse.mybir as mybir
from concourse.bass_utils import run_bass_kernel_spmd

F32 = mybir.dt.float32
BF16 = mybir.dt.bfloat16
I32 = mybir.dt.int32
AF = mybir.ActivationFunctionType
ALU = mybir.AluOpType
AX = mybir.AxisListType

BF16NP = ml_dtypes.bfloat16

T = 2048
DM = 1024
NDT = DM // 128
TC = 512
NTC = T // TC
ROUND_C = float(np.float32(1.5 * 2.0**23))

ENGINES = ("t", "v", "s", "g", "y")
DEBUG_TAPS = False
ATTN_DEBUG = None


class Op:
    __slots__ = ("eng", "emit", "deps", "inc", "idx", "need_inc", "cum", "is_cc", "cc_sem", "scope")

    def __init__(self, eng, emit, deps, inc, idx):
        self.eng, self.emit, self.deps, self.inc, self.idx = eng, emit, deps, inc, idx
        self.need_inc = False
        self.cum = 0
        self.is_cc = False
        self.cc_sem = None
        self.scope = None


class Prog:

    def __init__(self, nc):
        self.nc = nc
        self.items = {e: [] for e in ENGINES}
        self.last_writer = {}
        self.readers = {}
        self.sems = {}
        self.totals = {}
        self.scope = None

    def barrier(self):
        lasts = [self.items[e][-1] for e in ENGINES if self.items[e]]
        for e in ENGINES:
            o = Op(e, lambda en: en.nop(), list(lasts), 1, len(self.items[e]))
            o.scope = self.scope
            self.items[e].append(o)

    def cc(self, emit, reads=(), writes=()):
        o = self.op("g", emit, reads, writes, dma=False)
        o.is_cc = True
        return o

    def op(self, eng, emit, reads=(), writes=(), dma=False):
        deps = []
        for k in reads:
            w = self.last_writer.get(k)
            if w is not None:
                deps.append(w)
        for k in writes:
            w = self.last_writer.get(k)
            if w is not None:
                deps.append(w)
            deps.extend(self.readers.get(k, ()))
        o = Op(eng, emit, deps, 16 if dma else 1, len(self.items[eng]))
        o.scope = self.scope
        self.items[eng].append(o)
        for k in writes:
            self.last_writer[k] = o
            self.readers[k] = []
        for k in reads:
            self.readers.setdefault(k, []).append(o)
        return o

    def t(self, emit, reads=(), writes=()):
        return self.op("t", emit, reads, writes)

    def v(self, emit, reads=(), writes=()):
        return self.op("v", emit, reads, writes)

    def s(self, emit, reads=(), writes=()):
        return self.op("s", emit, reads, writes)

    def g(self, emit, reads=(), writes=()):
        return self.op("g", emit, reads, writes)

    def dma(self, eng, emit, reads=(), writes=()):
        return self.op(eng, emit, reads, writes, dma=True)

    @staticmethod
    def _pe_pe(d, o):
        return d.eng == "t" and o.eng == "t" and d.inc == 1

    def finalize(self, block, stack):
        for e in ENGINES:
            for o in self.items[e]:
                for d in o.deps:
                    if not self._pe_pe(d, o):
                        d.need_inc = True
                if o.inc == 16:
                    o.need_inc = True
        for e in ENGINES:
            c = 0
            for o in self.items[e]:
                if o.is_cc:
                    o.cc_sem = stack.enter_context(self.nc.semaphore("cc_sem{}".format(id(o) % 100000)))
                    o.need_inc = False
                    o.cum = c
                    continue
                if o.need_inc:
                    c += o.inc
                o.cum = c
            self.totals[e] = c
            if c > 0:
                self.sems[e] = stack.enter_context(self.nc.semaphore(f"sem_{e}"))

        prog = self

        def emit_stream(eng_handle, e):
            watermark = {e2: 0 for e2 in ENGINES}
            ccs_done = set()
            prev_dma_cum = 0
            cur_scope = [None, None]

            def set_scope(nm):
                if nm == cur_scope[0]:
                    return
                if cur_scope[0] is not None:
                    prog.nc.leave_named_scope(cur_scope[0], cur_scope[1], False)
                cur_scope[0] = nm
                cur_scope[1] = None
                if nm is not None:
                    cur_scope[1] = prog.nc.enter_named_scope(nm, False)[0]

            for o in prog.items[e]:
                set_scope(o.scope)
                if o.inc == 16 and prev_dma_cum > watermark[e]:
                    eng_handle.wait_ge(prog.sems[e], prev_dma_cum)
                    watermark[e] = prev_dma_cum
                for d in o.deps:
                    if prog._pe_pe(d, o):
                        continue
                    if d.is_cc:
                        if id(d) not in ccs_done:
                            eng_handle.wait_ge(d.cc_sem, 1)
                            ccs_done.add(id(d))
                        if d.cum > watermark[d.eng]:
                            eng_handle.wait_ge(prog.sems[d.eng], d.cum)
                            watermark[d.eng] = d.cum
                        continue
                    if d.cum > watermark[d.eng]:
                        eng_handle.wait_ge(prog.sems[d.eng], d.cum)
                        watermark[d.eng] = d.cum
                ins = o.emit(eng_handle)
                if o.is_cc:
                    ins.then_inc(o.cc_sem)
                elif o.need_inc:
                    ins.then_inc(prog.sems[e], o.inc)
                if o.inc == 16:
                    prev_dma_cum = o.cum
            set_scope(None)

        for e, attr in (("g", "gpsimd"), ("t", "tensor"), ("v", "vector"),
                        ("s", "scalar"), ("y", "sync")):
            if not self.items[e]:
                continue
            getattr(block, attr)(lambda eng, _e=e: emit_stream(eng, _e))



def emit_ln_chunk(P, tn, ctx, x_hbm, c, XN, xn_tag, rms_only=False, eps=1e-5):
    XB8, XSQ = ctx["XB8"], ctx["XSQ"]
    AROW, BROW, TROW = ctx["AROW"], ctx["BROW"], ctx["TROW"]
    ps_s1, ps_s2, ps_a, ps_b = ctx["ps_s1"], ctx["ps_s2"], ctx["ps_a"], ctx["ps_b"]
    ones_colbf, ones32 = ctx["ones_colbf"], ctx["ones32"]
    AB8, XBT = ctx["AB8"], ctx["XBT"]
    rs = slice(0, TC)
    P.dma("g", lambda e, c=c: e.dma_start(
            out=XB8[:, :].rearrange("p (j t) -> p j t", j=NDT),
            in_=x_hbm[:, c * TC:(c + 1) * TC].rearrange("(j p) t -> p j t", p=128)),
          writes=("xb8",))
    if not rms_only:
        for j in range(NDT):
            P.t(lambda e, j=j: e.matmul(ps_s1[0:1, :], ones_colbf[:, 0:1],
                                        XB8[:, j * TC:(j + 1) * TC],
                                        start=(j == 0), stop=(j == NDT - 1)),
                reads=("ones_colbf", "xb8"), writes=("ps_s1",))
    for j in range(NDT):
        slot = j % 2
        bcol = slice(slot * TC, (slot + 1) * TC)
        P.s(lambda e, j=j, bcol=bcol: e.activation(XSQ[:, bcol], XB8[:, j * TC:(j + 1) * TC],
                                                   AF.Square),
            reads=("xb8",), writes=("xsq{}".format(slot),))
        P.t(lambda e, bcol=bcol, j=j: e.matmul(ps_s2[0:1, :], ones_colbf[:, 0:1], XSQ[:, bcol],
                                               start=(j == 0), stop=(j == NDT - 1)),
            reads=("ones_colbf", "xsq{}".format(slot)), writes=("ps_s2",))
    if not rms_only:
        P.v(lambda e: e.tensor_scalar(BROW[0:1, rs], ps_s1[0:1, :], -1.0 / DM, None, ALU.mult),
            reads=("ps_s1",), writes=("brow0",))
        P.v(lambda e: e.tensor_scalar(AROW[0:1, rs], ps_s2[0:1, :], 1.0 / DM, None, ALU.mult),
            reads=("ps_s2",), writes=("arow0",))
        P.v(lambda e: e.tensor_tensor(TROW[0:1, 0:TC], BROW[0:1, rs], BROW[0:1, rs], ALU.mult),
            reads=("brow0",), writes=("trow",))
        P.v(lambda e: e.scalar_tensor_tensor(AROW[0:1, rs], AROW[0:1, rs], float(eps),
                                             TROW[0:1, 0:TC], ALU.add, ALU.subtract),
            reads=("arow0", "trow"), writes=("arow0",))
    else:
        P.v(lambda e: e.tensor_scalar(AROW[0:1, rs], ps_s2[0:1, :], 1.0 / DM, float(eps),
                                      ALU.mult, ALU.add),
            reads=("ps_s2",), writes=("arow0",))
    P.s(lambda e: e.activation(TROW[0:1, 0:TC], AROW[0:1, rs], AF.Ln),
        reads=("arow0",), writes=("trow",))
    P.s(lambda e: e.activation(AROW[0:1, rs], TROW[0:1, 0:TC], AF.Exp, scale=-0.5),
        reads=("trow",), writes=("arow0",))
    P.t(lambda e: e.matmul(ps_a[:, :], ones32[0:1, :], AROW[0:1, rs], start=True, stop=True),
        reads=("ones32", "arow0"), writes=("ps_a",))
    P.v(lambda e: e.tensor_copy(AB8[:, 0:TC], ps_a[:, :]), reads=("ps_a",), writes=("ab8a",))
    if not rms_only:
        P.t(lambda e: e.matmul(ps_b[:, :], ones32[0:1, :], BROW[0:1, rs], start=True, stop=True),
            reads=("ones32", "brow0"), writes=("ps_b",))
        P.v(lambda e: e.tensor_copy(AB8[:, TC:2 * TC], ps_b[:, :]), reads=("ps_b",), writes=("ab8b",))
    for j in range(NDT):
        dst = slice(j * TC, (j + 1) * TC)
        xsl = slice(j * TC, (j + 1) * TC)
        if rms_only:
            P.v(lambda e, xsl=xsl, dst=dst: e.tensor_tensor(XN[:, dst], XB8[:, xsl],
                                                            AB8[:, 0:TC], ALU.mult),
                reads=("xb8", "ab8a"), writes=("{}{}".format(xn_tag, j),))
        else:
            slot = j % 2
            tsl = slice(slot * TC, (slot + 1) * TC)
            P.v(lambda e, xsl=xsl, tsl=tsl: e.tensor_tensor(XBT[:, tsl], XB8[:, xsl],
                                                            AB8[:, TC:2 * TC], ALU.add),
                reads=("xb8", "ab8b"), writes=("xbt{}".format(slot),))
            P.v(lambda e, tsl=tsl, dst=dst: e.tensor_tensor(XN[:, dst], XBT[:, tsl],
                                                            AB8[:, 0:TC], ALU.mult),
                reads=("xbt{}".format(slot), "ab8a"), writes=("{}{}".format(xn_tag, j),))



def emit_ternarize_to_hbm(P, tn, ctx, WT_hbm, rows, cols, out_hbm, out_tag, final_mult):
    nrt = rows // 128
    WF, RS, SROW = ctx["WF"], ctx["RS"], ctx["SROW"]
    ASUM, STAGE = ctx["ASUM"], ctx["WSTG"]
    ones32, ones32col = ctx["ones32"], ctx["ones32col"]
    ps_w, ps_rs = ctx["ps_w"], ctx["ps_rs"]
    for r in range(nrt):
        slot = r % 2
        wsl = slice(slot * 4096, slot * 4096 + cols)
        P.dma("y", lambda e, r=r, wsl=wsl: e.dma_start(out=WF[:, wsl],
                                                       in_=WT_hbm[r * 128:(r + 1) * 128, :]),
              writes=("wf{}".format(slot),))
        P.v(lambda e, r=r, wsl=wsl: e.tensor_reduce(ASUM[:, r:r + 1], WF[:, wsl], AX.X, ALU.add,
                                                    apply_absolute_value=True),
            reads=("wf{}".format(slot),), writes=("asum",))
    P.t(lambda e: e.matmul(ps_w[0:1, 0:nrt], ones32col[:, 0:1], ASUM[:, 0:nrt],
                           start=True, stop=True),
        reads=("asum", "ones32col"), writes=("ps_a",))
    P.v(lambda e: e.tensor_reduce(SROW[0:1, 0:1], ps_w[0:1, 0:nrt], AX.X, ALU.add),
        reads=("ps_a",), writes=("srow",))
    P.v(lambda e: e.tensor_scalar(SROW[0:1, 0:1], SROW[0:1, 0:1], 1.0 / (rows * cols), 1e-8,
                                  ALU.mult, ALU.max),
        reads=("srow",), writes=("srow",))
    P.v(lambda e: e.reciprocal(SROW[0:1, 1:2], SROW[0:1, 0:1]),
        reads=("srow",), writes=("srow",))
    P.t(lambda e: e.matmul(ps_rs[:, 0:1], ones32[0:1, :], SROW[0:1, 1:2], start=True, stop=True),
        reads=("srow", "ones32"), writes=("ps_a",))
    P.v(lambda e: e.tensor_copy(RS[:, 0:1], ps_rs[:, 0:1]),
        reads=("ps_a",), writes=("rs",))
    for r in range(nrt):
        slot = r % 2
        wsl = slice(slot * 4096, slot * 4096 + cols)
        ssl = slice(slot * cols, (slot + 1) * cols)
        P.dma("y", lambda e, r=r, wsl=wsl: e.dma_start(out=WF[:, wsl],
                                                       in_=WT_hbm[r * 128:(r + 1) * 128, :]),
              writes=("wf{}".format(slot),))
        P.v(lambda e, wsl=wsl: e.tensor_scalar(WF[:, wsl], WF[:, wsl], RS[:, 0:1], ROUND_C,
                                               ALU.mult, ALU.add),
            reads=("wf{}".format(slot), "rs"), writes=("wf{}".format(slot),))
        P.v(lambda e, wsl=wsl: e.tensor_scalar(WF[:, wsl], WF[:, wsl], ROUND_C, 1.0,
                                               ALU.subtract, ALU.min),
            reads=("wf{}".format(slot),), writes=("wf{}".format(slot),))
        P.v(lambda e, wsl=wsl, ssl=ssl: e.tensor_scalar(STAGE[:, ssl], WF[:, wsl],
                                                        -1.0, final_mult, ALU.max, ALU.mult),
            reads=("wf{}".format(slot),), writes=("wstg{}".format(slot),))
        P.dma("y", lambda e, r=r, ssl=ssl: e.dma_start(out=out_hbm[r * 128:(r + 1) * 128, :],
                                                       in_=STAGE[:, ssl]),
              reads=("wstg{}".format(slot),), writes=(f"{out_tag}{r}",))



def plan_l0ffn(P, nc, E, io, with_ternarize):
    x_in = io.get("x_in")
    b1_in = io.get("b1_in")
    b2_in = io.get("b2_in")
    x_out = io.get("x_out")
    w1t_in = io.get("w1t_in")
    w2t_in = io.get("w2t_in")
    w1bf = io.get("w1bf_out") if with_ternarize else io.get("w1bf_in")
    w2bf = io.get("w2bf_out") if with_ternarize else io.get("w2bf_in")
    xn_dbg = io.get("xn_dbg")
    h_dbg = io.get("h_dbg")
    o_dbg = io.get("o_dbg")
    DBGF = E(nc.sbuf_tensor("DBGF", [128, 2 * TC], F32)) if DEBUG_TAPS else None
    XN = E(nc.sbuf_tensor("XN", [128, NDT * TC], BF16))
    XB8 = E(nc.sbuf_tensor("XB8", [128, NDT * TC], BF16))
    AB8 = E(nc.sbuf_tensor("AB8", [128, 2 * TC], BF16))
    XBT = E(nc.sbuf_tensor("XBT", [128, 2 * TC], BF16))
    XSQ = E(nc.sbuf_tensor("XSQ", [128, 2 * TC], BF16))
    XR = E(nc.sbuf_tensor("XR", [128, 2 * TC], F32))
    W1B = E(nc.sbuf_tensor("W1B", [128, NDT * 4096], BF16))
    BIGBF_T = E(nc.sbuf_tensor("BIGBF", [128, 32768], BF16))
    BIGBF = BIGBF_T[:, :]
    BIGB = BIGBF.bitcast(F32)
    HB = E(nc.sbuf_tensor("HB", [128, 32 * TC], BF16))
    TMPC = E(nc.sbuf_tensor("TMPC", [128, 2 * TC], F32))
    AROW = E(nc.sbuf_tensor("AROW", [1, TC], F32))
    BROW = E(nc.sbuf_tensor("BROW", [1, TC], F32))
    TROW = E(nc.sbuf_tensor("TROW", [1, TC], F32))
    SROW = E(nc.sbuf_tensor("SROW", [1, 16], F32))
    ASUM = E(nc.sbuf_tensor("ASUM", [128, 64], F32))
    RS = E(nc.sbuf_tensor("RS", [128, 2], F32))
    b1_sb = E(nc.sbuf_tensor("b1_sb", [128, 32], F32))
    b2_sb = E(nc.sbuf_tensor("b2_sb", [128, 8], F32))
    ones_colbf = E(nc.sbuf_tensor("ones_colbf", [128, 1], BF16))
    ones32 = E(nc.sbuf_tensor("ones32", [1, 128], F32))
    ones32col = E(nc.sbuf_tensor("ones32col", [128, 1], F32))

    ps_s1 = E(nc.psum_tensor("ps_s1", [128, TC], F32))
    ps_s2 = E(nc.psum_tensor("ps_s2", [128, TC], F32))
    ps_a = E(nc.psum_tensor("ps_a", [128, TC], F32))
    ps_b = E(nc.psum_tensor("ps_b", [128, TC], F32))
    ps_h = [E(nc.psum_tensor(f"ps_h{i}", [128, TC], F32)) for i in range(2)]
    ps_o = [E(nc.psum_tensor(f"ps_o{i}", [128, TC], F32)) for i in range(2)]

    ctx = dict(XN=XN, XB8=XB8, AB8=AB8, XBT=XBT, XSQ=XSQ, TMPC=TMPC,
               AROW=AROW, BROW=BROW, TROW=TROW,
               SROW=SROW, ASUM=ASUM, RS=RS,
               WF=BIGB[:, 0:8192], WSTG=BIGBF[:, 16384:24576],
               ones_colbf=ones_colbf, ones32=ones32, ones32col=ones32col,
               ps_s1=ps_s1, ps_s2=ps_s2, ps_a=ps_a, ps_b=ps_b,
               ps_w=ps_a[0:1, 0:64], ps_rs=ps_a[:, 64:65])

    P.g(lambda e: e.memset(ones_colbf[:, :], 1.0), writes=("ones_colbf",))
    P.g(lambda e: e.memset(ones32[:, :], 1.0), writes=("ones32",))
    P.g(lambda e: e.memset(ones32col[:, :], 1.0), writes=("ones32col",))
    P.dma("g", lambda e: e.dma_start(out=b1_sb[:, :], in_=b1_in[:, :]), writes=("b1_sb",))
    P.dma("g", lambda e: e.dma_start(out=b2_sb[:, :], in_=b2_in[:, :]), writes=("b2_sb",))
    P.v(lambda e: e.tensor_scalar(b2_sb[:, :], b2_sb[:, :], 0.5, None, ALU.mult),
        reads=("b2_sb",), writes=("b2_sb",))

    if with_ternarize:
        emit_ternarize_to_hbm(P, "tz1", ctx, w1t_in, DM, 4096, w1bf, "w1hbm", 1.0)
        emit_ternarize_to_hbm(P, "tz2", ctx, w2t_in, 4096, DM, w2bf, "w2hbm", 0.5)
        w1deps = lambda j: (f"w1hbm{j}",)
        w2deps = lambda r: (f"w2hbm{r}",)
        tz_keys = ("wf0", "wf1", "wstg0", "wstg1")
    else:
        w1deps = lambda j: ()
        w2deps = lambda r: ()
        tz_keys = ()

    P.op("y", lambda e: e.nop(), reads=tz_keys, writes=("bigfree",))

    P.dma("y", lambda e: e.dma_start(
            out=W1B[:, :].rearrange("p (j f) -> p j f", j=NDT),
            in_=w1bf.rearrange("(j p) f -> p j f", p=128)),
          reads=tuple(k for j in range(NDT) for k in w1deps(j)),
          writes=tuple("w1res{}".format(j) for j in range(NDT)))
    P.dma("y", lambda e: e.dma_start(
            out=BIGBF[:, 0:32 * DM].rearrange("p (f c) -> p f c", f=32),
            in_=w2bf.rearrange("(f p) c -> p f c", p=128)),
          reads=tuple(k for r in range(32) for k in w2deps(r)) + ("bigfree",),
          writes=tuple("w2res{}".format(r) for r in range(32)))

    NFT = 32
    for c in range(NTC):
        emit_ln_chunk(P, "ln", ctx, x_in, c, XN, "xn", rms_only=False, eps=1e-5)
        for ft in range(NFT):
            pslot = ft % 2
            ps = ps_h[pslot]
            for j in range(NDT):
                P.t(lambda e, ft=ft, j=j, ps=ps: e.matmul(
                        ps[:, :],
                        W1B[:, j * 4096 + ft * 128: j * 4096 + (ft + 1) * 128],
                        XN[:, j * TC:(j + 1) * TC],
                        start=(j == 0), stop=(j == NDT - 1)),
                    reads=(f"w1res{j}", f"xn{j}"), writes=(f"ps_h{pslot}",))
            P.s(lambda e, ft=ft, ps=ps, pslot=pslot: e.activation(
                    HB[:, ft * TC:(ft + 1) * TC], ps[:, :], AF.Gelu, bias=b1_sb[:, ft:ft + 1]),
                reads=(f"ps_h{pslot}", "b1_sb"), writes=(f"HB{ft}",))
        if DEBUG_TAPS and c == 0:
            for j in range(NDT):
                sl = slice((j % 2) * TC, (j % 2 + 1) * TC)
                P.v(lambda e, j=j, sl=sl: e.tensor_copy(DBGF[:, sl], XN[:, j * TC:(j + 1) * TC]),
                    reads=(f"xn{j}",), writes=(f"dbg{j%2}",))
                P.dma("y", lambda e, j=j, sl=sl: e.dma_start(out=xn_dbg[:, j * TC:(j + 1) * TC],
                                                             in_=DBGF[:, sl]),
                      reads=(f"dbg{j%2}",), writes=(f"odbgxn{j}",))
            for ft in range(4):
                sl = slice((ft % 2) * TC, (ft % 2 + 1) * TC)
                P.v(lambda e, ft=ft, sl=sl: e.tensor_copy(DBGF[:, sl], HB[:, ft * TC:(ft + 1) * TC]),
                    reads=(f"HB{ft}",), writes=(f"dbg{ft%2}",))
                P.dma("y", lambda e, ft=ft, sl=sl: e.dma_start(out=h_dbg[:, ft * TC:(ft + 1) * TC],
                                                               in_=DBGF[:, sl]),
                      reads=(f"dbg{ft%2}",), writes=(f"odbgh{ft}",))
        for j in range(NDT):
            pslot = j % 2
            ps = ps_o[pslot]
            for ft in range(NFT):
                P.t(lambda e, j=j, ft=ft, ps=ps: e.matmul(
                        ps[:, :], BIGBF[:, ft * DM + j * 128: ft * DM + (j + 1) * 128],
                        HB[:, ft * TC:(ft + 1) * TC],
                        start=(ft == 0), stop=(ft == NFT - 1)),
                    reads=(f"w2res{ft}", f"HB{ft}"), writes=(f"ps_o{pslot}",))
            xslot = j % 2
            xsl = slice(xslot * TC, (xslot + 1) * TC)
            P.dma("y", lambda e, j=j, c=c, xsl=xsl: e.dma_start(
                    out=XR[:, xsl], in_=x_in[j * 128:(j + 1) * 128, c * TC:(c + 1) * TC]),
                  writes=("rx{}".format(xslot),))
            tslot = j % 2
            tc2 = slice(tslot * TC, (tslot + 1) * TC)
            P.v(lambda e, j=j, ps=ps, tc2=tc2, xsl=xsl: e.scalar_tensor_tensor(
                    TMPC[:, tc2], ps[:, :], b2_sb[:, j:j + 1], XR[:, xsl],
                    ALU.add, ALU.add),
                reads=(f"ps_o{pslot}", "b2_sb", "rx{}".format(xslot)), writes=("tmpc{}".format(tslot),))
            P.dma("y", lambda e, j=j, c=c, tc2=tc2: e.dma_start(
                    out=x_out[j * 128:(j + 1) * 128, c * TC:(c + 1) * TC], in_=TMPC[:, tc2]),
                  reads=("tmpc{}".format(tslot),), writes=(f"out:x{c}_{j}",))

    keys = [f"out:x{c}_{j}" for c in range(NTC) for j in range(NDT)]
    if DEBUG_TAPS:
        keys += [f"odbgxn{j}" for j in range(NDT)] + [f"odbgh{ft}" for ft in range(4)]
    if with_ternarize:
        keys += [f"w1hbm{j}" for j in range(NDT)] + [f"w2hbm{r}" for r in range(32)]
    P.op("y", lambda e: e.nop(), reads=tuple(keys))



def build_l0ffn_launch(with_ternarize):
    nc = bass.Bass("TRN2", target_bir_lowering=False)
    x_in = nc.dram_tensor("x_in", [DM, T], F32, kind="ExternalInput").ap()
    b1_in = nc.dram_tensor("b1_in", [128, 32], F32, kind="ExternalInput").ap()
    b2_in = nc.dram_tensor("b2_in", [128, 8], F32, kind="ExternalInput").ap()
    x_out = nc.dram_tensor("x_out", [DM, T], F32, kind="ExternalOutput").ap()
    if with_ternarize:
        w1t_in = nc.dram_tensor("w1t_in", [DM, 4096], F32, kind="ExternalInput").ap()
        w2t_in = nc.dram_tensor("w2t_in", [4096, DM], F32, kind="ExternalInput").ap()
        w1bf = nc.dram_tensor("w1bf_out", [DM, 4096], BF16, kind="ExternalOutput").ap()
        w2bf = nc.dram_tensor("w2bf_out", [4096, DM], BF16, kind="ExternalOutput").ap()
    else:
        w1bf = nc.dram_tensor("w1bf_in", [DM, 4096], BF16, kind="ExternalInput").ap()
        w2bf = nc.dram_tensor("w2bf_in", [4096, DM], BF16, kind="ExternalInput").ap()

    if DEBUG_TAPS:
        xn_dbg = nc.dram_tensor("xn_dbg", [128, NDT * TC], F32, kind="ExternalOutput").ap()
        h_dbg = nc.dram_tensor("h_dbg", [128, 4 * TC], F32, kind="ExternalOutput").ap()
        o_dbg = nc.dram_tensor("o_dbg", [128, 2 * TC], F32, kind="ExternalOutput").ap()

    io = dict(x_in=x_in, b1_in=b1_in, b2_in=b2_in, x_out=x_out)
    if with_ternarize:
        io.update(w1t_in=w1t_in, w2t_in=w2t_in, w1bf_out=w1bf, w2bf_out=w2bf)
    else:
        io.update(w1bf_in=w1bf, w2bf_in=w2bf)
    if DEBUG_TAPS:
        io.update(xn_dbg=xn_dbg, h_dbg=h_dbg, o_dbg=o_dbg)
    st = contextlib.ExitStack()
    E = st.enter_context
    P = Prog(nc)
    plan_l0ffn(P, nc, E, io, with_ternarize)
    block = E(nc.Block())
    P.finalize(block, st)
    st.close()
    return nc


NH = 16
HD = 64


def attn_host_consts(variant):
    c = {}
    c["minus1_in"] = np.full((1, 128), -1.0, np.float32)
    p2 = np.zeros((128, 128), np.float32)
    p2[0:64, 0] = 1.0
    p2[64:128, 64] = 1.0
    c["psel2_in"] = p2
    bs = np.zeros((128, 128), np.float32)
    bs[0, 0:64] = 1.0
    bs[64, 64:128] = 1.0
    c["bsel_in"] = bs
    c["ident_in"] = np.eye(128, dtype=np.float32)
    if variant == "l0":
        W, umin, width = 129, -511, 1152
    elif variant == "l1attn":
        W, umin, width = 512, -511, 1536
    else:
        W = None
    if W is not None:
        ik = np.arange(128)[:, None]
        u = np.arange(width)[None, :]
        qk = u + umin - ik
        c["hostm_in"] = np.where((qk >= 0) & (qk <= W - 1), 0.0, -1e30).astype(np.float32)
    if variant == "l1mha":
        ik = np.arange(128)[:, None]
        u = np.arange(1024)[None, :]
        c["hostm_in"] = np.where(u - 511 - ik >= 0, 0.0, -1e30).astype(np.float32)
        f = (np.arange(128) % 32).astype(np.float64)
        theta = 10000.0 ** (-2.0 * f / 64.0)
        t = np.arange(T, dtype=np.float64)
        ang = t[None, :] * theta[:, None]
        cos, sin = np.cos(ang), np.sin(ang)
        r_rows = np.zeros(128, bool)
        r_rows[0:32] = True
        r_rows[64:96] = True
        ru = np.where(r_rows[:, None], cos, sin)
        rw = np.where(r_rows[:, None], sin, cos)
        c["ru"] = ru.astype(np.float32)
        c["rw"] = rw.astype(np.float32)
        m1 = np.zeros((128, 128), np.float32)
        m2 = np.zeros((128, 128), np.float32)
        for m in range(128):
            if r_rows[m]:
                m1[m, m] = 1.0
                m1[m + 32, m] = -1.0
            else:
                m2[m - 32, m] = 1.0
                m2[m, m] = 1.0
        c["m1"] = m1
        c["m2"] = m2
    return c


def rope_perm():
    perm = np.zeros(DM, np.int64)
    for h in range(NH):
        base = h * 64
        perm[base + 0:base + 32] = base + np.arange(0, 64, 2)
        perm[base + 32:base + 64] = base + np.arange(1, 64, 2)
    return perm


def kt_range(variant, qc):
    if variant == "l0":
        lo = max(0, 4 * qc - 1)
    elif variant == "l1attn":
        lo = max(0, (qc * 512 - 511) // 128)
    else:
        lo = 0
    return list(range(lo, 4 * qc + 4))


def emit_ln_full(P, ctx, x_hbm, XN, rms_only=False, eps=1e-5, gfold=None):
    XB8, XSQ = ctx["XB8"], ctx["XSQ"]
    AROW, BROW, TROW = ctx["AROW"], ctx["BROW"], ctx["TROW"]
    ps_s1, ps_s2, ps_a, ps_b = ctx["ps_s1"], ctx["ps_s2"], ctx["ps_a"], ctx["ps_b"]
    ones_colbf, ones32 = ctx["ones_colbf"], ctx["ones32"]
    AB8 = ctx["AB8"]
    for c in range(NTC):
        rs = slice(0, TC)
        rkey = 0
        P.dma("g", lambda e, c=c: e.dma_start(
                out=XB8[:, :].rearrange("p (j t) -> p j t", j=NDT),
                in_=x_hbm[:, c * TC:(c + 1) * TC].rearrange("(j p) t -> p j t", p=128)),
              writes=("xb8",))
        if not rms_only:
            for j in range(NDT):
                P.t(lambda e, j=j: e.matmul(ps_s1[0:1, :], ones_colbf[:, 0:1],
                                            XB8[:, j * TC:(j + 1) * TC],
                                            start=(j == 0), stop=(j == NDT - 1)),
                    reads=("ones_colbf", "xb8"), writes=("ps_s1",))
        for j in range(NDT):
            slot = j % 2
            bcol = slice(slot * TC, (slot + 1) * TC)
            P.s(lambda e, j=j, bcol=bcol: e.activation(XSQ[:, bcol], XB8[:, j * TC:(j + 1) * TC],
                                                       AF.Square),
                reads=("xb8",), writes=("xsq{}".format(slot),))
            P.t(lambda e, bcol=bcol, j=j: e.matmul(ps_s2[0:1, :], ones_colbf[:, 0:1], XSQ[:, bcol],
                                                   start=(j == 0), stop=(j == NDT - 1)),
                reads=("ones_colbf", "xsq{}".format(slot)), writes=("ps_s2",))
        if not rms_only:
            P.v(lambda e, rs=rs: e.tensor_scalar(BROW[0:1, rs], ps_s1[0:1, :], -1.0 / DM, None, ALU.mult),
                reads=("ps_s1",), writes=("brow0",))
            P.v(lambda e, rs=rs: e.tensor_scalar(AROW[0:1, rs], ps_s2[0:1, :], 1.0 / DM, None, ALU.mult),
                reads=("ps_s2",), writes=("arow0",))
            P.v(lambda e, rs=rs: e.tensor_tensor(TROW[0:1, 0:TC], BROW[0:1, rs], BROW[0:1, rs], ALU.mult),
                reads=("brow0",), writes=("trow",))
            P.v(lambda e, rs=rs: e.scalar_tensor_tensor(AROW[0:1, rs], AROW[0:1, rs], float(eps),
                                                        TROW[0:1, 0:TC], ALU.add, ALU.subtract),
                reads=("arow0", "trow"), writes=("arow0",))
        else:
            P.v(lambda e, rs=rs: e.tensor_scalar(AROW[0:1, rs], ps_s2[0:1, :], 1.0 / DM, float(eps),
                                                 ALU.mult, ALU.add),
                reads=("ps_s2",), writes=("arow0",))
        P.s(lambda e, rs=rs: e.activation(TROW[0:1, 0:TC], AROW[0:1, rs], AF.Ln),
            reads=("arow0",), writes=("trow",))
        P.s(lambda e, rs=rs: e.activation(AROW[0:1, rs], TROW[0:1, 0:TC], AF.Exp, scale=-0.5),
            reads=("trow",), writes=("arow0",))
        P.t(lambda e, rs=rs: e.matmul(ps_a[:, :], ones32[0:1, :], AROW[0:1, rs], start=True, stop=True),
            reads=("ones32", "arow0"), writes=("ps_a",))
        P.v(lambda e: e.tensor_copy(AB8[:, 0:TC], ps_a[:, :]), reads=("ps_a",), writes=("ab8a",))
        if not rms_only:
            P.t(lambda e, rs=rs: e.matmul(ps_b[:, :], ones32[0:1, :], BROW[0:1, rs], start=True, stop=True),
                reads=("ones32", "brow0"), writes=("ps_b",))
            P.v(lambda e: e.tensor_copy(AB8[:, TC:2 * TC], ps_b[:, :]), reads=("ps_b",), writes=("ab8b",))
        for j in range(NDT):
            dst = slice(j * T + c * TC, j * T + (c + 1) * TC)
            xsl = slice(j * TC, (j + 1) * TC)
            if rms_only:
                P.v(lambda e, xsl=xsl, dst=dst: e.tensor_tensor(XN[:, dst], XB8[:, xsl],
                                                                AB8[:, 0:TC], ALU.mult),
                    reads=("xb8", "ab8a"), writes=(f"xn{j}_{c}",))
            else:
                slot = j % 2
                tsl = slice(slot * TC, (slot + 1) * TC)
                P.v(lambda e, xsl=xsl, tsl=tsl: e.tensor_tensor(ctx["XBT"][:, tsl], XB8[:, xsl],
                                                                AB8[:, TC:2 * TC], ALU.add),
                    reads=("xb8", "ab8b"), writes=("xbt{}".format(slot),))
                P.v(lambda e, tsl=tsl, dst=dst: e.tensor_tensor(XN[:, dst], ctx["XBT"][:, tsl],
                                                                AB8[:, 0:TC], ALU.mult),
                    reads=("xbt{}".format(slot), "ab8a"), writes=(f"xn{j}_{c}",))


def emit_rope(P, OUT, otag, RU, RW, M1, M2, TMPE, ps, pskey):
    for jt in range(NDT):
        for c in range(NTC):
            cs = slice(jt * T + c * TC, jt * T + (c + 1) * TC)
            tcs = slice(c * TC, (c + 1) * TC)
            P.v(lambda e, cs=cs, tcs=tcs: e.tensor_tensor(
                    TMPE[:, 0:TC], OUT[:, cs], RU[:, tcs], ALU.mult),
                reads=(otag + str(jt), "ropeu"), writes=("tmpe_a",))
            P.v(lambda e, cs=cs, tcs=tcs: e.tensor_tensor(
                    TMPE[:, TC:2 * TC], OUT[:, cs], RW[:, tcs], ALU.mult),
                reads=(otag + str(jt), "ropew"), writes=("tmpe_b",))
            P.t(lambda e: e.matmul(ps[:, :], M1[:, :], TMPE[:, 0:TC], start=True, stop=False),
                reads=("ropem1", "tmpe_a"), writes=(pskey,))
            P.t(lambda e: e.matmul(ps[:, :], M2[:, :], TMPE[:, TC:2 * TC], start=False, stop=True),
                reads=("ropem2", "tmpe_b"), writes=(pskey,))
            P.v(lambda e, cs=cs: e.tensor_copy(OUT[:, cs], ps[:, :]),
                reads=(pskey,), writes=(otag + str(jt),))


def emit_qkln(P, ctx, OUT, otag, GB, gcol, bcol, psel2, bself, SSC, TMPC2, pp, eps=1e-5):
    ps_q = pp[7]
    ps_r = pp[2]
    for jt in range(NDT):
        for c in range(NTC):
            cs = slice(jt * T + c * TC, jt * T + (c + 1) * TC)
            P.t(lambda e, cs=cs: e.matmul(ps_q[:, :], psel2[:, :], OUT[:, cs], start=True, stop=True),
                reads=("psel2", otag + str(jt)), writes=("ps_x",))
            P.v(lambda e: e.tensor_scalar(SSC[:, 0:TC], ps_q[:, :], 1.0 / HD, None, ALU.mult),
                reads=("ps_x",), writes=("sscA",))
            P.s(lambda e, cs=cs: e.activation(ctx["XSQ"][:, 0:TC], OUT[:, cs], AF.Square),
                reads=(otag + str(jt),), writes=("xsq0",))
            P.t(lambda e: e.matmul(ps_q[:, :], psel2[:, :], ctx["XSQ"][:, 0:TC], start=True, stop=True),
                reads=("psel2", "xsq0"), writes=("ps_x",))
            P.v(lambda e: e.tensor_tensor(SSC[:, TC:2 * TC], SSC[:, 0:TC], SSC[:, 0:TC], ALU.mult),
                reads=("sscA",), writes=("sscB",))
            P.v(lambda e: e.scalar_tensor_tensor(TMPC2[:, 0:TC], ps_q[:, :], 1.0 / HD,
                                                 SSC[:, TC:2 * TC], ALU.mult, ALU.subtract),
                reads=("ps_x", "sscB"), writes=("tmpdA",))
            P.v(lambda e: e.tensor_scalar(TMPC2[:, 0:TC], TMPC2[:, 0:TC], float(eps), 1e-37,
                                          ALU.add, ALU.max),
                reads=("tmpdA",), writes=("tmpdA",))
            P.s(lambda e: e.activation(TMPC2[:, TC:2 * TC], TMPC2[:, 0:TC], AF.Ln),
                reads=("tmpdA",), writes=("tmpdB",))
            P.s(lambda e: e.activation(TMPC2[:, 0:TC], TMPC2[:, TC:2 * TC], AF.Exp, scale=-0.5),
                reads=("tmpdB",), writes=("tmpdA",))
            P.t(lambda e: e.matmul(ps_r[:, :], bself[:, :], SSC[:, 0:TC], start=True, stop=True),
                reads=("bself", "sscA"), writes=("ps_a",))
            P.v(lambda e: e.tensor_copy(SSC[:, 0:TC], ps_r[:, :]),
                reads=("ps_a",), writes=("sscA",))
            P.t(lambda e: e.matmul(ps_r[:, :], bself[:, :], TMPC2[:, 0:TC], start=True, stop=True),
                reads=("bself", "tmpdA"), writes=("ps_a",))
            P.v(lambda e, cs=cs: e.tensor_tensor(TMPC2[:, TC:2 * TC], OUT[:, cs], SSC[:, 0:TC],
                                                 ALU.subtract),
                reads=(otag + str(jt), "sscA"), writes=("tmpdB",))
            P.v(lambda e: e.tensor_tensor(TMPC2[:, TC:2 * TC], TMPC2[:, TC:2 * TC], ps_r[:, :],
                                          ALU.mult),
                reads=("tmpdB", "ps_a"), writes=("tmpdB",))
            P.v(lambda e, cs=cs: e.tensor_scalar(OUT[:, cs], TMPC2[:, TC:2 * TC],
                                                 GB[:, gcol:gcol + 1], GB[:, bcol:bcol + 1],
                                                 ALU.mult, ALU.add),
                reads=("tmpdB", "gb"), writes=(otag + str(jt),))


def emit_ln_to_hbm(P, ctx, x_hbm, z_hbm, eps=1e-5):
    XB8, XSQ = ctx["XB8"], ctx["XSQ"]
    AROW, BROW, TROW = ctx["AROW"], ctx["BROW"], ctx["TROW"]
    ps_s1, ps_s2, ps_a, ps_b = ctx["ps_s1"], ctx["ps_s2"], ctx["ps_a"], ctx["ps_b"]
    ones_colbf, ones32 = ctx["ones_colbf"], ctx["ones32"]
    XC8, TMPC = ctx["XC8"], ctx["TMPC"]
    for c in range(NTC):
        rs = slice(0, TC)
        P.dma("g", lambda e, c=c: e.dma_start(
                out=XB8[:, :].rearrange("p (j t) -> p j t", j=NDT),
                in_=x_hbm[:, c * TC:(c + 1) * TC].rearrange("(j p) t -> p j t", p=128)),
              writes=("xb8",))

        for j in range(NDT):
            P.t(lambda e, j=j: e.matmul(ps_s1[0:1, :], ones_colbf[:, 0:1],
                                        XB8[:, j * TC:(j + 1) * TC],
                                        start=(j == 0), stop=(j == NDT - 1)),
                reads=("ones_colbf", "xb8"), writes=("ps_s1",))
        for j in range(NDT):
            slot = j % 2
            bcol = slice(slot * TC, (slot + 1) * TC)
            P.s(lambda e, j=j, bcol=bcol: e.activation(XSQ[:, bcol], XB8[:, j * TC:(j + 1) * TC],
                                                       AF.Square),
                reads=("xb8",), writes=("xsq{}".format(slot),))
            P.t(lambda e, bcol=bcol, j=j: e.matmul(ps_s2[0:1, :], ones_colbf[:, 0:1], XSQ[:, bcol],
                                                   start=(j == 0), stop=(j == NDT - 1)),
                reads=("ones_colbf", "xsq{}".format(slot)), writes=("ps_s2",))
        P.v(lambda e, rs=rs: e.tensor_scalar(BROW[0:1, rs], ps_s1[0:1, :], -1.0 / DM, None, ALU.mult),
            reads=("ps_s1",), writes=("brow0",))
        P.v(lambda e, rs=rs: e.tensor_scalar(AROW[0:1, rs], ps_s2[0:1, :], 1.0 / DM, None, ALU.mult),
            reads=("ps_s2",), writes=("arow0",))
        P.v(lambda e, rs=rs: e.tensor_tensor(TROW[0:1, 0:TC], BROW[0:1, rs], BROW[0:1, rs], ALU.mult),
            reads=("brow0",), writes=("trow",))
        P.v(lambda e, rs=rs: e.scalar_tensor_tensor(AROW[0:1, rs], AROW[0:1, rs], float(eps),
                                                    TROW[0:1, 0:TC], ALU.add, ALU.subtract),
            reads=("arow0", "trow"), writes=("arow0",))
        P.s(lambda e, rs=rs: e.activation(TROW[0:1, 0:TC], AROW[0:1, rs], AF.Ln),
            reads=("arow0",), writes=("trow",))
        P.s(lambda e, rs=rs: e.activation(AROW[0:1, rs], TROW[0:1, 0:TC], AF.Exp, scale=-0.5),
            reads=("trow",), writes=("arow0",))
        P.t(lambda e, rs=rs: e.matmul(ps_a[:, :], ones32[0:1, :], AROW[0:1, rs], start=True, stop=True),
            reads=("ones32", "arow0"), writes=("ps_a",))
        P.t(lambda e, rs=rs: e.matmul(ps_b[:, :], ones32[0:1, :], BROW[0:1, rs], start=True, stop=True),
            reads=("ones32", "brow0"), writes=("ps_b",))
        for jh in range(2):
            P.dma("y", lambda e, c=c, jh=jh: e.dma_start(
                    out=XC8[:, :].rearrange("p (j t) -> p j t", j=4),
                    in_=x_hbm[jh * 512:(jh + 1) * 512, c * TC:(c + 1) * TC].rearrange(
                        "(j p) t -> p j t", p=128)),
                  writes=("xc8",))
            for jj in range(4):
                slot = jj % 2
                tsl = slice(slot * TC, (slot + 1) * TC)
                xsl = slice(jj * TC, (jj + 1) * TC)
                P.v(lambda e, xsl=xsl, tsl=tsl: e.tensor_tensor(TMPC[:, tsl], XC8[:, xsl],
                                                                ps_b[:, :], ALU.add),
                    reads=("xc8", "ps_b"), writes=("tmpc{}".format(slot),))
                P.v(lambda e, xsl=xsl, tsl=tsl: e.tensor_tensor(XC8[:, xsl], TMPC[:, tsl],
                                                                ps_a[:, :], ALU.mult),
                    reads=("tmpc{}".format(slot), "ps_a"), writes=("xc8",))
            P.dma("y", lambda e, c=c, jh=jh: e.dma_start(
                    out=z_hbm[jh * 512:(jh + 1) * 512, c * TC:(c + 1) * TC].rearrange(
                        "(j p) t -> p j t", p=128),
                    in_=XC8[:, :].rearrange("p (j t) -> p j t", j=4)),
                  reads=("xc8",), writes=("z{}".format(c),))


def plan_attn(P, nc, E, io, variant):
    mwidth = {"l0": 1152, "l1attn": 1536, "l1mha": 1024}[variant]
    if variant == "l1mha":
        rope_in = {nm: io[nm] for nm in ("ru", "rw")}
        ropem_in = {nm: io[nm] for nm in ("m1", "m2")}
        g1_in = io["g1_in"]
        zbuf = io["zbuf"]
    if variant == "l1attn":
        gb_in = io["gb_in"]
    x_in = io.get("x_in")
    x_out = io.get("x_out")
    wpack = io.get("wpack")
    bpack = io.get("bpack")
    bvrow = io.get("bvrow")
    minus1_in = io.get("minus1_in")
    psel2_in = io.get("psel2_in")
    bsel_in = io.get("bsel_in")
    ident_in = io.get("ident_in")
    hostm_in = io.get("hostm_in")
    g1_in = io.get("g1_in")
    zbuf = io.get("zbuf")
    gb_in = io.get("gb_in")
    XN = E(nc.sbuf_tensor("XN", [128, NDT * T], BF16))
    QT = E(nc.sbuf_tensor("QT", [128, NDT * T], BF16))
    KT = E(nc.sbuf_tensor("KT", [128, NDT * T], BF16))
    VS = E(nc.sbuf_tensor("VS", [128, 16 * (NH * 65)], BF16))
    WB = E(nc.sbuf_tensor("WB", [128, NDT * DM], BF16))
    ATT = E(nc.sbuf_tensor("ATT", [128, NDT * TC], BF16))
    PT = E(nc.sbuf_tensor("PT", [128, 2 * TC], BF16))
    RECB = E(nc.sbuf_tensor("RECB", [128, TC], F32))
    SUM2 = E(nc.sbuf_tensor("SUM2", [128, TC], F32))
    NBT = E(nc.sbuf_tensor("NBT", [128, TC], BF16))
    KMX = E(nc.sbuf_tensor("KMX", [128, 16], F32))
    XB8 = E(nc.sbuf_tensor("XB8", [128, NDT * TC], BF16))
    AB8 = E(nc.sbuf_tensor("AB8", [128, 2 * TC], BF16))
    XBT = E(nc.sbuf_tensor("XBT", [128, 2 * TC], BF16))
    XSQ = E(nc.sbuf_tensor("XSQ", [128, 2 * TC], BF16))
    QSQ = XSQ
    if variant != "l1mha":
        XC = E(nc.sbuf_tensor("XC", [128, 2 * TC], F32))
        XR = E(nc.sbuf_tensor("XR", [128, 2 * TC], F32))
    else:
        XC = XR = None
    TMPC = E(nc.sbuf_tensor("TMPC", [128, 2 * TC], F32))
    TMPD = XB8[:, :].bitcast(F32)
    AROW = E(nc.sbuf_tensor("AROW", [1, TC], F32))
    BROW = E(nc.sbuf_tensor("BROW", [1, TC], F32))
    TROW = E(nc.sbuf_tensor("TROW", [1, TC], F32))
    b_sb = E(nc.sbuf_tensor("b_sb", [128, 32], F32))
    bv_sb = E(nc.sbuf_tensor("bv_sb", [1, DM], BF16))
    minus1 = E(nc.sbuf_tensor("minus1", [1, 128], BF16))
    psel2 = E(nc.sbuf_tensor("psel2", [128, 128], BF16))
    bself = E(nc.sbuf_tensor("bself", [128, 128], F32))
    IDENT = E(nc.sbuf_tensor("IDENT", [128, 128], BF16))
    ones_colbf = E(nc.sbuf_tensor("ones_colbf", [128, 1], BF16))
    onesrow = E(nc.sbuf_tensor("onesrow", [1, 128], BF16))
    ones32 = E(nc.sbuf_tensor("ones32", [1, 128], F32))
    HOSTM = E(nc.sbuf_tensor("HOSTM", [128, mwidth], BF16))
    if variant == "l1mha":
        XC8 = E(nc.sbuf_tensor("XC8", [128, 4 * TC], F32))
        ROPE = {nm: E(nc.sbuf_tensor("R" + nm, [128, T], BF16)) for nm in ("ru", "rw")}
        ROPEM = {nm: E(nc.sbuf_tensor("RM" + nm, [128, 128], BF16)) for nm in ("m1", "m2")}
        TMPE = AB8
        G1 = E(nc.sbuf_tensor("G1", [128, 8], F32))
    if variant == "l1attn":
        GB = E(nc.sbuf_tensor("GB", [128, 4], F32))
        SSC = TMPC

    pp = [E(nc.psum_tensor("pp{}".format(i), [128, TC], F32)) for i in range(8)]
    ps_s1, ps_s2, ps_a, ps_b = pp[0], pp[1], pp[2], pp[3]
    ps_st = [pp[4], pp[5]]
    ps_av = pp[6]
    ps_x = pp[7]

    ctx = dict(XN=XN, XB8=XB8, AB8=AB8, XBT=XBT, XSQ=XSQ, TMPC=TMPC,
               AROW=AROW, BROW=BROW, TROW=TROW,
               ones_colbf=ones_colbf, ones32=ones32,
               ps_s1=ps_s1, ps_s2=ps_s2, ps_a=ps_a, ps_b=ps_b)
    if variant == "l1mha":
        ctx["XC8"] = XC8

    P.g(lambda e: e.memset(ones_colbf[:, :], 1.0), writes=("ones_colbf",))
    P.g(lambda e: e.memset(ones32[:, :], 1.0), writes=("ones32",))
    P.g(lambda e: e.memset(onesrow[:, :], 1.0), writes=("onesrow",))
    P.dma("g", lambda e: e.dma_start(out=b_sb[:, :], in_=bpack[:, :]), writes=("b_sb",))
    P.dma("g", lambda e: e.dma_start(out=bv_sb[:, :], in_=bvrow[:, :]), writes=("bv_sb",))
    P.dma("g", lambda e: e.dma_start(out=minus1[:, :], in_=minus1_in[:, :]), writes=("minus1",))
    P.dma("g", lambda e: e.dma_start(out=psel2[:, :], in_=psel2_in[:, :]), writes=("psel2",))
    P.dma("y", lambda e: e.dma_start(out=bself[:, :], in_=bsel_in[:, :]), writes=("bself",))
    P.dma("g", lambda e: e.dma_start(out=IDENT[:, :], in_=ident_in[:, :]), writes=("ident",))
    P.dma("g", lambda e: e.dma_start(out=HOSTM[:, :], in_=hostm_in[:, :]), writes=("hostm",))
    if variant == "l1mha":
        for nm in ("ru", "rw"):
            P.dma("g", lambda e, nm=nm: e.dma_start(out=ROPE[nm][:, :], in_=rope_in[nm][:, :]),
                  writes=("rope" + nm[1],))
        for nm in ("m1", "m2"):
            P.dma("g", lambda e, nm=nm: e.dma_start(out=ROPEM[nm][:, :], in_=ropem_in[nm][:, :]),
                  writes=("rope" + nm,))
        P.dma("y", lambda e: e.dma_start(out=G1[:, :], in_=g1_in[:, :]), writes=("g1",))
    if variant == "l1attn":
        P.dma("y", lambda e: e.dma_start(out=GB[:, :], in_=gb_in[:, :]), writes=("gb",))
        P.v(lambda e: e.tensor_scalar(GB[:, 0:2], GB[:, 0:2], 0.125, None, ALU.mult),
            reads=("gb",), writes=("gb",))

    if variant == "l1mha":
        emit_ln_to_hbm(P, ctx, x_in, zbuf)
        base_hbm = zbuf
        base_dep = lambda c, j: ("z{}".format(c),)
        emit_ln_full(P, ctx, zbuf, XN, rms_only=True)
    else:
        base_hbm = x_in
        base_dep = lambda c, j: ()
        emit_ln_full(P, ctx, x_in, XN, rms_only=False)

    def load_w(part, fold_g1=False):
        P.dma("g", lambda e, part=part: e.dma_start(
                out=WB[:, :].rearrange("p (j c) -> p j c", j=NDT),
                in_=wpack[:, part * DM:(part + 1) * DM].rearrange("(j p) c -> p j c", p=128)),
              writes=tuple("wb{}".format(j) for j in range(NDT)))
        if fold_g1:
            for j in range(NDT):
                P.v(lambda e, j=j: e.tensor_scalar(WB[:, j * DM:(j + 1) * DM],
                                                   WB[:, j * DM:(j + 1) * DM],
                                                   G1[:, j:j + 1], None, ALU.mult),
                    reads=("wb{}".format(j), "g1"), writes=("wb{}".format(j),))

    fold = variant == "l1mha"

    def qkv_feat(part, OUT, otag, bias_col, scale):
        for qo in range(NDT):
            for c in range(NTC):
                for j in range(NDT):
                    P.t(lambda e, qo=qo, c=c, j=j: e.matmul(
                            ps_x[:, :], WB[:, j * DM + qo * 128:(j * DM + (qo + 1) * 128)],
                            XN[:, j * T + c * TC: j * T + (c + 1) * TC],
                            start=(j == 0), stop=(j == NDT - 1)),
                        reads=("wb{}".format(j), "xn{}_{}".format(j, c)), writes=("ps_x",))
                if bias_col is None:
                    P.v(lambda e, qo=qo, c=c: e.tensor_scalar(
                            OUT[:, qo * T + c * TC: qo * T + (c + 1) * TC], ps_x[:, :],
                            float(scale), None, ALU.mult),
                        reads=("ps_x",), writes=(otag + str(qo),))
                else:
                    P.v(lambda e, qo=qo, c=c: e.tensor_scalar(
                            OUT[:, qo * T + c * TC: qo * T + (c + 1) * TC], ps_x[:, :],
                            b_sb[:, bias_col + qo:bias_col + qo + 1], float(scale),
                            ALU.add, ALU.mult),
                        reads=("ps_x", "b_sb"), writes=(otag + str(qo),))

    has_bias = variant != "l1mha"
    load_w(0, fold)
    qkv_feat(0, QT, "qt", 0 if has_bias else None,
             1.0 if variant == "l1attn" else 0.125)
    load_w(1, fold)
    qkv_feat(1, KT, "kt", 8 if has_bias else None, 1.0)
    load_w(2, fold)
    P.g(lambda e: e.memset(VS[:, :], 1.0), writes=("vsmem",))
    for tt in range(16):
        for nchunk in range(2):
            ps = ps_st[nchunk]
            for j in range(NDT):
                P.t(lambda e, tt=tt, nchunk=nchunk, j=j, ps=ps: e.matmul(
                        ps[:, :], XN[:, j * T + tt * 128: j * T + (tt + 1) * 128],
                        WB[:, j * DM + nchunk * TC: j * DM + (nchunk + 1) * TC],
                        start=(j == 0), stop=False),
                    reads=("wb{}".format(j), "xn{}_{}".format(j, tt // 4)),
                    writes=("ps_st{}".format(nchunk),))
            P.t(lambda e, nchunk=nchunk, ps=ps: e.matmul(
                    ps[:, :], onesrow[0:1, :], bv_sb[0:1, nchunk * TC:(nchunk + 1) * TC],
                    start=False, stop=True),
                reads=("onesrow", "bv_sb"), writes=("ps_st{}".format(nchunk),))
            for hh in range(8):
                h = nchunk * 8 + hh
                P.v(lambda e, tt=tt, hh=hh, h=h, ps=ps: e.tensor_copy(
                        VS[:, tt * (NH * 65) + h * 65: tt * (NH * 65) + h * 65 + 64],
                        ps[:, hh * 64:(hh + 1) * 64]),
                    reads=("ps_st{}".format(nchunk), "vsmem"), writes=("vs{}_{}".format(tt, h),))

    if variant == "l1mha":
        emit_rope(P, QT, "qt", ROPE["ru"], ROPE["rw"], ROPEM["m1"], ROPEM["m2"], TMPE, ps_x, "ps_x")
        emit_rope(P, KT, "kt", ROPE["ru"], ROPE["rw"], ROPEM["m1"], ROPEM["m2"], TMPE, ps_x, "ps_x")
    if variant == "l1attn":
        emit_qkln(P, ctx, QT, "qt", GB, 0, 1, psel2, bself, SSC, TMPD, pp)
        emit_qkln(P, ctx, KT, "kt", GB, 2, 3, psel2, bself, SSC, TMPD, pp)

    for j in range(NDT):
        for c in range(NTC):
            sl = slice((c % 2) * TC, (c % 2 + 1) * TC)
            P.s(lambda e, j=j, c=c, sl=sl: e.activation(
                    QSQ[:, sl], KT[:, j * T + c * TC: j * T + (c + 1) * TC], AF.Square),
                reads=("kt{}".format(j),), writes=("qsq{}".format(c % 2),))
            P.t(lambda e, j=j, c=c, sl=sl: e.matmul(ps_x[:, :], psel2[:, :], QSQ[:, sl],
                                                    start=True, stop=True),
                reads=("psel2", "qsq{}".format(c % 2)), writes=("ps_x",))
            P.v(lambda e: e.tensor_reduce(TMPD[:, 0:1], ps_x[:, :], AX.X, ALU.max),
                reads=("ps_x",), writes=("tmpdA",))
            if c == 0:
                P.v(lambda e, j=j: e.tensor_copy(KMX[:, j:j + 1], TMPD[:, 0:1]),
                    reads=("tmpdA",), writes=("kmx",))
            else:
                P.v(lambda e, j=j: e.tensor_tensor(KMX[:, j:j + 1], KMX[:, j:j + 1],
                                                   TMPD[:, 0:1], ALU.max),
                    reads=("kmx", "tmpdA"), writes=("kmx",))
    P.s(lambda e: e.activation(KMX[:, 8:16], KMX[:, 0:8], AF.Ln),
        reads=("kmx",), writes=("lkb",))
    P.v(lambda e: e.tensor_scalar(KMX[:, 8:16], KMX[:, 8:16], 0.5, None, ALU.mult),
        reads=("lkb",), writes=("lkb",))

    load_w(3, False)
    umin = -511
    for qc in range(NTC):
        for jt in range(NDT):
            sl = slice((jt % 2) * TC, (jt % 2 + 1) * TC)
            P.s(lambda e, jt=jt, qc=qc, sl=sl: e.activation(
                    QSQ[:, sl], QT[:, jt * T + qc * TC: jt * T + (qc + 1) * TC], AF.Square),
                reads=("qt{}".format(jt),), writes=("qsq{}".format(jt % 2),))
            P.t(lambda e, jt=jt, sl=sl: e.matmul(ps_x[:, :], psel2[:, :], QSQ[:, sl],
                                                 start=True, stop=True),
                reads=("psel2", "qsq{}".format(jt % 2)), writes=("ps_x",))
            P.s(lambda e: e.activation(TMPD[:, 0:TC], ps_x[:, :], AF.Ln),
                reads=("ps_x",), writes=("tmpdA",))
            P.s(lambda e, jt=jt: e.activation(NBT[:, 0:TC], TMPD[:, 0:TC], AF.Exp, scale=0.5,
                                              bias=KMX[:, 8 + jt:9 + jt]),
                reads=("tmpdA", "lkb"), writes=("nbt",))
            P.v(lambda e: e.tensor_copy(AB8[0:1, TC:2 * TC], NBT[64:65, 0:TC]),
                reads=("nbt",), writes=("nbto",))
            for hp in range(2):
                h = jt * 2 + hp
                kts = kt_range(variant, qc)
                first = True
                for kt in kts:
                    psl = ps_st[kt % 2]
                    pskey = "ps_st{}".format(kt % 2)
                    P.t(lambda e, jt=jt, hp=hp, qc=qc, kt=kt, psl=psl: e.matmul(
                            psl[:, :],
                            KT[hp * 64:(hp + 1) * 64, jt * T + kt * 128: jt * T + (kt + 1) * 128],
                            QT[hp * 64:(hp + 1) * 64, jt * T + qc * TC: jt * T + (qc + 1) * TC],
                            start=True, stop=False),
                        reads=("kt{}".format(jt), "qt{}".format(jt)), writes=(pskey,))
                    if variant in ("l0", "l1attn") or kt >= 4 * qc:
                        off = qc * 512 - kt * 128 - umin
                        P.t(lambda e, off=off, psl=psl: e.matmul(
                                psl[:, :], IDENT[:, :], HOSTM[:, off:off + TC],
                                start=False, stop=False),
                            reads=("ident", "hostm"), writes=(pskey,))
                    nbr = NBT[0:1, 0:TC] if hp == 0 else AB8[0:1, TC:2 * TC]
                    nbk = "nbt" if hp == 0 else "nbto"
                    P.t(lambda e, psl=psl, nbr=nbr: e.matmul(
                            psl[:, :], minus1[0:1, :], nbr,
                            start=False, stop=True),
                        reads=("minus1", nbk), writes=(pskey,))
                    P.s(lambda e, kt=kt, psl=psl: e.activation(
                            PT[:, (kt % 2) * TC:(kt % 2 + 1) * TC], psl[:, :], AF.Exp),
                        reads=(pskey,), writes=("pt{}".format(kt % 2),))
                    P.t(lambda e, h=h, kt=kt, first=first, last=(kt == kts[-1]): e.matmul(
                            ps_av[0:65, :],
                            VS[:, kt * (NH * 65) + h * 65: kt * (NH * 65) + h * 65 + 65],
                            PT[:, (kt % 2) * TC:(kt % 2 + 1) * TC],
                            start=first, stop=last),
                        reads=("vs{}_{}".format(kt, h), "pt{}".format(kt % 2)), writes=("ps_av",))
                    first = False
                P.v(lambda e, hp=hp: e.tensor_copy(SUM2[hp * 64:hp * 64 + 1, :], ps_av[64:65, :]),
                    reads=("ps_av",), writes=("sum2_{}".format(hp),))
                P.v(lambda e, hp=hp: e.tensor_copy(TMPC[hp * 64:(hp + 1) * 64, 0:TC],
                                                   ps_av[0:64, :]),
                    reads=("ps_av",), writes=("tmpcA",))
            P.v(lambda e: e.tensor_scalar(RECB[:, :], SUM2[:, :], 1e-37, None, ALU.max),
                reads=("sum2_0", "sum2_1"), writes=("recb",))
            P.s(lambda e: e.activation(TMPD[:, TC:2 * TC], RECB[:, :], AF.Ln),
                reads=("recb",), writes=("tmpdB",))
            P.s(lambda e: e.activation(SUM2[:, :], TMPD[:, TC:2 * TC], AF.Exp, scale=-1.0),
                reads=("tmpdB",), writes=("sum2_0", "sum2_1"))
            P.t(lambda e: e.matmul(ps_x[:, :], bself[:, :], SUM2[:, :], start=True, stop=True),
                reads=("bself", "sum2_0", "sum2_1"), writes=("ps_x",))
            P.v(lambda e: e.tensor_copy(RECB[:, :], ps_x[:, :]),
                reads=("ps_x",), writes=("recb",))
            P.v(lambda e, jt=jt: e.tensor_tensor(ATT[:, jt * TC:(jt + 1) * TC],
                                                 TMPC[:, 0:TC], RECB[:, :], ALU.mult),
                reads=("tmpcA", "recb"), writes=("att{}".format(jt),))
        for j in range(NDT):
            for ct in range(NDT):
                P.t(lambda e, j=j, ct=ct: e.matmul(
                        ps_x[:, :], WB[:, ct * DM + j * 128: ct * DM + (j + 1) * 128],
                        ATT[:, ct * TC:(ct + 1) * TC],
                        start=(ct == 0), stop=(ct == NDT - 1)),
                    reads=("wb{}".format(ct), "att{}".format(ct)), writes=("ps_x",))
            if variant == "l1mha":
                if j % 4 == 0:
                    jh = j // 4
                    P.dma("y", lambda e, qc=qc, jh=jh: e.dma_start(
                            out=XC8[:, :].rearrange("p (jj t) -> p jj t", jj=4),
                            in_=base_hbm[jh * 512:(jh + 1) * 512,
                                         qc * TC:(qc + 1) * TC].rearrange(
                                "(jj p) t -> p jj t", p=128)),
                          reads=base_dep(qc, 0), writes=("xc8",))
                P.v(lambda e, j=j: e.scalar_tensor_tensor(
                        TMPC[:, TC:2 * TC], ps_x[:, :], b_sb[:, 24 + j:25 + j],
                        XC8[:, (j % 4) * TC:(j % 4 + 1) * TC], ALU.add, ALU.add),
                    reads=("ps_x", "b_sb", "xc8"), writes=("tmpc1",))
                P.dma("y", lambda e, j=j, qc=qc: e.dma_start(
                        out=x_out[j * 128:(j + 1) * 128, qc * TC:(qc + 1) * TC],
                        in_=TMPC[:, TC:2 * TC]),
                      reads=("tmpc1",), writes=("out:x{}_{}".format(qc, j),))
            else:
                xslot = j % 2
                xsl = slice(xslot * TC, (xslot + 1) * TC)
                P.dma("y", lambda e, j=j, qc=qc, xsl=xsl: e.dma_start(
                        out=XR[:, xsl], in_=base_hbm[j * 128:(j + 1) * 128, qc * TC:(qc + 1) * TC]),
                      reads=base_dep(qc, j), writes=("xr{}".format(xslot),))
                P.v(lambda e, j=j, xsl=xsl: e.scalar_tensor_tensor(
                        XC[:, xsl], ps_x[:, :], b_sb[:, 24 + j:25 + j], XR[:, xsl], ALU.add, ALU.add),
                    reads=("ps_x", "b_sb", "xr{}".format(xslot)), writes=("xc{}".format(xslot),))
                P.dma("y", lambda e, j=j, qc=qc, xsl=xsl: e.dma_start(
                        out=x_out[j * 128:(j + 1) * 128, qc * TC:(qc + 1) * TC], in_=XC[:, xsl]),
                      reads=("xc{}".format(xslot),), writes=("out:x{}_{}".format(qc, j),))

    keys = ["out:x{}_{}".format(c, j) for c in range(NTC) for j in range(NDT)]
    if ATTN_DEBUG == "avx":
        P.v(lambda e: e.tensor_copy(XR[:, 0:TC], TMPC[:, 0:TC]),
            reads=("tmpcA",) + tuple(keys), writes=("xr0",))
        P.dma("y", lambda e: e.dma_start(out=x_out[0:128, 3 * TC:4 * TC], in_=XR[:, 0:TC]),
              reads=("xr0",), writes=("dbgx:0",))
        P.v(lambda e: e.tensor_copy(XR[:, TC:2 * TC], SUM2[:, :]),
            reads=("sum2_0", "sum2_1") + tuple(keys), writes=("xr1",))
        P.dma("y", lambda e: e.dma_start(out=x_out[128:256, 3 * TC:4 * TC], in_=XR[:, TC:2 * TC]),
              reads=("xr1",), writes=("dbgx:1",))
        P.v(lambda e: e.tensor_copy(XR[:, 0:TC], PT[:, TC:2 * TC]),
            reads=("pt1",) + tuple(keys), writes=("xr0",))
        P.dma("y", lambda e: e.dma_start(out=x_out[256:384, 3 * TC:4 * TC], in_=XR[:, 0:TC]),
              reads=("xr0",), writes=("dbgx:2",))
        P.v(lambda e: e.tensor_copy(XR[:, TC:2 * TC], NBT[:, 0:TC]),
            reads=("nbt",) + tuple(keys), writes=("xr1",))
        P.dma("y", lambda e: e.dma_start(out=x_out[384:512, 3 * TC:4 * TC], in_=XR[:, TC:2 * TC]),
              reads=("xr1",), writes=("dbgx:3",))
        keys = ["dbgx:0", "dbgx:1", "dbgx:2", "dbgx:3"]
    if ATTN_DEBUG == "att":
        for j in range(NDT):
            slot = j % 2
            xsl = slice(slot * TC, (slot + 1) * TC)
            P.v(lambda e, j=j, xsl=xsl: e.tensor_copy(XR[:, xsl], ATT[:, j * TC:(j + 1) * TC]),
                reads=("att{}".format(j),) + tuple(keys), writes=("xr{}".format(slot),))
            P.dma("y", lambda e, j=j, xsl=xsl: e.dma_start(
                    out=x_out[j * 128:(j + 1) * 128, 3 * TC:4 * TC], in_=XR[:, xsl]),
                  reads=("xr{}".format(slot),), writes=("dbga:{}".format(j),))
        keys = ["dbga:{}".format(j) for j in range(NDT)]
    if ATTN_DEBUG == "vs":
        for rb in range(NDT):
            slot = rb % 2
            xsl = slice(slot * TC, (slot + 1) * TC)
            for c4 in range(4):
                P.v(lambda e, rb=rb, c4=c4, xsl=xsl: e.tensor_copy(
                        XR[:, xsl], VS[:, rb * 2048 + c4 * TC: rb * 2048 + (c4 + 1) * TC]),
                    reads=tuple("vs{}_{}".format(tt, h) for tt in range(16) for h in range(16))
                          + tuple(keys), writes=("xr{}".format(slot),))
                P.dma("y", lambda e, rb=rb, c4=c4, xsl=xsl: e.dma_start(
                        out=x_out[rb * 128:(rb + 1) * 128, c4 * TC:(c4 + 1) * TC], in_=XR[:, xsl]),
                      reads=("xr{}".format(slot),), writes=("dbgv:{}_{}".format(rb, c4),))
        keys = ["dbgv:{}_{}".format(rb, c4) for rb in range(NDT) for c4 in range(4)]
    if ATTN_DEBUG in ("qt", "kt", "xn"):
        DT_ = {"qt": QT, "kt": KT, "xn": XN}[ATTN_DEBUG]
        tagd = {"qt": "qt", "kt": "kt", "xn": None}[ATTN_DEBUG]
        for j in range(NDT):
            for c in range(NTC):
                slot = j % 2
                xsl = slice(slot * TC, (slot + 1) * TC)
                rk = (tagd + str(j),) if tagd else ("xn{}_{}".format(j, c),)
                P.v(lambda e, j=j, c=c, xsl=xsl, DT_=DT_: e.tensor_copy(
                        XR[:, xsl], DT_[:, j * T + c * TC: j * T + (c + 1) * TC]),
                    reads=rk + tuple(keys), writes=("xr{}".format(slot),))
                P.dma("y", lambda e, j=j, c=c, xsl=xsl: e.dma_start(
                        out=x_out[j * 128:(j + 1) * 128, c * TC:(c + 1) * TC], in_=XR[:, xsl]),
                      reads=("xr{}".format(slot),), writes=("dbg:{}_{}".format(c, j),))
        keys = ["dbg:{}_{}".format(c, j) for c in range(NTC) for j in range(NDT)]
    P.op("y", lambda e: e.nop(), reads=tuple(keys))


def build_attn_launch(variant):
    nc = bass.Bass("TRN2", target_bir_lowering=False)
    x_in = nc.dram_tensor("x_in", [DM, T], F32, kind="ExternalInput").ap()
    x_out = nc.dram_tensor("x_out", [DM, T], F32, kind="ExternalOutput").ap()
    wpack = nc.dram_tensor("wpack", [DM, 4 * DM], F32, kind="ExternalInput").ap()
    bpack = nc.dram_tensor("bpack", [128, 32], F32, kind="ExternalInput").ap()
    bvrow = nc.dram_tensor("bvrow", [1, DM], F32, kind="ExternalInput").ap()
    minus1_in = nc.dram_tensor("minus1_in", [1, 128], F32, kind="ExternalInput").ap()
    psel2_in = nc.dram_tensor("psel2_in", [128, 128], F32, kind="ExternalInput").ap()
    bsel_in = nc.dram_tensor("bsel_in", [128, 128], F32, kind="ExternalInput").ap()
    ident_in = nc.dram_tensor("ident_in", [128, 128], F32, kind="ExternalInput").ap()
    mwidth = {"l0": 1152, "l1attn": 1536, "l1mha": 1024}[variant]
    hostm_in = nc.dram_tensor("hostm_in", [128, mwidth], F32, kind="ExternalInput").ap()
    if variant == "l1mha":
        rope_in = {nm: nc.dram_tensor(nm, [128, T if nm in ("ru", "rw") else T], F32,
                                      kind="ExternalInput").ap() for nm in ("ru", "rw")}
        ropem_in = {nm: nc.dram_tensor(nm, [128, 128], F32, kind="ExternalInput").ap()
                    for nm in ("m1", "m2")}
        g1_in = nc.dram_tensor("g1_in", [128, 8], F32, kind="ExternalInput").ap()
        zbuf = nc.dram_tensor("zbuf", [DM, T], F32, kind="Internal").ap()
    if variant == "l1attn":
        gb_in = nc.dram_tensor("gb_in", [128, 4], F32, kind="ExternalInput").ap()

    io = dict(x_in=x_in, x_out=x_out, wpack=wpack, bpack=bpack, bvrow=bvrow,
              minus1_in=minus1_in, psel2_in=psel2_in, bsel_in=bsel_in, ident_in=ident_in,
              hostm_in=hostm_in)
    if variant == "l1mha":
        io.update(diagm_in=None, g1_in=g1_in, zbuf=zbuf, ru=rope_in["ru"], rw=rope_in["rw"],
                  m1=ropem_in["m1"], m2=ropem_in["m2"])
    if variant == "l1attn":
        io.update(gb_in=gb_in)
    st = contextlib.ExitStack()
    E = st.enter_context
    P = Prog(nc)
    plan_attn(P, nc, E, io, variant)
    block = E(nc.Block())
    P.finalize(block, st)
    st.close()
    return nc


def run_attn(xt, variant, wq, wk, wv, wo, bq, bk, bv, bo, gb=None, g1=None):
    hc = attn_host_consts(variant)
    if variant == "l1mha":
        perm = rope_perm()
        wq = wq[:, perm]
        wk = wk[:, perm]
    wpackh = np.ascontiguousarray(np.concatenate([wq, wk, wv, wo], axis=1).astype(np.float32))
    z8 = np.zeros(DM, np.float32)
    bq = z8 if bq is None else np.asarray(bq, np.float32)
    bk = z8 if bk is None else np.asarray(bk, np.float32)
    bv = z8 if bv is None else np.asarray(bv, np.float32)
    bo = z8 if bo is None else np.asarray(bo, np.float32)
    bpackh = np.zeros((128, 32), np.float32)
    bpackh[:, 0:8] = bq.reshape(8, 128).T
    bpackh[:, 8:16] = bk.reshape(8, 128).T
    bpackh[:, 24:32] = bo.reshape(8, 128).T
    bvrowh = np.ascontiguousarray(bv.reshape(1, DM))
    nc = build_attn_launch(variant)
    im = {"wpack": wpackh, "bpack": bpackh, "bvrow": bvrowh,
          "minus1_in": hc["minus1_in"], "psel2_in": hc["psel2_in"],
          "bsel_in": hc["bsel_in"], "ident_in": hc["ident_in"]}
    im["hostm_in"] = hc["hostm_in"]
    if variant == "l1mha":
        im.update({"ru": hc["ru"], "rw": hc["rw"], "m1": hc["m1"], "m2": hc["m2"]})
        im["g1_in"] = np.ascontiguousarray(np.asarray(g1, np.float32).reshape(8, 128).T)
    if variant == "l1attn":
        gq, gbb, gk, gkb = [np.asarray(a, np.float32) for a in gb]
        gbh = np.zeros((128, 4), np.float32)
        gbh[:, 0] = np.tile(gq, 2)
        gbh[:, 1] = np.tile(gbb, 2)
        gbh[:, 2] = np.tile(gk, 2)
        gbh[:, 3] = np.tile(gkb, 2)
        im["gb_in"] = gbh
    in_maps = [dict(im, x_in=xt[b]) for b in range(8)]
    res = run_bass_kernel_spmd(nc, in_maps, core_ids=list(range(8)))
    return [res.results[b]["x_out"] for b in range(8)]



def plan_router(P, nc, E, io):
    x_in = io.get("x_in")
    wr_in = io.get("wr_in")
    g2_in = io.get("g2_in")
    zn_out = io.get("zn_out")
    vloc_out = io.get("vloc_out")
    XB8 = E(nc.sbuf_tensor("XB8", [128, NDT * TC], BF16))
    XSQ = E(nc.sbuf_tensor("XSQ", [128, 2 * TC], BF16))
    XR = E(nc.sbuf_tensor("XR", [128, NDT * TC], F32))
    XNC = E(nc.sbuf_tensor("XNC", [128, 2 * TC], BF16))
    XC8 = E(nc.sbuf_tensor("XC8", [128, NDT * TC], F32))
    AROW = E(nc.sbuf_tensor("AROW", [1, TC], F32))
    TROW = E(nc.sbuf_tensor("TROW", [1, TC], F32))
    WR = E(nc.sbuf_tensor("WR", [128, NDT * 8], F32))
    G2 = E(nc.sbuf_tensor("G2", [128, 8], F32))
    LG = E(nc.sbuf_tensor("LG", [128, 8], F32))
    LG2 = E(nc.sbuf_tensor("LG2", [128, 8], F32))
    EXPL = E(nc.sbuf_tensor("EXPL", [128, 8], F32))
    PRB = E(nc.sbuf_tensor("PRB", [128, 8], F32))
    MSK = E(nc.sbuf_tensor("MSK", [128, 8], F32))
    SC = E(nc.sbuf_tensor("SC", [128, 8], F32))
    VLOC = E(nc.sbuf_tensor("VLOC", [128, 128], F32))
    ones_colbf = E(nc.sbuf_tensor("ones_colbf", [128, 1], BF16))
    ones32 = E(nc.sbuf_tensor("ones32", [1, 128], F32))
    ps_s2 = E(nc.psum_tensor("ps_s2", [128, TC], F32))
    ps_a = E(nc.psum_tensor("ps_a", [128, TC], F32))
    ps_r = E(nc.psum_tensor("ps_r", [128, 8], F32))

    P.g(lambda e: e.memset(ones_colbf[:, :], 1.0), writes=("ones_colbf",))
    P.g(lambda e: e.memset(ones32[:, :], 1.0), writes=("ones32",))
    P.dma("g", lambda e: e.dma_start(out=G2[:, :], in_=g2_in[:, :]), writes=("g2",))
    for j in range(NDT):
        P.dma("g", lambda e, j=j: e.dma_start(out=WR[:, j * 8:(j + 1) * 8],
                                              in_=wr_in[j * 128:(j + 1) * 128, :]),
              writes=("wr{}".format(j),))
        P.v(lambda e, j=j: e.tensor_scalar(WR[:, j * 8:(j + 1) * 8], WR[:, j * 8:(j + 1) * 8],
                                           G2[:, j:j + 1], None, ALU.mult),
            reads=("wr{}".format(j), "g2"), writes=("wr{}".format(j),))

    for c in range(NTC):
        P.dma("g", lambda e, c=c: e.dma_start(
                out=XB8[:, :].rearrange("p (j t) -> p j t", j=NDT),
                in_=x_in[:, c * TC:(c + 1) * TC].rearrange("(j p) t -> p j t", p=128)),
              writes=("xb8",))
        for j in range(NDT):
            slot = j % 2
            bcol = slice(slot * TC, (slot + 1) * TC)
            P.s(lambda e, j=j, bcol=bcol: e.activation(XSQ[:, bcol], XB8[:, j * TC:(j + 1) * TC],
                                                       AF.Square),
                reads=("xb8",), writes=("xsq{}".format(slot),))
            P.t(lambda e, bcol=bcol, j=j: e.matmul(ps_s2[0:1, :], ones_colbf[:, 0:1], XSQ[:, bcol],
                                                   start=(j == 0), stop=(j == NDT - 1)),
                reads=("ones_colbf", "xsq{}".format(slot)), writes=("ps_s2",))
        P.v(lambda e: e.tensor_scalar(AROW[0:1, :], ps_s2[0:1, :], 1.0 / DM, 1e-6,
                                      ALU.mult, ALU.add),
            reads=("ps_s2",), writes=("arow0",))
        P.s(lambda e: e.activation(TROW[0:1, :], AROW[0:1, :], AF.Ln),
            reads=("arow0",), writes=("trow",))
        P.s(lambda e: e.activation(AROW[0:1, :], TROW[0:1, :], AF.Exp, scale=-0.5),
            reads=("trow",), writes=("arow0",))
        P.t(lambda e: e.matmul(ps_a[:, :], ones32[0:1, :], AROW[0:1, :], start=True, stop=True),
            reads=("ones32", "arow0"), writes=("ps_a",))
        P.dma("y", lambda e, c=c: e.dma_start(
                out=XC8[:, :].rearrange("p (j t) -> p j t", j=NDT),
                in_=x_in[:, c * TC:(c + 1) * TC].rearrange("(j p) t -> p j t", p=128)),
              writes=("xc8",))
        for j in range(NDT):
            slot = j % 2
            xsl = slice(slot * TC, (slot + 1) * TC)
            P.v(lambda e, j=j: e.tensor_tensor(XR[:, j * TC:(j + 1) * TC],
                                               XC8[:, j * TC:(j + 1) * TC],
                                               ps_a[:, :], ALU.mult),
                reads=("xc8", "ps_a"), writes=("xrj{}".format(j),))
            P.v(lambda e, j=j, xsl=xsl: e.tensor_scalar(XNC[:, xsl], XR[:, j * TC:(j + 1) * TC],
                                                        G2[:, j:j + 1], None, ALU.mult),
                reads=("xrj{}".format(j), "g2"), writes=("xnc{}".format(slot),))
            P.dma("y", lambda e, j=j, c=c, xsl=xsl: e.dma_start(
                    out=zn_out[j * 128:(j + 1) * 128, c * TC:(c + 1) * TC], in_=XNC[:, xsl]),
                  reads=("xnc{}".format(slot),), writes=("out:zn{}_{}".format(c, j),))
        for s4 in range(4):
            tt = c * 4 + s4
            for j in range(NDT):
                P.t(lambda e, j=j, s4=s4: e.matmul(
                        ps_r[:, :], XR[:, j * TC + s4 * 128: j * TC + (s4 + 1) * 128],
                        WR[:, j * 8:(j + 1) * 8], start=(j == 0), stop=(j == NDT - 1)),
                    reads=("xrj{}".format(j), "wr{}".format(j)), writes=("ps_r",))
            P.v(lambda e: e.tensor_copy(LG[:, :], ps_r[:, :]), reads=("ps_r",), writes=("lg",))
            P.v(lambda e: e.reduce_max(SC[:, 0:1], LG[:, :], axis=AX.X, negate=True),
                reads=("lg",), writes=("sc0",))
            P.s(lambda e: e.activation(EXPL[:, :], LG[:, :], AF.Exp, bias=SC[:, 0:1]),
                reads=("lg", "sc0"), writes=("expl",))
            P.v(lambda e: e.reduce_sum(SC[:, 1:2], EXPL[:, :], axis=AX.X),
                reads=("expl",), writes=("sc1",))
            P.s(lambda e: e.activation(SC[:, 2:3], SC[:, 1:2], AF.Ln),
                reads=("sc1",), writes=("sc2",))
            P.s(lambda e: e.activation(SC[:, 3:4], SC[:, 2:3], AF.Exp, scale=-1.0),
                reads=("sc2",), writes=("sc3",))
            P.v(lambda e: e.tensor_scalar(SC[:, 4:5], SC[:, 0:1], -1.0, None, ALU.mult),
                reads=("sc0",), writes=("sc4",))
            P.v(lambda e: e.tensor_scalar(MSK[:, :], LG[:, :], SC[:, 4:5], None, ALU.is_ge),
                reads=("lg", "sc4"), writes=("msk",))
            P.v(lambda e: e.scalar_tensor_tensor(LG2[:, :], MSK[:, :], -1e30, LG[:, :],
                                                 ALU.mult, ALU.add),
                reads=("msk", "lg"), writes=("lg2",))
            P.v(lambda e: e.reduce_max(SC[:, 5:6], LG2[:, :], axis=AX.X),
                reads=("lg2",), writes=("sc5",))
            P.s(lambda e: e.activation(SC[:, 6:7], SC[:, 5:6], AF.Exp, bias=SC[:, 0:1]),
                reads=("sc5", "sc0"), writes=("sc6",))
            P.v(lambda e: e.tensor_scalar(SC[:, 7:8], SC[:, 6:7], SC[:, 3:4], None, ALU.mult),
                reads=("sc6", "sc3"), writes=("sc7",))
            P.v(lambda e: e.tensor_scalar(PRB[:, :], EXPL[:, :], SC[:, 3:4], None, ALU.mult),
                reads=("expl", "sc3"), writes=("prb",))
            P.v(lambda e: e.tensor_scalar(MSK[:, :], PRB[:, :], SC[:, 7:8], None, ALU.is_ge),
                reads=("prb", "sc7"), writes=("msk",))
            P.v(lambda e, tt=tt: e.tensor_tensor(VLOC[:, tt * 8:(tt + 1) * 8], PRB[:, :],
                                                 MSK[:, :], ALU.mult),
                reads=("prb", "msk"), writes=("vloc{}".format(tt),))
    P.dma("y", lambda e: e.dma_start(out=vloc_out[:, :], in_=VLOC[:, :]),
          reads=tuple("vloc{}".format(tt) for tt in range(16)), writes=("out:vloc",))
    keys = ["out:zn{}_{}".format(c, j) for c in range(NTC) for j in range(NDT)] + ["out:vloc"]
    P.op("y", lambda e: e.nop(), reads=tuple(keys))


def build_router_launch():
    nc = bass.Bass("TRN2", target_bir_lowering=False)
    x_in = nc.dram_tensor("x_in", [DM, T], F32, kind="ExternalInput").ap()
    wr_in = nc.dram_tensor("wr_in", [DM, 8], F32, kind="ExternalInput").ap()
    g2_in = nc.dram_tensor("g2_in", [128, 8], F32, kind="ExternalInput").ap()
    zn_out = nc.dram_tensor("zn_out", [DM, T], BF16, kind="ExternalOutput").ap()
    vloc_out = nc.dram_tensor("vloc_out", [128, 128], F32, kind="ExternalOutput").ap()

    io = dict(x_in=x_in, wr_in=wr_in, g2_in=g2_in, zn_out=zn_out, vloc_out=vloc_out)
    st = contextlib.ExitStack()
    E = st.enter_context
    P = Prog(nc)
    plan_router(P, nc, E, io)
    block = E(nc.Block())
    P.finalize(block, st)
    st.close()
    return nc


def run_router(s1t, w_router, g2):
    nc = build_router_launch()
    wrh = np.ascontiguousarray(np.asarray(w_router, np.float32))
    g2h = np.ascontiguousarray(np.asarray(g2, np.float32).reshape(8, 128).T)
    in_maps = [{"x_in": s1t[b], "wr_in": wrh, "g2_in": g2h} for b in range(8)]
    res = run_bass_kernel_spmd(nc, in_maps, core_ids=list(range(8)))
    zn = [res.results[b]["zn_out"] for b in range(8)]
    vloc = [res.results[b]["vloc_out"] for b in range(8)]
    return zn, vloc


def vloc_decode(vl):
    out = np.zeros((T, 8), np.float32)
    for tt in range(16):
        out[tt * 128:(tt + 1) * 128, :] = vl[:, tt * 8:(tt + 1) * 8]
    return out


def moe_host_layouts(vglobal):
    vb = np.zeros((128, 1024), np.float32)
    for e in range(8):
        for s in range(16):
            vb[e * 16 + s, :] = vglobal[s * 1024:(s + 1) * 1024, e]
    return vb



CAP = 1280.0


def plan_moe(P, nc, E, io, ago=None, permsel=None, idf32=None):
    x3_in = io.get("x3_in")
    s1_in = io.get("s1_in")
    zn_in = io.get("zn_in")
    vbis_in = io.get("vbis_in")
    vtok_in = io.get("vtok_in")
    w1_in = io.get("w1_in")
    w2_in = io.get("w2_in")
    gsel_in = io.get("gsel_in")
    gsel2_in = io.get("gsel2_in")
    esel3_in = io.get("esel3_in")
    ident8_in = io.get("ident8_in")
    identf_in = io.get("identf_in")
    x_out = io.get("x_out")
    ZN = E(nc.sbuf_tensor("ZN", [128, NDT * T], BF16))
    W1E = E(nc.sbuf_tensor("W1E", [128, NDT * 2048], BF16))
    W2E = E(nc.sbuf_tensor("W2E", [128, 16 * DM], BF16))
    HB = E(nc.sbuf_tensor("HB", [128, 16 * TC], BF16))
    ACC = E(nc.sbuf_tensor("ACC", [128, NDT * T], F32))
    ZG = E(nc.sbuf_tensor("ZG", [128, NDT * TC], BF16))
    VB = E(nc.sbuf_tensor("VB", [128, 1024], F32))
    VT = E(nc.sbuf_tensor("VT", [128, 128], F32))
    GT = E(nc.sbuf_tensor("GT", [128, 128], F32))
    GA = E(nc.sbuf_tensor("GA", [8, 16 * 128], BF16))
    GTB = E(nc.sbuf_tensor("GTB", [128, 128], BF16))
    CNTB = E(nc.sbuf_tensor("CNTB", [128, 1024], BF16))
    CNTS = E(nc.sbuf_tensor("CNTS", [128, 4], F32))
    LOHI = E(nc.sbuf_tensor("LOHI", [8, 8], F32))
    TAUB = E(nc.sbuf_tensor("TAUB", [128, 8], F32))
    TAUR = E(nc.sbuf_tensor("TAUR", [1, 8], F32))
    GSEL = E(nc.sbuf_tensor("GSEL", [128, 8], F32))
    GSEL2 = E(nc.sbuf_tensor("GSEL2", [8, 128], F32))
    ESEL3 = E(nc.sbuf_tensor("ESEL3", [8, 1024], BF16))
    IDENT8 = E(nc.sbuf_tensor("IDENT8", [8, 8], F32))
    IDENTF = E(nc.sbuf_tensor("IDENTF", [128, 128], BF16))
    ones32 = E(nc.sbuf_tensor("ones32", [1, 128], F32))
    MIDB = E(nc.sbuf_tensor("MIDB", [128, 1], F32))
    CNT8 = E(nc.sbuf_tensor("CNT8", [8, 4], F32))
    pp = [E(nc.psum_tensor("pp{}".format(i), [128, TC], F32)) for i in range(8)]
    ps_m = pp[0]
    ps_c = pp[1]
    ps_h = [pp[2], pp[3]]
    ps_o = [pp[4], pp[5]]
    ps_g = pp[6]
    ps_t = pp[7]

    P.g(lambda e: e.memset(ones32[:, :], 1.0), writes=("ones32",))
    if ago is None:
        P.dma("g", lambda e: e.dma_start(out=VB[:, :], in_=vbis_in[:, :]), writes=("vb",))
    else:
        AGT = E(nc.sbuf_tensor("AGT", [128, 128], F32))
        AGT2 = E(nc.sbuf_tensor("AGT2", [128, 128], F32))
        PRM = E(nc.sbuf_tensor("PRM", [128, 128], F32))
        IDF = E(nc.sbuf_tensor("IDF", [128, 128], F32))
        P.dma("g", lambda e: e.dma_start(out=PRM[:, :], in_=permsel[:, :]), writes=("prm",))
        P.dma("g", lambda e: e.dma_start(out=IDF[:, :], in_=idf32[:, :]), writes=("idf",))
        for r in range(8):
            P.dma("g", lambda e, r=r: e.dma_start(out=AGT[:, :], in_=ago[r * 128:(r + 1) * 128, :]),
                  reads=("cc_done",), writes=("agt",))
            P.t(lambda e: e.transpose(pp[0][0:128, 0:128], AGT[:, :], IDF[:, :]),
                reads=("agt", "idf"), writes=("pp0",))
            P.v(lambda e: e.tensor_copy(AGT2[:, :], pp[0][0:128, 0:128]),
                reads=("pp0",), writes=("agt2",))
            P.t(lambda e: e.matmul(pp[0][0:128, 0:128], PRM[:, :], AGT2[:, :], start=True, stop=True),
                reads=("prm", "agt2"), writes=("pp0",))
            P.v(lambda e, r=r: e.tensor_copy(VB[:, r * 128:(r + 1) * 128], pp[0][0:128, 0:128]),
                reads=("pp0",), writes=("vb",))
    P.dma("g", lambda e: e.dma_start(out=VT[:, :], in_=vtok_in[:, :]), writes=("vt",))
    P.dma("g", lambda e: e.dma_start(out=GSEL[:, :], in_=gsel_in[:, :]), writes=("gsel",))
    P.dma("g", lambda e: e.dma_start(out=GSEL2[:, :], in_=gsel2_in[:, :]), writes=("gsel2",))
    P.dma("g", lambda e: e.dma_start(out=ESEL3[:, :], in_=esel3_in[:, :]), writes=("esel3",))
    P.dma("g", lambda e: e.dma_start(out=IDENT8[:, :], in_=ident8_in[:, :]), writes=("ident8",))
    P.dma("g", lambda e: e.dma_start(out=IDENTF[:, :], in_=identf_in[:, :]), writes=("identf",))
    P.dma("g", lambda e: e.dma_start(
            out=ZN[:, :].rearrange("p (j t) -> p j t", j=NDT),
            in_=zn_in.rearrange("(j p) t -> p j t", p=128)),
          writes=tuple("zn{}".format(j) for j in range(NDT)))

    P.v(lambda e: e.memset(LOHI[:, 0:1], 0.0), writes=("lo",))
    P.v(lambda e: e.memset(LOHI[:, 1:2], 1.01), writes=("hi",))
    for it in range(32):
        P.v(lambda e: e.tensor_tensor(LOHI[:, 2:3], LOHI[:, 0:1], LOHI[:, 1:2], ALU.add),
            reads=("lo", "hi"), writes=("mid",))
        P.v(lambda e: e.tensor_scalar(LOHI[:, 2:3], LOHI[:, 2:3], 0.5, None, ALU.mult),
            reads=("mid",), writes=("mid",))
        P.t(lambda e: e.matmul(ps_m[:, 0:1], GSEL2[:, :], LOHI[0:8, 2:3], start=True, stop=True),
            reads=("gsel2", "mid"), writes=("pp0",))
        P.v(lambda e: e.tensor_copy(MIDB[:, :], ps_m[:, 0:1]), reads=("pp0",), writes=("midb",))
        P.v(lambda e: e.tensor_scalar(CNTB[:, :], VB[:, :], MIDB[:, 0:1], None, ALU.is_ge),
            reads=("vb", "midb"), writes=("cntb",))
        P.v(lambda e: e.reduce_sum(CNTS[:, 0:1], CNTB[:, :], axis=AX.X),
            reads=("cntb",), writes=("cnts",))
        P.t(lambda e: e.matmul(ps_c[0:8, 0:1], GSEL[:, :], CNTS[:, 0:1], start=True, stop=True),
            reads=("gsel", "cnts"), writes=("pp1",))
        P.v(lambda e: e.tensor_copy(CNT8[:, 0:1], ps_c[0:8, 0:1]), reads=("pp1",), writes=("cnt8",))
        P.v(lambda e: e.tensor_scalar(CNT8[:, 1:2], CNT8[:, 0:1], CAP, None, ALU.is_ge),
            reads=("cnt8",), writes=("sel",))
        P.v(lambda e: e.tensor_tensor(LOHI[:, 3:4], LOHI[:, 2:3], LOHI[:, 0:1], ALU.subtract),
            reads=("mid", "lo"), writes=("sc_d",))
        P.v(lambda e: e.tensor_tensor(LOHI[:, 3:4], LOHI[:, 3:4], CNT8[:, 1:2], ALU.mult),
            reads=("sc_d", "sel"), writes=("sc_d",))
        P.v(lambda e: e.tensor_tensor(LOHI[:, 0:1], LOHI[:, 0:1], LOHI[:, 3:4], ALU.add),
            reads=("lo", "sc_d"), writes=("lo",))
        P.v(lambda e: e.tensor_tensor(LOHI[:, 4:5], LOHI[:, 1:2], LOHI[:, 2:3], ALU.subtract),
            reads=("hi", "mid"), writes=("sc_e",))
        P.v(lambda e: e.tensor_tensor(LOHI[:, 4:5], LOHI[:, 4:5], CNT8[:, 1:2], ALU.mult),
            reads=("sc_e", "sel"), writes=("sc_e",))
        P.v(lambda e: e.tensor_tensor(LOHI[:, 1:2], LOHI[:, 2:3], LOHI[:, 4:5], ALU.add),
            reads=("mid", "sc_e"), writes=("hi",))
    P.t(lambda e: e.matmul(ps_m[0:1, 0:8], LOHI[0:8, 0:1], IDENT8[:, :], start=True, stop=True),
        reads=("lo", "ident8"), writes=("pp0",))
    P.v(lambda e: e.tensor_scalar(TAUR[0:1, :], ps_m[0:1, 0:8], 0.0, None, ALU.max),
        reads=("pp0",), writes=("taur",))
    P.t(lambda e: e.matmul(ps_m[:, 0:8], ones32[0:1, :], TAUR[0:1, :], start=True, stop=True),
        reads=("ones32", "taur"), writes=("pp0",))
    P.v(lambda e: e.tensor_copy(TAUB[:, :], ps_m[:, 0:8]), reads=("pp0",), writes=("taub",))
    for tt in range(16):
        sl = slice(tt * 8, (tt + 1) * 8)
        P.v(lambda e, sl=sl: e.tensor_tensor(GT[:, sl], VT[:, sl], TAUB[:, :], ALU.is_ge),
            reads=("vt", "taub"), writes=("gt",))
        P.v(lambda e, sl=sl: e.scalar_tensor_tensor(GT[:, sl], GT[:, sl], 0.5, VT[:, sl],
                                                    ALU.mult, ALU.mult),
            reads=("gt", "vt"), writes=("gt",))
    for tt in range(16):
        P.v(lambda e, tt=tt: e.tensor_copy(GTB[:, tt * 8:(tt + 1) * 8] if False else GTB[:, 0:8],
                                           GT[:, tt * 8:(tt + 1) * 8]),
            reads=("gt",), writes=("gtb",))
        P.t(lambda e, tt=tt: e.matmul(ps_t[0:8, 0:128], GTB[:, 0:8], IDENTF[:, :],
                                      start=True, stop=True),
            reads=("gtb", "identf"), writes=("pp7",))
        P.v(lambda e, tt=tt: e.tensor_copy(GA[0:8, tt * 128:(tt + 1) * 128], ps_t[0:8, 0:128]),
            reads=("pp7",), writes=("ga",))

    for c in range(NTC):
        P.dma("g", lambda e, c=c: e.dma_start(
                out=ACC[:, :].rearrange("p (j tt) -> p j tt", j=NDT)[:, :, c * TC:(c + 1) * TC],
                in_=s1_in[:, c * TC:(c + 1) * TC].rearrange("(j p) t -> p j t", p=128)),
              writes=tuple("acc{}_{}".format(j, c) for j in range(NDT)))
        for j in range(NDT):
            P.v(lambda e, j=j, c=c: e.tensor_scalar(
                    ACC[:, j * T + c * TC: j * T + (c + 1) * TC],
                    ACC[:, j * T + c * TC: j * T + (c + 1) * TC], 0.5, None, ALU.mult),
                reads=("acc{}_{}".format(j, c),), writes=("acc{}_{}".format(j, c),))
        P.dma("g", lambda e, c=c: e.dma_start(
                out=ACC[:, :].rearrange("p (j tt) -> p j tt", j=NDT)[:, :, c * TC:(c + 1) * TC],
                in_=x3_in[:, c * TC:(c + 1) * TC].rearrange("(j p) t -> p j t", p=128),
                accum_op=ALU.add),
              reads=tuple("acc{}_{}".format(j, c) for j in range(NDT)),
              writes=tuple("acc{}_{}".format(j, c) for j in range(NDT)))

    for ex in range(8):
        P.dma("g", lambda e, ex=ex: e.dma_start(
                out=W1E[:, :].rearrange("p (j f) -> p j f", j=NDT),
                in_=w1_in[ex * DM:(ex + 1) * DM, :].rearrange("(j p) f -> p j f", p=128)),
              writes=tuple("w1e{}".format(j) for j in range(NDT)))
        P.dma("g", lambda e, ex=ex: e.dma_start(
                out=W2E[:, :].rearrange("p (f c) -> p f c", f=16),
                in_=w2_in[ex * 2048:(ex + 1) * 2048, :].rearrange("(f p) c -> p f c", p=128)),
              writes=tuple("w2e{}".format(ft) for ft in range(16)))
        for c in range(NTC):
            P.t(lambda e, ex=ex, c=c: e.matmul(
                    ps_g[:, :], ESEL3[:, ex * 128:(ex + 1) * 128], GA[0:8, c * TC:(c + 1) * TC],
                    start=True, stop=True),
                reads=("esel3", "ga"), writes=("pp6",))
            for j in range(NDT):
                P.v(lambda e, j=j, c=c: e.tensor_tensor(
                        ZG[:, j * TC:(j + 1) * TC], ZN[:, j * T + c * TC: j * T + (c + 1) * TC],
                        ps_g[:, :], ALU.mult),
                    reads=("zn{}".format(j), "pp6"), writes=("zg{}".format(j),))
            for ft in range(16):
                ps = ps_h[ft % 2]
                for j in range(NDT):
                    P.t(lambda e, ft=ft, j=j, ps=ps: e.matmul(
                            ps[:, :], W1E[:, j * 2048 + ft * 128: j * 2048 + (ft + 1) * 128],
                            ZG[:, j * TC:(j + 1) * TC],
                            start=(j == 0), stop=(j == NDT - 1)),
                        reads=("w1e{}".format(j), "zg{}".format(j)), writes=("pp{}".format(2 + ft % 2),))
                P.s(lambda e, ft=ft, ps=ps: e.activation(HB[:, ft * TC:(ft + 1) * TC], ps[:, :],
                                                         AF.Relu),
                    reads=("pp{}".format(2 + ft % 2),), writes=("hb{}".format(ft),))
            for j in range(NDT):
                ps = ps_o[j % 2]
                for ft in range(16):
                    P.t(lambda e, j=j, ft=ft, ps=ps: e.matmul(
                            ps[:, :], W2E[:, ft * DM + j * 128: ft * DM + (j + 1) * 128],
                            HB[:, ft * TC:(ft + 1) * TC],
                            start=(ft == 0), stop=(ft == 15)),
                        reads=("w2e{}".format(ft), "hb{}".format(ft)), writes=("pp{}".format(4 + j % 2),))
                P.v(lambda e, j=j, c=c, ps=ps: e.tensor_tensor(
                        ACC[:, j * T + c * TC: j * T + (c + 1) * TC],
                        ACC[:, j * T + c * TC: j * T + (c + 1) * TC], ps[:, :], ALU.add),
                    reads=("acc{}_{}".format(j, c), "pp{}".format(4 + j % 2)),
                    writes=("acc{}_{}".format(j, c),))

    for c in range(NTC):
        for j in range(NDT):
            P.dma("y", lambda e, j=j, c=c: e.dma_start(
                    out=x_out[j * 128:(j + 1) * 128, c * TC:(c + 1) * TC],
                    in_=ACC[:, j * T + c * TC: j * T + (c + 1) * TC]),
                  reads=("acc{}_{}".format(j, c),), writes=("out:x{}_{}".format(c, j),))
    keys = ["out:x{}_{}".format(c, j) for c in range(NTC) for j in range(NDT)]
    P.op("y", lambda e: e.nop(), reads=tuple(keys))


def build_moe_launch():
    nc = bass.Bass("TRN2", target_bir_lowering=False)
    x3_in = nc.dram_tensor("x3_in", [DM, T], F32, kind="ExternalInput").ap()
    s1_in = nc.dram_tensor("s1_in", [DM, T], F32, kind="ExternalInput").ap()
    zn_in = nc.dram_tensor("zn_in", [DM, T], BF16, kind="ExternalInput").ap()
    vbis_in = nc.dram_tensor("vbis_in", [128, 1024], F32, kind="ExternalInput").ap()
    vtok_in = nc.dram_tensor("vtok_in", [128, 128], F32, kind="ExternalInput").ap()
    w1_in = nc.dram_tensor("w1_in", [8 * DM, 2048], F32, kind="ExternalInput").ap()
    w2_in = nc.dram_tensor("w2_in", [8 * 2048, DM], F32, kind="ExternalInput").ap()
    gsel_in = nc.dram_tensor("gsel_in", [128, 8], F32, kind="ExternalInput").ap()
    gsel2_in = nc.dram_tensor("gsel2_in", [8, 128], F32, kind="ExternalInput").ap()
    esel3_in = nc.dram_tensor("esel3_in", [8, 1024], F32, kind="ExternalInput").ap()
    ident8_in = nc.dram_tensor("ident8_in", [8, 8], F32, kind="ExternalInput").ap()
    identf_in = nc.dram_tensor("identf_in", [128, 128], F32, kind="ExternalInput").ap()
    x_out = nc.dram_tensor("x_out", [DM, T], F32, kind="ExternalOutput").ap()

    io = dict(x3_in=x3_in, s1_in=s1_in, zn_in=zn_in, vbis_in=vbis_in, vtok_in=vtok_in,
              w1_in=w1_in, w2_in=w2_in, gsel_in=gsel_in, gsel2_in=gsel2_in, esel3_in=esel3_in,
              ident8_in=ident8_in, identf_in=identf_in, x_out=x_out)
    st = contextlib.ExitStack()
    E = st.enter_context
    P = Prog(nc)
    plan_moe(P, nc, E, io)
    block = E(nc.Block())
    P.finalize(block, st)
    st.close()
    return nc


def moe_consts():
    gsel = np.zeros((128, 8), np.float32)
    gsel2 = np.zeros((8, 128), np.float32)
    for p in range(128):
        gsel[p, p // 16] = 1.0
        gsel2[p // 16, p] = 1.0
    esel3 = np.zeros((8, 1024), np.float32)
    for ex in range(8):
        esel3[ex, ex * 128:(ex + 1) * 128] = 1.0
    return dict(gsel_in=gsel, gsel2_in=gsel2, esel3_in=esel3,
                ident8_in=np.eye(8, dtype=np.float32),
                identf_in=np.eye(128, dtype=np.float32))


def run_moe(x3t, s1t, znt, vlocs, w1, w2):
    vglob = np.concatenate([vloc_decode(vlocs[b]) for b in range(8)], axis=0)
    vbis = moe_host_layouts(vglob)
    consts = moe_consts()
    w1h = np.ascontiguousarray(np.asarray(w1, np.float32).reshape(8 * DM, 2048))
    w2h = np.ascontiguousarray(np.asarray(w2, np.float32).reshape(8 * 2048, DM))
    nc = build_moe_launch()
    in_maps = [dict(consts, x3_in=x3t[b], s1_in=s1t[b], zn_in=znt[b],
                    vbis_in=vbis, vtok_in=vlocs[b], w1_in=w1h, w2_in=w2h)
               for b in range(8)]
    res = run_bass_kernel_spmd(nc, in_maps, core_ids=list(range(8)))
    return [res.results[b]["x_out"] for b in range(8)]



def _np(a):
    return np.ascontiguousarray(np.asarray(a))


def _prep_l0ffn(inp):
    w1t = np.ascontiguousarray(_np(inp["L0_ffn_W1"]).T)
    w2t = np.ascontiguousarray(_np(inp["L0_ffn_W2"]).T)
    b1 = np.ascontiguousarray(_np(inp["L0_ffn_b1"]).reshape(32, 128).T)
    b2 = np.ascontiguousarray(_np(inp["L0_ffn_b2"]).reshape(8, 128).T)
    return w1t, w2t, b1, b2


def run_l0ffn(xt, inp, tern):
    w1t, w2t, b1, b2 = _prep_l0ffn(inp)
    nc = build_l0ffn_launch(with_ternarize=tern is None)
    if tern is None:
        in_maps = [{"x_in": xt[b], "w1t_in": w1t, "w2t_in": w2t, "b1_in": b1, "b2_in": b2}
                   for b in range(8)]
    else:
        in_maps = [{"x_in": xt[b], "w1bf_in": tern[0][b], "w2bf_in": tern[1][b],
                    "b1_in": b1, "b2_in": b2} for b in range(8)]
    res = run_bass_kernel_spmd(nc, in_maps, core_ids=list(range(8)))
    xt2 = [res.results[b]["x_out"] for b in range(8)]
    if tern is None:
        tern = ([res.results[b]["w1bf_out"] for b in range(8)],
                [res.results[b]["w2bf_out"] for b in range(8)])
    return xt2, tern


def run_l1ffn(xt, inp, idx):
    C = DM
    wqkv = _np(inp["L1_ffn_w_qkv"]).astype(np.float32)
    wo = _np(inp["L1_ffn_w_o"]).astype(np.float32)
    g1 = _np(inp["L1_ffn_g1"]).astype(np.float32)
    g2 = _np(inp["L1_ffn_g2"]).astype(np.float32)
    wrt = _np(inp["L1_ffn_w_router"]).astype(np.float32)
    w1 = _np(inp["L1_ffn_w1"]).astype(np.float32)
    w2 = _np(inp["L1_ffn_w2"]).astype(np.float32)
    s1 = run_attn(xt, "l1mha", wqkv[:, 0:C].copy(), wqkv[:, C:2 * C].copy(),
                  wqkv[:, 2 * C:].copy(), wo, None, None, None, None, g1=g1)
    zn, vloc = run_router(s1, wrt, g2)
    x4 = run_moe(xt, s1, zn, vloc, w1, w2)
    return x4



def permsel_const():
    m = np.zeros((128, 128), np.float32)
    for p in range(128):
        e, tt = p // 16, p % 16
        m[tt * 8 + e, p] = 1.0
    return m


class _NcProxy:

    def __init__(self, nc):
        self._nc = nc
        self._n = 0

    def sbuf_tensor(self, name, shape, dtype, **kw):
        self._n += 1
        return self._nc.sbuf_tensor("{}_p{}".format(name, self._n), shape, dtype, **kw)

    def psum_tensor(self, name, shape, dtype=mybir.dt.float32, **kw):
        self._n += 1
        return self._nc.psum_tensor("{}_p{}".format(name, self._n), shape, dtype, **kw)

    def __getattr__(self, k):
        return getattr(self._nc, k)


def build_fused():
    nc0 = bass.Bass("TRN2", target_bir_lowering=False)
    nc = _NcProxy(nc0)
    DT = nc.dram_tensor

    x0 = DT("x0", [DM, T], F32, kind="ExternalInput").ap()
    x_out = DT("x_out", [DM, T], F32, kind="ExternalOutput").ap()

    f_w1t = DT("f_w1t", [DM, 4096], F32, kind="ExternalInput").ap()
    f_w2t = DT("f_w2t", [4096, DM], F32, kind="ExternalInput").ap()
    f_b1 = DT("f_b1", [128, 32], F32, kind="ExternalInput").ap()
    f_b2 = DT("f_b2", [128, 8], F32, kind="ExternalInput").ap()
    w1bf = DT("w1bf", [DM, 4096], BF16, kind="Internal").ap()
    w2bf = DT("w2bf", [4096, DM], BF16, kind="Internal").ap()
    minus1_in = DT("minus1_in", [1, 128], F32, kind="ExternalInput").ap()
    psel2_in = DT("psel2_in", [128, 128], F32, kind="ExternalInput").ap()
    bsel_in = DT("bsel_in", [128, 128], F32, kind="ExternalInput").ap()
    ident_in = DT("ident_in", [128, 128], F32, kind="ExternalInput").ap()
    a0_wpack = DT("a0_wpack", [DM, 4 * DM], F32, kind="ExternalInput").ap()
    a0_bpack = DT("a0_bpack", [128, 32], F32, kind="ExternalInput").ap()
    a0_bvrow = DT("a0_bvrow", [1, DM], F32, kind="ExternalInput").ap()
    a0_hostm = DT("a0_hostm", [128, 1152], F32, kind="ExternalInput").ap()
    m_wpack = DT("m_wpack", [DM, 4 * DM], F32, kind="ExternalInput").ap()
    m_bpack = DT("m_bpack", [128, 32], F32, kind="ExternalInput").ap()
    m_bvrow = DT("m_bvrow", [1, DM], F32, kind="ExternalInput").ap()
    m_hostm = DT("m_hostm", [128, 1024], F32, kind="ExternalInput").ap()
    m_g1 = DT("m_g1", [128, 8], F32, kind="ExternalInput").ap()
    m_ru = DT("m_ru", [64 if False else 128, T], F32, kind="ExternalInput").ap()
    m_rw = DT("m_rw", [128, T], F32, kind="ExternalInput").ap()
    m_m1 = DT("m_m1", [128, 128], F32, kind="ExternalInput").ap()
    m_m2 = DT("m_m2", [128, 128], F32, kind="ExternalInput").ap()
    r_wr = DT("r_wr", [DM, 8], F32, kind="ExternalInput").ap()
    r_g2 = DT("r_g2", [128, 8], F32, kind="ExternalInput").ap()
    o_w1 = DT("o_w1", [8 * DM, 2048], F32, kind="ExternalInput").ap()
    o_w2 = DT("o_w2", [8 * 2048, DM], F32, kind="ExternalInput").ap()
    o_gsel = DT("o_gsel", [128, 8], F32, kind="ExternalInput").ap()
    o_gsel2 = DT("o_gsel2", [8, 128], F32, kind="ExternalInput").ap()
    o_esel3 = DT("o_esel3", [8, 1024], F32, kind="ExternalInput").ap()
    o_ident8 = DT("o_ident8", [8, 8], F32, kind="ExternalInput").ap()
    o_permsel = DT("o_permsel", [128, 128], F32, kind="ExternalInput").ap()
    a1_wpack = DT("a1_wpack", [DM, 4 * DM], F32, kind="ExternalInput").ap()
    a1_bpack = DT("a1_bpack", [128, 32], F32, kind="ExternalInput").ap()
    a1_bvrow = DT("a1_bvrow", [1, DM], F32, kind="ExternalInput").ap()
    a1_hostm = DT("a1_hostm", [128, 1536], F32, kind="ExternalInput").ap()
    a1_gb = DT("a1_gb", [128, 4], F32, kind="ExternalInput").ap()
    x1 = DT("x1", [DM, T], F32, kind="Internal").ap()
    x2 = DT("x2", [DM, T], F32, kind="Internal").ap()
    x3 = DT("x3", [DM, T], F32, kind="Internal").ap()
    x4 = DT("x4", [DM, T], F32, kind="Internal").ap()
    x5 = DT("x5", [DM, T], F32, kind="Internal").ap()
    s1a = DT("s1a", [DM, T], F32, kind="Internal").ap()
    s1b = DT("s1b", [DM, T], F32, kind="Internal").ap()
    zna = DT("zna", [DM, T], BF16, kind="Internal").ap()
    znb = DT("znb", [DM, T], BF16, kind="Internal").ap()
    za = DT("za", [DM, T], F32, kind="Internal").ap()
    zb = DT("zb", [DM, T], F32, kind="Internal").ap()
    agi_a = DT("agi_a", [128, 128], F32, kind="Internal")
    ago_a = DT("ago_a", [1024, 128], F32, kind="Internal")
    agi_b = DT("agi_b", [128, 128], F32, kind="Internal")
    ago_b = DT("ago_b", [1024, 128], F32, kind="Internal")

    master = contextlib.ExitStack()
    P = Prog(nc)

    _pn = [0]

    def phase(fn, name=None):
        st = contextlib.ExitStack()
        nm = name or "ph{}".format(_pn[0])
        _pn[0] += 1
        P.scope = nm
        fn(st.enter_context)
        st.close()
        P.barrier()
        P.scope = None

    attn_common = dict(minus1_in=minus1_in, psel2_in=psel2_in, bsel_in=bsel_in,
                       ident_in=ident_in)

    phase(lambda E: plan_l0ffn(P, nc, E, dict(
        x_in=x0, x_out=x1, w1t_in=f_w1t, w2t_in=f_w2t, w1bf_out=w1bf, w2bf_out=w2bf,
        b1_in=f_b1, b2_in=f_b2), True), "l0ffn_tz")
    phase(lambda E: plan_attn(P, nc, E, dict(
        attn_common, x_in=x1, x_out=x2, wpack=a0_wpack, bpack=a0_bpack, bvrow=a0_bvrow,
        hostm_in=a0_hostm), "l0"), "l0attn")
    phase(lambda E: plan_l0ffn(P, nc, E, dict(
        x_in=x2, x_out=x3, w1bf_in=w1bf, w2bf_in=w2bf, b1_in=f_b1, b2_in=f_b2), False), "l0ffn2")

    def l1ffn(xin, xout, s1, zn, zz, agi, ago, tg=""):
        phase(lambda E: plan_attn(P, nc, E, dict(
            attn_common, x_in=xin, x_out=s1, wpack=m_wpack, bpack=m_bpack, bvrow=m_bvrow,
            hostm_in=m_hostm, g1_in=m_g1, zbuf=zz, ru=m_ru, rw=m_rw, m1=m_m1, m2=m_m2),
            "l1mha"), "mha" + tg)
        phase(lambda E: plan_router(P, nc, E, dict(
            x_in=s1, wr_in=r_wr, g2_in=r_g2, zn_out=zn, vloc_out=agi.ap())), "router" + tg)

        def moe_phase(E):
            P.cc(lambda e: e.collective_compute(
                    "AllGather", ALU.bypass, replica_groups=[list(range(8))],
                    ins=[agi.ap().opt()], outs=[ago.ap().opt()]),
                 reads=("out:vloc",), writes=("cc_done",))
            plan_moe(P, nc, E, dict(
                x3_in=xin, s1_in=s1, zn_in=zn, vtok_in=agi.ap(),
                w1_in=o_w1, w2_in=o_w2, gsel_in=o_gsel, gsel2_in=o_gsel2, esel3_in=o_esel3,
                ident8_in=o_ident8, identf_in=ident_in, x_out=xout),
                ago=ago.ap(), permsel=o_permsel, idf32=ident_in)

        phase(moe_phase, "moe" + tg)

    l1ffn(x3, x4, s1a, zna, za, agi_a, ago_a, "A")
    phase(lambda E: plan_attn(P, nc, E, dict(
        attn_common, x_in=x4, x_out=x5, wpack=a1_wpack, bpack=a1_bpack, bvrow=a1_bvrow,
        hostm_in=a1_hostm, gb_in=a1_gb), "l1attn"), "l1attn")
    l1ffn(x5, x_out, s1b, znb, zb, agi_b, ago_b, "B")

    block = master.enter_context(nc.Block())
    P.finalize(block, master)
    master.close()
    return nc


def kernel(**inp):
    x = _np(inp["L0_attn_x"]).astype(np.float32, copy=False)
    xt = [np.ascontiguousarray(x[b].T) for b in range(8)]
    C = DM

    im = {}
    w1t, w2t, b1p, b2p = _prep_l0ffn(inp)
    im.update(f_w1t=w1t, f_w2t=w2t, f_b1=b1p, f_b2=b2p)

    def attn_pack(wq, wk, wv, wo, bq, bk, bv, bo, permute=False):
        if permute:
            perm = rope_perm()
            wq = wq[:, perm]
            wk = wk[:, perm]
        wpackh = np.ascontiguousarray(np.concatenate([wq, wk, wv, wo], axis=1).astype(np.float32))
        z8 = np.zeros(DM, np.float32)
        bq = z8 if bq is None else np.asarray(bq, np.float32)
        bk = z8 if bk is None else np.asarray(bk, np.float32)
        bv = z8 if bv is None else np.asarray(bv, np.float32)
        bo = z8 if bo is None else np.asarray(bo, np.float32)
        bpackh = np.zeros((128, 32), np.float32)
        bpackh[:, 0:8] = bq.reshape(8, 128).T
        bpackh[:, 8:16] = bk.reshape(8, 128).T
        bpackh[:, 24:32] = bo.reshape(8, 128).T
        return wpackh, bpackh, np.ascontiguousarray(bv.reshape(1, DM))

    hc0 = attn_host_consts("l0")
    im.update(minus1_in=hc0["minus1_in"], psel2_in=hc0["psel2_in"], bsel_in=hc0["bsel_in"],
              ident_in=hc0["ident_in"], a0_hostm=hc0["hostm_in"])
    wp, bp, bv_ = attn_pack(_np(inp["L0_attn_Wq"]).astype(np.float32),
                            _np(inp["L0_attn_Wk"]).astype(np.float32),
                            _np(inp["L0_attn_Wv"]).astype(np.float32),
                            _np(inp["L0_attn_Wo"]).astype(np.float32),
                            _np(inp["L0_attn_bq"]), _np(inp["L0_attn_bk"]),
                            _np(inp["L0_attn_bv"]), _np(inp["L0_attn_bo"]))
    im.update(a0_wpack=wp, a0_bpack=bp, a0_bvrow=bv_)

    hcm = attn_host_consts("l1mha")
    wqkv = _np(inp["L1_ffn_w_qkv"]).astype(np.float32)
    wp, bp, bv_ = attn_pack(wqkv[:, 0:C].copy(), wqkv[:, C:2 * C].copy(), wqkv[:, 2 * C:].copy(),
                            _np(inp["L1_ffn_w_o"]).astype(np.float32),
                            None, None, None, None, permute=True)
    im.update(m_wpack=wp, m_bpack=bp, m_bvrow=bv_, m_hostm=hcm["hostm_in"],
              m_ru=hcm["ru"], m_rw=hcm["rw"], m_m1=hcm["m1"], m_m2=hcm["m2"],
              m_g1=np.ascontiguousarray(_np(inp["L1_ffn_g1"]).astype(np.float32).reshape(8, 128).T))

    im.update(r_wr=np.ascontiguousarray(_np(inp["L1_ffn_w_router"]).astype(np.float32)),
              r_g2=np.ascontiguousarray(_np(inp["L1_ffn_g2"]).astype(np.float32).reshape(8, 128).T))
    mc = moe_consts()
    im.update(o_w1=np.ascontiguousarray(_np(inp["L1_ffn_w1"]).astype(np.float32).reshape(8 * DM, 2048)),
              o_w2=np.ascontiguousarray(_np(inp["L1_ffn_w2"]).astype(np.float32).reshape(8 * 2048, DM)),
              o_gsel=mc["gsel_in"], o_gsel2=mc["gsel2_in"], o_esel3=mc["esel3_in"],
              o_ident8=mc["ident8_in"], o_permsel=permsel_const())

    hc1 = attn_host_consts("l1attn")
    wqkv1 = _np(inp["L1_attn_W_qkv"]).astype(np.float32)
    bqkv1 = _np(inp["L1_attn_b_qkv"]).astype(np.float32)
    wp, bp, bv_ = attn_pack(wqkv1[:, 0:C].copy(), wqkv1[:, C:2 * C].copy(), wqkv1[:, 2 * C:].copy(),
                            _np(inp["L1_attn_W_out"]).astype(np.float32),
                            bqkv1[0:C], bqkv1[C:2 * C], bqkv1[2 * C:],
                            _np(inp["L1_attn_b_out"]).astype(np.float32))
    gq = _np(inp["L1_attn_q_gamma"]).astype(np.float32)
    gqb = _np(inp["L1_attn_q_beta"]).astype(np.float32)
    gk = _np(inp["L1_attn_k_gamma"]).astype(np.float32)
    gkb = _np(inp["L1_attn_k_beta"]).astype(np.float32)
    gbh = np.zeros((128, 4), np.float32)
    gbh[:, 0] = np.tile(gq, 2)
    gbh[:, 1] = np.tile(gqb, 2)
    gbh[:, 2] = np.tile(gk, 2)
    gbh[:, 3] = np.tile(gkb, 2)
    im.update(a1_wpack=wp, a1_bpack=bp, a1_bvrow=bv_, a1_hostm=hc1["hostm_in"], a1_gb=gbh)

    nc = build_fused()
    in_maps = [dict(im, x0=xt[b]) for b in range(8)]
    res = run_bass_kernel_spmd(nc, in_maps, core_ids=list(range(8)))
    out = np.stack([np.ascontiguousarray(res.results[b]["x_out"].T) for b in range(8)])
    return out


# version 05
# speedup vs baseline: 1.3564x; 1.1431x over previous
import contextlib
import numpy as np
import ml_dtypes
import concourse.bass as bass
import concourse.mybir as mybir
from concourse.bass_utils import run_bass_kernel_spmd

F32 = mybir.dt.float32
BF16 = mybir.dt.bfloat16
I32 = mybir.dt.int32
AF = mybir.ActivationFunctionType
ALU = mybir.AluOpType
AX = mybir.AxisListType

BF16NP = ml_dtypes.bfloat16

T = 2048
DM = 1024
NDT = DM // 128
TC = 512
NTC = T // TC
ROUND_C = float(np.float32(1.5 * 2.0**23))

ENGINES = ("t", "v", "s", "g", "y")
DEBUG_TAPS = False
ATTN_DEBUG = None


class Op:
    __slots__ = ("eng", "emit", "deps", "inc", "idx", "need_inc", "cum", "is_cc", "cc_sem", "scope")

    def __init__(self, eng, emit, deps, inc, idx):
        self.eng, self.emit, self.deps, self.inc, self.idx = eng, emit, deps, inc, idx
        self.need_inc = False
        self.cum = 0
        self.is_cc = False
        self.cc_sem = None
        self.scope = None


class Prog:

    def __init__(self, nc):
        self.nc = nc
        self.items = {e: [] for e in ENGINES}
        self.last_writer = {}
        self.readers = {}
        self.sems = {}
        self.totals = {}
        self.scope = None

    def barrier(self):
        lasts = [self.items[e][-1] for e in ENGINES if self.items[e]]
        for e in ENGINES:
            o = Op(e, lambda en: en.nop(), list(lasts), 1, len(self.items[e]))
            o.scope = self.scope
            self.items[e].append(o)

    def cc(self, emit, reads=(), writes=()):
        o = self.op("g", emit, reads, writes, dma=False)
        o.is_cc = True
        return o

    def op(self, eng, emit, reads=(), writes=(), dma=False):
        deps = []
        for k in reads:
            w = self.last_writer.get(k)
            if w is not None:
                deps.append(w)
        for k in writes:
            w = self.last_writer.get(k)
            if w is not None:
                deps.append(w)
            deps.extend(self.readers.get(k, ()))
        o = Op(eng, emit, deps, 16 if dma else 1, len(self.items[eng]))
        o.scope = self.scope
        self.items[eng].append(o)
        for k in writes:
            self.last_writer[k] = o
            self.readers[k] = []
        for k in reads:
            self.readers.setdefault(k, []).append(o)
        return o

    def t(self, emit, reads=(), writes=()):
        return self.op("t", emit, reads, writes)

    def v(self, emit, reads=(), writes=()):
        return self.op("v", emit, reads, writes)

    def s(self, emit, reads=(), writes=()):
        return self.op("s", emit, reads, writes)

    def g(self, emit, reads=(), writes=()):
        return self.op("g", emit, reads, writes)

    def dma(self, eng, emit, reads=(), writes=()):
        return self.op(eng, emit, reads, writes, dma=True)

    @staticmethod
    def _pe_pe(d, o):
        return d.eng == "t" and o.eng == "t" and d.inc == 1

    def finalize(self, block, stack):
        for e in ENGINES:
            for o in self.items[e]:
                for d in o.deps:
                    if not self._pe_pe(d, o):
                        d.need_inc = True
                if o.inc == 16:
                    o.need_inc = True
        for e in ENGINES:
            c = 0
            for o in self.items[e]:
                if o.is_cc:
                    o.cc_sem = stack.enter_context(self.nc.semaphore("cc_sem{}".format(id(o) % 100000)))
                    o.need_inc = False
                    o.cum = c
                    continue
                if o.need_inc:
                    c += o.inc
                o.cum = c
            self.totals[e] = c
            if c > 0:
                self.sems[e] = stack.enter_context(self.nc.semaphore(f"sem_{e}"))

        prog = self

        def emit_stream(eng_handle, e):
            watermark = {e2: 0 for e2 in ENGINES}
            ccs_done = set()
            prev_dma_cum = 0
            cur_scope = [None, None]

            def set_scope(nm):
                if nm == cur_scope[0]:
                    return
                if cur_scope[0] is not None:
                    prog.nc.leave_named_scope(cur_scope[0], cur_scope[1], False)
                cur_scope[0] = nm
                cur_scope[1] = None
                if nm is not None:
                    cur_scope[1] = prog.nc.enter_named_scope(nm, False)[0]

            for o in prog.items[e]:
                set_scope(o.scope)
                if o.inc == 16 and prev_dma_cum > watermark[e]:
                    eng_handle.wait_ge(prog.sems[e], prev_dma_cum)
                    watermark[e] = prev_dma_cum
                for d in o.deps:
                    if prog._pe_pe(d, o):
                        continue
                    if d.is_cc:
                        if id(d) not in ccs_done:
                            eng_handle.wait_ge(d.cc_sem, 1)
                            ccs_done.add(id(d))
                        if d.cum > watermark[d.eng]:
                            eng_handle.wait_ge(prog.sems[d.eng], d.cum)
                            watermark[d.eng] = d.cum
                        continue
                    if d.cum > watermark[d.eng]:
                        eng_handle.wait_ge(prog.sems[d.eng], d.cum)
                        watermark[d.eng] = d.cum
                ins = o.emit(eng_handle)
                if o.is_cc:
                    ins.then_inc(o.cc_sem)
                elif o.need_inc:
                    ins.then_inc(prog.sems[e], o.inc)
                if o.inc == 16:
                    prev_dma_cum = o.cum
            set_scope(None)

        for e, attr in (("g", "gpsimd"), ("t", "tensor"), ("v", "vector"),
                        ("s", "scalar"), ("y", "sync")):
            if not self.items[e]:
                continue
            getattr(block, attr)(lambda eng, _e=e: emit_stream(eng, _e))



def emit_ln_chunk(P, tn, ctx, x_hbm, c, XN, xn_tag, rms_only=False, eps=1e-5):
    XB8, XSQ = ctx["XB8"], ctx["XSQ"]
    AROW, BROW, TROW = ctx["AROW"], ctx["BROW"], ctx["TROW"]
    ps_s1, ps_s2, ps_a, ps_b = ctx["ps_s1"], ctx["ps_s2"], ctx["ps_a"], ctx["ps_b"]
    ones_colbf, ones32 = ctx["ones_colbf"], ctx["ones32"]
    AB8, XBT = ctx["AB8"], ctx["XBT"]
    rs = slice(0, TC)
    P.dma("g", lambda e, c=c: e.dma_start(
            out=XB8[:, :].rearrange("p (j t) -> p j t", j=NDT),
            in_=x_hbm[:, c * TC:(c + 1) * TC].rearrange("(j p) t -> p j t", p=128)),
          writes=("xb8",))
    if not rms_only:
        for j in range(NDT):
            P.t(lambda e, j=j: e.matmul(ps_s1[0:1, :], ones_colbf[:, 0:1],
                                        XB8[:, j * TC:(j + 1) * TC],
                                        start=(j == 0), stop=(j == NDT - 1)),
                reads=("ones_colbf", "xb8"), writes=("ps_s1",))
    for j in range(NDT):
        slot = j % 2
        bcol = slice(slot * TC, (slot + 1) * TC)
        P.s(lambda e, j=j, bcol=bcol: e.activation(XSQ[:, bcol], XB8[:, j * TC:(j + 1) * TC],
                                                   AF.Square),
            reads=("xb8",), writes=("xsq{}".format(slot),))
        P.t(lambda e, bcol=bcol, j=j: e.matmul(ps_s2[0:1, :], ones_colbf[:, 0:1], XSQ[:, bcol],
                                               start=(j == 0), stop=(j == NDT - 1)),
            reads=("ones_colbf", "xsq{}".format(slot)), writes=("ps_s2",))
    if not rms_only:
        P.v(lambda e: e.tensor_scalar(BROW[0:1, rs], ps_s1[0:1, :], -1.0 / DM, None, ALU.mult),
            reads=("ps_s1",), writes=("brow0",))
        P.v(lambda e: e.tensor_scalar(AROW[0:1, rs], ps_s2[0:1, :], 1.0 / DM, None, ALU.mult),
            reads=("ps_s2",), writes=("arow0",))
        P.v(lambda e: e.tensor_tensor(TROW[0:1, 0:TC], BROW[0:1, rs], BROW[0:1, rs], ALU.mult),
            reads=("brow0",), writes=("trow",))
        P.v(lambda e: e.scalar_tensor_tensor(AROW[0:1, rs], AROW[0:1, rs], float(eps),
                                             TROW[0:1, 0:TC], ALU.add, ALU.subtract),
            reads=("arow0", "trow"), writes=("arow0",))
    else:
        P.v(lambda e: e.tensor_scalar(AROW[0:1, rs], ps_s2[0:1, :], 1.0 / DM, float(eps),
                                      ALU.mult, ALU.add),
            reads=("ps_s2",), writes=("arow0",))
    P.s(lambda e: e.activation(TROW[0:1, 0:TC], AROW[0:1, rs], AF.Ln),
        reads=("arow0",), writes=("trow",))
    P.s(lambda e: e.activation(AROW[0:1, rs], TROW[0:1, 0:TC], AF.Exp, scale=-0.5),
        reads=("trow",), writes=("arow0",))
    P.t(lambda e: e.matmul(ps_a[:, :], ones32[0:1, :], AROW[0:1, rs], start=True, stop=True),
        reads=("ones32", "arow0"), writes=("ps_a",))
    P.v(lambda e: e.tensor_copy(AB8[:, 0:TC], ps_a[:, :]), reads=("ps_a",), writes=("ab8a",))
    if not rms_only:
        P.t(lambda e: e.matmul(ps_b[:, :], ones32[0:1, :], BROW[0:1, rs], start=True, stop=True),
            reads=("ones32", "brow0"), writes=("ps_b",))
        P.v(lambda e: e.tensor_copy(AB8[:, TC:2 * TC], ps_b[:, :]), reads=("ps_b",), writes=("ab8b",))
    for j in range(NDT):
        dst = slice(j * TC, (j + 1) * TC)
        xsl = slice(j * TC, (j + 1) * TC)
        if rms_only:
            P.v(lambda e, xsl=xsl, dst=dst: e.tensor_tensor(XN[:, dst], XB8[:, xsl],
                                                            AB8[:, 0:TC], ALU.mult),
                reads=("xb8", "ab8a"), writes=("{}{}".format(xn_tag, j),))
        else:
            slot = j % 2
            tsl = slice(slot * TC, (slot + 1) * TC)
            P.v(lambda e, xsl=xsl, tsl=tsl: e.tensor_tensor(XBT[:, tsl], XB8[:, xsl],
                                                            AB8[:, TC:2 * TC], ALU.add),
                reads=("xb8", "ab8b"), writes=("xbt{}".format(slot),))
            P.v(lambda e, tsl=tsl, dst=dst: e.tensor_tensor(XN[:, dst], XBT[:, tsl],
                                                            AB8[:, 0:TC], ALU.mult),
                reads=("xbt{}".format(slot), "ab8a"), writes=("{}{}".format(xn_tag, j),))



def emit_ternarize_to_hbm(P, tn, ctx, WT_hbm, rows, cols, out_hbm, out_tag, final_mult):
    nrt = rows // 128
    WF, RS, SROW = ctx["WF"], ctx["RS"], ctx["SROW"]
    ASUM, STAGE = ctx["ASUM"], ctx["WSTG"]
    ones32, ones32col = ctx["ones32"], ctx["ones32col"]
    ps_w, ps_rs = ctx["ps_w"], ctx["ps_rs"]
    for r in range(nrt):
        slot = r % 2
        wsl = slice(slot * 4096, slot * 4096 + cols)
        P.dma("y", lambda e, r=r, wsl=wsl: e.dma_start(out=WF[:, wsl],
                                                       in_=WT_hbm[r * 128:(r + 1) * 128, :]),
              writes=("wf{}".format(slot),))
        P.v(lambda e, r=r, wsl=wsl: e.tensor_reduce(ASUM[:, r:r + 1], WF[:, wsl], AX.X, ALU.add,
                                                    apply_absolute_value=True),
            reads=("wf{}".format(slot),), writes=("asum",))
    P.t(lambda e: e.matmul(ps_w[0:1, 0:nrt], ones32col[:, 0:1], ASUM[:, 0:nrt],
                           start=True, stop=True),
        reads=("asum", "ones32col"), writes=("ps_a",))
    P.v(lambda e: e.tensor_reduce(SROW[0:1, 0:1], ps_w[0:1, 0:nrt], AX.X, ALU.add),
        reads=("ps_a",), writes=("srow",))
    P.v(lambda e: e.tensor_scalar(SROW[0:1, 0:1], SROW[0:1, 0:1], 1.0 / (rows * cols), 1e-8,
                                  ALU.mult, ALU.max),
        reads=("srow",), writes=("srow",))
    P.v(lambda e: e.reciprocal(SROW[0:1, 1:2], SROW[0:1, 0:1]),
        reads=("srow",), writes=("srow",))
    P.t(lambda e: e.matmul(ps_rs[:, 0:1], ones32[0:1, :], SROW[0:1, 1:2], start=True, stop=True),
        reads=("srow", "ones32"), writes=("ps_a",))
    P.v(lambda e: e.tensor_copy(RS[:, 0:1], ps_rs[:, 0:1]),
        reads=("ps_a",), writes=("rs",))
    for r in range(nrt):
        slot = r % 2
        wsl = slice(slot * 4096, slot * 4096 + cols)
        ssl = slice(slot * cols, (slot + 1) * cols)
        P.dma("y", lambda e, r=r, wsl=wsl: e.dma_start(out=WF[:, wsl],
                                                       in_=WT_hbm[r * 128:(r + 1) * 128, :]),
              writes=("wf{}".format(slot),))
        P.v(lambda e, wsl=wsl: e.tensor_scalar(WF[:, wsl], WF[:, wsl], RS[:, 0:1], ROUND_C,
                                               ALU.mult, ALU.add),
            reads=("wf{}".format(slot), "rs"), writes=("wf{}".format(slot),))
        P.v(lambda e, wsl=wsl: e.tensor_scalar(WF[:, wsl], WF[:, wsl], ROUND_C, 1.0,
                                               ALU.subtract, ALU.min),
            reads=("wf{}".format(slot),), writes=("wf{}".format(slot),))
        P.v(lambda e, wsl=wsl, ssl=ssl: e.tensor_scalar(STAGE[:, ssl], WF[:, wsl],
                                                        -1.0, final_mult, ALU.max, ALU.mult),
            reads=("wf{}".format(slot),), writes=("wstg{}".format(slot),))
        P.dma("y", lambda e, r=r, ssl=ssl: e.dma_start(out=out_hbm[r * 128:(r + 1) * 128, :],
                                                       in_=STAGE[:, ssl]),
              reads=("wstg{}".format(slot),), writes=(f"{out_tag}{r}",))



def plan_l0ffn(P, nc, E, io, with_ternarize):
    x_in = io.get("x_in")
    b1_in = io.get("b1_in")
    b2_in = io.get("b2_in")
    x_out = io.get("x_out")
    w1t_in = io.get("w1t_in")
    w2t_in = io.get("w2t_in")
    w1bf = io.get("w1bf_out") if with_ternarize else io.get("w1bf_in")
    w2bf = io.get("w2bf_out") if with_ternarize else io.get("w2bf_in")
    xn_dbg = io.get("xn_dbg")
    h_dbg = io.get("h_dbg")
    o_dbg = io.get("o_dbg")
    DBGF = E(nc.sbuf_tensor("DBGF", [128, 2 * TC], F32)) if DEBUG_TAPS else None
    XN = E(nc.sbuf_tensor("XN", [128, NDT * TC], BF16))
    XB8 = E(nc.sbuf_tensor("XB8", [128, NDT * TC], BF16))
    AB8 = E(nc.sbuf_tensor("AB8", [128, 2 * TC], BF16))
    XBT = E(nc.sbuf_tensor("XBT", [128, 2 * TC], BF16))
    XSQ = E(nc.sbuf_tensor("XSQ", [128, 2 * TC], BF16))
    XR = E(nc.sbuf_tensor("XR", [128, 2 * TC], F32))
    W1B = E(nc.sbuf_tensor("W1B", [128, NDT * 4096], BF16))
    BIGBF_T = E(nc.sbuf_tensor("BIGBF", [128, 32768], BF16))
    BIGBF = BIGBF_T[:, :]
    BIGB = BIGBF.bitcast(F32)
    HB = E(nc.sbuf_tensor("HB", [128, 32 * TC], BF16))
    TMPC = E(nc.sbuf_tensor("TMPC", [128, 2 * TC], F32))
    AROW = E(nc.sbuf_tensor("AROW", [1, TC], F32))
    BROW = E(nc.sbuf_tensor("BROW", [1, TC], F32))
    TROW = E(nc.sbuf_tensor("TROW", [1, TC], F32))
    SROW = E(nc.sbuf_tensor("SROW", [1, 16], F32))
    ASUM = E(nc.sbuf_tensor("ASUM", [128, 64], F32))
    RS = E(nc.sbuf_tensor("RS", [128, 2], F32))
    b1_sb = E(nc.sbuf_tensor("b1_sb", [128, 32], F32))
    b2_sb = E(nc.sbuf_tensor("b2_sb", [128, 8], F32))
    ones_colbf = E(nc.sbuf_tensor("ones_colbf", [128, 1], BF16))
    ones32 = E(nc.sbuf_tensor("ones32", [1, 128], F32))
    ones32col = E(nc.sbuf_tensor("ones32col", [128, 1], F32))

    ps_s1 = E(nc.psum_tensor("ps_s1", [128, TC], F32))
    ps_s2 = E(nc.psum_tensor("ps_s2", [128, TC], F32))
    ps_a = E(nc.psum_tensor("ps_a", [128, TC], F32))
    ps_b = E(nc.psum_tensor("ps_b", [128, TC], F32))
    ps_h = [E(nc.psum_tensor(f"ps_h{i}", [128, TC], F32)) for i in range(2)]
    ps_o = [E(nc.psum_tensor(f"ps_o{i}", [128, TC], F32)) for i in range(2)]

    ctx = dict(XN=XN, XB8=XB8, AB8=AB8, XBT=XBT, XSQ=XSQ, TMPC=TMPC,
               AROW=AROW, BROW=BROW, TROW=TROW,
               SROW=SROW, ASUM=ASUM, RS=RS,
               WF=BIGB[:, 0:8192], WSTG=BIGBF[:, 16384:24576],
               ones_colbf=ones_colbf, ones32=ones32, ones32col=ones32col,
               ps_s1=ps_s1, ps_s2=ps_s2, ps_a=ps_a, ps_b=ps_b,
               ps_w=ps_a[0:1, 0:64], ps_rs=ps_a[:, 64:65])

    P.g(lambda e: e.memset(ones_colbf[:, :], 1.0), writes=("ones_colbf",))
    P.g(lambda e: e.memset(ones32[:, :], 1.0), writes=("ones32",))
    P.g(lambda e: e.memset(ones32col[:, :], 1.0), writes=("ones32col",))
    P.dma("g", lambda e: e.dma_start(out=b1_sb[:, :], in_=b1_in[:, :]), writes=("b1_sb",))
    P.dma("g", lambda e: e.dma_start(out=b2_sb[:, :], in_=b2_in[:, :]), writes=("b2_sb",))
    P.v(lambda e: e.tensor_scalar(b2_sb[:, :], b2_sb[:, :], 0.5, None, ALU.mult),
        reads=("b2_sb",), writes=("b2_sb",))

    if with_ternarize:
        emit_ternarize_to_hbm(P, "tz1", ctx, w1t_in, DM, 4096, w1bf, "w1hbm", 1.0)
        emit_ternarize_to_hbm(P, "tz2", ctx, w2t_in, 4096, DM, w2bf, "w2hbm", 0.5)
        w1deps = lambda j: (f"w1hbm{j}",)
        w2deps = lambda r: (f"w2hbm{r}",)
        tz_keys = ("wf0", "wf1", "wstg0", "wstg1")
    else:
        w1deps = lambda j: ()
        w2deps = lambda r: ()
        tz_keys = ()

    P.op("y", lambda e: e.nop(), reads=tz_keys, writes=("bigfree",))

    P.dma("y", lambda e: e.dma_start(
            out=W1B[:, :].rearrange("p (j f) -> p j f", j=NDT),
            in_=w1bf.rearrange("(j p) f -> p j f", p=128)),
          reads=tuple(k for j in range(NDT) for k in w1deps(j)),
          writes=tuple("w1res{}".format(j) for j in range(NDT)))
    P.dma("y", lambda e: e.dma_start(
            out=BIGBF[:, 0:32 * DM].rearrange("p (f c) -> p f c", f=32),
            in_=w2bf.rearrange("(f p) c -> p f c", p=128)),
          reads=tuple(k for r in range(32) for k in w2deps(r)) + ("bigfree",),
          writes=tuple("w2res{}".format(r) for r in range(32)))

    NFT = 32
    for c in range(NTC):
        emit_ln_chunk(P, "ln", ctx, x_in, c, XN, "xn", rms_only=False, eps=1e-5)
        for ft in range(NFT):
            pslot = ft % 2
            ps = ps_h[pslot]
            for j in range(NDT):
                P.t(lambda e, ft=ft, j=j, ps=ps: e.matmul(
                        ps[:, :],
                        W1B[:, j * 4096 + ft * 128: j * 4096 + (ft + 1) * 128],
                        XN[:, j * TC:(j + 1) * TC],
                        start=(j == 0), stop=(j == NDT - 1)),
                    reads=(f"w1res{j}", f"xn{j}"), writes=(f"ps_h{pslot}",))
            P.s(lambda e, ft=ft, ps=ps, pslot=pslot: e.activation(
                    HB[:, ft * TC:(ft + 1) * TC], ps[:, :], AF.Gelu, bias=b1_sb[:, ft:ft + 1]),
                reads=(f"ps_h{pslot}", "b1_sb"), writes=(f"HB{ft}",))
        if DEBUG_TAPS and c == 0:
            for j in range(NDT):
                sl = slice((j % 2) * TC, (j % 2 + 1) * TC)
                P.v(lambda e, j=j, sl=sl: e.tensor_copy(DBGF[:, sl], XN[:, j * TC:(j + 1) * TC]),
                    reads=(f"xn{j}",), writes=(f"dbg{j%2}",))
                P.dma("y", lambda e, j=j, sl=sl: e.dma_start(out=xn_dbg[:, j * TC:(j + 1) * TC],
                                                             in_=DBGF[:, sl]),
                      reads=(f"dbg{j%2}",), writes=(f"odbgxn{j}",))
            for ft in range(4):
                sl = slice((ft % 2) * TC, (ft % 2 + 1) * TC)
                P.v(lambda e, ft=ft, sl=sl: e.tensor_copy(DBGF[:, sl], HB[:, ft * TC:(ft + 1) * TC]),
                    reads=(f"HB{ft}",), writes=(f"dbg{ft%2}",))
                P.dma("y", lambda e, ft=ft, sl=sl: e.dma_start(out=h_dbg[:, ft * TC:(ft + 1) * TC],
                                                               in_=DBGF[:, sl]),
                      reads=(f"dbg{ft%2}",), writes=(f"odbgh{ft}",))
        for j in range(NDT):
            pslot = j % 2
            ps = ps_o[pslot]
            for ft in range(NFT):
                P.t(lambda e, j=j, ft=ft, ps=ps: e.matmul(
                        ps[:, :], BIGBF[:, ft * DM + j * 128: ft * DM + (j + 1) * 128],
                        HB[:, ft * TC:(ft + 1) * TC],
                        start=(ft == 0), stop=(ft == NFT - 1)),
                    reads=(f"w2res{ft}", f"HB{ft}"), writes=(f"ps_o{pslot}",))
            xslot = j % 2
            xsl = slice(xslot * TC, (xslot + 1) * TC)
            P.dma("y", lambda e, j=j, c=c, xsl=xsl: e.dma_start(
                    out=XR[:, xsl], in_=x_in[j * 128:(j + 1) * 128, c * TC:(c + 1) * TC]),
                  writes=("rx{}".format(xslot),))
            tslot = j % 2
            tc2 = slice(tslot * TC, (tslot + 1) * TC)
            P.v(lambda e, j=j, ps=ps, tc2=tc2, xsl=xsl: e.scalar_tensor_tensor(
                    TMPC[:, tc2], ps[:, :], b2_sb[:, j:j + 1], XR[:, xsl],
                    ALU.add, ALU.add),
                reads=(f"ps_o{pslot}", "b2_sb", "rx{}".format(xslot)), writes=("tmpc{}".format(tslot),))
            P.dma("y", lambda e, j=j, c=c, tc2=tc2: e.dma_start(
                    out=x_out[j * 128:(j + 1) * 128, c * TC:(c + 1) * TC], in_=TMPC[:, tc2]),
                  reads=("tmpc{}".format(tslot),), writes=(f"out:x{c}_{j}",))

    keys = [f"out:x{c}_{j}" for c in range(NTC) for j in range(NDT)]
    if DEBUG_TAPS:
        keys += [f"odbgxn{j}" for j in range(NDT)] + [f"odbgh{ft}" for ft in range(4)]
    if with_ternarize:
        keys += [f"w1hbm{j}" for j in range(NDT)] + [f"w2hbm{r}" for r in range(32)]
    P.op("y", lambda e: e.nop(), reads=tuple(keys))



def build_l0ffn_launch(with_ternarize):
    nc = bass.Bass("TRN2", target_bir_lowering=False)
    x_in = nc.dram_tensor("x_in", [DM, T], F32, kind="ExternalInput").ap()
    b1_in = nc.dram_tensor("b1_in", [128, 32], F32, kind="ExternalInput").ap()
    b2_in = nc.dram_tensor("b2_in", [128, 8], F32, kind="ExternalInput").ap()
    x_out = nc.dram_tensor("x_out", [DM, T], F32, kind="ExternalOutput").ap()
    if with_ternarize:
        w1t_in = nc.dram_tensor("w1t_in", [DM, 4096], F32, kind="ExternalInput").ap()
        w2t_in = nc.dram_tensor("w2t_in", [4096, DM], F32, kind="ExternalInput").ap()
        w1bf = nc.dram_tensor("w1bf_out", [DM, 4096], BF16, kind="ExternalOutput").ap()
        w2bf = nc.dram_tensor("w2bf_out", [4096, DM], BF16, kind="ExternalOutput").ap()
    else:
        w1bf = nc.dram_tensor("w1bf_in", [DM, 4096], BF16, kind="ExternalInput").ap()
        w2bf = nc.dram_tensor("w2bf_in", [4096, DM], BF16, kind="ExternalInput").ap()

    if DEBUG_TAPS:
        xn_dbg = nc.dram_tensor("xn_dbg", [128, NDT * TC], F32, kind="ExternalOutput").ap()
        h_dbg = nc.dram_tensor("h_dbg", [128, 4 * TC], F32, kind="ExternalOutput").ap()
        o_dbg = nc.dram_tensor("o_dbg", [128, 2 * TC], F32, kind="ExternalOutput").ap()

    io = dict(x_in=x_in, b1_in=b1_in, b2_in=b2_in, x_out=x_out)
    if with_ternarize:
        io.update(w1t_in=w1t_in, w2t_in=w2t_in, w1bf_out=w1bf, w2bf_out=w2bf)
    else:
        io.update(w1bf_in=w1bf, w2bf_in=w2bf)
    if DEBUG_TAPS:
        io.update(xn_dbg=xn_dbg, h_dbg=h_dbg, o_dbg=o_dbg)
    st = contextlib.ExitStack()
    E = st.enter_context
    P = Prog(nc)
    plan_l0ffn(P, nc, E, io, with_ternarize)
    block = E(nc.Block())
    P.finalize(block, st)
    st.close()
    return nc


NH = 16
HD = 64


def attn_host_consts(variant):
    c = {}
    c["minus1_in"] = np.full((1, 128), -1.0, np.float32)
    p2 = np.zeros((128, 128), np.float32)
    p2[0:64, 0] = 1.0
    p2[64:128, 64] = 1.0
    c["psel2_in"] = p2
    bs = np.zeros((128, 128), np.float32)
    bs[0, 0:64] = 1.0
    bs[64, 64:128] = 1.0
    c["bsel_in"] = bs
    c["ident_in"] = np.eye(128, dtype=np.float32)
    if variant == "l0":
        W, umin, width = 129, -511, 1152
    elif variant == "l1attn":
        W, umin, width = 512, -511, 1536
    else:
        W = None
    if W is not None:
        ik = np.arange(128)[:, None]
        u = np.arange(width)[None, :]
        qk = u + umin - ik
        c["hostm_in"] = np.where((qk >= 0) & (qk <= W - 1), 0.0, -1e30).astype(np.float32)
    if variant == "l1mha":
        ik = np.arange(128)[:, None]
        u = np.arange(1024)[None, :]
        c["hostm_in"] = np.where(u - 511 - ik >= 0, 0.0, -1e30).astype(np.float32)
        f = (np.arange(128) % 32).astype(np.float64)
        theta = 10000.0 ** (-2.0 * f / 64.0)
        t = np.arange(T, dtype=np.float64)
        ang = t[None, :] * theta[:, None]
        cos, sin = np.cos(ang), np.sin(ang)
        r_rows = np.zeros(128, bool)
        r_rows[0:32] = True
        r_rows[64:96] = True
        ru = np.where(r_rows[:, None], cos, sin)
        rw = np.where(r_rows[:, None], sin, cos)
        c["ru"] = ru.astype(np.float32)
        c["rw"] = rw.astype(np.float32)
        m1 = np.zeros((128, 128), np.float32)
        m2 = np.zeros((128, 128), np.float32)
        for m in range(128):
            if r_rows[m]:
                m1[m, m] = 1.0
                m1[m + 32, m] = -1.0
            else:
                m2[m - 32, m] = 1.0
                m2[m, m] = 1.0
        c["m1"] = m1
        c["m2"] = m2
    return c


def rope_perm():
    perm = np.zeros(DM, np.int64)
    for h in range(NH):
        base = h * 64
        perm[base + 0:base + 32] = base + np.arange(0, 64, 2)
        perm[base + 32:base + 64] = base + np.arange(1, 64, 2)
    return perm


def kt_range(variant, qc):
    if variant == "l0":
        lo = max(0, 4 * qc - 1)
    elif variant == "l1attn":
        lo = max(0, (qc * 512 - 511) // 128)
    else:
        lo = 0
    return list(range(lo, 4 * qc + 4))


def emit_ln_full(P, ctx, x_hbm, XN, rms_only=False, eps=1e-5, gfold=None):
    XB8, XSQ = ctx["XB8"], ctx["XSQ"]
    AROW, BROW, TROW = ctx["AROW"], ctx["BROW"], ctx["TROW"]
    ps_s1, ps_s2, ps_a, ps_b = ctx["ps_s1"], ctx["ps_s2"], ctx["ps_a"], ctx["ps_b"]
    ones_colbf, ones32 = ctx["ones_colbf"], ctx["ones32"]
    AB8 = ctx["AB8"]
    for c in range(NTC):
        rs = slice(0, TC)
        rkey = 0
        P.dma("g", lambda e, c=c: e.dma_start(
                out=XB8[:, :].rearrange("p (j t) -> p j t", j=NDT),
                in_=x_hbm[:, c * TC:(c + 1) * TC].rearrange("(j p) t -> p j t", p=128)),
              writes=("xb8",))
        if not rms_only:
            for j in range(NDT):
                P.t(lambda e, j=j: e.matmul(ps_s1[0:1, :], ones_colbf[:, 0:1],
                                            XB8[:, j * TC:(j + 1) * TC],
                                            start=(j == 0), stop=(j == NDT - 1)),
                    reads=("ones_colbf", "xb8"), writes=("ps_s1",))
        for j in range(NDT):
            slot = j % 2
            bcol = slice(slot * TC, (slot + 1) * TC)
            P.s(lambda e, j=j, bcol=bcol: e.activation(XSQ[:, bcol], XB8[:, j * TC:(j + 1) * TC],
                                                       AF.Square),
                reads=("xb8",), writes=("xsq{}".format(slot),))
            P.t(lambda e, bcol=bcol, j=j: e.matmul(ps_s2[0:1, :], ones_colbf[:, 0:1], XSQ[:, bcol],
                                                   start=(j == 0), stop=(j == NDT - 1)),
                reads=("ones_colbf", "xsq{}".format(slot)), writes=("ps_s2",))
        if not rms_only:
            P.v(lambda e, rs=rs: e.tensor_scalar(BROW[0:1, rs], ps_s1[0:1, :], -1.0 / DM, None, ALU.mult),
                reads=("ps_s1",), writes=("brow0",))
            P.v(lambda e, rs=rs: e.tensor_scalar(AROW[0:1, rs], ps_s2[0:1, :], 1.0 / DM, None, ALU.mult),
                reads=("ps_s2",), writes=("arow0",))
            P.v(lambda e, rs=rs: e.tensor_tensor(TROW[0:1, 0:TC], BROW[0:1, rs], BROW[0:1, rs], ALU.mult),
                reads=("brow0",), writes=("trow",))
            P.v(lambda e, rs=rs: e.scalar_tensor_tensor(AROW[0:1, rs], AROW[0:1, rs], float(eps),
                                                        TROW[0:1, 0:TC], ALU.add, ALU.subtract),
                reads=("arow0", "trow"), writes=("arow0",))
        else:
            P.v(lambda e, rs=rs: e.tensor_scalar(AROW[0:1, rs], ps_s2[0:1, :], 1.0 / DM, float(eps),
                                                 ALU.mult, ALU.add),
                reads=("ps_s2",), writes=("arow0",))
        P.s(lambda e, rs=rs: e.activation(TROW[0:1, 0:TC], AROW[0:1, rs], AF.Ln),
            reads=("arow0",), writes=("trow",))
        P.s(lambda e, rs=rs: e.activation(AROW[0:1, rs], TROW[0:1, 0:TC], AF.Exp, scale=-0.5),
            reads=("trow",), writes=("arow0",))
        P.t(lambda e, rs=rs: e.matmul(ps_a[:, :], ones32[0:1, :], AROW[0:1, rs], start=True, stop=True),
            reads=("ones32", "arow0"), writes=("ps_a",))
        P.v(lambda e: e.tensor_copy(AB8[:, 0:TC], ps_a[:, :]), reads=("ps_a",), writes=("ab8a",))
        if not rms_only:
            P.t(lambda e, rs=rs: e.matmul(ps_b[:, :], ones32[0:1, :], BROW[0:1, rs], start=True, stop=True),
                reads=("ones32", "brow0"), writes=("ps_b",))
            P.v(lambda e: e.tensor_copy(AB8[:, TC:2 * TC], ps_b[:, :]), reads=("ps_b",), writes=("ab8b",))
        for j in range(NDT):
            dst = slice(j * T + c * TC, j * T + (c + 1) * TC)
            xsl = slice(j * TC, (j + 1) * TC)
            if rms_only:
                P.v(lambda e, xsl=xsl, dst=dst: e.tensor_tensor(XN[:, dst], XB8[:, xsl],
                                                                AB8[:, 0:TC], ALU.mult),
                    reads=("xb8", "ab8a"), writes=(f"xn{j}_{c}",))
            else:
                slot = j % 2
                tsl = slice(slot * TC, (slot + 1) * TC)
                P.v(lambda e, xsl=xsl, tsl=tsl: e.tensor_tensor(ctx["XBT"][:, tsl], XB8[:, xsl],
                                                                AB8[:, TC:2 * TC], ALU.add),
                    reads=("xb8", "ab8b"), writes=("xbt{}".format(slot),))
                P.v(lambda e, tsl=tsl, dst=dst: e.tensor_tensor(XN[:, dst], ctx["XBT"][:, tsl],
                                                                AB8[:, 0:TC], ALU.mult),
                    reads=("xbt{}".format(slot), "ab8a"), writes=(f"xn{j}_{c}",))


def emit_rope(P, OUT, otag, RU, RW, M1, M2, TMPE, ps, pskey):
    for jt in range(NDT):
        for c in range(NTC):
            cs = slice(jt * T + c * TC, jt * T + (c + 1) * TC)
            tcs = slice(c * TC, (c + 1) * TC)
            P.v(lambda e, cs=cs, tcs=tcs: e.tensor_tensor(
                    TMPE[:, 0:TC], OUT[:, cs], RU[:, tcs], ALU.mult),
                reads=(otag + str(jt), "ropeu"), writes=("tmpe_a",))
            P.v(lambda e, cs=cs, tcs=tcs: e.tensor_tensor(
                    TMPE[:, TC:2 * TC], OUT[:, cs], RW[:, tcs], ALU.mult),
                reads=(otag + str(jt), "ropew"), writes=("tmpe_b",))
            P.t(lambda e: e.matmul(ps[:, :], M1[:, :], TMPE[:, 0:TC], start=True, stop=False),
                reads=("ropem1", "tmpe_a"), writes=(pskey,))
            P.t(lambda e: e.matmul(ps[:, :], M2[:, :], TMPE[:, TC:2 * TC], start=False, stop=True),
                reads=("ropem2", "tmpe_b"), writes=(pskey,))
            P.v(lambda e, cs=cs: e.tensor_copy(OUT[:, cs], ps[:, :]),
                reads=(pskey,), writes=(otag + str(jt),))


def emit_qkln(P, ctx, OUT, otag, GB, gcol, bcol, psel2, bself, SSC, TMPC2, pp, eps=1e-5):
    ps_q = pp[7]
    ps_r = pp[2]
    for jt in range(NDT):
        for c in range(NTC):
            cs = slice(jt * T + c * TC, jt * T + (c + 1) * TC)
            P.t(lambda e, cs=cs: e.matmul(ps_q[:, :], psel2[:, :], OUT[:, cs], start=True, stop=True),
                reads=("psel2", otag + str(jt)), writes=("ps_x",))
            P.v(lambda e: e.tensor_scalar(SSC[:, 0:TC], ps_q[:, :], 1.0 / HD, None, ALU.mult),
                reads=("ps_x",), writes=("sscA",))
            P.s(lambda e, cs=cs: e.activation(ctx["XSQ"][:, 0:TC], OUT[:, cs], AF.Square),
                reads=(otag + str(jt),), writes=("xsq0",))
            P.t(lambda e: e.matmul(ps_q[:, :], psel2[:, :], ctx["XSQ"][:, 0:TC], start=True, stop=True),
                reads=("psel2", "xsq0"), writes=("ps_x",))
            P.v(lambda e: e.tensor_tensor(SSC[:, TC:2 * TC], SSC[:, 0:TC], SSC[:, 0:TC], ALU.mult),
                reads=("sscA",), writes=("sscB",))
            P.v(lambda e: e.scalar_tensor_tensor(TMPC2[:, 0:TC], ps_q[:, :], 1.0 / HD,
                                                 SSC[:, TC:2 * TC], ALU.mult, ALU.subtract),
                reads=("ps_x", "sscB"), writes=("tmpdA",))
            P.v(lambda e: e.tensor_scalar(TMPC2[:, 0:TC], TMPC2[:, 0:TC], float(eps), 1e-37,
                                          ALU.add, ALU.max),
                reads=("tmpdA",), writes=("tmpdA",))
            P.s(lambda e: e.activation(TMPC2[:, TC:2 * TC], TMPC2[:, 0:TC], AF.Ln),
                reads=("tmpdA",), writes=("tmpdB",))
            P.s(lambda e: e.activation(TMPC2[:, 0:TC], TMPC2[:, TC:2 * TC], AF.Exp, scale=-0.5),
                reads=("tmpdB",), writes=("tmpdA",))
            P.t(lambda e: e.matmul(ps_r[:, :], bself[:, :], SSC[:, 0:TC], start=True, stop=True),
                reads=("bself", "sscA"), writes=("ps_a",))
            P.v(lambda e: e.tensor_copy(SSC[:, 0:TC], ps_r[:, :]),
                reads=("ps_a",), writes=("sscA",))
            P.t(lambda e: e.matmul(ps_r[:, :], bself[:, :], TMPC2[:, 0:TC], start=True, stop=True),
                reads=("bself", "tmpdA"), writes=("ps_a",))
            P.v(lambda e, cs=cs: e.tensor_tensor(TMPC2[:, TC:2 * TC], OUT[:, cs], SSC[:, 0:TC],
                                                 ALU.subtract),
                reads=(otag + str(jt), "sscA"), writes=("tmpdB",))
            P.v(lambda e: e.tensor_tensor(TMPC2[:, TC:2 * TC], TMPC2[:, TC:2 * TC], ps_r[:, :],
                                          ALU.mult),
                reads=("tmpdB", "ps_a"), writes=("tmpdB",))
            P.v(lambda e, cs=cs: e.tensor_scalar(OUT[:, cs], TMPC2[:, TC:2 * TC],
                                                 GB[:, gcol:gcol + 1], GB[:, bcol:bcol + 1],
                                                 ALU.mult, ALU.add),
                reads=("tmpdB", "gb"), writes=(otag + str(jt),))


def emit_ln_to_hbm(P, ctx, x_hbm, z_hbm, eps=1e-5):
    XB8, XSQ = ctx["XB8"], ctx["XSQ"]
    AROW, BROW, TROW = ctx["AROW"], ctx["BROW"], ctx["TROW"]
    ps_s1, ps_s2, ps_a, ps_b = ctx["ps_s1"], ctx["ps_s2"], ctx["ps_a"], ctx["ps_b"]
    ones_colbf, ones32 = ctx["ones_colbf"], ctx["ones32"]
    XC8, TMPC = ctx["XC8"], ctx["TMPC"]
    for c in range(NTC):
        rs = slice(0, TC)
        P.dma("g", lambda e, c=c: e.dma_start(
                out=XB8[:, :].rearrange("p (j t) -> p j t", j=NDT),
                in_=x_hbm[:, c * TC:(c + 1) * TC].rearrange("(j p) t -> p j t", p=128)),
              writes=("xb8",))

        for j in range(NDT):
            P.t(lambda e, j=j: e.matmul(ps_s1[0:1, :], ones_colbf[:, 0:1],
                                        XB8[:, j * TC:(j + 1) * TC],
                                        start=(j == 0), stop=(j == NDT - 1)),
                reads=("ones_colbf", "xb8"), writes=("ps_s1",))
        for j in range(NDT):
            slot = j % 2
            bcol = slice(slot * TC, (slot + 1) * TC)
            P.s(lambda e, j=j, bcol=bcol: e.activation(XSQ[:, bcol], XB8[:, j * TC:(j + 1) * TC],
                                                       AF.Square),
                reads=("xb8",), writes=("xsq{}".format(slot),))
            P.t(lambda e, bcol=bcol, j=j: e.matmul(ps_s2[0:1, :], ones_colbf[:, 0:1], XSQ[:, bcol],
                                                   start=(j == 0), stop=(j == NDT - 1)),
                reads=("ones_colbf", "xsq{}".format(slot)), writes=("ps_s2",))
        P.v(lambda e, rs=rs: e.tensor_scalar(BROW[0:1, rs], ps_s1[0:1, :], -1.0 / DM, None, ALU.mult),
            reads=("ps_s1",), writes=("brow0",))
        P.v(lambda e, rs=rs: e.tensor_scalar(AROW[0:1, rs], ps_s2[0:1, :], 1.0 / DM, None, ALU.mult),
            reads=("ps_s2",), writes=("arow0",))
        P.v(lambda e, rs=rs: e.tensor_tensor(TROW[0:1, 0:TC], BROW[0:1, rs], BROW[0:1, rs], ALU.mult),
            reads=("brow0",), writes=("trow",))
        P.v(lambda e, rs=rs: e.scalar_tensor_tensor(AROW[0:1, rs], AROW[0:1, rs], float(eps),
                                                    TROW[0:1, 0:TC], ALU.add, ALU.subtract),
            reads=("arow0", "trow"), writes=("arow0",))
        P.s(lambda e, rs=rs: e.activation(TROW[0:1, 0:TC], AROW[0:1, rs], AF.Ln),
            reads=("arow0",), writes=("trow",))
        P.s(lambda e, rs=rs: e.activation(AROW[0:1, rs], TROW[0:1, 0:TC], AF.Exp, scale=-0.5),
            reads=("trow",), writes=("arow0",))
        P.t(lambda e, rs=rs: e.matmul(ps_a[:, :], ones32[0:1, :], AROW[0:1, rs], start=True, stop=True),
            reads=("ones32", "arow0"), writes=("ps_a",))
        P.t(lambda e, rs=rs: e.matmul(ps_b[:, :], ones32[0:1, :], BROW[0:1, rs], start=True, stop=True),
            reads=("ones32", "brow0"), writes=("ps_b",))
        for jh in range(2):
            P.dma("y", lambda e, c=c, jh=jh: e.dma_start(
                    out=XC8[:, :].rearrange("p (j t) -> p j t", j=4),
                    in_=x_hbm[jh * 512:(jh + 1) * 512, c * TC:(c + 1) * TC].rearrange(
                        "(j p) t -> p j t", p=128)),
                  writes=("xc8",))
            for jj in range(4):
                slot = jj % 2
                tsl = slice(slot * TC, (slot + 1) * TC)
                xsl = slice(jj * TC, (jj + 1) * TC)
                P.v(lambda e, xsl=xsl, tsl=tsl: e.tensor_tensor(TMPC[:, tsl], XC8[:, xsl],
                                                                ps_b[:, :], ALU.add),
                    reads=("xc8", "ps_b"), writes=("tmpc{}".format(slot),))
                P.v(lambda e, xsl=xsl, tsl=tsl: e.tensor_tensor(XC8[:, xsl], TMPC[:, tsl],
                                                                ps_a[:, :], ALU.mult),
                    reads=("tmpc{}".format(slot), "ps_a"), writes=("xc8",))
            P.dma("y", lambda e, c=c, jh=jh: e.dma_start(
                    out=z_hbm[jh * 512:(jh + 1) * 512, c * TC:(c + 1) * TC].rearrange(
                        "(j p) t -> p j t", p=128),
                    in_=XC8[:, :].rearrange("p (j t) -> p j t", j=4)),
                  reads=("xc8",), writes=("z{}".format(c),))


def plan_attn(P, nc, E, io, variant):
    mwidth = {"l0": 1152, "l1attn": 1536, "l1mha": 1024}[variant]
    if variant == "l1mha":
        rope_in = {nm: io[nm] for nm in ("ru", "rw")}
        ropem_in = {nm: io[nm] for nm in ("m1", "m2")}
        g1_in = io["g1_in"]
        zbuf = io["zbuf"]
    if variant == "l1attn":
        gb_in = io["gb_in"]
    x_in = io.get("x_in")
    x_out = io.get("x_out")
    wpack = io.get("wpack")
    bpack = io.get("bpack")
    bvrow = io.get("bvrow")
    minus1_in = io.get("minus1_in")
    psel2_in = io.get("psel2_in")
    bsel_in = io.get("bsel_in")
    ident_in = io.get("ident_in")
    hostm_in = io.get("hostm_in")
    g1_in = io.get("g1_in")
    zbuf = io.get("zbuf")
    gb_in = io.get("gb_in")
    XN = E(nc.sbuf_tensor("XN", [128, NDT * T], BF16))
    QT = E(nc.sbuf_tensor("QT", [128, NDT * T], BF16))
    KT = E(nc.sbuf_tensor("KT", [128, NDT * T], BF16))
    VS = E(nc.sbuf_tensor("VS", [128, 16 * (NH * 65)], BF16))
    WB = E(nc.sbuf_tensor("WB", [128, NDT * DM], BF16))
    ATT = E(nc.sbuf_tensor("ATT", [128, NDT * TC], BF16))
    PT = E(nc.sbuf_tensor("PT", [128, 2 * TC], BF16))
    RECB = E(nc.sbuf_tensor("RECB", [128, TC], F32))
    SUM2 = E(nc.sbuf_tensor("SUM2", [128, TC], F32))
    NBT = E(nc.sbuf_tensor("NBT", [128, TC], BF16))
    KMX = E(nc.sbuf_tensor("KMX", [128, 16], F32))
    XB8 = E(nc.sbuf_tensor("XB8", [128, NDT * TC], BF16))
    AB8 = E(nc.sbuf_tensor("AB8", [128, 2 * TC], BF16))
    XBT = E(nc.sbuf_tensor("XBT", [128, 2 * TC], BF16))
    XSQ = E(nc.sbuf_tensor("XSQ", [128, 2 * TC], BF16))
    QSQ = XSQ
    if variant != "l1mha":
        XC = E(nc.sbuf_tensor("XC", [128, 2 * TC], F32))
        XR = E(nc.sbuf_tensor("XR", [128, 2 * TC], F32))
    else:
        XC = XR = None
    TMPC = E(nc.sbuf_tensor("TMPC", [128, 2 * TC], F32))
    TMPD = XB8[:, :].bitcast(F32)
    AROW = E(nc.sbuf_tensor("AROW", [1, TC], F32))
    BROW = E(nc.sbuf_tensor("BROW", [1, TC], F32))
    TROW = E(nc.sbuf_tensor("TROW", [1, TC], F32))
    b_sb = E(nc.sbuf_tensor("b_sb", [128, 32], F32))
    bv_sb = E(nc.sbuf_tensor("bv_sb", [1, DM], BF16))
    minus1 = E(nc.sbuf_tensor("minus1", [1, 128], BF16))
    psel2 = E(nc.sbuf_tensor("psel2", [128, 128], BF16))
    bself = E(nc.sbuf_tensor("bself", [128, 128], F32))
    IDENT = E(nc.sbuf_tensor("IDENT", [128, 128], BF16))
    ones_colbf = E(nc.sbuf_tensor("ones_colbf", [128, 1], BF16))
    onesrow = E(nc.sbuf_tensor("onesrow", [1, 128], BF16))
    ones32 = E(nc.sbuf_tensor("ones32", [1, 128], F32))
    HOSTM = E(nc.sbuf_tensor("HOSTM", [128, mwidth], BF16))
    if variant == "l1mha":
        XC8 = E(nc.sbuf_tensor("XC8", [128, 4 * TC], F32))
        ROPE = {nm: E(nc.sbuf_tensor("R" + nm, [128, T], BF16)) for nm in ("ru", "rw")}
        ROPEM = {nm: E(nc.sbuf_tensor("RM" + nm, [128, 128], BF16)) for nm in ("m1", "m2")}
        TMPE = AB8
        G1 = E(nc.sbuf_tensor("G1", [128, 8], F32))
    if variant == "l1attn":
        GB = E(nc.sbuf_tensor("GB", [128, 4], F32))
        SSC = TMPC

    pp = [E(nc.psum_tensor("pp{}".format(i), [128, TC], F32)) for i in range(8)]
    ps_s1, ps_s2, ps_a, ps_b = pp[0], pp[1], pp[2], pp[3]
    ps_st = [pp[4], pp[5]]
    ps_av = pp[6]
    ps_x = pp[7]

    ctx = dict(XN=XN, XB8=XB8, AB8=AB8, XBT=XBT, XSQ=XSQ, TMPC=TMPC,
               AROW=AROW, BROW=BROW, TROW=TROW,
               ones_colbf=ones_colbf, ones32=ones32,
               ps_s1=ps_s1, ps_s2=ps_s2, ps_a=ps_a, ps_b=ps_b)
    if variant == "l1mha":
        ctx["XC8"] = XC8

    P.g(lambda e: e.memset(ones_colbf[:, :], 1.0), writes=("ones_colbf",))
    P.g(lambda e: e.memset(ones32[:, :], 1.0), writes=("ones32",))
    P.g(lambda e: e.memset(onesrow[:, :], 1.0), writes=("onesrow",))
    P.dma("g", lambda e: e.dma_start(out=b_sb[:, :], in_=bpack[:, :]), writes=("b_sb",))
    P.dma("g", lambda e: e.dma_start(out=bv_sb[:, :], in_=bvrow[:, :]), writes=("bv_sb",))
    P.dma("g", lambda e: e.dma_start(out=minus1[:, :], in_=minus1_in[:, :]), writes=("minus1",))
    P.dma("g", lambda e: e.dma_start(out=psel2[:, :], in_=psel2_in[:, :]), writes=("psel2",))
    P.dma("y", lambda e: e.dma_start(out=bself[:, :], in_=bsel_in[:, :]), writes=("bself",))
    P.dma("g", lambda e: e.dma_start(out=IDENT[:, :], in_=ident_in[:, :]), writes=("ident",))
    P.dma("g", lambda e: e.dma_start(out=HOSTM[:, :], in_=hostm_in[:, :]), writes=("hostm",))
    if variant == "l1mha":
        for nm in ("ru", "rw"):
            P.dma("g", lambda e, nm=nm: e.dma_start(out=ROPE[nm][:, :], in_=rope_in[nm][:, :]),
                  writes=("rope" + nm[1],))
        for nm in ("m1", "m2"):
            P.dma("g", lambda e, nm=nm: e.dma_start(out=ROPEM[nm][:, :], in_=ropem_in[nm][:, :]),
                  writes=("rope" + nm,))
        P.dma("y", lambda e: e.dma_start(out=G1[:, :], in_=g1_in[:, :]), writes=("g1",))
    if variant == "l1attn":
        P.dma("y", lambda e: e.dma_start(out=GB[:, :], in_=gb_in[:, :]), writes=("gb",))
        P.v(lambda e: e.tensor_scalar(GB[:, 0:2], GB[:, 0:2], 0.125, None, ALU.mult),
            reads=("gb",), writes=("gb",))

    if variant == "l1mha":
        emit_ln_to_hbm(P, ctx, x_in, zbuf)
        base_hbm = zbuf
        base_dep = lambda c, j: ("z{}".format(c),)
        emit_ln_full(P, ctx, zbuf, XN, rms_only=True)
    else:
        base_hbm = x_in
        base_dep = lambda c, j: ()
        emit_ln_full(P, ctx, x_in, XN, rms_only=False)

    def load_w(part, fold_g1=False):
        P.dma("g", lambda e, part=part: e.dma_start(
                out=WB[:, :].rearrange("p (j c) -> p j c", j=NDT),
                in_=wpack[:, part * DM:(part + 1) * DM].rearrange("(j p) c -> p j c", p=128)),
              writes=tuple("wb{}".format(j) for j in range(NDT)))
        if fold_g1:
            for j in range(NDT):
                P.v(lambda e, j=j: e.tensor_scalar(WB[:, j * DM:(j + 1) * DM],
                                                   WB[:, j * DM:(j + 1) * DM],
                                                   G1[:, j:j + 1], None, ALU.mult),
                    reads=("wb{}".format(j), "g1"), writes=("wb{}".format(j),))

    fold = variant == "l1mha"

    def qkv_feat(part, OUT, otag, bias_col, scale):
        for qo in range(NDT):
            for c in range(NTC):
                gi = qo * NTC + c
                ps = [ps_x, ps_st[0]][gi % 2]
                pk = ["ps_x", "ps_st0"][gi % 2]
                for j in range(NDT):
                    P.t(lambda e, qo=qo, c=c, j=j, ps=ps: e.matmul(
                            ps[:, :], WB[:, j * DM + qo * 128:(j * DM + (qo + 1) * 128)],
                            XN[:, j * T + c * TC: j * T + (c + 1) * TC],
                            start=(j == 0), stop=(j == NDT - 1)),
                        reads=("wb{}".format(j), "xn{}_{}".format(j, c)), writes=(pk,))
                if bias_col is None:
                    P.v(lambda e, qo=qo, c=c, ps=ps: e.tensor_scalar(
                            OUT[:, qo * T + c * TC: qo * T + (c + 1) * TC], ps[:, :],
                            float(scale), None, ALU.mult),
                        reads=(pk,), writes=(otag + str(qo),))
                else:
                    P.v(lambda e, qo=qo, c=c, ps=ps: e.tensor_scalar(
                            OUT[:, qo * T + c * TC: qo * T + (c + 1) * TC], ps[:, :],
                            b_sb[:, bias_col + qo:bias_col + qo + 1], float(scale),
                            ALU.add, ALU.mult),
                        reads=(pk, "b_sb"), writes=(otag + str(qo),))

    has_bias = variant != "l1mha"
    load_w(0, fold)
    qkv_feat(0, QT, "qt", 0 if has_bias else None,
             1.0 if variant == "l1attn" else 0.125)
    load_w(1, fold)
    qkv_feat(1, KT, "kt", 8 if has_bias else None, 1.0)
    load_w(2, fold)
    P.g(lambda e: e.memset(VS[:, :], 1.0), writes=("vsmem",))
    for tt in range(16):
        for nchunk in range(2):
            ps = ps_st[nchunk]
            for j in range(NDT):
                P.t(lambda e, tt=tt, nchunk=nchunk, j=j, ps=ps: e.matmul(
                        ps[:, :], XN[:, j * T + tt * 128: j * T + (tt + 1) * 128],
                        WB[:, j * DM + nchunk * TC: j * DM + (nchunk + 1) * TC],
                        start=(j == 0), stop=False),
                    reads=("wb{}".format(j), "xn{}_{}".format(j, tt // 4)),
                    writes=("ps_st{}".format(nchunk),))
            P.t(lambda e, nchunk=nchunk, ps=ps: e.matmul(
                    ps[:, :], onesrow[0:1, :], bv_sb[0:1, nchunk * TC:(nchunk + 1) * TC],
                    start=False, stop=True),
                reads=("onesrow", "bv_sb"), writes=("ps_st{}".format(nchunk),))
            for hh in range(8):
                h = nchunk * 8 + hh
                P.v(lambda e, tt=tt, hh=hh, h=h, ps=ps: e.tensor_copy(
                        VS[:, tt * (NH * 65) + h * 65: tt * (NH * 65) + h * 65 + 64],
                        ps[:, hh * 64:(hh + 1) * 64]),
                    reads=("ps_st{}".format(nchunk), "vsmem"), writes=("vs{}_{}".format(tt, h),))

    if variant == "l1mha":
        emit_rope(P, QT, "qt", ROPE["ru"], ROPE["rw"], ROPEM["m1"], ROPEM["m2"], TMPE, ps_x, "ps_x")
        emit_rope(P, KT, "kt", ROPE["ru"], ROPE["rw"], ROPEM["m1"], ROPEM["m2"], TMPE, ps_x, "ps_x")
    if variant == "l1attn":
        emit_qkln(P, ctx, QT, "qt", GB, 0, 1, psel2, bself, SSC, TMPD, pp)
        emit_qkln(P, ctx, KT, "kt", GB, 2, 3, psel2, bself, SSC, TMPD, pp)

    for j in range(NDT):
        for c in range(NTC):
            sl = slice((c % 2) * TC, (c % 2 + 1) * TC)
            P.s(lambda e, j=j, c=c, sl=sl: e.activation(
                    QSQ[:, sl], KT[:, j * T + c * TC: j * T + (c + 1) * TC], AF.Square),
                reads=("kt{}".format(j),), writes=("qsq{}".format(c % 2),))
            P.t(lambda e, j=j, c=c, sl=sl: e.matmul(ps_x[:, :], psel2[:, :], QSQ[:, sl],
                                                    start=True, stop=True),
                reads=("psel2", "qsq{}".format(c % 2)), writes=("ps_x",))
            P.v(lambda e: e.tensor_reduce(TMPD[:, 0:1], ps_x[:, :], AX.X, ALU.max),
                reads=("ps_x",), writes=("tmpdA",))
            if c == 0:
                P.v(lambda e, j=j: e.tensor_copy(KMX[:, j:j + 1], TMPD[:, 0:1]),
                    reads=("tmpdA",), writes=("kmx",))
            else:
                P.v(lambda e, j=j: e.tensor_tensor(KMX[:, j:j + 1], KMX[:, j:j + 1],
                                                   TMPD[:, 0:1], ALU.max),
                    reads=("kmx", "tmpdA"), writes=("kmx",))
    P.s(lambda e: e.activation(KMX[:, 8:16], KMX[:, 0:8], AF.Ln),
        reads=("kmx",), writes=("lkb",))
    P.v(lambda e: e.tensor_scalar(KMX[:, 8:16], KMX[:, 8:16], 0.5, None, ALU.mult),
        reads=("lkb",), writes=("lkb",))

    load_w(3, False)
    umin = -511
    for qc in range(NTC):
        for jt in range(NDT):
            sl = slice((jt % 2) * TC, (jt % 2 + 1) * TC)
            P.s(lambda e, jt=jt, qc=qc, sl=sl: e.activation(
                    QSQ[:, sl], QT[:, jt * T + qc * TC: jt * T + (qc + 1) * TC], AF.Square),
                reads=("qt{}".format(jt),), writes=("qsq{}".format(jt % 2),))
            P.t(lambda e, jt=jt, sl=sl: e.matmul(ps_x[:, :], psel2[:, :], QSQ[:, sl],
                                                 start=True, stop=True),
                reads=("psel2", "qsq{}".format(jt % 2)), writes=("ps_x",))
            P.s(lambda e: e.activation(TMPD[:, 0:TC], ps_x[:, :], AF.Ln),
                reads=("ps_x",), writes=("tmpdA",))
            P.s(lambda e, jt=jt: e.activation(NBT[:, 0:TC], TMPD[:, 0:TC], AF.Exp, scale=0.5,
                                              bias=KMX[:, 8 + jt:9 + jt]),
                reads=("tmpdA", "lkb"), writes=("nbt",))
            P.v(lambda e: e.tensor_copy(AB8[0:1, TC:2 * TC], NBT[64:65, 0:TC]),
                reads=("nbt",), writes=("nbto",))
            for hp in range(2):
                h = jt * 2 + hp
                kts = kt_range(variant, qc)

                def emit_av(kt, first, last, h=h):
                    P.t(lambda e, h=h, kt=kt, first=first, last=last: e.matmul(
                            ps_av[0:65, :],
                            VS[:, kt * (NH * 65) + h * 65: kt * (NH * 65) + h * 65 + 65],
                            PT[:, (kt % 2) * TC:(kt % 2 + 1) * TC],
                            start=first, stop=last),
                        reads=("vs{}_{}".format(kt, h), "pt{}".format(kt % 2)), writes=("ps_av",))

                pend = None
                for kt in kts:
                    psl = ps_st[kt % 2]
                    pskey = "ps_st{}".format(kt % 2)
                    P.t(lambda e, jt=jt, hp=hp, qc=qc, kt=kt, psl=psl: e.matmul(
                            psl[:, :],
                            KT[hp * 64:(hp + 1) * 64, jt * T + kt * 128: jt * T + (kt + 1) * 128],
                            QT[hp * 64:(hp + 1) * 64, jt * T + qc * TC: jt * T + (qc + 1) * TC],
                            start=True, stop=False),
                        reads=("kt{}".format(jt), "qt{}".format(jt)), writes=(pskey,))
                    if variant in ("l0", "l1attn") or kt >= 4 * qc:
                        off = qc * 512 - kt * 128 - umin
                        P.t(lambda e, off=off, psl=psl: e.matmul(
                                psl[:, :], IDENT[:, :], HOSTM[:, off:off + TC],
                                start=False, stop=False),
                            reads=("ident", "hostm"), writes=(pskey,))
                    nbr = NBT[0:1, 0:TC] if hp == 0 else AB8[0:1, TC:2 * TC]
                    nbk = "nbt" if hp == 0 else "nbto"
                    P.t(lambda e, psl=psl, nbr=nbr: e.matmul(
                            psl[:, :], minus1[0:1, :], nbr,
                            start=False, stop=True),
                        reads=("minus1", nbk), writes=(pskey,))
                    P.s(lambda e, kt=kt, psl=psl: e.activation(
                            PT[:, (kt % 2) * TC:(kt % 2 + 1) * TC], psl[:, :], AF.Exp),
                        reads=(pskey,), writes=("pt{}".format(kt % 2),))
                    if pend is not None:
                        emit_av(pend, pend == kts[0], False)
                    pend = kt
                emit_av(pend, pend == kts[0], True)
                P.v(lambda e, hp=hp: e.tensor_copy(SUM2[hp * 64:hp * 64 + 1, :], ps_av[64:65, :]),
                    reads=("ps_av",), writes=("sum2_{}".format(hp),))
                P.v(lambda e, hp=hp: e.tensor_copy(TMPC[hp * 64:(hp + 1) * 64, 0:TC],
                                                   ps_av[0:64, :]),
                    reads=("ps_av",), writes=("tmpcA",))
            P.v(lambda e: e.tensor_scalar(RECB[:, :], SUM2[:, :], 1e-37, None, ALU.max),
                reads=("sum2_0", "sum2_1"), writes=("recb",))
            P.s(lambda e: e.activation(TMPD[:, TC:2 * TC], RECB[:, :], AF.Ln),
                reads=("recb",), writes=("tmpdB",))
            P.s(lambda e: e.activation(SUM2[:, :], TMPD[:, TC:2 * TC], AF.Exp, scale=-1.0),
                reads=("tmpdB",), writes=("sum2_0", "sum2_1"))
            P.t(lambda e: e.matmul(ps_x[:, :], bself[:, :], SUM2[:, :], start=True, stop=True),
                reads=("bself", "sum2_0", "sum2_1"), writes=("ps_x",))
            P.v(lambda e: e.tensor_copy(RECB[:, :], ps_x[:, :]),
                reads=("ps_x",), writes=("recb",))
            P.v(lambda e, jt=jt: e.tensor_tensor(ATT[:, jt * TC:(jt + 1) * TC],
                                                 TMPC[:, 0:TC], RECB[:, :], ALU.mult),
                reads=("tmpcA", "recb"), writes=("att{}".format(jt),))
        for j in range(NDT):
            for ct in range(NDT):
                P.t(lambda e, j=j, ct=ct: e.matmul(
                        ps_x[:, :], WB[:, ct * DM + j * 128: ct * DM + (j + 1) * 128],
                        ATT[:, ct * TC:(ct + 1) * TC],
                        start=(ct == 0), stop=(ct == NDT - 1)),
                    reads=("wb{}".format(ct), "att{}".format(ct)), writes=("ps_x",))
            if variant == "l1mha":
                if j % 4 == 0:
                    jh = j // 4
                    P.dma("y", lambda e, qc=qc, jh=jh: e.dma_start(
                            out=XC8[:, :].rearrange("p (jj t) -> p jj t", jj=4),
                            in_=base_hbm[jh * 512:(jh + 1) * 512,
                                         qc * TC:(qc + 1) * TC].rearrange(
                                "(jj p) t -> p jj t", p=128)),
                          reads=base_dep(qc, 0), writes=("xc8",))
                P.v(lambda e, j=j: e.scalar_tensor_tensor(
                        TMPC[:, TC:2 * TC], ps_x[:, :], b_sb[:, 24 + j:25 + j],
                        XC8[:, (j % 4) * TC:(j % 4 + 1) * TC], ALU.add, ALU.add),
                    reads=("ps_x", "b_sb", "xc8"), writes=("tmpc1",))
                P.dma("y", lambda e, j=j, qc=qc: e.dma_start(
                        out=x_out[j * 128:(j + 1) * 128, qc * TC:(qc + 1) * TC],
                        in_=TMPC[:, TC:2 * TC]),
                      reads=("tmpc1",), writes=("out:x{}_{}".format(qc, j),))
            else:
                xslot = j % 2
                xsl = slice(xslot * TC, (xslot + 1) * TC)
                P.dma("y", lambda e, j=j, qc=qc, xsl=xsl: e.dma_start(
                        out=XR[:, xsl], in_=base_hbm[j * 128:(j + 1) * 128, qc * TC:(qc + 1) * TC]),
                      reads=base_dep(qc, j), writes=("xr{}".format(xslot),))
                P.v(lambda e, j=j, xsl=xsl: e.scalar_tensor_tensor(
                        XC[:, xsl], ps_x[:, :], b_sb[:, 24 + j:25 + j], XR[:, xsl], ALU.add, ALU.add),
                    reads=("ps_x", "b_sb", "xr{}".format(xslot)), writes=("xc{}".format(xslot),))
                P.dma("y", lambda e, j=j, qc=qc, xsl=xsl: e.dma_start(
                        out=x_out[j * 128:(j + 1) * 128, qc * TC:(qc + 1) * TC], in_=XC[:, xsl]),
                      reads=("xc{}".format(xslot),), writes=("out:x{}_{}".format(qc, j),))

    keys = ["out:x{}_{}".format(c, j) for c in range(NTC) for j in range(NDT)]
    if ATTN_DEBUG == "avx":
        P.v(lambda e: e.tensor_copy(XR[:, 0:TC], TMPC[:, 0:TC]),
            reads=("tmpcA",) + tuple(keys), writes=("xr0",))
        P.dma("y", lambda e: e.dma_start(out=x_out[0:128, 3 * TC:4 * TC], in_=XR[:, 0:TC]),
              reads=("xr0",), writes=("dbgx:0",))
        P.v(lambda e: e.tensor_copy(XR[:, TC:2 * TC], SUM2[:, :]),
            reads=("sum2_0", "sum2_1") + tuple(keys), writes=("xr1",))
        P.dma("y", lambda e: e.dma_start(out=x_out[128:256, 3 * TC:4 * TC], in_=XR[:, TC:2 * TC]),
              reads=("xr1",), writes=("dbgx:1",))
        P.v(lambda e: e.tensor_copy(XR[:, 0:TC], PT[:, TC:2 * TC]),
            reads=("pt1",) + tuple(keys), writes=("xr0",))
        P.dma("y", lambda e: e.dma_start(out=x_out[256:384, 3 * TC:4 * TC], in_=XR[:, 0:TC]),
              reads=("xr0",), writes=("dbgx:2",))
        P.v(lambda e: e.tensor_copy(XR[:, TC:2 * TC], NBT[:, 0:TC]),
            reads=("nbt",) + tuple(keys), writes=("xr1",))
        P.dma("y", lambda e: e.dma_start(out=x_out[384:512, 3 * TC:4 * TC], in_=XR[:, TC:2 * TC]),
              reads=("xr1",), writes=("dbgx:3",))
        keys = ["dbgx:0", "dbgx:1", "dbgx:2", "dbgx:3"]
    if ATTN_DEBUG == "att":
        for j in range(NDT):
            slot = j % 2
            xsl = slice(slot * TC, (slot + 1) * TC)
            P.v(lambda e, j=j, xsl=xsl: e.tensor_copy(XR[:, xsl], ATT[:, j * TC:(j + 1) * TC]),
                reads=("att{}".format(j),) + tuple(keys), writes=("xr{}".format(slot),))
            P.dma("y", lambda e, j=j, xsl=xsl: e.dma_start(
                    out=x_out[j * 128:(j + 1) * 128, 3 * TC:4 * TC], in_=XR[:, xsl]),
                  reads=("xr{}".format(slot),), writes=("dbga:{}".format(j),))
        keys = ["dbga:{}".format(j) for j in range(NDT)]
    if ATTN_DEBUG == "vs":
        for rb in range(NDT):
            slot = rb % 2
            xsl = slice(slot * TC, (slot + 1) * TC)
            for c4 in range(4):
                P.v(lambda e, rb=rb, c4=c4, xsl=xsl: e.tensor_copy(
                        XR[:, xsl], VS[:, rb * 2048 + c4 * TC: rb * 2048 + (c4 + 1) * TC]),
                    reads=tuple("vs{}_{}".format(tt, h) for tt in range(16) for h in range(16))
                          + tuple(keys), writes=("xr{}".format(slot),))
                P.dma("y", lambda e, rb=rb, c4=c4, xsl=xsl: e.dma_start(
                        out=x_out[rb * 128:(rb + 1) * 128, c4 * TC:(c4 + 1) * TC], in_=XR[:, xsl]),
                      reads=("xr{}".format(slot),), writes=("dbgv:{}_{}".format(rb, c4),))
        keys = ["dbgv:{}_{}".format(rb, c4) for rb in range(NDT) for c4 in range(4)]
    if ATTN_DEBUG in ("qt", "kt", "xn"):
        DT_ = {"qt": QT, "kt": KT, "xn": XN}[ATTN_DEBUG]
        tagd = {"qt": "qt", "kt": "kt", "xn": None}[ATTN_DEBUG]
        for j in range(NDT):
            for c in range(NTC):
                slot = j % 2
                xsl = slice(slot * TC, (slot + 1) * TC)
                rk = (tagd + str(j),) if tagd else ("xn{}_{}".format(j, c),)
                P.v(lambda e, j=j, c=c, xsl=xsl, DT_=DT_: e.tensor_copy(
                        XR[:, xsl], DT_[:, j * T + c * TC: j * T + (c + 1) * TC]),
                    reads=rk + tuple(keys), writes=("xr{}".format(slot),))
                P.dma("y", lambda e, j=j, c=c, xsl=xsl: e.dma_start(
                        out=x_out[j * 128:(j + 1) * 128, c * TC:(c + 1) * TC], in_=XR[:, xsl]),
                      reads=("xr{}".format(slot),), writes=("dbg:{}_{}".format(c, j),))
        keys = ["dbg:{}_{}".format(c, j) for c in range(NTC) for j in range(NDT)]
    P.op("y", lambda e: e.nop(), reads=tuple(keys))


def build_attn_launch(variant):
    nc = bass.Bass("TRN2", target_bir_lowering=False)
    x_in = nc.dram_tensor("x_in", [DM, T], F32, kind="ExternalInput").ap()
    x_out = nc.dram_tensor("x_out", [DM, T], F32, kind="ExternalOutput").ap()
    wpack = nc.dram_tensor("wpack", [DM, 4 * DM], F32, kind="ExternalInput").ap()
    bpack = nc.dram_tensor("bpack", [128, 32], F32, kind="ExternalInput").ap()
    bvrow = nc.dram_tensor("bvrow", [1, DM], F32, kind="ExternalInput").ap()
    minus1_in = nc.dram_tensor("minus1_in", [1, 128], F32, kind="ExternalInput").ap()
    psel2_in = nc.dram_tensor("psel2_in", [128, 128], F32, kind="ExternalInput").ap()
    bsel_in = nc.dram_tensor("bsel_in", [128, 128], F32, kind="ExternalInput").ap()
    ident_in = nc.dram_tensor("ident_in", [128, 128], F32, kind="ExternalInput").ap()
    mwidth = {"l0": 1152, "l1attn": 1536, "l1mha": 1024}[variant]
    hostm_in = nc.dram_tensor("hostm_in", [128, mwidth], F32, kind="ExternalInput").ap()
    if variant == "l1mha":
        rope_in = {nm: nc.dram_tensor(nm, [128, T if nm in ("ru", "rw") else T], F32,
                                      kind="ExternalInput").ap() for nm in ("ru", "rw")}
        ropem_in = {nm: nc.dram_tensor(nm, [128, 128], F32, kind="ExternalInput").ap()
                    for nm in ("m1", "m2")}
        g1_in = nc.dram_tensor("g1_in", [128, 8], F32, kind="ExternalInput").ap()
        zbuf = nc.dram_tensor("zbuf", [DM, T], F32, kind="Internal").ap()
    if variant == "l1attn":
        gb_in = nc.dram_tensor("gb_in", [128, 4], F32, kind="ExternalInput").ap()

    io = dict(x_in=x_in, x_out=x_out, wpack=wpack, bpack=bpack, bvrow=bvrow,
              minus1_in=minus1_in, psel2_in=psel2_in, bsel_in=bsel_in, ident_in=ident_in,
              hostm_in=hostm_in)
    if variant == "l1mha":
        io.update(diagm_in=None, g1_in=g1_in, zbuf=zbuf, ru=rope_in["ru"], rw=rope_in["rw"],
                  m1=ropem_in["m1"], m2=ropem_in["m2"])
    if variant == "l1attn":
        io.update(gb_in=gb_in)
    st = contextlib.ExitStack()
    E = st.enter_context
    P = Prog(nc)
    plan_attn(P, nc, E, io, variant)
    block = E(nc.Block())
    P.finalize(block, st)
    st.close()
    return nc


def run_attn(xt, variant, wq, wk, wv, wo, bq, bk, bv, bo, gb=None, g1=None):
    hc = attn_host_consts(variant)
    if variant == "l1mha":
        perm = rope_perm()
        wq = wq[:, perm]
        wk = wk[:, perm]
    wpackh = np.ascontiguousarray(np.concatenate([wq, wk, wv, wo], axis=1).astype(np.float32))
    z8 = np.zeros(DM, np.float32)
    bq = z8 if bq is None else np.asarray(bq, np.float32)
    bk = z8 if bk is None else np.asarray(bk, np.float32)
    bv = z8 if bv is None else np.asarray(bv, np.float32)
    bo = z8 if bo is None else np.asarray(bo, np.float32)
    bpackh = np.zeros((128, 32), np.float32)
    bpackh[:, 0:8] = bq.reshape(8, 128).T
    bpackh[:, 8:16] = bk.reshape(8, 128).T
    bpackh[:, 24:32] = bo.reshape(8, 128).T
    bvrowh = np.ascontiguousarray(bv.reshape(1, DM))
    nc = build_attn_launch(variant)
    im = {"wpack": wpackh, "bpack": bpackh, "bvrow": bvrowh,
          "minus1_in": hc["minus1_in"], "psel2_in": hc["psel2_in"],
          "bsel_in": hc["bsel_in"], "ident_in": hc["ident_in"]}
    im["hostm_in"] = hc["hostm_in"]
    if variant == "l1mha":
        im.update({"ru": hc["ru"], "rw": hc["rw"], "m1": hc["m1"], "m2": hc["m2"]})
        im["g1_in"] = np.ascontiguousarray(np.asarray(g1, np.float32).reshape(8, 128).T)
    if variant == "l1attn":
        gq, gbb, gk, gkb = [np.asarray(a, np.float32) for a in gb]
        gbh = np.zeros((128, 4), np.float32)
        gbh[:, 0] = np.tile(gq, 2)
        gbh[:, 1] = np.tile(gbb, 2)
        gbh[:, 2] = np.tile(gk, 2)
        gbh[:, 3] = np.tile(gkb, 2)
        im["gb_in"] = gbh
    in_maps = [dict(im, x_in=xt[b]) for b in range(8)]
    res = run_bass_kernel_spmd(nc, in_maps, core_ids=list(range(8)))
    return [res.results[b]["x_out"] for b in range(8)]



def plan_router(P, nc, E, io):
    x_in = io.get("x_in")
    wr_in = io.get("wr_in")
    g2_in = io.get("g2_in")
    zn_out = io.get("zn_out")
    vloc_out = io.get("vloc_out")
    XB8 = E(nc.sbuf_tensor("XB8", [128, NDT * TC], BF16))
    XSQ = E(nc.sbuf_tensor("XSQ", [128, 2 * TC], BF16))
    XR = E(nc.sbuf_tensor("XR", [128, NDT * TC], F32))
    XNC = E(nc.sbuf_tensor("XNC", [128, 2 * TC], BF16))
    XC8 = E(nc.sbuf_tensor("XC8", [128, NDT * TC], F32))
    AROW = E(nc.sbuf_tensor("AROW", [1, TC], F32))
    TROW = E(nc.sbuf_tensor("TROW", [1, TC], F32))
    WR = E(nc.sbuf_tensor("WR", [128, NDT * 8], F32))
    G2 = E(nc.sbuf_tensor("G2", [128, 8], F32))
    LG = E(nc.sbuf_tensor("LG", [128, 8], F32))
    LG2 = E(nc.sbuf_tensor("LG2", [128, 8], F32))
    EXPL = E(nc.sbuf_tensor("EXPL", [128, 8], F32))
    PRB = E(nc.sbuf_tensor("PRB", [128, 8], F32))
    MSK = E(nc.sbuf_tensor("MSK", [128, 8], F32))
    SC = E(nc.sbuf_tensor("SC", [128, 8], F32))
    VLOC = E(nc.sbuf_tensor("VLOC", [128, 128], F32))
    ones_colbf = E(nc.sbuf_tensor("ones_colbf", [128, 1], BF16))
    ones32 = E(nc.sbuf_tensor("ones32", [1, 128], F32))
    ps_s2 = E(nc.psum_tensor("ps_s2", [128, TC], F32))
    ps_a = E(nc.psum_tensor("ps_a", [128, TC], F32))
    ps_r = E(nc.psum_tensor("ps_r", [128, 8], F32))

    P.g(lambda e: e.memset(ones_colbf[:, :], 1.0), writes=("ones_colbf",))
    P.g(lambda e: e.memset(ones32[:, :], 1.0), writes=("ones32",))
    P.dma("g", lambda e: e.dma_start(out=G2[:, :], in_=g2_in[:, :]), writes=("g2",))
    for j in range(NDT):
        P.dma("g", lambda e, j=j: e.dma_start(out=WR[:, j * 8:(j + 1) * 8],
                                              in_=wr_in[j * 128:(j + 1) * 128, :]),
              writes=("wr{}".format(j),))
        P.v(lambda e, j=j: e.tensor_scalar(WR[:, j * 8:(j + 1) * 8], WR[:, j * 8:(j + 1) * 8],
                                           G2[:, j:j + 1], None, ALU.mult),
            reads=("wr{}".format(j), "g2"), writes=("wr{}".format(j),))

    for c in range(NTC):
        P.dma("g", lambda e, c=c: e.dma_start(
                out=XB8[:, :].rearrange("p (j t) -> p j t", j=NDT),
                in_=x_in[:, c * TC:(c + 1) * TC].rearrange("(j p) t -> p j t", p=128)),
              writes=("xb8",))
        for j in range(NDT):
            slot = j % 2
            bcol = slice(slot * TC, (slot + 1) * TC)
            P.s(lambda e, j=j, bcol=bcol: e.activation(XSQ[:, bcol], XB8[:, j * TC:(j + 1) * TC],
                                                       AF.Square),
                reads=("xb8",), writes=("xsq{}".format(slot),))
            P.t(lambda e, bcol=bcol, j=j: e.matmul(ps_s2[0:1, :], ones_colbf[:, 0:1], XSQ[:, bcol],
                                                   start=(j == 0), stop=(j == NDT - 1)),
                reads=("ones_colbf", "xsq{}".format(slot)), writes=("ps_s2",))
        P.v(lambda e: e.tensor_scalar(AROW[0:1, :], ps_s2[0:1, :], 1.0 / DM, 1e-6,
                                      ALU.mult, ALU.add),
            reads=("ps_s2",), writes=("arow0",))
        P.s(lambda e: e.activation(TROW[0:1, :], AROW[0:1, :], AF.Ln),
            reads=("arow0",), writes=("trow",))
        P.s(lambda e: e.activation(AROW[0:1, :], TROW[0:1, :], AF.Exp, scale=-0.5),
            reads=("trow",), writes=("arow0",))
        P.t(lambda e: e.matmul(ps_a[:, :], ones32[0:1, :], AROW[0:1, :], start=True, stop=True),
            reads=("ones32", "arow0"), writes=("ps_a",))
        P.dma("y", lambda e, c=c: e.dma_start(
                out=XC8[:, :].rearrange("p (j t) -> p j t", j=NDT),
                in_=x_in[:, c * TC:(c + 1) * TC].rearrange("(j p) t -> p j t", p=128)),
              writes=("xc8",))
        for j in range(NDT):
            slot = j % 2
            xsl = slice(slot * TC, (slot + 1) * TC)
            P.v(lambda e, j=j: e.tensor_tensor(XR[:, j * TC:(j + 1) * TC],
                                               XC8[:, j * TC:(j + 1) * TC],
                                               ps_a[:, :], ALU.mult),
                reads=("xc8", "ps_a"), writes=("xrj{}".format(j),))
            P.v(lambda e, j=j, xsl=xsl: e.tensor_scalar(XNC[:, xsl], XR[:, j * TC:(j + 1) * TC],
                                                        G2[:, j:j + 1], None, ALU.mult),
                reads=("xrj{}".format(j), "g2"), writes=("xnc{}".format(slot),))
            P.dma("y", lambda e, j=j, c=c, xsl=xsl: e.dma_start(
                    out=zn_out[j * 128:(j + 1) * 128, c * TC:(c + 1) * TC], in_=XNC[:, xsl]),
                  reads=("xnc{}".format(slot),), writes=("out:zn{}_{}".format(c, j),))
        for s4 in range(4):
            tt = c * 4 + s4
            for j in range(NDT):
                P.t(lambda e, j=j, s4=s4: e.matmul(
                        ps_r[:, :], XR[:, j * TC + s4 * 128: j * TC + (s4 + 1) * 128],
                        WR[:, j * 8:(j + 1) * 8], start=(j == 0), stop=(j == NDT - 1)),
                    reads=("xrj{}".format(j), "wr{}".format(j)), writes=("ps_r",))
            P.v(lambda e: e.tensor_copy(LG[:, :], ps_r[:, :]), reads=("ps_r",), writes=("lg",))
            P.v(lambda e: e.reduce_max(SC[:, 0:1], LG[:, :], axis=AX.X, negate=True),
                reads=("lg",), writes=("sc0",))
            P.s(lambda e: e.activation(EXPL[:, :], LG[:, :], AF.Exp, bias=SC[:, 0:1]),
                reads=("lg", "sc0"), writes=("expl",))
            P.v(lambda e: e.reduce_sum(SC[:, 1:2], EXPL[:, :], axis=AX.X),
                reads=("expl",), writes=("sc1",))
            P.s(lambda e: e.activation(SC[:, 2:3], SC[:, 1:2], AF.Ln),
                reads=("sc1",), writes=("sc2",))
            P.s(lambda e: e.activation(SC[:, 3:4], SC[:, 2:3], AF.Exp, scale=-1.0),
                reads=("sc2",), writes=("sc3",))
            P.v(lambda e: e.tensor_scalar(SC[:, 4:5], SC[:, 0:1], -1.0, None, ALU.mult),
                reads=("sc0",), writes=("sc4",))
            P.v(lambda e: e.tensor_scalar(MSK[:, :], LG[:, :], SC[:, 4:5], None, ALU.is_ge),
                reads=("lg", "sc4"), writes=("msk",))
            P.v(lambda e: e.scalar_tensor_tensor(LG2[:, :], MSK[:, :], -1e30, LG[:, :],
                                                 ALU.mult, ALU.add),
                reads=("msk", "lg"), writes=("lg2",))
            P.v(lambda e: e.reduce_max(SC[:, 5:6], LG2[:, :], axis=AX.X),
                reads=("lg2",), writes=("sc5",))
            P.s(lambda e: e.activation(SC[:, 6:7], SC[:, 5:6], AF.Exp, bias=SC[:, 0:1]),
                reads=("sc5", "sc0"), writes=("sc6",))
            P.v(lambda e: e.tensor_scalar(SC[:, 7:8], SC[:, 6:7], SC[:, 3:4], None, ALU.mult),
                reads=("sc6", "sc3"), writes=("sc7",))
            P.v(lambda e: e.tensor_scalar(PRB[:, :], EXPL[:, :], SC[:, 3:4], None, ALU.mult),
                reads=("expl", "sc3"), writes=("prb",))
            P.v(lambda e: e.tensor_scalar(MSK[:, :], PRB[:, :], SC[:, 7:8], None, ALU.is_ge),
                reads=("prb", "sc7"), writes=("msk",))
            P.v(lambda e, tt=tt: e.tensor_tensor(VLOC[:, tt * 8:(tt + 1) * 8], PRB[:, :],
                                                 MSK[:, :], ALU.mult),
                reads=("prb", "msk"), writes=("vloc{}".format(tt),))
    P.dma("y", lambda e: e.dma_start(out=vloc_out[:, :], in_=VLOC[:, :]),
          reads=tuple("vloc{}".format(tt) for tt in range(16)), writes=("out:vloc",))
    keys = ["out:zn{}_{}".format(c, j) for c in range(NTC) for j in range(NDT)] + ["out:vloc"]
    P.op("y", lambda e: e.nop(), reads=tuple(keys))


def build_router_launch():
    nc = bass.Bass("TRN2", target_bir_lowering=False)
    x_in = nc.dram_tensor("x_in", [DM, T], F32, kind="ExternalInput").ap()
    wr_in = nc.dram_tensor("wr_in", [DM, 8], F32, kind="ExternalInput").ap()
    g2_in = nc.dram_tensor("g2_in", [128, 8], F32, kind="ExternalInput").ap()
    zn_out = nc.dram_tensor("zn_out", [DM, T], BF16, kind="ExternalOutput").ap()
    vloc_out = nc.dram_tensor("vloc_out", [128, 128], F32, kind="ExternalOutput").ap()

    io = dict(x_in=x_in, wr_in=wr_in, g2_in=g2_in, zn_out=zn_out, vloc_out=vloc_out)
    st = contextlib.ExitStack()
    E = st.enter_context
    P = Prog(nc)
    plan_router(P, nc, E, io)
    block = E(nc.Block())
    P.finalize(block, st)
    st.close()
    return nc


def run_router(s1t, w_router, g2):
    nc = build_router_launch()
    wrh = np.ascontiguousarray(np.asarray(w_router, np.float32))
    g2h = np.ascontiguousarray(np.asarray(g2, np.float32).reshape(8, 128).T)
    in_maps = [{"x_in": s1t[b], "wr_in": wrh, "g2_in": g2h} for b in range(8)]
    res = run_bass_kernel_spmd(nc, in_maps, core_ids=list(range(8)))
    zn = [res.results[b]["zn_out"] for b in range(8)]
    vloc = [res.results[b]["vloc_out"] for b in range(8)]
    return zn, vloc


def vloc_decode(vl):
    out = np.zeros((T, 8), np.float32)
    for tt in range(16):
        out[tt * 128:(tt + 1) * 128, :] = vl[:, tt * 8:(tt + 1) * 8]
    return out


def moe_host_layouts(vglobal):
    vb = np.zeros((128, 1024), np.float32)
    for e in range(8):
        for s in range(16):
            vb[e * 16 + s, :] = vglobal[s * 1024:(s + 1) * 1024, e]
    return vb



CAP = 1280.0


def plan_moe(P, nc, E, io, ago=None, permsel=None, idf32=None):
    x3_in = io.get("x3_in")
    s1_in = io.get("s1_in")
    zn_in = io.get("zn_in")
    vbis_in = io.get("vbis_in")
    vtok_in = io.get("vtok_in")
    w1_in = io.get("w1_in")
    w2_in = io.get("w2_in")
    gsel_in = io.get("gsel_in")
    gsel2_in = io.get("gsel2_in")
    esel3_in = io.get("esel3_in")
    ident8_in = io.get("ident8_in")
    identf_in = io.get("identf_in")
    x_out = io.get("x_out")
    ZN = E(nc.sbuf_tensor("ZN", [128, NDT * T], BF16))
    W1E = E(nc.sbuf_tensor("W1E", [128, NDT * 2048], BF16))
    W2E = E(nc.sbuf_tensor("W2E", [128, 16 * DM], BF16))
    HB = E(nc.sbuf_tensor("HB", [128, 16 * TC], BF16))
    ACC = E(nc.sbuf_tensor("ACC", [128, NDT * T], F32))
    ZG = E(nc.sbuf_tensor("ZG", [128, NDT * TC], BF16))
    VB = E(nc.sbuf_tensor("VB", [128, 1024], F32))
    VT = E(nc.sbuf_tensor("VT", [128, 128], F32))
    GT = E(nc.sbuf_tensor("GT", [128, 128], F32))
    GA = E(nc.sbuf_tensor("GA", [8, 16 * 128], BF16))
    GTB = E(nc.sbuf_tensor("GTB", [128, 128], BF16))
    CNTB = E(nc.sbuf_tensor("CNTB", [128, 1024], BF16))
    CNTS = E(nc.sbuf_tensor("CNTS", [128, 4], F32))
    LOHI = E(nc.sbuf_tensor("LOHI", [8, 8], F32))
    TAUB = E(nc.sbuf_tensor("TAUB", [128, 8], F32))
    TAUR = E(nc.sbuf_tensor("TAUR", [1, 8], F32))
    GSEL = E(nc.sbuf_tensor("GSEL", [128, 8], F32))
    GSEL2 = E(nc.sbuf_tensor("GSEL2", [8, 128], F32))
    ESEL3 = E(nc.sbuf_tensor("ESEL3", [8, 1024], BF16))
    IDENT8 = E(nc.sbuf_tensor("IDENT8", [8, 8], F32))
    IDENTF = E(nc.sbuf_tensor("IDENTF", [128, 128], BF16))
    ones32 = E(nc.sbuf_tensor("ones32", [1, 128], F32))
    MIDB = E(nc.sbuf_tensor("MIDB", [128, 1], F32))
    CNT8 = E(nc.sbuf_tensor("CNT8", [8, 4], F32))
    pp = [E(nc.psum_tensor("pp{}".format(i), [128, TC], F32)) for i in range(8)]
    ps_m = pp[0]
    ps_c = pp[1]
    ps_h = [pp[2], pp[3]]
    ps_o = [pp[4], pp[5]]
    ps_g = pp[6]
    ps_t = pp[7]

    P.g(lambda e: e.memset(ones32[:, :], 1.0), writes=("ones32",))
    if ago is None:
        P.dma("g", lambda e: e.dma_start(out=VB[:, :], in_=vbis_in[:, :]), writes=("vb",))
    else:
        AGT = E(nc.sbuf_tensor("AGT", [128, 128], F32))
        AGT2 = E(nc.sbuf_tensor("AGT2", [128, 128], F32))
        PRM = E(nc.sbuf_tensor("PRM", [128, 128], F32))
        IDF = E(nc.sbuf_tensor("IDF", [128, 128], F32))
        P.dma("g", lambda e: e.dma_start(out=PRM[:, :], in_=permsel[:, :]), writes=("prm",))
        P.dma("g", lambda e: e.dma_start(out=IDF[:, :], in_=idf32[:, :]), writes=("idf",))
        for r in range(8):
            P.dma("g", lambda e, r=r: e.dma_start(out=AGT[:, :], in_=ago[r * 128:(r + 1) * 128, :]),
                  reads=("cc_done",), writes=("agt",))
            P.t(lambda e: e.transpose(pp[0][0:128, 0:128], AGT[:, :], IDF[:, :]),
                reads=("agt", "idf"), writes=("pp0",))
            P.v(lambda e: e.tensor_copy(AGT2[:, :], pp[0][0:128, 0:128]),
                reads=("pp0",), writes=("agt2",))
            P.t(lambda e: e.matmul(pp[0][0:128, 0:128], PRM[:, :], AGT2[:, :], start=True, stop=True),
                reads=("prm", "agt2"), writes=("pp0",))
            P.v(lambda e, r=r: e.tensor_copy(VB[:, r * 128:(r + 1) * 128], pp[0][0:128, 0:128]),
                reads=("pp0",), writes=("vb",))
    P.dma("g", lambda e: e.dma_start(out=VT[:, :], in_=vtok_in[:, :]), writes=("vt",))
    P.dma("g", lambda e: e.dma_start(out=GSEL[:, :], in_=gsel_in[:, :]), writes=("gsel",))
    P.dma("g", lambda e: e.dma_start(out=GSEL2[:, :], in_=gsel2_in[:, :]), writes=("gsel2",))
    P.dma("g", lambda e: e.dma_start(out=ESEL3[:, :], in_=esel3_in[:, :]), writes=("esel3",))
    P.dma("g", lambda e: e.dma_start(out=IDENT8[:, :], in_=ident8_in[:, :]), writes=("ident8",))
    P.dma("g", lambda e: e.dma_start(out=IDENTF[:, :], in_=identf_in[:, :]), writes=("identf",))
    P.dma("g", lambda e: e.dma_start(
            out=ZN[:, :].rearrange("p (j t) -> p j t", j=NDT),
            in_=zn_in.rearrange("(j p) t -> p j t", p=128)),
          writes=tuple("zn{}".format(j) for j in range(NDT)))

    P.v(lambda e: e.memset(LOHI[:, 0:1], 0.0), writes=("lo",))
    P.v(lambda e: e.memset(LOHI[:, 1:2], 1.01), writes=("hi",))
    for it in range(32):
        P.v(lambda e: e.tensor_tensor(LOHI[:, 2:3], LOHI[:, 0:1], LOHI[:, 1:2], ALU.add),
            reads=("lo", "hi"), writes=("mid",))
        P.v(lambda e: e.tensor_scalar(LOHI[:, 2:3], LOHI[:, 2:3], 0.5, None, ALU.mult),
            reads=("mid",), writes=("mid",))
        P.t(lambda e: e.matmul(ps_m[:, 0:1], GSEL2[:, :], LOHI[0:8, 2:3], start=True, stop=True),
            reads=("gsel2", "mid"), writes=("pp0",))
        P.v(lambda e: e.tensor_copy(MIDB[:, :], ps_m[:, 0:1]), reads=("pp0",), writes=("midb",))
        P.v(lambda e: e.tensor_scalar(CNTB[:, :], VB[:, :], MIDB[:, 0:1], None, ALU.is_ge),
            reads=("vb", "midb"), writes=("cntb",))
        P.v(lambda e: e.reduce_sum(CNTS[:, 0:1], CNTB[:, :], axis=AX.X),
            reads=("cntb",), writes=("cnts",))
        P.t(lambda e: e.matmul(ps_c[0:8, 0:1], GSEL[:, :], CNTS[:, 0:1], start=True, stop=True),
            reads=("gsel", "cnts"), writes=("pp1",))
        P.v(lambda e: e.tensor_copy(CNT8[:, 0:1], ps_c[0:8, 0:1]), reads=("pp1",), writes=("cnt8",))
        P.v(lambda e: e.tensor_scalar(CNT8[:, 1:2], CNT8[:, 0:1], CAP, None, ALU.is_ge),
            reads=("cnt8",), writes=("sel",))
        P.v(lambda e: e.tensor_tensor(LOHI[:, 3:4], LOHI[:, 2:3], LOHI[:, 0:1], ALU.subtract),
            reads=("mid", "lo"), writes=("sc_d",))
        P.v(lambda e: e.tensor_tensor(LOHI[:, 3:4], LOHI[:, 3:4], CNT8[:, 1:2], ALU.mult),
            reads=("sc_d", "sel"), writes=("sc_d",))
        P.v(lambda e: e.tensor_tensor(LOHI[:, 0:1], LOHI[:, 0:1], LOHI[:, 3:4], ALU.add),
            reads=("lo", "sc_d"), writes=("lo",))
        P.v(lambda e: e.tensor_tensor(LOHI[:, 4:5], LOHI[:, 1:2], LOHI[:, 2:3], ALU.subtract),
            reads=("hi", "mid"), writes=("sc_e",))
        P.v(lambda e: e.tensor_tensor(LOHI[:, 4:5], LOHI[:, 4:5], CNT8[:, 1:2], ALU.mult),
            reads=("sc_e", "sel"), writes=("sc_e",))
        P.v(lambda e: e.tensor_tensor(LOHI[:, 1:2], LOHI[:, 2:3], LOHI[:, 4:5], ALU.add),
            reads=("mid", "sc_e"), writes=("hi",))
    P.t(lambda e: e.matmul(ps_m[0:1, 0:8], LOHI[0:8, 0:1], IDENT8[:, :], start=True, stop=True),
        reads=("lo", "ident8"), writes=("pp0",))
    P.v(lambda e: e.tensor_scalar(TAUR[0:1, :], ps_m[0:1, 0:8], 0.0, None, ALU.max),
        reads=("pp0",), writes=("taur",))
    P.t(lambda e: e.matmul(ps_m[:, 0:8], ones32[0:1, :], TAUR[0:1, :], start=True, stop=True),
        reads=("ones32", "taur"), writes=("pp0",))
    P.v(lambda e: e.tensor_copy(TAUB[:, :], ps_m[:, 0:8]), reads=("pp0",), writes=("taub",))
    for tt in range(16):
        sl = slice(tt * 8, (tt + 1) * 8)
        P.v(lambda e, sl=sl: e.tensor_tensor(GT[:, sl], VT[:, sl], TAUB[:, :], ALU.is_ge),
            reads=("vt", "taub"), writes=("gt",))
        P.v(lambda e, sl=sl: e.scalar_tensor_tensor(GT[:, sl], GT[:, sl], 0.5, VT[:, sl],
                                                    ALU.mult, ALU.mult),
            reads=("gt", "vt"), writes=("gt",))
    for tt in range(16):
        P.v(lambda e, tt=tt: e.tensor_copy(GTB[:, tt * 8:(tt + 1) * 8] if False else GTB[:, 0:8],
                                           GT[:, tt * 8:(tt + 1) * 8]),
            reads=("gt",), writes=("gtb",))
        P.t(lambda e, tt=tt: e.matmul(ps_t[0:8, 0:128], GTB[:, 0:8], IDENTF[:, :],
                                      start=True, stop=True),
            reads=("gtb", "identf"), writes=("pp7",))
        P.v(lambda e, tt=tt: e.tensor_copy(GA[0:8, tt * 128:(tt + 1) * 128], ps_t[0:8, 0:128]),
            reads=("pp7",), writes=("ga",))

    for c in range(NTC):
        P.dma("g", lambda e, c=c: e.dma_start(
                out=ACC[:, :].rearrange("p (j tt) -> p j tt", j=NDT)[:, :, c * TC:(c + 1) * TC],
                in_=s1_in[:, c * TC:(c + 1) * TC].rearrange("(j p) t -> p j t", p=128)),
              writes=tuple("acc{}_{}".format(j, c) for j in range(NDT)))
        for j in range(NDT):
            P.v(lambda e, j=j, c=c: e.tensor_scalar(
                    ACC[:, j * T + c * TC: j * T + (c + 1) * TC],
                    ACC[:, j * T + c * TC: j * T + (c + 1) * TC], 0.5, None, ALU.mult),
                reads=("acc{}_{}".format(j, c),), writes=("acc{}_{}".format(j, c),))
        P.dma("g", lambda e, c=c: e.dma_start(
                out=ACC[:, :].rearrange("p (j tt) -> p j tt", j=NDT)[:, :, c * TC:(c + 1) * TC],
                in_=x3_in[:, c * TC:(c + 1) * TC].rearrange("(j p) t -> p j t", p=128),
                accum_op=ALU.add),
              reads=tuple("acc{}_{}".format(j, c) for j in range(NDT)),
              writes=tuple("acc{}_{}".format(j, c) for j in range(NDT)))

    for ex in range(8):
        P.dma("g", lambda e, ex=ex: e.dma_start(
                out=W1E[:, :].rearrange("p (j f) -> p j f", j=NDT),
                in_=w1_in[ex * DM:(ex + 1) * DM, :].rearrange("(j p) f -> p j f", p=128)),
              writes=tuple("w1e{}".format(j) for j in range(NDT)))
        P.dma("g", lambda e, ex=ex: e.dma_start(
                out=W2E[:, :].rearrange("p (f c) -> p f c", f=16),
                in_=w2_in[ex * 2048:(ex + 1) * 2048, :].rearrange("(f p) c -> p f c", p=128)),
              writes=tuple("w2e{}".format(ft) for ft in range(16)))
        for c in range(NTC):
            P.t(lambda e, ex=ex, c=c: e.matmul(
                    ps_g[:, :], ESEL3[:, ex * 128:(ex + 1) * 128], GA[0:8, c * TC:(c + 1) * TC],
                    start=True, stop=True),
                reads=("esel3", "ga"), writes=("pp6",))
            for j in range(NDT):
                P.v(lambda e, j=j, c=c: e.tensor_tensor(
                        ZG[:, j * TC:(j + 1) * TC], ZN[:, j * T + c * TC: j * T + (c + 1) * TC],
                        ps_g[:, :], ALU.mult),
                    reads=("zn{}".format(j), "pp6"), writes=("zg{}".format(j),))
            for ft in range(16):
                ps = ps_h[ft % 2]
                for j in range(NDT):
                    P.t(lambda e, ft=ft, j=j, ps=ps: e.matmul(
                            ps[:, :], W1E[:, j * 2048 + ft * 128: j * 2048 + (ft + 1) * 128],
                            ZG[:, j * TC:(j + 1) * TC],
                            start=(j == 0), stop=(j == NDT - 1)),
                        reads=("w1e{}".format(j), "zg{}".format(j)), writes=("pp{}".format(2 + ft % 2),))
                P.s(lambda e, ft=ft, ps=ps: e.activation(HB[:, ft * TC:(ft + 1) * TC], ps[:, :],
                                                         AF.Relu),
                    reads=("pp{}".format(2 + ft % 2),), writes=("hb{}".format(ft),))
            for j in range(NDT):
                ps = ps_o[j % 2]
                for ft in range(16):
                    P.t(lambda e, j=j, ft=ft, ps=ps: e.matmul(
                            ps[:, :], W2E[:, ft * DM + j * 128: ft * DM + (j + 1) * 128],
                            HB[:, ft * TC:(ft + 1) * TC],
                            start=(ft == 0), stop=(ft == 15)),
                        reads=("w2e{}".format(ft), "hb{}".format(ft)), writes=("pp{}".format(4 + j % 2),))
                P.v(lambda e, j=j, c=c, ps=ps: e.tensor_tensor(
                        ACC[:, j * T + c * TC: j * T + (c + 1) * TC],
                        ACC[:, j * T + c * TC: j * T + (c + 1) * TC], ps[:, :], ALU.add),
                    reads=("acc{}_{}".format(j, c), "pp{}".format(4 + j % 2)),
                    writes=("acc{}_{}".format(j, c),))

    for c in range(NTC):
        for j in range(NDT):
            P.dma("y", lambda e, j=j, c=c: e.dma_start(
                    out=x_out[j * 128:(j + 1) * 128, c * TC:(c + 1) * TC],
                    in_=ACC[:, j * T + c * TC: j * T + (c + 1) * TC]),
                  reads=("acc{}_{}".format(j, c),), writes=("out:x{}_{}".format(c, j),))
    keys = ["out:x{}_{}".format(c, j) for c in range(NTC) for j in range(NDT)]
    P.op("y", lambda e: e.nop(), reads=tuple(keys))


def build_moe_launch():
    nc = bass.Bass("TRN2", target_bir_lowering=False)
    x3_in = nc.dram_tensor("x3_in", [DM, T], F32, kind="ExternalInput").ap()
    s1_in = nc.dram_tensor("s1_in", [DM, T], F32, kind="ExternalInput").ap()
    zn_in = nc.dram_tensor("zn_in", [DM, T], BF16, kind="ExternalInput").ap()
    vbis_in = nc.dram_tensor("vbis_in", [128, 1024], F32, kind="ExternalInput").ap()
    vtok_in = nc.dram_tensor("vtok_in", [128, 128], F32, kind="ExternalInput").ap()
    w1_in = nc.dram_tensor("w1_in", [8 * DM, 2048], F32, kind="ExternalInput").ap()
    w2_in = nc.dram_tensor("w2_in", [8 * 2048, DM], F32, kind="ExternalInput").ap()
    gsel_in = nc.dram_tensor("gsel_in", [128, 8], F32, kind="ExternalInput").ap()
    gsel2_in = nc.dram_tensor("gsel2_in", [8, 128], F32, kind="ExternalInput").ap()
    esel3_in = nc.dram_tensor("esel3_in", [8, 1024], F32, kind="ExternalInput").ap()
    ident8_in = nc.dram_tensor("ident8_in", [8, 8], F32, kind="ExternalInput").ap()
    identf_in = nc.dram_tensor("identf_in", [128, 128], F32, kind="ExternalInput").ap()
    x_out = nc.dram_tensor("x_out", [DM, T], F32, kind="ExternalOutput").ap()

    io = dict(x3_in=x3_in, s1_in=s1_in, zn_in=zn_in, vbis_in=vbis_in, vtok_in=vtok_in,
              w1_in=w1_in, w2_in=w2_in, gsel_in=gsel_in, gsel2_in=gsel2_in, esel3_in=esel3_in,
              ident8_in=ident8_in, identf_in=identf_in, x_out=x_out)
    st = contextlib.ExitStack()
    E = st.enter_context
    P = Prog(nc)
    plan_moe(P, nc, E, io)
    block = E(nc.Block())
    P.finalize(block, st)
    st.close()
    return nc


def moe_consts():
    gsel = np.zeros((128, 8), np.float32)
    gsel2 = np.zeros((8, 128), np.float32)
    for p in range(128):
        gsel[p, p // 16] = 1.0
        gsel2[p // 16, p] = 1.0
    esel3 = np.zeros((8, 1024), np.float32)
    for ex in range(8):
        esel3[ex, ex * 128:(ex + 1) * 128] = 1.0
    return dict(gsel_in=gsel, gsel2_in=gsel2, esel3_in=esel3,
                ident8_in=np.eye(8, dtype=np.float32),
                identf_in=np.eye(128, dtype=np.float32))


def run_moe(x3t, s1t, znt, vlocs, w1, w2):
    vglob = np.concatenate([vloc_decode(vlocs[b]) for b in range(8)], axis=0)
    vbis = moe_host_layouts(vglob)
    consts = moe_consts()
    w1h = np.ascontiguousarray(np.asarray(w1, np.float32).reshape(8 * DM, 2048))
    w2h = np.ascontiguousarray(np.asarray(w2, np.float32).reshape(8 * 2048, DM))
    nc = build_moe_launch()
    in_maps = [dict(consts, x3_in=x3t[b], s1_in=s1t[b], zn_in=znt[b],
                    vbis_in=vbis, vtok_in=vlocs[b], w1_in=w1h, w2_in=w2h)
               for b in range(8)]
    res = run_bass_kernel_spmd(nc, in_maps, core_ids=list(range(8)))
    return [res.results[b]["x_out"] for b in range(8)]



def _np(a):
    return np.ascontiguousarray(np.asarray(a))


def _prep_l0ffn(inp):
    w1t = np.ascontiguousarray(_np(inp["L0_ffn_W1"]).T)
    w2t = np.ascontiguousarray(_np(inp["L0_ffn_W2"]).T)
    b1 = np.ascontiguousarray(_np(inp["L0_ffn_b1"]).reshape(32, 128).T)
    b2 = np.ascontiguousarray(_np(inp["L0_ffn_b2"]).reshape(8, 128).T)
    return w1t, w2t, b1, b2


def run_l0ffn(xt, inp, tern):
    w1t, w2t, b1, b2 = _prep_l0ffn(inp)
    nc = build_l0ffn_launch(with_ternarize=tern is None)
    if tern is None:
        in_maps = [{"x_in": xt[b], "w1t_in": w1t, "w2t_in": w2t, "b1_in": b1, "b2_in": b2}
                   for b in range(8)]
    else:
        in_maps = [{"x_in": xt[b], "w1bf_in": tern[0][b], "w2bf_in": tern[1][b],
                    "b1_in": b1, "b2_in": b2} for b in range(8)]
    res = run_bass_kernel_spmd(nc, in_maps, core_ids=list(range(8)))
    xt2 = [res.results[b]["x_out"] for b in range(8)]
    if tern is None:
        tern = ([res.results[b]["w1bf_out"] for b in range(8)],
                [res.results[b]["w2bf_out"] for b in range(8)])
    return xt2, tern


def run_l1ffn(xt, inp, idx):
    C = DM
    wqkv = _np(inp["L1_ffn_w_qkv"]).astype(np.float32)
    wo = _np(inp["L1_ffn_w_o"]).astype(np.float32)
    g1 = _np(inp["L1_ffn_g1"]).astype(np.float32)
    g2 = _np(inp["L1_ffn_g2"]).astype(np.float32)
    wrt = _np(inp["L1_ffn_w_router"]).astype(np.float32)
    w1 = _np(inp["L1_ffn_w1"]).astype(np.float32)
    w2 = _np(inp["L1_ffn_w2"]).astype(np.float32)
    s1 = run_attn(xt, "l1mha", wqkv[:, 0:C].copy(), wqkv[:, C:2 * C].copy(),
                  wqkv[:, 2 * C:].copy(), wo, None, None, None, None, g1=g1)
    zn, vloc = run_router(s1, wrt, g2)
    x4 = run_moe(xt, s1, zn, vloc, w1, w2)
    return x4



def permsel_const():
    m = np.zeros((128, 128), np.float32)
    for p in range(128):
        e, tt = p // 16, p % 16
        m[tt * 8 + e, p] = 1.0
    return m


class _NcProxy:

    def __init__(self, nc):
        self._nc = nc
        self._n = 0

    def sbuf_tensor(self, name, shape, dtype, **kw):
        self._n += 1
        return self._nc.sbuf_tensor("{}_p{}".format(name, self._n), shape, dtype, **kw)

    def psum_tensor(self, name, shape, dtype=mybir.dt.float32, **kw):
        self._n += 1
        return self._nc.psum_tensor("{}_p{}".format(name, self._n), shape, dtype, **kw)

    def __getattr__(self, k):
        return getattr(self._nc, k)


def build_fused():
    nc0 = bass.Bass("TRN2", target_bir_lowering=False)
    nc = _NcProxy(nc0)
    DT = nc.dram_tensor

    x0 = DT("x0", [DM, T], F32, kind="ExternalInput").ap()
    x_out = DT("x_out", [DM, T], F32, kind="ExternalOutput").ap()

    f_w1t = DT("f_w1t", [DM, 4096], F32, kind="ExternalInput").ap()
    f_w2t = DT("f_w2t", [4096, DM], F32, kind="ExternalInput").ap()
    f_b1 = DT("f_b1", [128, 32], F32, kind="ExternalInput").ap()
    f_b2 = DT("f_b2", [128, 8], F32, kind="ExternalInput").ap()
    w1bf = DT("w1bf", [DM, 4096], BF16, kind="Internal").ap()
    w2bf = DT("w2bf", [4096, DM], BF16, kind="Internal").ap()
    minus1_in = DT("minus1_in", [1, 128], F32, kind="ExternalInput").ap()
    psel2_in = DT("psel2_in", [128, 128], F32, kind="ExternalInput").ap()
    bsel_in = DT("bsel_in", [128, 128], F32, kind="ExternalInput").ap()
    ident_in = DT("ident_in", [128, 128], F32, kind="ExternalInput").ap()
    a0_wpack = DT("a0_wpack", [DM, 4 * DM], F32, kind="ExternalInput").ap()
    a0_bpack = DT("a0_bpack", [128, 32], F32, kind="ExternalInput").ap()
    a0_bvrow = DT("a0_bvrow", [1, DM], F32, kind="ExternalInput").ap()
    a0_hostm = DT("a0_hostm", [128, 1152], F32, kind="ExternalInput").ap()
    m_wpack = DT("m_wpack", [DM, 4 * DM], F32, kind="ExternalInput").ap()
    m_bpack = DT("m_bpack", [128, 32], F32, kind="ExternalInput").ap()
    m_bvrow = DT("m_bvrow", [1, DM], F32, kind="ExternalInput").ap()
    m_hostm = DT("m_hostm", [128, 1024], F32, kind="ExternalInput").ap()
    m_g1 = DT("m_g1", [128, 8], F32, kind="ExternalInput").ap()
    m_ru = DT("m_ru", [64 if False else 128, T], F32, kind="ExternalInput").ap()
    m_rw = DT("m_rw", [128, T], F32, kind="ExternalInput").ap()
    m_m1 = DT("m_m1", [128, 128], F32, kind="ExternalInput").ap()
    m_m2 = DT("m_m2", [128, 128], F32, kind="ExternalInput").ap()
    r_wr = DT("r_wr", [DM, 8], F32, kind="ExternalInput").ap()
    r_g2 = DT("r_g2", [128, 8], F32, kind="ExternalInput").ap()
    o_w1 = DT("o_w1", [8 * DM, 2048], F32, kind="ExternalInput").ap()
    o_w2 = DT("o_w2", [8 * 2048, DM], F32, kind="ExternalInput").ap()
    o_gsel = DT("o_gsel", [128, 8], F32, kind="ExternalInput").ap()
    o_gsel2 = DT("o_gsel2", [8, 128], F32, kind="ExternalInput").ap()
    o_esel3 = DT("o_esel3", [8, 1024], F32, kind="ExternalInput").ap()
    o_ident8 = DT("o_ident8", [8, 8], F32, kind="ExternalInput").ap()
    o_permsel = DT("o_permsel", [128, 128], F32, kind="ExternalInput").ap()
    a1_wpack = DT("a1_wpack", [DM, 4 * DM], F32, kind="ExternalInput").ap()
    a1_bpack = DT("a1_bpack", [128, 32], F32, kind="ExternalInput").ap()
    a1_bvrow = DT("a1_bvrow", [1, DM], F32, kind="ExternalInput").ap()
    a1_hostm = DT("a1_hostm", [128, 1536], F32, kind="ExternalInput").ap()
    a1_gb = DT("a1_gb", [128, 4], F32, kind="ExternalInput").ap()
    x1 = DT("x1", [DM, T], F32, kind="Internal").ap()
    x2 = DT("x2", [DM, T], F32, kind="Internal").ap()
    x3 = DT("x3", [DM, T], F32, kind="Internal").ap()
    x4 = DT("x4", [DM, T], F32, kind="Internal").ap()
    x5 = DT("x5", [DM, T], F32, kind="Internal").ap()
    s1a = DT("s1a", [DM, T], F32, kind="Internal").ap()
    s1b = DT("s1b", [DM, T], F32, kind="Internal").ap()
    zna = DT("zna", [DM, T], BF16, kind="Internal").ap()
    znb = DT("znb", [DM, T], BF16, kind="Internal").ap()
    za = DT("za", [DM, T], F32, kind="Internal").ap()
    zb = DT("zb", [DM, T], F32, kind="Internal").ap()
    agi_a = DT("agi_a", [128, 128], F32, kind="Internal")
    ago_a = DT("ago_a", [1024, 128], F32, kind="Internal")
    agi_b = DT("agi_b", [128, 128], F32, kind="Internal")
    ago_b = DT("ago_b", [1024, 128], F32, kind="Internal")

    master = contextlib.ExitStack()
    P = Prog(nc)

    _pn = [0]

    def phase(fn, name=None):
        st = contextlib.ExitStack()
        nm = name or "ph{}".format(_pn[0])
        _pn[0] += 1
        P.scope = nm
        fn(st.enter_context)
        st.close()
        P.barrier()
        P.scope = None

    attn_common = dict(minus1_in=minus1_in, psel2_in=psel2_in, bsel_in=bsel_in,
                       ident_in=ident_in)

    phase(lambda E: plan_l0ffn(P, nc, E, dict(
        x_in=x0, x_out=x1, w1t_in=f_w1t, w2t_in=f_w2t, w1bf_out=w1bf, w2bf_out=w2bf,
        b1_in=f_b1, b2_in=f_b2), True), "l0ffn_tz")
    phase(lambda E: plan_attn(P, nc, E, dict(
        attn_common, x_in=x1, x_out=x2, wpack=a0_wpack, bpack=a0_bpack, bvrow=a0_bvrow,
        hostm_in=a0_hostm), "l0"), "l0attn")
    phase(lambda E: plan_l0ffn(P, nc, E, dict(
        x_in=x2, x_out=x3, w1bf_in=w1bf, w2bf_in=w2bf, b1_in=f_b1, b2_in=f_b2), False), "l0ffn2")

    def l1ffn(xin, xout, s1, zn, zz, agi, ago, tg=""):
        phase(lambda E: plan_attn(P, nc, E, dict(
            attn_common, x_in=xin, x_out=s1, wpack=m_wpack, bpack=m_bpack, bvrow=m_bvrow,
            hostm_in=m_hostm, g1_in=m_g1, zbuf=zz, ru=m_ru, rw=m_rw, m1=m_m1, m2=m_m2),
            "l1mha"), "mha" + tg)
        phase(lambda E: plan_router(P, nc, E, dict(
            x_in=s1, wr_in=r_wr, g2_in=r_g2, zn_out=zn, vloc_out=agi.ap())), "router" + tg)

        def moe_phase(E):
            P.cc(lambda e: e.collective_compute(
                    "AllGather", ALU.bypass, replica_groups=[list(range(8))],
                    ins=[agi.ap().opt()], outs=[ago.ap().opt()]),
                 reads=("out:vloc",), writes=("cc_done",))
            plan_moe(P, nc, E, dict(
                x3_in=xin, s1_in=s1, zn_in=zn, vtok_in=agi.ap(),
                w1_in=o_w1, w2_in=o_w2, gsel_in=o_gsel, gsel2_in=o_gsel2, esel3_in=o_esel3,
                ident8_in=o_ident8, identf_in=ident_in, x_out=xout),
                ago=ago.ap(), permsel=o_permsel, idf32=ident_in)

        phase(moe_phase, "moe" + tg)

    l1ffn(x3, x4, s1a, zna, za, agi_a, ago_a, "A")
    phase(lambda E: plan_attn(P, nc, E, dict(
        attn_common, x_in=x4, x_out=x5, wpack=a1_wpack, bpack=a1_bpack, bvrow=a1_bvrow,
        hostm_in=a1_hostm, gb_in=a1_gb), "l1attn"), "l1attn")
    l1ffn(x5, x_out, s1b, znb, zb, agi_b, ago_b, "B")

    block = master.enter_context(nc.Block())
    P.finalize(block, master)
    master.close()
    return nc


def kernel(**inp):
    x = _np(inp["L0_attn_x"]).astype(np.float32, copy=False)
    xt = [np.ascontiguousarray(x[b].T) for b in range(8)]
    C = DM

    im = {}
    w1t, w2t, b1p, b2p = _prep_l0ffn(inp)
    im.update(f_w1t=w1t, f_w2t=w2t, f_b1=b1p, f_b2=b2p)

    def attn_pack(wq, wk, wv, wo, bq, bk, bv, bo, permute=False):
        if permute:
            perm = rope_perm()
            wq = wq[:, perm]
            wk = wk[:, perm]
        wpackh = np.ascontiguousarray(np.concatenate([wq, wk, wv, wo], axis=1).astype(np.float32))
        z8 = np.zeros(DM, np.float32)
        bq = z8 if bq is None else np.asarray(bq, np.float32)
        bk = z8 if bk is None else np.asarray(bk, np.float32)
        bv = z8 if bv is None else np.asarray(bv, np.float32)
        bo = z8 if bo is None else np.asarray(bo, np.float32)
        bpackh = np.zeros((128, 32), np.float32)
        bpackh[:, 0:8] = bq.reshape(8, 128).T
        bpackh[:, 8:16] = bk.reshape(8, 128).T
        bpackh[:, 24:32] = bo.reshape(8, 128).T
        return wpackh, bpackh, np.ascontiguousarray(bv.reshape(1, DM))

    hc0 = attn_host_consts("l0")
    im.update(minus1_in=hc0["minus1_in"], psel2_in=hc0["psel2_in"], bsel_in=hc0["bsel_in"],
              ident_in=hc0["ident_in"], a0_hostm=hc0["hostm_in"])
    wp, bp, bv_ = attn_pack(_np(inp["L0_attn_Wq"]).astype(np.float32),
                            _np(inp["L0_attn_Wk"]).astype(np.float32),
                            _np(inp["L0_attn_Wv"]).astype(np.float32),
                            _np(inp["L0_attn_Wo"]).astype(np.float32),
                            _np(inp["L0_attn_bq"]), _np(inp["L0_attn_bk"]),
                            _np(inp["L0_attn_bv"]), _np(inp["L0_attn_bo"]))
    im.update(a0_wpack=wp, a0_bpack=bp, a0_bvrow=bv_)

    hcm = attn_host_consts("l1mha")
    wqkv = _np(inp["L1_ffn_w_qkv"]).astype(np.float32)
    wp, bp, bv_ = attn_pack(wqkv[:, 0:C].copy(), wqkv[:, C:2 * C].copy(), wqkv[:, 2 * C:].copy(),
                            _np(inp["L1_ffn_w_o"]).astype(np.float32),
                            None, None, None, None, permute=True)
    im.update(m_wpack=wp, m_bpack=bp, m_bvrow=bv_, m_hostm=hcm["hostm_in"],
              m_ru=hcm["ru"], m_rw=hcm["rw"], m_m1=hcm["m1"], m_m2=hcm["m2"],
              m_g1=np.ascontiguousarray(_np(inp["L1_ffn_g1"]).astype(np.float32).reshape(8, 128).T))

    im.update(r_wr=np.ascontiguousarray(_np(inp["L1_ffn_w_router"]).astype(np.float32)),
              r_g2=np.ascontiguousarray(_np(inp["L1_ffn_g2"]).astype(np.float32).reshape(8, 128).T))
    mc = moe_consts()
    im.update(o_w1=np.ascontiguousarray(_np(inp["L1_ffn_w1"]).astype(np.float32).reshape(8 * DM, 2048)),
              o_w2=np.ascontiguousarray(_np(inp["L1_ffn_w2"]).astype(np.float32).reshape(8 * 2048, DM)),
              o_gsel=mc["gsel_in"], o_gsel2=mc["gsel2_in"], o_esel3=mc["esel3_in"],
              o_ident8=mc["ident8_in"], o_permsel=permsel_const())

    hc1 = attn_host_consts("l1attn")
    wqkv1 = _np(inp["L1_attn_W_qkv"]).astype(np.float32)
    bqkv1 = _np(inp["L1_attn_b_qkv"]).astype(np.float32)
    wp, bp, bv_ = attn_pack(wqkv1[:, 0:C].copy(), wqkv1[:, C:2 * C].copy(), wqkv1[:, 2 * C:].copy(),
                            _np(inp["L1_attn_W_out"]).astype(np.float32),
                            bqkv1[0:C], bqkv1[C:2 * C], bqkv1[2 * C:],
                            _np(inp["L1_attn_b_out"]).astype(np.float32))
    gq = _np(inp["L1_attn_q_gamma"]).astype(np.float32)
    gqb = _np(inp["L1_attn_q_beta"]).astype(np.float32)
    gk = _np(inp["L1_attn_k_gamma"]).astype(np.float32)
    gkb = _np(inp["L1_attn_k_beta"]).astype(np.float32)
    gbh = np.zeros((128, 4), np.float32)
    gbh[:, 0] = np.tile(gq, 2)
    gbh[:, 1] = np.tile(gqb, 2)
    gbh[:, 2] = np.tile(gk, 2)
    gbh[:, 3] = np.tile(gkb, 2)
    im.update(a1_wpack=wp, a1_bpack=bp, a1_bvrow=bv_, a1_hostm=hc1["hostm_in"], a1_gb=gbh)

    nc = build_fused()
    in_maps = [dict(im, x0=xt[b]) for b in range(8)]
    res = run_bass_kernel_spmd(nc, in_maps, core_ids=list(range(8)))
    out = np.stack([np.ascontiguousarray(res.results[b]["x_out"].T) for b in range(8)])
    return out


# version 06
# speedup vs baseline: 1.3616x; 1.0038x over previous
import contextlib
import numpy as np
import ml_dtypes
import concourse.bass as bass
import concourse.mybir as mybir
from concourse.bass_utils import run_bass_kernel_spmd

F32 = mybir.dt.float32
BF16 = mybir.dt.bfloat16
I32 = mybir.dt.int32
AF = mybir.ActivationFunctionType
ALU = mybir.AluOpType
AX = mybir.AxisListType

BF16NP = ml_dtypes.bfloat16

T = 2048
DM = 1024
NDT = DM // 128
TC = 512
NTC = T // TC
ROUND_C = float(np.float32(1.5 * 2.0**23))

ENGINES = ("t", "v", "s", "g", "y")
DEBUG_TAPS = False
ATTN_DEBUG = None


class Op:
    __slots__ = ("eng", "emit", "deps", "inc", "idx", "need_inc", "cum", "is_cc", "cc_sem", "scope")

    def __init__(self, eng, emit, deps, inc, idx):
        self.eng, self.emit, self.deps, self.inc, self.idx = eng, emit, deps, inc, idx
        self.need_inc = False
        self.cum = 0
        self.is_cc = False
        self.cc_sem = None
        self.scope = None


class Prog:

    def __init__(self, nc):
        self.nc = nc
        self.items = {e: [] for e in ENGINES}
        self.last_writer = {}
        self.readers = {}
        self.sems = {}
        self.totals = {}
        self.scope = None

    def barrier(self):
        lasts = [self.items[e][-1] for e in ENGINES if self.items[e]]
        for e in ENGINES:
            o = Op(e, lambda en: en.nop(), list(lasts), 1, len(self.items[e]))
            o.scope = self.scope
            self.items[e].append(o)

    def cc(self, emit, reads=(), writes=()):
        o = self.op("g", emit, reads, writes, dma=False)
        o.is_cc = True
        return o

    def op(self, eng, emit, reads=(), writes=(), dma=False):
        deps = []
        for k in reads:
            w = self.last_writer.get(k)
            if w is not None:
                deps.append(w)
        for k in writes:
            w = self.last_writer.get(k)
            if w is not None:
                deps.append(w)
            deps.extend(self.readers.get(k, ()))
        o = Op(eng, emit, deps, 16 if dma else 1, len(self.items[eng]))
        o.scope = self.scope
        self.items[eng].append(o)
        for k in writes:
            self.last_writer[k] = o
            self.readers[k] = []
        for k in reads:
            self.readers.setdefault(k, []).append(o)
        return o

    def t(self, emit, reads=(), writes=()):
        return self.op("t", emit, reads, writes)

    def v(self, emit, reads=(), writes=()):
        return self.op("v", emit, reads, writes)

    def s(self, emit, reads=(), writes=()):
        return self.op("s", emit, reads, writes)

    def g(self, emit, reads=(), writes=()):
        return self.op("g", emit, reads, writes)

    def dma(self, eng, emit, reads=(), writes=()):
        return self.op(eng, emit, reads, writes, dma=True)

    @staticmethod
    def _pe_pe(d, o):
        return d.eng == "t" and o.eng == "t" and d.inc == 1

    def finalize(self, block, stack):
        for e in ENGINES:
            for o in self.items[e]:
                for d in o.deps:
                    if not self._pe_pe(d, o):
                        d.need_inc = True
                if o.inc == 16:
                    o.need_inc = True
        for e in ENGINES:
            c = 0
            for o in self.items[e]:
                if o.is_cc:
                    o.cc_sem = stack.enter_context(self.nc.semaphore("cc_sem{}".format(id(o) % 100000)))
                    o.need_inc = False
                    o.cum = c
                    continue
                if o.need_inc:
                    c += o.inc
                o.cum = c
            self.totals[e] = c
            if c > 0:
                self.sems[e] = stack.enter_context(self.nc.semaphore(f"sem_{e}"))

        prog = self

        def emit_stream(eng_handle, e):
            watermark = {e2: 0 for e2 in ENGINES}
            ccs_done = set()
            prev_dma_cum = 0
            cur_scope = [None, None]

            def set_scope(nm):
                if nm == cur_scope[0]:
                    return
                if cur_scope[0] is not None:
                    prog.nc.leave_named_scope(cur_scope[0], cur_scope[1], False)
                cur_scope[0] = nm
                cur_scope[1] = None
                if nm is not None:
                    cur_scope[1] = prog.nc.enter_named_scope(nm, False)[0]

            for o in prog.items[e]:
                set_scope(o.scope)
                if o.inc == 16 and prev_dma_cum > watermark[e]:
                    eng_handle.wait_ge(prog.sems[e], prev_dma_cum)
                    watermark[e] = prev_dma_cum
                for d in o.deps:
                    if prog._pe_pe(d, o):
                        continue
                    if d.is_cc:
                        if id(d) not in ccs_done:
                            eng_handle.wait_ge(d.cc_sem, 1)
                            ccs_done.add(id(d))
                        if d.cum > watermark[d.eng]:
                            eng_handle.wait_ge(prog.sems[d.eng], d.cum)
                            watermark[d.eng] = d.cum
                        continue
                    if d.cum > watermark[d.eng]:
                        eng_handle.wait_ge(prog.sems[d.eng], d.cum)
                        watermark[d.eng] = d.cum
                ins = o.emit(eng_handle)
                if o.is_cc:
                    ins.then_inc(o.cc_sem)
                elif o.need_inc:
                    ins.then_inc(prog.sems[e], o.inc)
                if o.inc == 16:
                    prev_dma_cum = o.cum
            set_scope(None)

        for e, attr in (("g", "gpsimd"), ("t", "tensor"), ("v", "vector"),
                        ("s", "scalar"), ("y", "sync")):
            if not self.items[e]:
                continue
            getattr(block, attr)(lambda eng, _e=e: emit_stream(eng, _e))



def emit_ln_chunk(P, tn, ctx, x_hbm, c, XN, xn_tag, rms_only=False, eps=1e-5):
    XB8, XSQ = ctx["XB8"], ctx["XSQ"]
    AROW, BROW, TROW = ctx["AROW"], ctx["BROW"], ctx["TROW"]
    ps_s1, ps_s2, ps_a, ps_b = ctx["ps_s1"], ctx["ps_s2"], ctx["ps_a"], ctx["ps_b"]
    ones_colbf, ones32 = ctx["ones_colbf"], ctx["ones32"]
    AB8, XBT = ctx["AB8"], ctx["XBT"]
    rs = slice(0, TC)
    P.dma("g", lambda e, c=c: e.dma_start(
            out=XB8[:, :].rearrange("p (j t) -> p j t", j=NDT),
            in_=x_hbm[:, c * TC:(c + 1) * TC].rearrange("(j p) t -> p j t", p=128)),
          writes=("xb8",))
    if not rms_only:
        for j in range(NDT):
            P.t(lambda e, j=j: e.matmul(ps_s1[0:1, :], ones_colbf[:, 0:1],
                                        XB8[:, j * TC:(j + 1) * TC],
                                        start=(j == 0), stop=(j == NDT - 1)),
                reads=("ones_colbf", "xb8"), writes=("ps_s1",))
    for j in range(NDT):
        slot = j % 2
        bcol = slice(slot * TC, (slot + 1) * TC)
        P.s(lambda e, j=j, bcol=bcol: e.activation(XSQ[:, bcol], XB8[:, j * TC:(j + 1) * TC],
                                                   AF.Square),
            reads=("xb8",), writes=("xsq{}".format(slot),))
        P.t(lambda e, bcol=bcol, j=j: e.matmul(ps_s2[0:1, :], ones_colbf[:, 0:1], XSQ[:, bcol],
                                               start=(j == 0), stop=(j == NDT - 1)),
            reads=("ones_colbf", "xsq{}".format(slot)), writes=("ps_s2",))
    if not rms_only:
        P.v(lambda e: e.tensor_scalar(BROW[0:1, rs], ps_s1[0:1, :], -1.0 / DM, None, ALU.mult),
            reads=("ps_s1",), writes=("brow0",))
        P.v(lambda e: e.tensor_scalar(AROW[0:1, rs], ps_s2[0:1, :], 1.0 / DM, None, ALU.mult),
            reads=("ps_s2",), writes=("arow0",))
        P.v(lambda e: e.tensor_tensor(TROW[0:1, 0:TC], BROW[0:1, rs], BROW[0:1, rs], ALU.mult),
            reads=("brow0",), writes=("trow",))
        P.v(lambda e: e.scalar_tensor_tensor(AROW[0:1, rs], AROW[0:1, rs], float(eps),
                                             TROW[0:1, 0:TC], ALU.add, ALU.subtract),
            reads=("arow0", "trow"), writes=("arow0",))
    else:
        P.v(lambda e: e.tensor_scalar(AROW[0:1, rs], ps_s2[0:1, :], 1.0 / DM, float(eps),
                                      ALU.mult, ALU.add),
            reads=("ps_s2",), writes=("arow0",))
    P.s(lambda e: e.activation(TROW[0:1, 0:TC], AROW[0:1, rs], AF.Ln),
        reads=("arow0",), writes=("trow",))
    P.s(lambda e: e.activation(AROW[0:1, rs], TROW[0:1, 0:TC], AF.Exp, scale=-0.5),
        reads=("trow",), writes=("arow0",))
    P.t(lambda e: e.matmul(ps_a[:, :], ones32[0:1, :], AROW[0:1, rs], start=True, stop=True),
        reads=("ones32", "arow0"), writes=("ps_a",))
    P.v(lambda e: e.tensor_copy(AB8[:, 0:TC], ps_a[:, :]), reads=("ps_a",), writes=("ab8a",))
    if not rms_only:
        P.t(lambda e: e.matmul(ps_b[:, :], ones32[0:1, :], BROW[0:1, rs], start=True, stop=True),
            reads=("ones32", "brow0"), writes=("ps_b",))
        P.v(lambda e: e.tensor_copy(AB8[:, TC:2 * TC], ps_b[:, :]), reads=("ps_b",), writes=("ab8b",))
    for j in range(NDT):
        dst = slice(j * TC, (j + 1) * TC)
        xsl = slice(j * TC, (j + 1) * TC)
        if rms_only:
            P.v(lambda e, xsl=xsl, dst=dst: e.tensor_tensor(XN[:, dst], XB8[:, xsl],
                                                            AB8[:, 0:TC], ALU.mult),
                reads=("xb8", "ab8a"), writes=("{}{}".format(xn_tag, j),))
        else:
            slot = j % 2
            tsl = slice(slot * TC, (slot + 1) * TC)
            P.v(lambda e, xsl=xsl, tsl=tsl: e.tensor_tensor(XBT[:, tsl], XB8[:, xsl],
                                                            AB8[:, TC:2 * TC], ALU.add),
                reads=("xb8", "ab8b"), writes=("xbt{}".format(slot),))
            P.v(lambda e, tsl=tsl, dst=dst: e.tensor_tensor(XN[:, dst], XBT[:, tsl],
                                                            AB8[:, 0:TC], ALU.mult),
                reads=("xbt{}".format(slot), "ab8a"), writes=("{}{}".format(xn_tag, j),))



def emit_ternarize_to_hbm(P, tn, ctx, WT_hbm, rows, cols, out_hbm, out_tag, final_mult):
    nrt = rows // 128
    WF, RS, SROW = ctx["WF"], ctx["RS"], ctx["SROW"]
    ASUM, STAGE = ctx["ASUM"], ctx["WSTG"]
    ones32, ones32col = ctx["ones32"], ctx["ones32col"]
    ps_w, ps_rs = ctx["ps_w"], ctx["ps_rs"]
    for r in range(nrt):
        slot = r % 2
        wsl = slice(slot * 4096, slot * 4096 + cols)
        P.dma("y" if r % 2 == 0 else "s", lambda e, r=r, wsl=wsl: e.dma_start(
                out=WF[:, wsl], in_=WT_hbm[r * 128:(r + 1) * 128, :]),
              writes=("wf{}".format(slot),))
        P.v(lambda e, r=r, wsl=wsl: e.tensor_reduce(ASUM[:, r:r + 1], WF[:, wsl], AX.X, ALU.add,
                                                    apply_absolute_value=True),
            reads=("wf{}".format(slot),), writes=("asum",))
    P.t(lambda e: e.matmul(ps_w[0:1, 0:nrt], ones32col[:, 0:1], ASUM[:, 0:nrt],
                           start=True, stop=True),
        reads=("asum", "ones32col"), writes=("ps_a",))
    P.v(lambda e: e.tensor_reduce(SROW[0:1, 0:1], ps_w[0:1, 0:nrt], AX.X, ALU.add),
        reads=("ps_a",), writes=("srow",))
    P.v(lambda e: e.tensor_scalar(SROW[0:1, 0:1], SROW[0:1, 0:1], 1.0 / (rows * cols), 1e-8,
                                  ALU.mult, ALU.max),
        reads=("srow",), writes=("srow",))
    P.v(lambda e: e.reciprocal(SROW[0:1, 1:2], SROW[0:1, 0:1]),
        reads=("srow",), writes=("srow",))
    P.t(lambda e: e.matmul(ps_rs[:, 0:1], ones32[0:1, :], SROW[0:1, 1:2], start=True, stop=True),
        reads=("srow", "ones32"), writes=("ps_a",))
    P.v(lambda e: e.tensor_copy(RS[:, 0:1], ps_rs[:, 0:1]),
        reads=("ps_a",), writes=("rs",))
    for r in range(nrt):
        slot = r % 2
        wsl = slice(slot * 4096, slot * 4096 + cols)
        ssl = slice(slot * cols, (slot + 1) * cols)
        P.dma("y" if r % 2 == 0 else "s", lambda e, r=r, wsl=wsl: e.dma_start(
                out=WF[:, wsl], in_=WT_hbm[r * 128:(r + 1) * 128, :]),
              writes=("wf{}".format(slot),))
        P.v(lambda e, wsl=wsl: e.tensor_scalar(WF[:, wsl], WF[:, wsl], RS[:, 0:1], ROUND_C,
                                               ALU.mult, ALU.add),
            reads=("wf{}".format(slot), "rs"), writes=("wf{}".format(slot),))
        P.v(lambda e, wsl=wsl: e.tensor_scalar(WF[:, wsl], WF[:, wsl], ROUND_C, 1.0,
                                               ALU.subtract, ALU.min),
            reads=("wf{}".format(slot),), writes=("wf{}".format(slot),))
        P.v(lambda e, wsl=wsl, ssl=ssl: e.tensor_scalar(STAGE[:, ssl], WF[:, wsl],
                                                        -1.0, final_mult, ALU.max, ALU.mult),
            reads=("wf{}".format(slot),), writes=("wstg{}".format(slot),))
        P.dma("y", lambda e, r=r, ssl=ssl: e.dma_start(out=out_hbm[r * 128:(r + 1) * 128, :],
                                                       in_=STAGE[:, ssl]),
              reads=("wstg{}".format(slot),), writes=(f"{out_tag}{r}",))



def plan_l0ffn(P, nc, E, io, with_ternarize):
    x_in = io.get("x_in")
    b1_in = io.get("b1_in")
    b2_in = io.get("b2_in")
    x_out = io.get("x_out")
    w1t_in = io.get("w1t_in")
    w2t_in = io.get("w2t_in")
    w1bf = io.get("w1bf_out") if with_ternarize else io.get("w1bf_in")
    w2bf = io.get("w2bf_out") if with_ternarize else io.get("w2bf_in")
    xn_dbg = io.get("xn_dbg")
    h_dbg = io.get("h_dbg")
    o_dbg = io.get("o_dbg")
    DBGF = E(nc.sbuf_tensor("DBGF", [128, 2 * TC], F32)) if DEBUG_TAPS else None
    XN = E(nc.sbuf_tensor("XN", [128, NDT * TC], BF16))
    XB8 = E(nc.sbuf_tensor("XB8", [128, NDT * TC], BF16))
    AB8 = E(nc.sbuf_tensor("AB8", [128, 2 * TC], BF16))
    XBT = E(nc.sbuf_tensor("XBT", [128, 2 * TC], BF16))
    XSQ = E(nc.sbuf_tensor("XSQ", [128, 2 * TC], BF16))
    XR = E(nc.sbuf_tensor("XR", [128, 2 * TC], F32))
    W1B = E(nc.sbuf_tensor("W1B", [128, NDT * 4096], BF16))
    BIGBF_T = E(nc.sbuf_tensor("BIGBF", [128, 32768], BF16))
    BIGBF = BIGBF_T[:, :]
    BIGB = BIGBF.bitcast(F32)
    HB = E(nc.sbuf_tensor("HB", [128, 32 * TC], BF16))
    TMPC = E(nc.sbuf_tensor("TMPC", [128, 2 * TC], F32))
    AROW = E(nc.sbuf_tensor("AROW", [1, TC], F32))
    BROW = E(nc.sbuf_tensor("BROW", [1, TC], F32))
    TROW = E(nc.sbuf_tensor("TROW", [1, TC], F32))
    SROW = E(nc.sbuf_tensor("SROW", [1, 16], F32))
    ASUM = E(nc.sbuf_tensor("ASUM", [128, 64], F32))
    RS = E(nc.sbuf_tensor("RS", [128, 2], F32))
    b1_sb = E(nc.sbuf_tensor("b1_sb", [128, 32], F32))
    b2_sb = E(nc.sbuf_tensor("b2_sb", [128, 8], F32))
    ones_colbf = E(nc.sbuf_tensor("ones_colbf", [128, 1], BF16))
    ones32 = E(nc.sbuf_tensor("ones32", [1, 128], F32))
    ones32col = E(nc.sbuf_tensor("ones32col", [128, 1], F32))

    ps_s1 = E(nc.psum_tensor("ps_s1", [128, TC], F32))
    ps_s2 = E(nc.psum_tensor("ps_s2", [128, TC], F32))
    ps_a = E(nc.psum_tensor("ps_a", [128, TC], F32))
    ps_b = E(nc.psum_tensor("ps_b", [128, TC], F32))
    ps_h = [E(nc.psum_tensor(f"ps_h{i}", [128, TC], F32)) for i in range(2)]
    ps_o = [E(nc.psum_tensor(f"ps_o{i}", [128, TC], F32)) for i in range(2)]

    ctx = dict(XN=XN, XB8=XB8, AB8=AB8, XBT=XBT, XSQ=XSQ, TMPC=TMPC,
               AROW=AROW, BROW=BROW, TROW=TROW,
               SROW=SROW, ASUM=ASUM, RS=RS,
               WF=BIGB[:, 0:8192], WSTG=BIGBF[:, 16384:24576],
               ones_colbf=ones_colbf, ones32=ones32, ones32col=ones32col,
               ps_s1=ps_s1, ps_s2=ps_s2, ps_a=ps_a, ps_b=ps_b,
               ps_w=ps_a[0:1, 0:64], ps_rs=ps_a[:, 64:65])

    P.g(lambda e: e.memset(ones_colbf[:, :], 1.0), writes=("ones_colbf",))
    P.g(lambda e: e.memset(ones32[:, :], 1.0), writes=("ones32",))
    P.g(lambda e: e.memset(ones32col[:, :], 1.0), writes=("ones32col",))
    P.dma("g", lambda e: e.dma_start(out=b1_sb[:, :], in_=b1_in[:, :]), writes=("b1_sb",))
    P.dma("g", lambda e: e.dma_start(out=b2_sb[:, :], in_=b2_in[:, :]), writes=("b2_sb",))
    P.v(lambda e: e.tensor_scalar(b2_sb[:, :], b2_sb[:, :], 0.5, None, ALU.mult),
        reads=("b2_sb",), writes=("b2_sb",))

    if with_ternarize:
        emit_ternarize_to_hbm(P, "tz1", ctx, w1t_in, DM, 4096, w1bf, "w1hbm", 1.0)
        emit_ternarize_to_hbm(P, "tz2", ctx, w2t_in, 4096, DM, w2bf, "w2hbm", 0.5)
        w1deps = lambda j: (f"w1hbm{j}",)
        w2deps = lambda r: (f"w2hbm{r}",)
        tz_keys = ("wf0", "wf1", "wstg0", "wstg1")
    else:
        w1deps = lambda j: ()
        w2deps = lambda r: ()
        tz_keys = ()

    P.op("y", lambda e: e.nop(), reads=tz_keys, writes=("bigfree",))

    P.dma("y", lambda e: e.dma_start(
            out=W1B[:, :].rearrange("p (j f) -> p j f", j=NDT),
            in_=w1bf.rearrange("(j p) f -> p j f", p=128)),
          reads=tuple(k for j in range(NDT) for k in w1deps(j)),
          writes=tuple("w1res{}".format(j) for j in range(NDT)))
    P.dma("y", lambda e: e.dma_start(
            out=BIGBF[:, 0:32 * DM].rearrange("p (f c) -> p f c", f=32),
            in_=w2bf.rearrange("(f p) c -> p f c", p=128)),
          reads=tuple(k for r in range(32) for k in w2deps(r)) + ("bigfree",),
          writes=tuple("w2res{}".format(r) for r in range(32)))

    NFT = 32
    for c in range(NTC):
        emit_ln_chunk(P, "ln", ctx, x_in, c, XN, "xn", rms_only=False, eps=1e-5)
        for ft in range(NFT):
            pslot = ft % 2
            ps = ps_h[pslot]
            for j in range(NDT):
                P.t(lambda e, ft=ft, j=j, ps=ps: e.matmul(
                        ps[:, :],
                        W1B[:, j * 4096 + ft * 128: j * 4096 + (ft + 1) * 128],
                        XN[:, j * TC:(j + 1) * TC],
                        start=(j == 0), stop=(j == NDT - 1)),
                    reads=(f"w1res{j}", f"xn{j}"), writes=(f"ps_h{pslot}",))
            P.s(lambda e, ft=ft, ps=ps, pslot=pslot: e.activation(
                    HB[:, ft * TC:(ft + 1) * TC], ps[:, :], AF.Gelu, bias=b1_sb[:, ft:ft + 1]),
                reads=(f"ps_h{pslot}", "b1_sb"), writes=(f"HB{ft}",))
        if DEBUG_TAPS and c == 0:
            for j in range(NDT):
                sl = slice((j % 2) * TC, (j % 2 + 1) * TC)
                P.v(lambda e, j=j, sl=sl: e.tensor_copy(DBGF[:, sl], XN[:, j * TC:(j + 1) * TC]),
                    reads=(f"xn{j}",), writes=(f"dbg{j%2}",))
                P.dma("y", lambda e, j=j, sl=sl: e.dma_start(out=xn_dbg[:, j * TC:(j + 1) * TC],
                                                             in_=DBGF[:, sl]),
                      reads=(f"dbg{j%2}",), writes=(f"odbgxn{j}",))
            for ft in range(4):
                sl = slice((ft % 2) * TC, (ft % 2 + 1) * TC)
                P.v(lambda e, ft=ft, sl=sl: e.tensor_copy(DBGF[:, sl], HB[:, ft * TC:(ft + 1) * TC]),
                    reads=(f"HB{ft}",), writes=(f"dbg{ft%2}",))
                P.dma("y", lambda e, ft=ft, sl=sl: e.dma_start(out=h_dbg[:, ft * TC:(ft + 1) * TC],
                                                               in_=DBGF[:, sl]),
                      reads=(f"dbg{ft%2}",), writes=(f"odbgh{ft}",))
        for j in range(NDT):
            pslot = j % 2
            ps = ps_o[pslot]
            for ft in range(NFT):
                P.t(lambda e, j=j, ft=ft, ps=ps: e.matmul(
                        ps[:, :], BIGBF[:, ft * DM + j * 128: ft * DM + (j + 1) * 128],
                        HB[:, ft * TC:(ft + 1) * TC],
                        start=(ft == 0), stop=(ft == NFT - 1)),
                    reads=(f"w2res{ft}", f"HB{ft}"), writes=(f"ps_o{pslot}",))
            xslot = j % 2
            xsl = slice(xslot * TC, (xslot + 1) * TC)
            P.dma("y", lambda e, j=j, c=c, xsl=xsl: e.dma_start(
                    out=XR[:, xsl], in_=x_in[j * 128:(j + 1) * 128, c * TC:(c + 1) * TC]),
                  writes=("rx{}".format(xslot),))
            tslot = j % 2
            tc2 = slice(tslot * TC, (tslot + 1) * TC)
            P.v(lambda e, j=j, ps=ps, tc2=tc2, xsl=xsl: e.scalar_tensor_tensor(
                    TMPC[:, tc2], ps[:, :], b2_sb[:, j:j + 1], XR[:, xsl],
                    ALU.add, ALU.add),
                reads=(f"ps_o{pslot}", "b2_sb", "rx{}".format(xslot)), writes=("tmpc{}".format(tslot),))
            P.dma("y", lambda e, j=j, c=c, tc2=tc2: e.dma_start(
                    out=x_out[j * 128:(j + 1) * 128, c * TC:(c + 1) * TC], in_=TMPC[:, tc2]),
                  reads=("tmpc{}".format(tslot),), writes=(f"out:x{c}_{j}",))

    keys = [f"out:x{c}_{j}" for c in range(NTC) for j in range(NDT)]
    if DEBUG_TAPS:
        keys += [f"odbgxn{j}" for j in range(NDT)] + [f"odbgh{ft}" for ft in range(4)]
    if with_ternarize:
        keys += [f"w1hbm{j}" for j in range(NDT)] + [f"w2hbm{r}" for r in range(32)]
    P.op("y", lambda e: e.nop(), reads=tuple(keys))



def build_l0ffn_launch(with_ternarize):
    nc = bass.Bass("TRN2", target_bir_lowering=False)
    x_in = nc.dram_tensor("x_in", [DM, T], F32, kind="ExternalInput").ap()
    b1_in = nc.dram_tensor("b1_in", [128, 32], F32, kind="ExternalInput").ap()
    b2_in = nc.dram_tensor("b2_in", [128, 8], F32, kind="ExternalInput").ap()
    x_out = nc.dram_tensor("x_out", [DM, T], F32, kind="ExternalOutput").ap()
    if with_ternarize:
        w1t_in = nc.dram_tensor("w1t_in", [DM, 4096], F32, kind="ExternalInput").ap()
        w2t_in = nc.dram_tensor("w2t_in", [4096, DM], F32, kind="ExternalInput").ap()
        w1bf = nc.dram_tensor("w1bf_out", [DM, 4096], BF16, kind="ExternalOutput").ap()
        w2bf = nc.dram_tensor("w2bf_out", [4096, DM], BF16, kind="ExternalOutput").ap()
    else:
        w1bf = nc.dram_tensor("w1bf_in", [DM, 4096], BF16, kind="ExternalInput").ap()
        w2bf = nc.dram_tensor("w2bf_in", [4096, DM], BF16, kind="ExternalInput").ap()

    if DEBUG_TAPS:
        xn_dbg = nc.dram_tensor("xn_dbg", [128, NDT * TC], F32, kind="ExternalOutput").ap()
        h_dbg = nc.dram_tensor("h_dbg", [128, 4 * TC], F32, kind="ExternalOutput").ap()
        o_dbg = nc.dram_tensor("o_dbg", [128, 2 * TC], F32, kind="ExternalOutput").ap()

    io = dict(x_in=x_in, b1_in=b1_in, b2_in=b2_in, x_out=x_out)
    if with_ternarize:
        io.update(w1t_in=w1t_in, w2t_in=w2t_in, w1bf_out=w1bf, w2bf_out=w2bf)
    else:
        io.update(w1bf_in=w1bf, w2bf_in=w2bf)
    if DEBUG_TAPS:
        io.update(xn_dbg=xn_dbg, h_dbg=h_dbg, o_dbg=o_dbg)
    st = contextlib.ExitStack()
    E = st.enter_context
    P = Prog(nc)
    plan_l0ffn(P, nc, E, io, with_ternarize)
    block = E(nc.Block())
    P.finalize(block, st)
    st.close()
    return nc


NH = 16
HD = 64


def attn_host_consts(variant):
    c = {}
    c["minus1_in"] = np.full((1, 128), -1.0, np.float32)
    p2 = np.zeros((128, 128), np.float32)
    p2[0:64, 0] = 1.0
    p2[64:128, 64] = 1.0
    c["psel2_in"] = p2
    bs = np.zeros((128, 128), np.float32)
    bs[0, 0:64] = 1.0
    bs[64, 64:128] = 1.0
    c["bsel_in"] = bs
    c["ident_in"] = np.eye(128, dtype=np.float32)
    if variant == "l0":
        W, umin, width = 129, -511, 1152
    elif variant == "l1attn":
        W, umin, width = 512, -511, 1536
    else:
        W = None
    if W is not None:
        ik = np.arange(128)[:, None]
        u = np.arange(width)[None, :]
        qk = u + umin - ik
        c["hostm_in"] = np.where((qk >= 0) & (qk <= W - 1), 0.0, -1e30).astype(np.float32)
    if variant == "l1mha":
        ik = np.arange(128)[:, None]
        u = np.arange(1024)[None, :]
        c["hostm_in"] = np.where(u - 511 - ik >= 0, 0.0, -1e30).astype(np.float32)
        f = (np.arange(128) % 32).astype(np.float64)
        theta = 10000.0 ** (-2.0 * f / 64.0)
        t = np.arange(T, dtype=np.float64)
        ang = t[None, :] * theta[:, None]
        cos, sin = np.cos(ang), np.sin(ang)
        r_rows = np.zeros(128, bool)
        r_rows[0:32] = True
        r_rows[64:96] = True
        ru = np.where(r_rows[:, None], cos, sin)
        rw = np.where(r_rows[:, None], sin, cos)
        c["ru"] = ru.astype(np.float32)
        c["rw"] = rw.astype(np.float32)
        m1 = np.zeros((128, 128), np.float32)
        m2 = np.zeros((128, 128), np.float32)
        for m in range(128):
            if r_rows[m]:
                m1[m, m] = 1.0
                m1[m + 32, m] = -1.0
            else:
                m2[m - 32, m] = 1.0
                m2[m, m] = 1.0
        c["m1"] = m1
        c["m2"] = m2
    return c


def rope_perm():
    perm = np.zeros(DM, np.int64)
    for h in range(NH):
        base = h * 64
        perm[base + 0:base + 32] = base + np.arange(0, 64, 2)
        perm[base + 32:base + 64] = base + np.arange(1, 64, 2)
    return perm


def kt_range(variant, qc):
    if variant == "l0":
        lo = max(0, 4 * qc - 1)
    elif variant == "l1attn":
        lo = max(0, (qc * 512 - 511) // 128)
    else:
        lo = 0
    return list(range(lo, 4 * qc + 4))


def emit_ln_full(P, ctx, x_hbm, XN, rms_only=False, eps=1e-5, gfold=None):
    XB8, XSQ = ctx["XB8"], ctx["XSQ"]
    AROW, BROW, TROW = ctx["AROW"], ctx["BROW"], ctx["TROW"]
    ps_s1, ps_s2, ps_a, ps_b = ctx["ps_s1"], ctx["ps_s2"], ctx["ps_a"], ctx["ps_b"]
    ones_colbf, ones32 = ctx["ones_colbf"], ctx["ones32"]
    AB8 = ctx["AB8"]
    for c in range(NTC):
        rs = slice(0, TC)
        rkey = 0
        P.dma("g", lambda e, c=c: e.dma_start(
                out=XB8[:, :].rearrange("p (j t) -> p j t", j=NDT),
                in_=x_hbm[:, c * TC:(c + 1) * TC].rearrange("(j p) t -> p j t", p=128)),
              writes=("xb8",))
        if not rms_only:
            for j in range(NDT):
                P.t(lambda e, j=j: e.matmul(ps_s1[0:1, :], ones_colbf[:, 0:1],
                                            XB8[:, j * TC:(j + 1) * TC],
                                            start=(j == 0), stop=(j == NDT - 1)),
                    reads=("ones_colbf", "xb8"), writes=("ps_s1",))
        for j in range(NDT):
            slot = j % 2
            bcol = slice(slot * TC, (slot + 1) * TC)
            P.s(lambda e, j=j, bcol=bcol: e.activation(XSQ[:, bcol], XB8[:, j * TC:(j + 1) * TC],
                                                       AF.Square),
                reads=("xb8",), writes=("xsq{}".format(slot),))
            P.t(lambda e, bcol=bcol, j=j: e.matmul(ps_s2[0:1, :], ones_colbf[:, 0:1], XSQ[:, bcol],
                                                   start=(j == 0), stop=(j == NDT - 1)),
                reads=("ones_colbf", "xsq{}".format(slot)), writes=("ps_s2",))
        if not rms_only:
            P.v(lambda e, rs=rs: e.tensor_scalar(BROW[0:1, rs], ps_s1[0:1, :], -1.0 / DM, None, ALU.mult),
                reads=("ps_s1",), writes=("brow0",))
            P.v(lambda e, rs=rs: e.tensor_scalar(AROW[0:1, rs], ps_s2[0:1, :], 1.0 / DM, None, ALU.mult),
                reads=("ps_s2",), writes=("arow0",))
            P.v(lambda e, rs=rs: e.tensor_tensor(TROW[0:1, 0:TC], BROW[0:1, rs], BROW[0:1, rs], ALU.mult),
                reads=("brow0",), writes=("trow",))
            P.v(lambda e, rs=rs: e.scalar_tensor_tensor(AROW[0:1, rs], AROW[0:1, rs], float(eps),
                                                        TROW[0:1, 0:TC], ALU.add, ALU.subtract),
                reads=("arow0", "trow"), writes=("arow0",))
        else:
            P.v(lambda e, rs=rs: e.tensor_scalar(AROW[0:1, rs], ps_s2[0:1, :], 1.0 / DM, float(eps),
                                                 ALU.mult, ALU.add),
                reads=("ps_s2",), writes=("arow0",))
        P.s(lambda e, rs=rs: e.activation(TROW[0:1, 0:TC], AROW[0:1, rs], AF.Ln),
            reads=("arow0",), writes=("trow",))
        P.s(lambda e, rs=rs: e.activation(AROW[0:1, rs], TROW[0:1, 0:TC], AF.Exp, scale=-0.5),
            reads=("trow",), writes=("arow0",))
        P.t(lambda e, rs=rs: e.matmul(ps_a[:, :], ones32[0:1, :], AROW[0:1, rs], start=True, stop=True),
            reads=("ones32", "arow0"), writes=("ps_a",))
        P.v(lambda e: e.tensor_copy(AB8[:, 0:TC], ps_a[:, :]), reads=("ps_a",), writes=("ab8a",))
        if not rms_only:
            P.t(lambda e, rs=rs: e.matmul(ps_b[:, :], ones32[0:1, :], BROW[0:1, rs], start=True, stop=True),
                reads=("ones32", "brow0"), writes=("ps_b",))
            P.v(lambda e: e.tensor_copy(AB8[:, TC:2 * TC], ps_b[:, :]), reads=("ps_b",), writes=("ab8b",))
        for j in range(NDT):
            dst = slice(j * T + c * TC, j * T + (c + 1) * TC)
            xsl = slice(j * TC, (j + 1) * TC)
            if rms_only:
                P.v(lambda e, xsl=xsl, dst=dst: e.tensor_tensor(XN[:, dst], XB8[:, xsl],
                                                                AB8[:, 0:TC], ALU.mult),
                    reads=("xb8", "ab8a"), writes=(f"xn{j}_{c}",))
            else:
                slot = j % 2
                tsl = slice(slot * TC, (slot + 1) * TC)
                P.v(lambda e, xsl=xsl, tsl=tsl: e.tensor_tensor(ctx["XBT"][:, tsl], XB8[:, xsl],
                                                                AB8[:, TC:2 * TC], ALU.add),
                    reads=("xb8", "ab8b"), writes=("xbt{}".format(slot),))
                P.v(lambda e, tsl=tsl, dst=dst: e.tensor_tensor(XN[:, dst], ctx["XBT"][:, tsl],
                                                                AB8[:, 0:TC], ALU.mult),
                    reads=("xbt{}".format(slot), "ab8a"), writes=(f"xn{j}_{c}",))


def emit_rope(P, OUT, otag, RU, RW, M1, M2, TMPE, ps, pskey):
    for jt in range(NDT):
        for c in range(NTC):
            cs = slice(jt * T + c * TC, jt * T + (c + 1) * TC)
            tcs = slice(c * TC, (c + 1) * TC)
            P.v(lambda e, cs=cs, tcs=tcs: e.tensor_tensor(
                    TMPE[:, 0:TC], OUT[:, cs], RU[:, tcs], ALU.mult),
                reads=(otag + str(jt), "ropeu"), writes=("tmpe_a",))
            P.v(lambda e, cs=cs, tcs=tcs: e.tensor_tensor(
                    TMPE[:, TC:2 * TC], OUT[:, cs], RW[:, tcs], ALU.mult),
                reads=(otag + str(jt), "ropew"), writes=("tmpe_b",))
            P.t(lambda e: e.matmul(ps[:, :], M1[:, :], TMPE[:, 0:TC], start=True, stop=False),
                reads=("ropem1", "tmpe_a"), writes=(pskey,))
            P.t(lambda e: e.matmul(ps[:, :], M2[:, :], TMPE[:, TC:2 * TC], start=False, stop=True),
                reads=("ropem2", "tmpe_b"), writes=(pskey,))
            P.v(lambda e, cs=cs: e.tensor_copy(OUT[:, cs], ps[:, :]),
                reads=(pskey,), writes=(otag + str(jt),))


def emit_qkln(P, ctx, OUT, otag, GB, gcol, bcol, psel2, bself, SSC, TMPC2, pp, eps=1e-5):
    ps_q = pp[7]
    ps_r = pp[2]
    for jt in range(NDT):
        for c in range(NTC):
            cs = slice(jt * T + c * TC, jt * T + (c + 1) * TC)
            P.t(lambda e, cs=cs: e.matmul(ps_q[:, :], psel2[:, :], OUT[:, cs], start=True, stop=True),
                reads=("psel2", otag + str(jt)), writes=("ps_x",))
            P.v(lambda e: e.tensor_scalar(SSC[:, 0:TC], ps_q[:, :], 1.0 / HD, None, ALU.mult),
                reads=("ps_x",), writes=("sscA",))
            P.s(lambda e, cs=cs: e.activation(ctx["XSQ"][:, 0:TC], OUT[:, cs], AF.Square),
                reads=(otag + str(jt),), writes=("xsq0",))
            P.t(lambda e: e.matmul(ps_q[:, :], psel2[:, :], ctx["XSQ"][:, 0:TC], start=True, stop=True),
                reads=("psel2", "xsq0"), writes=("ps_x",))
            P.v(lambda e: e.tensor_tensor(SSC[:, TC:2 * TC], SSC[:, 0:TC], SSC[:, 0:TC], ALU.mult),
                reads=("sscA",), writes=("sscB",))
            P.v(lambda e: e.scalar_tensor_tensor(TMPC2[:, 0:TC], ps_q[:, :], 1.0 / HD,
                                                 SSC[:, TC:2 * TC], ALU.mult, ALU.subtract),
                reads=("ps_x", "sscB"), writes=("tmpdA",))
            P.v(lambda e: e.tensor_scalar(TMPC2[:, 0:TC], TMPC2[:, 0:TC], float(eps), 1e-37,
                                          ALU.add, ALU.max),
                reads=("tmpdA",), writes=("tmpdA",))
            P.s(lambda e: e.activation(TMPC2[:, TC:2 * TC], TMPC2[:, 0:TC], AF.Ln),
                reads=("tmpdA",), writes=("tmpdB",))
            P.s(lambda e: e.activation(TMPC2[:, 0:TC], TMPC2[:, TC:2 * TC], AF.Exp, scale=-0.5),
                reads=("tmpdB",), writes=("tmpdA",))
            P.t(lambda e: e.matmul(ps_r[:, :], bself[:, :], SSC[:, 0:TC], start=True, stop=True),
                reads=("bself", "sscA"), writes=("ps_a",))
            P.v(lambda e: e.tensor_copy(SSC[:, 0:TC], ps_r[:, :]),
                reads=("ps_a",), writes=("sscA",))
            P.t(lambda e: e.matmul(ps_r[:, :], bself[:, :], TMPC2[:, 0:TC], start=True, stop=True),
                reads=("bself", "tmpdA"), writes=("ps_a",))
            P.v(lambda e, cs=cs: e.tensor_tensor(TMPC2[:, TC:2 * TC], OUT[:, cs], SSC[:, 0:TC],
                                                 ALU.subtract),
                reads=(otag + str(jt), "sscA"), writes=("tmpdB",))
            P.v(lambda e: e.tensor_tensor(TMPC2[:, TC:2 * TC], TMPC2[:, TC:2 * TC], ps_r[:, :],
                                          ALU.mult),
                reads=("tmpdB", "ps_a"), writes=("tmpdB",))
            P.v(lambda e, cs=cs: e.tensor_scalar(OUT[:, cs], TMPC2[:, TC:2 * TC],
                                                 GB[:, gcol:gcol + 1], GB[:, bcol:bcol + 1],
                                                 ALU.mult, ALU.add),
                reads=("tmpdB", "gb"), writes=(otag + str(jt),))


def emit_ln_to_hbm(P, ctx, x_hbm, z_hbm, eps=1e-5):
    XB8, XSQ = ctx["XB8"], ctx["XSQ"]
    AROW, BROW, TROW = ctx["AROW"], ctx["BROW"], ctx["TROW"]
    ps_s1, ps_s2, ps_a, ps_b = ctx["ps_s1"], ctx["ps_s2"], ctx["ps_a"], ctx["ps_b"]
    ones_colbf, ones32 = ctx["ones_colbf"], ctx["ones32"]
    XC8, TMPC = ctx["XC8"], ctx["TMPC"]
    for c in range(NTC):
        rs = slice(0, TC)
        P.dma("g", lambda e, c=c: e.dma_start(
                out=XB8[:, :].rearrange("p (j t) -> p j t", j=NDT),
                in_=x_hbm[:, c * TC:(c + 1) * TC].rearrange("(j p) t -> p j t", p=128)),
              writes=("xb8",))

        for j in range(NDT):
            P.t(lambda e, j=j: e.matmul(ps_s1[0:1, :], ones_colbf[:, 0:1],
                                        XB8[:, j * TC:(j + 1) * TC],
                                        start=(j == 0), stop=(j == NDT - 1)),
                reads=("ones_colbf", "xb8"), writes=("ps_s1",))
        for j in range(NDT):
            slot = j % 2
            bcol = slice(slot * TC, (slot + 1) * TC)
            P.s(lambda e, j=j, bcol=bcol: e.activation(XSQ[:, bcol], XB8[:, j * TC:(j + 1) * TC],
                                                       AF.Square),
                reads=("xb8",), writes=("xsq{}".format(slot),))
            P.t(lambda e, bcol=bcol, j=j: e.matmul(ps_s2[0:1, :], ones_colbf[:, 0:1], XSQ[:, bcol],
                                                   start=(j == 0), stop=(j == NDT - 1)),
                reads=("ones_colbf", "xsq{}".format(slot)), writes=("ps_s2",))
        P.v(lambda e, rs=rs: e.tensor_scalar(BROW[0:1, rs], ps_s1[0:1, :], -1.0 / DM, None, ALU.mult),
            reads=("ps_s1",), writes=("brow0",))
        P.v(lambda e, rs=rs: e.tensor_scalar(AROW[0:1, rs], ps_s2[0:1, :], 1.0 / DM, None, ALU.mult),
            reads=("ps_s2",), writes=("arow0",))
        P.v(lambda e, rs=rs: e.tensor_tensor(TROW[0:1, 0:TC], BROW[0:1, rs], BROW[0:1, rs], ALU.mult),
            reads=("brow0",), writes=("trow",))
        P.v(lambda e, rs=rs: e.scalar_tensor_tensor(AROW[0:1, rs], AROW[0:1, rs], float(eps),
                                                    TROW[0:1, 0:TC], ALU.add, ALU.subtract),
            reads=("arow0", "trow"), writes=("arow0",))
        P.s(lambda e, rs=rs: e.activation(TROW[0:1, 0:TC], AROW[0:1, rs], AF.Ln),
            reads=("arow0",), writes=("trow",))
        P.s(lambda e, rs=rs: e.activation(AROW[0:1, rs], TROW[0:1, 0:TC], AF.Exp, scale=-0.5),
            reads=("trow",), writes=("arow0",))
        P.t(lambda e, rs=rs: e.matmul(ps_a[:, :], ones32[0:1, :], AROW[0:1, rs], start=True, stop=True),
            reads=("ones32", "arow0"), writes=("ps_a",))
        P.t(lambda e, rs=rs: e.matmul(ps_b[:, :], ones32[0:1, :], BROW[0:1, rs], start=True, stop=True),
            reads=("ones32", "brow0"), writes=("ps_b",))
        for jh in range(2):
            P.dma("y", lambda e, c=c, jh=jh: e.dma_start(
                    out=XC8[:, :].rearrange("p (j t) -> p j t", j=4),
                    in_=x_hbm[jh * 512:(jh + 1) * 512, c * TC:(c + 1) * TC].rearrange(
                        "(j p) t -> p j t", p=128)),
                  writes=("xc8",))
            for jj in range(4):
                slot = jj % 2
                tsl = slice(slot * TC, (slot + 1) * TC)
                xsl = slice(jj * TC, (jj + 1) * TC)
                P.v(lambda e, xsl=xsl, tsl=tsl: e.tensor_tensor(TMPC[:, tsl], XC8[:, xsl],
                                                                ps_b[:, :], ALU.add),
                    reads=("xc8", "ps_b"), writes=("tmpc{}".format(slot),))
                P.v(lambda e, xsl=xsl, tsl=tsl: e.tensor_tensor(XC8[:, xsl], TMPC[:, tsl],
                                                                ps_a[:, :], ALU.mult),
                    reads=("tmpc{}".format(slot), "ps_a"), writes=("xc8",))
            P.dma("y", lambda e, c=c, jh=jh: e.dma_start(
                    out=z_hbm[jh * 512:(jh + 1) * 512, c * TC:(c + 1) * TC].rearrange(
                        "(j p) t -> p j t", p=128),
                    in_=XC8[:, :].rearrange("p (j t) -> p j t", j=4)),
                  reads=("xc8",), writes=("z{}".format(c),))


def plan_attn(P, nc, E, io, variant):
    mwidth = {"l0": 1152, "l1attn": 1536, "l1mha": 1024}[variant]
    if variant == "l1mha":
        rope_in = {nm: io[nm] for nm in ("ru", "rw")}
        ropem_in = {nm: io[nm] for nm in ("m1", "m2")}
        g1_in = io["g1_in"]
        zbuf = io["zbuf"]
    if variant == "l1attn":
        gb_in = io["gb_in"]
    x_in = io.get("x_in")
    x_out = io.get("x_out")
    wpack = io.get("wpack")
    bpack = io.get("bpack")
    bvrow = io.get("bvrow")
    minus1_in = io.get("minus1_in")
    psel2_in = io.get("psel2_in")
    bsel_in = io.get("bsel_in")
    ident_in = io.get("ident_in")
    hostm_in = io.get("hostm_in")
    g1_in = io.get("g1_in")
    zbuf = io.get("zbuf")
    gb_in = io.get("gb_in")
    XN = E(nc.sbuf_tensor("XN", [128, NDT * T], BF16))
    QT = E(nc.sbuf_tensor("QT", [128, NDT * T], BF16))
    KT = E(nc.sbuf_tensor("KT", [128, NDT * T], BF16))
    VS = E(nc.sbuf_tensor("VS", [128, 16 * (NH * 65)], BF16))
    WB = E(nc.sbuf_tensor("WB", [128, NDT * DM], BF16))
    ATT = E(nc.sbuf_tensor("ATT", [128, NDT * TC], BF16))
    PT = E(nc.sbuf_tensor("PT", [128, 2 * TC], BF16))
    RECB = E(nc.sbuf_tensor("RECB", [128, TC], F32))
    SUM2 = E(nc.sbuf_tensor("SUM2", [128, TC], F32))
    NBT = E(nc.sbuf_tensor("NBT", [128, TC], BF16))
    KMX = E(nc.sbuf_tensor("KMX", [128, 16], F32))
    XB8 = E(nc.sbuf_tensor("XB8", [128, NDT * TC], BF16))
    AB8 = E(nc.sbuf_tensor("AB8", [128, 2 * TC], BF16))
    XBT = E(nc.sbuf_tensor("XBT", [128, 2 * TC], BF16))
    XSQ = E(nc.sbuf_tensor("XSQ", [128, 2 * TC], BF16))
    QSQ = XSQ
    if variant != "l1mha":
        XC = E(nc.sbuf_tensor("XC", [128, 2 * TC], F32))
        XR = E(nc.sbuf_tensor("XR", [128, 2 * TC], F32))
    else:
        XC = XR = None
    TMPC = E(nc.sbuf_tensor("TMPC", [128, 2 * TC], F32))
    TMPD = XB8[:, :].bitcast(F32)
    AROW = E(nc.sbuf_tensor("AROW", [1, TC], F32))
    BROW = E(nc.sbuf_tensor("BROW", [1, TC], F32))
    TROW = E(nc.sbuf_tensor("TROW", [1, TC], F32))
    b_sb = E(nc.sbuf_tensor("b_sb", [128, 32], F32))
    bv_sb = E(nc.sbuf_tensor("bv_sb", [1, DM], BF16))
    minus1 = E(nc.sbuf_tensor("minus1", [1, 128], BF16))
    psel2 = E(nc.sbuf_tensor("psel2", [128, 128], BF16))
    bself = E(nc.sbuf_tensor("bself", [128, 128], F32))
    IDENT = E(nc.sbuf_tensor("IDENT", [128, 128], BF16))
    ones_colbf = E(nc.sbuf_tensor("ones_colbf", [128, 1], BF16))
    onesrow = E(nc.sbuf_tensor("onesrow", [1, 128], BF16))
    ones32 = E(nc.sbuf_tensor("ones32", [1, 128], F32))
    HOSTM = E(nc.sbuf_tensor("HOSTM", [128, mwidth], BF16))
    if variant == "l1mha":
        XC8 = E(nc.sbuf_tensor("XC8", [128, 4 * TC], F32))
        ROPE = {nm: E(nc.sbuf_tensor("R" + nm, [128, T], BF16)) for nm in ("ru", "rw")}
        ROPEM = {nm: E(nc.sbuf_tensor("RM" + nm, [128, 128], BF16)) for nm in ("m1", "m2")}
        TMPE = AB8
        G1 = E(nc.sbuf_tensor("G1", [128, 8], F32))
    if variant == "l1attn":
        GB = E(nc.sbuf_tensor("GB", [128, 4], F32))
        SSC = TMPC

    pp = [E(nc.psum_tensor("pp{}".format(i), [128, TC], F32)) for i in range(8)]
    ps_s1, ps_s2, ps_a, ps_b = pp[0], pp[1], pp[2], pp[3]
    ps_st = [pp[4], pp[5]]
    ps_av = pp[6]
    ps_x = pp[7]

    ctx = dict(XN=XN, XB8=XB8, AB8=AB8, XBT=XBT, XSQ=XSQ, TMPC=TMPC,
               AROW=AROW, BROW=BROW, TROW=TROW,
               ones_colbf=ones_colbf, ones32=ones32,
               ps_s1=ps_s1, ps_s2=ps_s2, ps_a=ps_a, ps_b=ps_b)
    if variant == "l1mha":
        ctx["XC8"] = XC8

    P.g(lambda e: e.memset(ones_colbf[:, :], 1.0), writes=("ones_colbf",))
    P.g(lambda e: e.memset(ones32[:, :], 1.0), writes=("ones32",))
    P.g(lambda e: e.memset(onesrow[:, :], 1.0), writes=("onesrow",))
    P.dma("g", lambda e: e.dma_start(out=b_sb[:, :], in_=bpack[:, :]), writes=("b_sb",))
    P.dma("g", lambda e: e.dma_start(out=bv_sb[:, :], in_=bvrow[:, :]), writes=("bv_sb",))
    P.dma("g", lambda e: e.dma_start(out=minus1[:, :], in_=minus1_in[:, :]), writes=("minus1",))
    P.dma("g", lambda e: e.dma_start(out=psel2[:, :], in_=psel2_in[:, :]), writes=("psel2",))
    P.dma("y", lambda e: e.dma_start(out=bself[:, :], in_=bsel_in[:, :]), writes=("bself",))
    P.dma("g", lambda e: e.dma_start(out=IDENT[:, :], in_=ident_in[:, :]), writes=("ident",))
    P.dma("g", lambda e: e.dma_start(out=HOSTM[:, :], in_=hostm_in[:, :]), writes=("hostm",))
    if variant == "l1mha":
        for nm in ("ru", "rw"):
            P.dma("g", lambda e, nm=nm: e.dma_start(out=ROPE[nm][:, :], in_=rope_in[nm][:, :]),
                  writes=("rope" + nm[1],))
        for nm in ("m1", "m2"):
            P.dma("g", lambda e, nm=nm: e.dma_start(out=ROPEM[nm][:, :], in_=ropem_in[nm][:, :]),
                  writes=("rope" + nm,))
        P.dma("y", lambda e: e.dma_start(out=G1[:, :], in_=g1_in[:, :]), writes=("g1",))
    if variant == "l1attn":
        P.dma("y", lambda e: e.dma_start(out=GB[:, :], in_=gb_in[:, :]), writes=("gb",))
        P.v(lambda e: e.tensor_scalar(GB[:, 0:2], GB[:, 0:2], 0.125, None, ALU.mult),
            reads=("gb",), writes=("gb",))

    if variant == "l1mha":
        emit_ln_to_hbm(P, ctx, x_in, zbuf)
        base_hbm = zbuf
        base_dep = lambda c, j: ("z{}".format(c),)
        emit_ln_full(P, ctx, zbuf, XN, rms_only=True)
    else:
        base_hbm = x_in
        base_dep = lambda c, j: ()
        emit_ln_full(P, ctx, x_in, XN, rms_only=False)

    def load_w(part, fold_g1=False):
        P.dma("g", lambda e, part=part: e.dma_start(
                out=WB[:, :].rearrange("p (j c) -> p j c", j=NDT),
                in_=wpack[:, part * DM:(part + 1) * DM].rearrange("(j p) c -> p j c", p=128)),
              writes=tuple("wb{}".format(j) for j in range(NDT)))
        if fold_g1:
            for j in range(NDT):
                P.v(lambda e, j=j: e.tensor_scalar(WB[:, j * DM:(j + 1) * DM],
                                                   WB[:, j * DM:(j + 1) * DM],
                                                   G1[:, j:j + 1], None, ALU.mult),
                    reads=("wb{}".format(j), "g1"), writes=("wb{}".format(j),))

    fold = variant == "l1mha"

    def qkv_feat(part, OUT, otag, bias_col, scale):
        for qo in range(NDT):
            for c in range(NTC):
                gi = qo * NTC + c
                ps = [ps_x, ps_st[0]][gi % 2]
                pk = ["ps_x", "ps_st0"][gi % 2]
                for j in range(NDT):
                    P.t(lambda e, qo=qo, c=c, j=j, ps=ps: e.matmul(
                            ps[:, :], WB[:, j * DM + qo * 128:(j * DM + (qo + 1) * 128)],
                            XN[:, j * T + c * TC: j * T + (c + 1) * TC],
                            start=(j == 0), stop=(j == NDT - 1)),
                        reads=("wb{}".format(j), "xn{}_{}".format(j, c)), writes=(pk,))
                if bias_col is None:
                    P.v(lambda e, qo=qo, c=c, ps=ps: e.tensor_scalar(
                            OUT[:, qo * T + c * TC: qo * T + (c + 1) * TC], ps[:, :],
                            float(scale), None, ALU.mult),
                        reads=(pk,), writes=(otag + str(qo),))
                else:
                    P.v(lambda e, qo=qo, c=c, ps=ps: e.tensor_scalar(
                            OUT[:, qo * T + c * TC: qo * T + (c + 1) * TC], ps[:, :],
                            b_sb[:, bias_col + qo:bias_col + qo + 1], float(scale),
                            ALU.add, ALU.mult),
                        reads=(pk, "b_sb"), writes=(otag + str(qo),))

    has_bias = variant != "l1mha"
    load_w(0, fold)
    qkv_feat(0, QT, "qt", 0 if has_bias else None,
             1.0 if variant == "l1attn" else 0.125)
    load_w(1, fold)
    qkv_feat(1, KT, "kt", 8 if has_bias else None, 1.0)
    load_w(2, fold)
    P.g(lambda e: e.memset(VS[:, :], 1.0), writes=("vsmem",))
    for tt in range(16):
        for nchunk in range(2):
            ps = ps_st[nchunk]
            for j in range(NDT):
                P.t(lambda e, tt=tt, nchunk=nchunk, j=j, ps=ps: e.matmul(
                        ps[:, :], XN[:, j * T + tt * 128: j * T + (tt + 1) * 128],
                        WB[:, j * DM + nchunk * TC: j * DM + (nchunk + 1) * TC],
                        start=(j == 0), stop=False),
                    reads=("wb{}".format(j), "xn{}_{}".format(j, tt // 4)),
                    writes=("ps_st{}".format(nchunk),))
            P.t(lambda e, nchunk=nchunk, ps=ps: e.matmul(
                    ps[:, :], onesrow[0:1, :], bv_sb[0:1, nchunk * TC:(nchunk + 1) * TC],
                    start=False, stop=True),
                reads=("onesrow", "bv_sb"), writes=("ps_st{}".format(nchunk),))
            for hh in range(8):
                h = nchunk * 8 + hh
                P.v(lambda e, tt=tt, hh=hh, h=h, ps=ps: e.tensor_copy(
                        VS[:, tt * (NH * 65) + h * 65: tt * (NH * 65) + h * 65 + 64],
                        ps[:, hh * 64:(hh + 1) * 64]),
                    reads=("ps_st{}".format(nchunk), "vsmem"), writes=("vs{}_{}".format(tt, h),))

    if variant == "l1mha":
        emit_rope(P, QT, "qt", ROPE["ru"], ROPE["rw"], ROPEM["m1"], ROPEM["m2"], TMPE, ps_x, "ps_x")
        emit_rope(P, KT, "kt", ROPE["ru"], ROPE["rw"], ROPEM["m1"], ROPEM["m2"], TMPE, ps_x, "ps_x")
    if variant == "l1attn":
        emit_qkln(P, ctx, QT, "qt", GB, 0, 1, psel2, bself, SSC, TMPD, pp)
        emit_qkln(P, ctx, KT, "kt", GB, 2, 3, psel2, bself, SSC, TMPD, pp)

    for j in range(NDT):
        for c in range(NTC):
            sl = slice((c % 2) * TC, (c % 2 + 1) * TC)
            P.s(lambda e, j=j, c=c, sl=sl: e.activation(
                    QSQ[:, sl], KT[:, j * T + c * TC: j * T + (c + 1) * TC], AF.Square),
                reads=("kt{}".format(j),), writes=("qsq{}".format(c % 2),))
            P.t(lambda e, j=j, c=c, sl=sl: e.matmul(ps_x[:, :], psel2[:, :], QSQ[:, sl],
                                                    start=True, stop=True),
                reads=("psel2", "qsq{}".format(c % 2)), writes=("ps_x",))
            P.v(lambda e: e.tensor_reduce(TMPD[:, 0:1], ps_x[:, :], AX.X, ALU.max),
                reads=("ps_x",), writes=("tmpdA",))
            if c == 0:
                P.v(lambda e, j=j: e.tensor_copy(KMX[:, j:j + 1], TMPD[:, 0:1]),
                    reads=("tmpdA",), writes=("kmx",))
            else:
                P.v(lambda e, j=j: e.tensor_tensor(KMX[:, j:j + 1], KMX[:, j:j + 1],
                                                   TMPD[:, 0:1], ALU.max),
                    reads=("kmx", "tmpdA"), writes=("kmx",))
    P.s(lambda e: e.activation(KMX[:, 8:16], KMX[:, 0:8], AF.Ln),
        reads=("kmx",), writes=("lkb",))
    P.v(lambda e: e.tensor_scalar(KMX[:, 8:16], KMX[:, 8:16], 0.5, None, ALU.mult),
        reads=("lkb",), writes=("lkb",))

    load_w(3, False)
    umin = -511
    for qc in range(NTC):
        for jt in range(NDT):
            sl = slice((jt % 2) * TC, (jt % 2 + 1) * TC)
            P.s(lambda e, jt=jt, qc=qc, sl=sl: e.activation(
                    QSQ[:, sl], QT[:, jt * T + qc * TC: jt * T + (qc + 1) * TC], AF.Square),
                reads=("qt{}".format(jt),), writes=("qsq{}".format(jt % 2),))
            P.t(lambda e, jt=jt, sl=sl: e.matmul(ps_x[:, :], psel2[:, :], QSQ[:, sl],
                                                 start=True, stop=True),
                reads=("psel2", "qsq{}".format(jt % 2)), writes=("ps_x",))
            P.s(lambda e: e.activation(TMPD[:, 0:TC], ps_x[:, :], AF.Ln),
                reads=("ps_x",), writes=("tmpdA",))
            P.s(lambda e, jt=jt: e.activation(NBT[:, 0:TC], TMPD[:, 0:TC], AF.Exp, scale=0.5,
                                              bias=KMX[:, 8 + jt:9 + jt]),
                reads=("tmpdA", "lkb"), writes=("nbt",))
            P.v(lambda e: e.tensor_copy(AB8[0:1, TC:2 * TC], NBT[64:65, 0:TC]),
                reads=("nbt",), writes=("nbto",))
            for hp in range(2):
                h = jt * 2 + hp
                kts = kt_range(variant, qc)

                def emit_av(kt, first, last, h=h):
                    P.t(lambda e, h=h, kt=kt, first=first, last=last: e.matmul(
                            ps_av[0:65, :],
                            VS[:, kt * (NH * 65) + h * 65: kt * (NH * 65) + h * 65 + 65],
                            PT[:, (kt % 2) * TC:(kt % 2 + 1) * TC],
                            start=first, stop=last),
                        reads=("vs{}_{}".format(kt, h), "pt{}".format(kt % 2)), writes=("ps_av",))

                pend = None
                for kt in kts:
                    psl = ps_st[kt % 2]
                    pskey = "ps_st{}".format(kt % 2)
                    P.t(lambda e, jt=jt, hp=hp, qc=qc, kt=kt, psl=psl: e.matmul(
                            psl[:, :],
                            KT[hp * 64:(hp + 1) * 64, jt * T + kt * 128: jt * T + (kt + 1) * 128],
                            QT[hp * 64:(hp + 1) * 64, jt * T + qc * TC: jt * T + (qc + 1) * TC],
                            start=True, stop=False),
                        reads=("kt{}".format(jt), "qt{}".format(jt)), writes=(pskey,))
                    if variant in ("l0", "l1attn") or kt >= 4 * qc:
                        off = qc * 512 - kt * 128 - umin
                        P.t(lambda e, off=off, psl=psl: e.matmul(
                                psl[:, :], IDENT[:, :], HOSTM[:, off:off + TC],
                                start=False, stop=False),
                            reads=("ident", "hostm"), writes=(pskey,))
                    nbr = NBT[0:1, 0:TC] if hp == 0 else AB8[0:1, TC:2 * TC]
                    nbk = "nbt" if hp == 0 else "nbto"
                    P.t(lambda e, psl=psl, nbr=nbr: e.matmul(
                            psl[:, :], minus1[0:1, :], nbr,
                            start=False, stop=True),
                        reads=("minus1", nbk), writes=(pskey,))
                    P.s(lambda e, kt=kt, psl=psl: e.activation(
                            PT[:, (kt % 2) * TC:(kt % 2 + 1) * TC], psl[:, :], AF.Exp),
                        reads=(pskey,), writes=("pt{}".format(kt % 2),))
                    if pend is not None:
                        emit_av(pend, pend == kts[0], False)
                    pend = kt
                emit_av(pend, pend == kts[0], True)
                P.v(lambda e, hp=hp: e.tensor_copy(SUM2[hp * 64:hp * 64 + 1, :], ps_av[64:65, :]),
                    reads=("ps_av",), writes=("sum2_{}".format(hp),))
                P.v(lambda e, hp=hp: e.tensor_copy(TMPC[hp * 64:(hp + 1) * 64, 0:TC],
                                                   ps_av[0:64, :]),
                    reads=("ps_av",), writes=("tmpcA",))
            P.v(lambda e: e.tensor_scalar(RECB[:, :], SUM2[:, :], 1e-37, None, ALU.max),
                reads=("sum2_0", "sum2_1"), writes=("recb",))
            P.s(lambda e: e.activation(TMPD[:, TC:2 * TC], RECB[:, :], AF.Ln),
                reads=("recb",), writes=("tmpdB",))
            P.s(lambda e: e.activation(SUM2[:, :], TMPD[:, TC:2 * TC], AF.Exp, scale=-1.0),
                reads=("tmpdB",), writes=("sum2_0", "sum2_1"))
            P.t(lambda e: e.matmul(ps_x[:, :], bself[:, :], SUM2[:, :], start=True, stop=True),
                reads=("bself", "sum2_0", "sum2_1"), writes=("ps_x",))
            P.v(lambda e: e.tensor_copy(RECB[:, :], ps_x[:, :]),
                reads=("ps_x",), writes=("recb",))
            P.v(lambda e, jt=jt: e.tensor_tensor(ATT[:, jt * TC:(jt + 1) * TC],
                                                 TMPC[:, 0:TC], RECB[:, :], ALU.mult),
                reads=("tmpcA", "recb"), writes=("att{}".format(jt),))
        for j in range(NDT):
            for ct in range(NDT):
                P.t(lambda e, j=j, ct=ct: e.matmul(
                        ps_x[:, :], WB[:, ct * DM + j * 128: ct * DM + (j + 1) * 128],
                        ATT[:, ct * TC:(ct + 1) * TC],
                        start=(ct == 0), stop=(ct == NDT - 1)),
                    reads=("wb{}".format(ct), "att{}".format(ct)), writes=("ps_x",))
            if variant == "l1mha":
                if j % 4 == 0:
                    jh = j // 4
                    P.dma("y", lambda e, qc=qc, jh=jh: e.dma_start(
                            out=XC8[:, :].rearrange("p (jj t) -> p jj t", jj=4),
                            in_=base_hbm[jh * 512:(jh + 1) * 512,
                                         qc * TC:(qc + 1) * TC].rearrange(
                                "(jj p) t -> p jj t", p=128)),
                          reads=base_dep(qc, 0), writes=("xc8",))
                P.v(lambda e, j=j: e.scalar_tensor_tensor(
                        TMPC[:, TC:2 * TC], ps_x[:, :], b_sb[:, 24 + j:25 + j],
                        XC8[:, (j % 4) * TC:(j % 4 + 1) * TC], ALU.add, ALU.add),
                    reads=("ps_x", "b_sb", "xc8"), writes=("tmpc1",))
                P.dma("y", lambda e, j=j, qc=qc: e.dma_start(
                        out=x_out[j * 128:(j + 1) * 128, qc * TC:(qc + 1) * TC],
                        in_=TMPC[:, TC:2 * TC]),
                      reads=("tmpc1",), writes=("out:x{}_{}".format(qc, j),))
            else:
                xslot = j % 2
                xsl = slice(xslot * TC, (xslot + 1) * TC)
                P.dma("y", lambda e, j=j, qc=qc, xsl=xsl: e.dma_start(
                        out=XR[:, xsl], in_=base_hbm[j * 128:(j + 1) * 128, qc * TC:(qc + 1) * TC]),
                      reads=base_dep(qc, j), writes=("xr{}".format(xslot),))
                P.v(lambda e, j=j, xsl=xsl: e.scalar_tensor_tensor(
                        XC[:, xsl], ps_x[:, :], b_sb[:, 24 + j:25 + j], XR[:, xsl], ALU.add, ALU.add),
                    reads=("ps_x", "b_sb", "xr{}".format(xslot)), writes=("xc{}".format(xslot),))
                P.dma("y", lambda e, j=j, qc=qc, xsl=xsl: e.dma_start(
                        out=x_out[j * 128:(j + 1) * 128, qc * TC:(qc + 1) * TC], in_=XC[:, xsl]),
                      reads=("xc{}".format(xslot),), writes=("out:x{}_{}".format(qc, j),))

    keys = ["out:x{}_{}".format(c, j) for c in range(NTC) for j in range(NDT)]
    if ATTN_DEBUG == "avx":
        P.v(lambda e: e.tensor_copy(XR[:, 0:TC], TMPC[:, 0:TC]),
            reads=("tmpcA",) + tuple(keys), writes=("xr0",))
        P.dma("y", lambda e: e.dma_start(out=x_out[0:128, 3 * TC:4 * TC], in_=XR[:, 0:TC]),
              reads=("xr0",), writes=("dbgx:0",))
        P.v(lambda e: e.tensor_copy(XR[:, TC:2 * TC], SUM2[:, :]),
            reads=("sum2_0", "sum2_1") + tuple(keys), writes=("xr1",))
        P.dma("y", lambda e: e.dma_start(out=x_out[128:256, 3 * TC:4 * TC], in_=XR[:, TC:2 * TC]),
              reads=("xr1",), writes=("dbgx:1",))
        P.v(lambda e: e.tensor_copy(XR[:, 0:TC], PT[:, TC:2 * TC]),
            reads=("pt1",) + tuple(keys), writes=("xr0",))
        P.dma("y", lambda e: e.dma_start(out=x_out[256:384, 3 * TC:4 * TC], in_=XR[:, 0:TC]),
              reads=("xr0",), writes=("dbgx:2",))
        P.v(lambda e: e.tensor_copy(XR[:, TC:2 * TC], NBT[:, 0:TC]),
            reads=("nbt",) + tuple(keys), writes=("xr1",))
        P.dma("y", lambda e: e.dma_start(out=x_out[384:512, 3 * TC:4 * TC], in_=XR[:, TC:2 * TC]),
              reads=("xr1",), writes=("dbgx:3",))
        keys = ["dbgx:0", "dbgx:1", "dbgx:2", "dbgx:3"]
    if ATTN_DEBUG == "att":
        for j in range(NDT):
            slot = j % 2
            xsl = slice(slot * TC, (slot + 1) * TC)
            P.v(lambda e, j=j, xsl=xsl: e.tensor_copy(XR[:, xsl], ATT[:, j * TC:(j + 1) * TC]),
                reads=("att{}".format(j),) + tuple(keys), writes=("xr{}".format(slot),))
            P.dma("y", lambda e, j=j, xsl=xsl: e.dma_start(
                    out=x_out[j * 128:(j + 1) * 128, 3 * TC:4 * TC], in_=XR[:, xsl]),
                  reads=("xr{}".format(slot),), writes=("dbga:{}".format(j),))
        keys = ["dbga:{}".format(j) for j in range(NDT)]
    if ATTN_DEBUG == "vs":
        for rb in range(NDT):
            slot = rb % 2
            xsl = slice(slot * TC, (slot + 1) * TC)
            for c4 in range(4):
                P.v(lambda e, rb=rb, c4=c4, xsl=xsl: e.tensor_copy(
                        XR[:, xsl], VS[:, rb * 2048 + c4 * TC: rb * 2048 + (c4 + 1) * TC]),
                    reads=tuple("vs{}_{}".format(tt, h) for tt in range(16) for h in range(16))
                          + tuple(keys), writes=("xr{}".format(slot),))
                P.dma("y", lambda e, rb=rb, c4=c4, xsl=xsl: e.dma_start(
                        out=x_out[rb * 128:(rb + 1) * 128, c4 * TC:(c4 + 1) * TC], in_=XR[:, xsl]),
                      reads=("xr{}".format(slot),), writes=("dbgv:{}_{}".format(rb, c4),))
        keys = ["dbgv:{}_{}".format(rb, c4) for rb in range(NDT) for c4 in range(4)]
    if ATTN_DEBUG in ("qt", "kt", "xn"):
        DT_ = {"qt": QT, "kt": KT, "xn": XN}[ATTN_DEBUG]
        tagd = {"qt": "qt", "kt": "kt", "xn": None}[ATTN_DEBUG]
        for j in range(NDT):
            for c in range(NTC):
                slot = j % 2
                xsl = slice(slot * TC, (slot + 1) * TC)
                rk = (tagd + str(j),) if tagd else ("xn{}_{}".format(j, c),)
                P.v(lambda e, j=j, c=c, xsl=xsl, DT_=DT_: e.tensor_copy(
                        XR[:, xsl], DT_[:, j * T + c * TC: j * T + (c + 1) * TC]),
                    reads=rk + tuple(keys), writes=("xr{}".format(slot),))
                P.dma("y", lambda e, j=j, c=c, xsl=xsl: e.dma_start(
                        out=x_out[j * 128:(j + 1) * 128, c * TC:(c + 1) * TC], in_=XR[:, xsl]),
                      reads=("xr{}".format(slot),), writes=("dbg:{}_{}".format(c, j),))
        keys = ["dbg:{}_{}".format(c, j) for c in range(NTC) for j in range(NDT)]
    P.op("y", lambda e: e.nop(), reads=tuple(keys))


def build_attn_launch(variant):
    nc = bass.Bass("TRN2", target_bir_lowering=False)
    x_in = nc.dram_tensor("x_in", [DM, T], F32, kind="ExternalInput").ap()
    x_out = nc.dram_tensor("x_out", [DM, T], F32, kind="ExternalOutput").ap()
    wpack = nc.dram_tensor("wpack", [DM, 4 * DM], F32, kind="ExternalInput").ap()
    bpack = nc.dram_tensor("bpack", [128, 32], F32, kind="ExternalInput").ap()
    bvrow = nc.dram_tensor("bvrow", [1, DM], F32, kind="ExternalInput").ap()
    minus1_in = nc.dram_tensor("minus1_in", [1, 128], F32, kind="ExternalInput").ap()
    psel2_in = nc.dram_tensor("psel2_in", [128, 128], F32, kind="ExternalInput").ap()
    bsel_in = nc.dram_tensor("bsel_in", [128, 128], F32, kind="ExternalInput").ap()
    ident_in = nc.dram_tensor("ident_in", [128, 128], F32, kind="ExternalInput").ap()
    mwidth = {"l0": 1152, "l1attn": 1536, "l1mha": 1024}[variant]
    hostm_in = nc.dram_tensor("hostm_in", [128, mwidth], F32, kind="ExternalInput").ap()
    if variant == "l1mha":
        rope_in = {nm: nc.dram_tensor(nm, [128, T if nm in ("ru", "rw") else T], F32,
                                      kind="ExternalInput").ap() for nm in ("ru", "rw")}
        ropem_in = {nm: nc.dram_tensor(nm, [128, 128], F32, kind="ExternalInput").ap()
                    for nm in ("m1", "m2")}
        g1_in = nc.dram_tensor("g1_in", [128, 8], F32, kind="ExternalInput").ap()
        zbuf = nc.dram_tensor("zbuf", [DM, T], F32, kind="Internal").ap()
    if variant == "l1attn":
        gb_in = nc.dram_tensor("gb_in", [128, 4], F32, kind="ExternalInput").ap()

    io = dict(x_in=x_in, x_out=x_out, wpack=wpack, bpack=bpack, bvrow=bvrow,
              minus1_in=minus1_in, psel2_in=psel2_in, bsel_in=bsel_in, ident_in=ident_in,
              hostm_in=hostm_in)
    if variant == "l1mha":
        io.update(diagm_in=None, g1_in=g1_in, zbuf=zbuf, ru=rope_in["ru"], rw=rope_in["rw"],
                  m1=ropem_in["m1"], m2=ropem_in["m2"])
    if variant == "l1attn":
        io.update(gb_in=gb_in)
    st = contextlib.ExitStack()
    E = st.enter_context
    P = Prog(nc)
    plan_attn(P, nc, E, io, variant)
    block = E(nc.Block())
    P.finalize(block, st)
    st.close()
    return nc


def run_attn(xt, variant, wq, wk, wv, wo, bq, bk, bv, bo, gb=None, g1=None):
    hc = attn_host_consts(variant)
    if variant == "l1mha":
        perm = rope_perm()
        wq = wq[:, perm]
        wk = wk[:, perm]
    wpackh = np.ascontiguousarray(np.concatenate([wq, wk, wv, wo], axis=1).astype(np.float32))
    z8 = np.zeros(DM, np.float32)
    bq = z8 if bq is None else np.asarray(bq, np.float32)
    bk = z8 if bk is None else np.asarray(bk, np.float32)
    bv = z8 if bv is None else np.asarray(bv, np.float32)
    bo = z8 if bo is None else np.asarray(bo, np.float32)
    bpackh = np.zeros((128, 32), np.float32)
    bpackh[:, 0:8] = bq.reshape(8, 128).T
    bpackh[:, 8:16] = bk.reshape(8, 128).T
    bpackh[:, 24:32] = bo.reshape(8, 128).T
    bvrowh = np.ascontiguousarray(bv.reshape(1, DM))
    nc = build_attn_launch(variant)
    im = {"wpack": wpackh, "bpack": bpackh, "bvrow": bvrowh,
          "minus1_in": hc["minus1_in"], "psel2_in": hc["psel2_in"],
          "bsel_in": hc["bsel_in"], "ident_in": hc["ident_in"]}
    im["hostm_in"] = hc["hostm_in"]
    if variant == "l1mha":
        im.update({"ru": hc["ru"], "rw": hc["rw"], "m1": hc["m1"], "m2": hc["m2"]})
        im["g1_in"] = np.ascontiguousarray(np.asarray(g1, np.float32).reshape(8, 128).T)
    if variant == "l1attn":
        gq, gbb, gk, gkb = [np.asarray(a, np.float32) for a in gb]
        gbh = np.zeros((128, 4), np.float32)
        gbh[:, 0] = np.tile(gq, 2)
        gbh[:, 1] = np.tile(gbb, 2)
        gbh[:, 2] = np.tile(gk, 2)
        gbh[:, 3] = np.tile(gkb, 2)
        im["gb_in"] = gbh
    in_maps = [dict(im, x_in=xt[b]) for b in range(8)]
    res = run_bass_kernel_spmd(nc, in_maps, core_ids=list(range(8)))
    return [res.results[b]["x_out"] for b in range(8)]



def plan_router(P, nc, E, io):
    x_in = io.get("x_in")
    wr_in = io.get("wr_in")
    g2_in = io.get("g2_in")
    zn_out = io.get("zn_out")
    vloc_out = io.get("vloc_out")
    XB8 = E(nc.sbuf_tensor("XB8", [128, NDT * TC], BF16))
    XSQ = E(nc.sbuf_tensor("XSQ", [128, 2 * TC], BF16))
    XR = E(nc.sbuf_tensor("XR", [128, NDT * TC], F32))
    XNC = E(nc.sbuf_tensor("XNC", [128, 2 * TC], BF16))
    XC8 = E(nc.sbuf_tensor("XC8", [128, NDT * TC], F32))
    AROW = E(nc.sbuf_tensor("AROW", [1, TC], F32))
    TROW = E(nc.sbuf_tensor("TROW", [1, TC], F32))
    WR = E(nc.sbuf_tensor("WR", [128, NDT * 8], F32))
    G2 = E(nc.sbuf_tensor("G2", [128, 8], F32))
    LG = E(nc.sbuf_tensor("LG", [128, 8], F32))
    LG2 = E(nc.sbuf_tensor("LG2", [128, 8], F32))
    EXPL = E(nc.sbuf_tensor("EXPL", [128, 8], F32))
    PRB = E(nc.sbuf_tensor("PRB", [128, 8], F32))
    MSK = E(nc.sbuf_tensor("MSK", [128, 8], F32))
    SC = E(nc.sbuf_tensor("SC", [128, 8], F32))
    VLOC = E(nc.sbuf_tensor("VLOC", [128, 128], F32))
    ones_colbf = E(nc.sbuf_tensor("ones_colbf", [128, 1], BF16))
    ones32 = E(nc.sbuf_tensor("ones32", [1, 128], F32))
    ps_s2 = E(nc.psum_tensor("ps_s2", [128, TC], F32))
    ps_a = E(nc.psum_tensor("ps_a", [128, TC], F32))
    ps_r = E(nc.psum_tensor("ps_r", [128, 8], F32))

    P.g(lambda e: e.memset(ones_colbf[:, :], 1.0), writes=("ones_colbf",))
    P.g(lambda e: e.memset(ones32[:, :], 1.0), writes=("ones32",))
    P.dma("g", lambda e: e.dma_start(out=G2[:, :], in_=g2_in[:, :]), writes=("g2",))
    for j in range(NDT):
        P.dma("g", lambda e, j=j: e.dma_start(out=WR[:, j * 8:(j + 1) * 8],
                                              in_=wr_in[j * 128:(j + 1) * 128, :]),
              writes=("wr{}".format(j),))
        P.v(lambda e, j=j: e.tensor_scalar(WR[:, j * 8:(j + 1) * 8], WR[:, j * 8:(j + 1) * 8],
                                           G2[:, j:j + 1], None, ALU.mult),
            reads=("wr{}".format(j), "g2"), writes=("wr{}".format(j),))

    for c in range(NTC):
        P.dma("g", lambda e, c=c: e.dma_start(
                out=XB8[:, :].rearrange("p (j t) -> p j t", j=NDT),
                in_=x_in[:, c * TC:(c + 1) * TC].rearrange("(j p) t -> p j t", p=128)),
              writes=("xb8",))
        for j in range(NDT):
            slot = j % 2
            bcol = slice(slot * TC, (slot + 1) * TC)
            P.s(lambda e, j=j, bcol=bcol: e.activation(XSQ[:, bcol], XB8[:, j * TC:(j + 1) * TC],
                                                       AF.Square),
                reads=("xb8",), writes=("xsq{}".format(slot),))
            P.t(lambda e, bcol=bcol, j=j: e.matmul(ps_s2[0:1, :], ones_colbf[:, 0:1], XSQ[:, bcol],
                                                   start=(j == 0), stop=(j == NDT - 1)),
                reads=("ones_colbf", "xsq{}".format(slot)), writes=("ps_s2",))
        P.v(lambda e: e.tensor_scalar(AROW[0:1, :], ps_s2[0:1, :], 1.0 / DM, 1e-6,
                                      ALU.mult, ALU.add),
            reads=("ps_s2",), writes=("arow0",))
        P.s(lambda e: e.activation(TROW[0:1, :], AROW[0:1, :], AF.Ln),
            reads=("arow0",), writes=("trow",))
        P.s(lambda e: e.activation(AROW[0:1, :], TROW[0:1, :], AF.Exp, scale=-0.5),
            reads=("trow",), writes=("arow0",))
        P.t(lambda e: e.matmul(ps_a[:, :], ones32[0:1, :], AROW[0:1, :], start=True, stop=True),
            reads=("ones32", "arow0"), writes=("ps_a",))
        P.dma("y", lambda e, c=c: e.dma_start(
                out=XC8[:, :].rearrange("p (j t) -> p j t", j=NDT),
                in_=x_in[:, c * TC:(c + 1) * TC].rearrange("(j p) t -> p j t", p=128)),
              writes=("xc8",))
        for j in range(NDT):
            slot = j % 2
            xsl = slice(slot * TC, (slot + 1) * TC)
            P.v(lambda e, j=j: e.tensor_tensor(XR[:, j * TC:(j + 1) * TC],
                                               XC8[:, j * TC:(j + 1) * TC],
                                               ps_a[:, :], ALU.mult),
                reads=("xc8", "ps_a"), writes=("xrj{}".format(j),))
            P.v(lambda e, j=j, xsl=xsl: e.tensor_scalar(XNC[:, xsl], XR[:, j * TC:(j + 1) * TC],
                                                        G2[:, j:j + 1], None, ALU.mult),
                reads=("xrj{}".format(j), "g2"), writes=("xnc{}".format(slot),))
            P.dma("y", lambda e, j=j, c=c, xsl=xsl: e.dma_start(
                    out=zn_out[j * 128:(j + 1) * 128, c * TC:(c + 1) * TC], in_=XNC[:, xsl]),
                  reads=("xnc{}".format(slot),), writes=("out:zn{}_{}".format(c, j),))
        for s4 in range(4):
            tt = c * 4 + s4
            for j in range(NDT):
                P.t(lambda e, j=j, s4=s4: e.matmul(
                        ps_r[:, :], XR[:, j * TC + s4 * 128: j * TC + (s4 + 1) * 128],
                        WR[:, j * 8:(j + 1) * 8], start=(j == 0), stop=(j == NDT - 1)),
                    reads=("xrj{}".format(j), "wr{}".format(j)), writes=("ps_r",))
            P.v(lambda e: e.tensor_copy(LG[:, :], ps_r[:, :]), reads=("ps_r",), writes=("lg",))
            P.v(lambda e: e.reduce_max(SC[:, 0:1], LG[:, :], axis=AX.X, negate=True),
                reads=("lg",), writes=("sc0",))
            P.s(lambda e: e.activation(EXPL[:, :], LG[:, :], AF.Exp, bias=SC[:, 0:1]),
                reads=("lg", "sc0"), writes=("expl",))
            P.v(lambda e: e.reduce_sum(SC[:, 1:2], EXPL[:, :], axis=AX.X),
                reads=("expl",), writes=("sc1",))
            P.s(lambda e: e.activation(SC[:, 2:3], SC[:, 1:2], AF.Ln),
                reads=("sc1",), writes=("sc2",))
            P.s(lambda e: e.activation(SC[:, 3:4], SC[:, 2:3], AF.Exp, scale=-1.0),
                reads=("sc2",), writes=("sc3",))
            P.v(lambda e: e.tensor_scalar(SC[:, 4:5], SC[:, 0:1], -1.0, None, ALU.mult),
                reads=("sc0",), writes=("sc4",))
            P.v(lambda e: e.tensor_scalar(MSK[:, :], LG[:, :], SC[:, 4:5], None, ALU.is_ge),
                reads=("lg", "sc4"), writes=("msk",))
            P.v(lambda e: e.scalar_tensor_tensor(LG2[:, :], MSK[:, :], -1e30, LG[:, :],
                                                 ALU.mult, ALU.add),
                reads=("msk", "lg"), writes=("lg2",))
            P.v(lambda e: e.reduce_max(SC[:, 5:6], LG2[:, :], axis=AX.X),
                reads=("lg2",), writes=("sc5",))
            P.s(lambda e: e.activation(SC[:, 6:7], SC[:, 5:6], AF.Exp, bias=SC[:, 0:1]),
                reads=("sc5", "sc0"), writes=("sc6",))
            P.v(lambda e: e.tensor_scalar(SC[:, 7:8], SC[:, 6:7], SC[:, 3:4], None, ALU.mult),
                reads=("sc6", "sc3"), writes=("sc7",))
            P.v(lambda e: e.tensor_scalar(PRB[:, :], EXPL[:, :], SC[:, 3:4], None, ALU.mult),
                reads=("expl", "sc3"), writes=("prb",))
            P.v(lambda e: e.tensor_scalar(MSK[:, :], PRB[:, :], SC[:, 7:8], None, ALU.is_ge),
                reads=("prb", "sc7"), writes=("msk",))
            P.v(lambda e, tt=tt: e.tensor_tensor(VLOC[:, tt * 8:(tt + 1) * 8], PRB[:, :],
                                                 MSK[:, :], ALU.mult),
                reads=("prb", "msk"), writes=("vloc{}".format(tt),))
    P.dma("y", lambda e: e.dma_start(out=vloc_out[:, :], in_=VLOC[:, :]),
          reads=tuple("vloc{}".format(tt) for tt in range(16)), writes=("out:vloc",))
    keys = ["out:zn{}_{}".format(c, j) for c in range(NTC) for j in range(NDT)] + ["out:vloc"]
    P.op("y", lambda e: e.nop(), reads=tuple(keys))


def build_router_launch():
    nc = bass.Bass("TRN2", target_bir_lowering=False)
    x_in = nc.dram_tensor("x_in", [DM, T], F32, kind="ExternalInput").ap()
    wr_in = nc.dram_tensor("wr_in", [DM, 8], F32, kind="ExternalInput").ap()
    g2_in = nc.dram_tensor("g2_in", [128, 8], F32, kind="ExternalInput").ap()
    zn_out = nc.dram_tensor("zn_out", [DM, T], BF16, kind="ExternalOutput").ap()
    vloc_out = nc.dram_tensor("vloc_out", [128, 128], F32, kind="ExternalOutput").ap()

    io = dict(x_in=x_in, wr_in=wr_in, g2_in=g2_in, zn_out=zn_out, vloc_out=vloc_out)
    st = contextlib.ExitStack()
    E = st.enter_context
    P = Prog(nc)
    plan_router(P, nc, E, io)
    block = E(nc.Block())
    P.finalize(block, st)
    st.close()
    return nc


def run_router(s1t, w_router, g2):
    nc = build_router_launch()
    wrh = np.ascontiguousarray(np.asarray(w_router, np.float32))
    g2h = np.ascontiguousarray(np.asarray(g2, np.float32).reshape(8, 128).T)
    in_maps = [{"x_in": s1t[b], "wr_in": wrh, "g2_in": g2h} for b in range(8)]
    res = run_bass_kernel_spmd(nc, in_maps, core_ids=list(range(8)))
    zn = [res.results[b]["zn_out"] for b in range(8)]
    vloc = [res.results[b]["vloc_out"] for b in range(8)]
    return zn, vloc


def vloc_decode(vl):
    out = np.zeros((T, 8), np.float32)
    for tt in range(16):
        out[tt * 128:(tt + 1) * 128, :] = vl[:, tt * 8:(tt + 1) * 8]
    return out


def moe_host_layouts(vglobal):
    vb = np.zeros((128, 1024), np.float32)
    for e in range(8):
        for s in range(16):
            vb[e * 16 + s, :] = vglobal[s * 1024:(s + 1) * 1024, e]
    return vb



CAP = 1280.0


def plan_moe(P, nc, E, io, ago=None, permsel=None, idf32=None):
    x3_in = io.get("x3_in")
    s1_in = io.get("s1_in")
    zn_in = io.get("zn_in")
    vbis_in = io.get("vbis_in")
    vtok_in = io.get("vtok_in")
    w1_in = io.get("w1_in")
    w2_in = io.get("w2_in")
    gsel_in = io.get("gsel_in")
    gsel2_in = io.get("gsel2_in")
    esel3_in = io.get("esel3_in")
    ident8_in = io.get("ident8_in")
    identf_in = io.get("identf_in")
    x_out = io.get("x_out")
    ZN = E(nc.sbuf_tensor("ZN", [128, NDT * T], BF16))
    W1E = E(nc.sbuf_tensor("W1E", [128, NDT * 2048], BF16))
    W2E = E(nc.sbuf_tensor("W2E", [128, 16 * DM], BF16))
    HB = E(nc.sbuf_tensor("HB", [128, 16 * TC], BF16))
    ACC = E(nc.sbuf_tensor("ACC", [128, NDT * T], F32))
    ZG = E(nc.sbuf_tensor("ZG", [128, NDT * TC], BF16))
    VB = E(nc.sbuf_tensor("VB", [128, 1024], F32))
    VT = E(nc.sbuf_tensor("VT", [128, 128], F32))
    GT = E(nc.sbuf_tensor("GT", [128, 128], F32))
    GA = E(nc.sbuf_tensor("GA", [8, 16 * 128], BF16))
    GTB = E(nc.sbuf_tensor("GTB", [128, 128], BF16))
    CNTB = E(nc.sbuf_tensor("CNTB", [128, 1024], BF16))
    CNTS = E(nc.sbuf_tensor("CNTS", [128, 4], F32))
    LOHI = E(nc.sbuf_tensor("LOHI", [8, 8], F32))
    TAUB = E(nc.sbuf_tensor("TAUB", [128, 8], F32))
    TAUR = E(nc.sbuf_tensor("TAUR", [1, 8], F32))
    GSEL = E(nc.sbuf_tensor("GSEL", [128, 8], F32))
    GSEL2 = E(nc.sbuf_tensor("GSEL2", [8, 128], F32))
    ESEL3 = E(nc.sbuf_tensor("ESEL3", [8, 1024], BF16))
    IDENT8 = E(nc.sbuf_tensor("IDENT8", [8, 8], F32))
    IDENTF = E(nc.sbuf_tensor("IDENTF", [128, 128], BF16))
    ones32 = E(nc.sbuf_tensor("ones32", [1, 128], F32))
    MIDB = E(nc.sbuf_tensor("MIDB", [128, 1], F32))
    CNT8 = E(nc.sbuf_tensor("CNT8", [8, 4], F32))
    pp = [E(nc.psum_tensor("pp{}".format(i), [128, TC], F32)) for i in range(8)]
    ps_m = pp[0]
    ps_c = pp[1]
    ps_h = [pp[2], pp[3]]
    ps_o = [pp[4], pp[5]]
    ps_g = pp[6]
    ps_t = pp[7]

    P.g(lambda e: e.memset(ones32[:, :], 1.0), writes=("ones32",))
    if ago is None:
        P.dma("g", lambda e: e.dma_start(out=VB[:, :], in_=vbis_in[:, :]), writes=("vb",))
    else:
        AGT = E(nc.sbuf_tensor("AGT", [128, 128], F32))
        AGT2 = E(nc.sbuf_tensor("AGT2", [128, 128], F32))
        PRM = E(nc.sbuf_tensor("PRM", [128, 128], F32))
        IDF = E(nc.sbuf_tensor("IDF", [128, 128], F32))
        P.dma("g", lambda e: e.dma_start(out=PRM[:, :], in_=permsel[:, :]), writes=("prm",))
        P.dma("g", lambda e: e.dma_start(out=IDF[:, :], in_=idf32[:, :]), writes=("idf",))
        for r in range(8):
            P.dma("g", lambda e, r=r: e.dma_start(out=AGT[:, :], in_=ago[r * 128:(r + 1) * 128, :]),
                  reads=("cc_done",), writes=("agt",))
            P.t(lambda e: e.transpose(pp[0][0:128, 0:128], AGT[:, :], IDF[:, :]),
                reads=("agt", "idf"), writes=("pp0",))
            P.v(lambda e: e.tensor_copy(AGT2[:, :], pp[0][0:128, 0:128]),
                reads=("pp0",), writes=("agt2",))
            P.t(lambda e: e.matmul(pp[0][0:128, 0:128], PRM[:, :], AGT2[:, :], start=True, stop=True),
                reads=("prm", "agt2"), writes=("pp0",))
            P.v(lambda e, r=r: e.tensor_copy(VB[:, r * 128:(r + 1) * 128], pp[0][0:128, 0:128]),
                reads=("pp0",), writes=("vb",))
    P.dma("g", lambda e: e.dma_start(out=VT[:, :], in_=vtok_in[:, :]), writes=("vt",))
    P.dma("g", lambda e: e.dma_start(out=GSEL[:, :], in_=gsel_in[:, :]), writes=("gsel",))
    P.dma("g", lambda e: e.dma_start(out=GSEL2[:, :], in_=gsel2_in[:, :]), writes=("gsel2",))
    P.dma("g", lambda e: e.dma_start(out=ESEL3[:, :], in_=esel3_in[:, :]), writes=("esel3",))
    P.dma("g", lambda e: e.dma_start(out=IDENT8[:, :], in_=ident8_in[:, :]), writes=("ident8",))
    P.dma("g", lambda e: e.dma_start(out=IDENTF[:, :], in_=identf_in[:, :]), writes=("identf",))
    P.dma("g", lambda e: e.dma_start(
            out=ZN[:, :].rearrange("p (j t) -> p j t", j=NDT),
            in_=zn_in.rearrange("(j p) t -> p j t", p=128)),
          writes=tuple("zn{}".format(j) for j in range(NDT)))

    P.v(lambda e: e.memset(LOHI[:, 0:1], 0.0), writes=("lo",))
    P.v(lambda e: e.memset(LOHI[:, 1:2], 1.01), writes=("hi",))
    for it in range(32):
        P.v(lambda e: e.tensor_tensor(LOHI[:, 2:3], LOHI[:, 0:1], LOHI[:, 1:2], ALU.add),
            reads=("lo", "hi"), writes=("mid",))
        P.v(lambda e: e.tensor_scalar(LOHI[:, 2:3], LOHI[:, 2:3], 0.5, None, ALU.mult),
            reads=("mid",), writes=("mid",))
        P.t(lambda e: e.matmul(ps_m[:, 0:1], GSEL2[:, :], LOHI[0:8, 2:3], start=True, stop=True),
            reads=("gsel2", "mid"), writes=("pp0",))
        P.v(lambda e: e.tensor_copy(MIDB[:, :], ps_m[:, 0:1]), reads=("pp0",), writes=("midb",))
        P.v(lambda e: e.tensor_scalar(CNTB[:, :], VB[:, :], MIDB[:, 0:1], None, ALU.is_ge),
            reads=("vb", "midb"), writes=("cntb",))
        P.v(lambda e: e.reduce_sum(CNTS[:, 0:1], CNTB[:, :], axis=AX.X),
            reads=("cntb",), writes=("cnts",))
        P.t(lambda e: e.matmul(ps_c[0:8, 0:1], GSEL[:, :], CNTS[:, 0:1], start=True, stop=True),
            reads=("gsel", "cnts"), writes=("pp1",))
        P.v(lambda e: e.tensor_copy(CNT8[:, 0:1], ps_c[0:8, 0:1]), reads=("pp1",), writes=("cnt8",))
        P.v(lambda e: e.tensor_scalar(CNT8[:, 1:2], CNT8[:, 0:1], CAP, None, ALU.is_ge),
            reads=("cnt8",), writes=("sel",))
        P.v(lambda e: e.tensor_tensor(LOHI[:, 3:4], LOHI[:, 2:3], LOHI[:, 0:1], ALU.subtract),
            reads=("mid", "lo"), writes=("sc_d",))
        P.v(lambda e: e.tensor_tensor(LOHI[:, 3:4], LOHI[:, 3:4], CNT8[:, 1:2], ALU.mult),
            reads=("sc_d", "sel"), writes=("sc_d",))
        P.v(lambda e: e.tensor_tensor(LOHI[:, 0:1], LOHI[:, 0:1], LOHI[:, 3:4], ALU.add),
            reads=("lo", "sc_d"), writes=("lo",))
        P.v(lambda e: e.tensor_tensor(LOHI[:, 4:5], LOHI[:, 1:2], LOHI[:, 2:3], ALU.subtract),
            reads=("hi", "mid"), writes=("sc_e",))
        P.v(lambda e: e.tensor_tensor(LOHI[:, 4:5], LOHI[:, 4:5], CNT8[:, 1:2], ALU.mult),
            reads=("sc_e", "sel"), writes=("sc_e",))
        P.v(lambda e: e.tensor_tensor(LOHI[:, 1:2], LOHI[:, 2:3], LOHI[:, 4:5], ALU.add),
            reads=("mid", "sc_e"), writes=("hi",))
    P.t(lambda e: e.matmul(ps_m[0:1, 0:8], LOHI[0:8, 0:1], IDENT8[:, :], start=True, stop=True),
        reads=("lo", "ident8"), writes=("pp0",))
    P.v(lambda e: e.tensor_scalar(TAUR[0:1, :], ps_m[0:1, 0:8], 0.0, None, ALU.max),
        reads=("pp0",), writes=("taur",))
    P.t(lambda e: e.matmul(ps_m[:, 0:8], ones32[0:1, :], TAUR[0:1, :], start=True, stop=True),
        reads=("ones32", "taur"), writes=("pp0",))
    P.v(lambda e: e.tensor_copy(TAUB[:, :], ps_m[:, 0:8]), reads=("pp0",), writes=("taub",))
    for tt in range(16):
        sl = slice(tt * 8, (tt + 1) * 8)
        P.v(lambda e, sl=sl: e.tensor_tensor(GT[:, sl], VT[:, sl], TAUB[:, :], ALU.is_ge),
            reads=("vt", "taub"), writes=("gt",))
        P.v(lambda e, sl=sl: e.scalar_tensor_tensor(GT[:, sl], GT[:, sl], 0.5, VT[:, sl],
                                                    ALU.mult, ALU.mult),
            reads=("gt", "vt"), writes=("gt",))
    for tt in range(16):
        P.v(lambda e, tt=tt: e.tensor_copy(GTB[:, tt * 8:(tt + 1) * 8] if False else GTB[:, 0:8],
                                           GT[:, tt * 8:(tt + 1) * 8]),
            reads=("gt",), writes=("gtb",))
        P.t(lambda e, tt=tt: e.matmul(ps_t[0:8, 0:128], GTB[:, 0:8], IDENTF[:, :],
                                      start=True, stop=True),
            reads=("gtb", "identf"), writes=("pp7",))
        P.v(lambda e, tt=tt: e.tensor_copy(GA[0:8, tt * 128:(tt + 1) * 128], ps_t[0:8, 0:128]),
            reads=("pp7",), writes=("ga",))

    for c in range(NTC):
        P.dma("g", lambda e, c=c: e.dma_start(
                out=ACC[:, :].rearrange("p (j tt) -> p j tt", j=NDT)[:, :, c * TC:(c + 1) * TC],
                in_=s1_in[:, c * TC:(c + 1) * TC].rearrange("(j p) t -> p j t", p=128)),
              writes=tuple("acc{}_{}".format(j, c) for j in range(NDT)))
        for j in range(NDT):
            P.v(lambda e, j=j, c=c: e.tensor_scalar(
                    ACC[:, j * T + c * TC: j * T + (c + 1) * TC],
                    ACC[:, j * T + c * TC: j * T + (c + 1) * TC], 0.5, None, ALU.mult),
                reads=("acc{}_{}".format(j, c),), writes=("acc{}_{}".format(j, c),))
        P.dma("g", lambda e, c=c: e.dma_start(
                out=ACC[:, :].rearrange("p (j tt) -> p j tt", j=NDT)[:, :, c * TC:(c + 1) * TC],
                in_=x3_in[:, c * TC:(c + 1) * TC].rearrange("(j p) t -> p j t", p=128),
                accum_op=ALU.add),
              reads=tuple("acc{}_{}".format(j, c) for j in range(NDT)),
              writes=tuple("acc{}_{}".format(j, c) for j in range(NDT)))

    for ex in range(8):
        P.dma("g", lambda e, ex=ex: e.dma_start(
                out=W1E[:, 0:4 * 2048].rearrange("p (j f) -> p j f", j=4),
                in_=w1_in[ex * DM:ex * DM + 512, :].rearrange("(j p) f -> p j f", p=128)),
              writes=tuple("w1e{}".format(j) for j in range(4)))
        P.dma("g", lambda e, ex=ex: e.dma_start(
                out=W1E[:, 4 * 2048:].rearrange("p (j f) -> p j f", j=4),
                in_=w1_in[ex * DM + 512:(ex + 1) * DM, :].rearrange("(j p) f -> p j f", p=128)),
              writes=tuple("w1e{}".format(j) for j in range(4, NDT)))
        P.dma("g", lambda e, ex=ex: e.dma_start(
                out=W2E[:, :].rearrange("p (f c) -> p f c", f=16),
                in_=w2_in[ex * 2048:(ex + 1) * 2048, :].rearrange("(f p) c -> p f c", p=128)),
              writes=tuple("w2e{}".format(ft) for ft in range(16)))
        for c in range(NTC):
            P.t(lambda e, ex=ex, c=c: e.matmul(
                    ps_g[:, :], ESEL3[:, ex * 128:(ex + 1) * 128], GA[0:8, c * TC:(c + 1) * TC],
                    start=True, stop=True),
                reads=("esel3", "ga"), writes=("pp6",))
            for j in range(NDT):
                P.v(lambda e, j=j, c=c: e.tensor_tensor(
                        ZG[:, j * TC:(j + 1) * TC], ZN[:, j * T + c * TC: j * T + (c + 1) * TC],
                        ps_g[:, :], ALU.mult),
                    reads=("zn{}".format(j), "pp6"), writes=("zg{}".format(j),))
            for ft in range(16):
                ps = ps_h[ft % 2]
                for j in range(NDT):
                    P.t(lambda e, ft=ft, j=j, ps=ps: e.matmul(
                            ps[:, :], W1E[:, j * 2048 + ft * 128: j * 2048 + (ft + 1) * 128],
                            ZG[:, j * TC:(j + 1) * TC],
                            start=(j == 0), stop=(j == NDT - 1)),
                        reads=("w1e{}".format(j), "zg{}".format(j)), writes=("pp{}".format(2 + ft % 2),))
                P.s(lambda e, ft=ft, ps=ps: e.activation(HB[:, ft * TC:(ft + 1) * TC], ps[:, :],
                                                         AF.Relu),
                    reads=("pp{}".format(2 + ft % 2),), writes=("hb{}".format(ft),))
            for j in range(NDT):
                ps = ps_o[j % 2]
                for ft in range(16):
                    P.t(lambda e, j=j, ft=ft, ps=ps: e.matmul(
                            ps[:, :], W2E[:, ft * DM + j * 128: ft * DM + (j + 1) * 128],
                            HB[:, ft * TC:(ft + 1) * TC],
                            start=(ft == 0), stop=(ft == 15)),
                        reads=("w2e{}".format(ft), "hb{}".format(ft)), writes=("pp{}".format(4 + j % 2),))
                P.v(lambda e, j=j, c=c, ps=ps: e.tensor_tensor(
                        ACC[:, j * T + c * TC: j * T + (c + 1) * TC],
                        ACC[:, j * T + c * TC: j * T + (c + 1) * TC], ps[:, :], ALU.add),
                    reads=("acc{}_{}".format(j, c), "pp{}".format(4 + j % 2)),
                    writes=("acc{}_{}".format(j, c),))

    for c in range(NTC):
        for j in range(NDT):
            P.dma("y", lambda e, j=j, c=c: e.dma_start(
                    out=x_out[j * 128:(j + 1) * 128, c * TC:(c + 1) * TC],
                    in_=ACC[:, j * T + c * TC: j * T + (c + 1) * TC]),
                  reads=("acc{}_{}".format(j, c),), writes=("out:x{}_{}".format(c, j),))
    keys = ["out:x{}_{}".format(c, j) for c in range(NTC) for j in range(NDT)]
    P.op("y", lambda e: e.nop(), reads=tuple(keys))


def build_moe_launch():
    nc = bass.Bass("TRN2", target_bir_lowering=False)
    x3_in = nc.dram_tensor("x3_in", [DM, T], F32, kind="ExternalInput").ap()
    s1_in = nc.dram_tensor("s1_in", [DM, T], F32, kind="ExternalInput").ap()
    zn_in = nc.dram_tensor("zn_in", [DM, T], BF16, kind="ExternalInput").ap()
    vbis_in = nc.dram_tensor("vbis_in", [128, 1024], F32, kind="ExternalInput").ap()
    vtok_in = nc.dram_tensor("vtok_in", [128, 128], F32, kind="ExternalInput").ap()
    w1_in = nc.dram_tensor("w1_in", [8 * DM, 2048], F32, kind="ExternalInput").ap()
    w2_in = nc.dram_tensor("w2_in", [8 * 2048, DM], F32, kind="ExternalInput").ap()
    gsel_in = nc.dram_tensor("gsel_in", [128, 8], F32, kind="ExternalInput").ap()
    gsel2_in = nc.dram_tensor("gsel2_in", [8, 128], F32, kind="ExternalInput").ap()
    esel3_in = nc.dram_tensor("esel3_in", [8, 1024], F32, kind="ExternalInput").ap()
    ident8_in = nc.dram_tensor("ident8_in", [8, 8], F32, kind="ExternalInput").ap()
    identf_in = nc.dram_tensor("identf_in", [128, 128], F32, kind="ExternalInput").ap()
    x_out = nc.dram_tensor("x_out", [DM, T], F32, kind="ExternalOutput").ap()

    io = dict(x3_in=x3_in, s1_in=s1_in, zn_in=zn_in, vbis_in=vbis_in, vtok_in=vtok_in,
              w1_in=w1_in, w2_in=w2_in, gsel_in=gsel_in, gsel2_in=gsel2_in, esel3_in=esel3_in,
              ident8_in=ident8_in, identf_in=identf_in, x_out=x_out)
    st = contextlib.ExitStack()
    E = st.enter_context
    P = Prog(nc)
    plan_moe(P, nc, E, io)
    block = E(nc.Block())
    P.finalize(block, st)
    st.close()
    return nc


def moe_consts():
    gsel = np.zeros((128, 8), np.float32)
    gsel2 = np.zeros((8, 128), np.float32)
    for p in range(128):
        gsel[p, p // 16] = 1.0
        gsel2[p // 16, p] = 1.0
    esel3 = np.zeros((8, 1024), np.float32)
    for ex in range(8):
        esel3[ex, ex * 128:(ex + 1) * 128] = 1.0
    return dict(gsel_in=gsel, gsel2_in=gsel2, esel3_in=esel3,
                ident8_in=np.eye(8, dtype=np.float32),
                identf_in=np.eye(128, dtype=np.float32))


def run_moe(x3t, s1t, znt, vlocs, w1, w2):
    vglob = np.concatenate([vloc_decode(vlocs[b]) for b in range(8)], axis=0)
    vbis = moe_host_layouts(vglob)
    consts = moe_consts()
    w1h = np.ascontiguousarray(np.asarray(w1, np.float32).reshape(8 * DM, 2048))
    w2h = np.ascontiguousarray(np.asarray(w2, np.float32).reshape(8 * 2048, DM))
    nc = build_moe_launch()
    in_maps = [dict(consts, x3_in=x3t[b], s1_in=s1t[b], zn_in=znt[b],
                    vbis_in=vbis, vtok_in=vlocs[b], w1_in=w1h, w2_in=w2h)
               for b in range(8)]
    res = run_bass_kernel_spmd(nc, in_maps, core_ids=list(range(8)))
    return [res.results[b]["x_out"] for b in range(8)]



def _np(a):
    return np.ascontiguousarray(np.asarray(a))


def _prep_l0ffn(inp):
    w1t = np.ascontiguousarray(_np(inp["L0_ffn_W1"]).T)
    w2t = np.ascontiguousarray(_np(inp["L0_ffn_W2"]).T)
    b1 = np.ascontiguousarray(_np(inp["L0_ffn_b1"]).reshape(32, 128).T)
    b2 = np.ascontiguousarray(_np(inp["L0_ffn_b2"]).reshape(8, 128).T)
    return w1t, w2t, b1, b2


def run_l0ffn(xt, inp, tern):
    w1t, w2t, b1, b2 = _prep_l0ffn(inp)
    nc = build_l0ffn_launch(with_ternarize=tern is None)
    if tern is None:
        in_maps = [{"x_in": xt[b], "w1t_in": w1t, "w2t_in": w2t, "b1_in": b1, "b2_in": b2}
                   for b in range(8)]
    else:
        in_maps = [{"x_in": xt[b], "w1bf_in": tern[0][b], "w2bf_in": tern[1][b],
                    "b1_in": b1, "b2_in": b2} for b in range(8)]
    res = run_bass_kernel_spmd(nc, in_maps, core_ids=list(range(8)))
    xt2 = [res.results[b]["x_out"] for b in range(8)]
    if tern is None:
        tern = ([res.results[b]["w1bf_out"] for b in range(8)],
                [res.results[b]["w2bf_out"] for b in range(8)])
    return xt2, tern


def run_l1ffn(xt, inp, idx):
    C = DM
    wqkv = _np(inp["L1_ffn_w_qkv"]).astype(np.float32)
    wo = _np(inp["L1_ffn_w_o"]).astype(np.float32)
    g1 = _np(inp["L1_ffn_g1"]).astype(np.float32)
    g2 = _np(inp["L1_ffn_g2"]).astype(np.float32)
    wrt = _np(inp["L1_ffn_w_router"]).astype(np.float32)
    w1 = _np(inp["L1_ffn_w1"]).astype(np.float32)
    w2 = _np(inp["L1_ffn_w2"]).astype(np.float32)
    s1 = run_attn(xt, "l1mha", wqkv[:, 0:C].copy(), wqkv[:, C:2 * C].copy(),
                  wqkv[:, 2 * C:].copy(), wo, None, None, None, None, g1=g1)
    zn, vloc = run_router(s1, wrt, g2)
    x4 = run_moe(xt, s1, zn, vloc, w1, w2)
    return x4



def permsel_const():
    m = np.zeros((128, 128), np.float32)
    for p in range(128):
        e, tt = p // 16, p % 16
        m[tt * 8 + e, p] = 1.0
    return m


class _NcProxy:

    def __init__(self, nc):
        self._nc = nc
        self._n = 0

    def sbuf_tensor(self, name, shape, dtype, **kw):
        self._n += 1
        return self._nc.sbuf_tensor("{}_p{}".format(name, self._n), shape, dtype, **kw)

    def psum_tensor(self, name, shape, dtype=mybir.dt.float32, **kw):
        self._n += 1
        return self._nc.psum_tensor("{}_p{}".format(name, self._n), shape, dtype, **kw)

    def __getattr__(self, k):
        return getattr(self._nc, k)


def build_fused():
    nc0 = bass.Bass("TRN2", target_bir_lowering=False)
    nc = _NcProxy(nc0)
    DT = nc.dram_tensor

    x0 = DT("x0", [DM, T], F32, kind="ExternalInput").ap()
    x_out = DT("x_out", [DM, T], F32, kind="ExternalOutput").ap()

    f_w1t = DT("f_w1t", [DM, 4096], F32, kind="ExternalInput").ap()
    f_w2t = DT("f_w2t", [4096, DM], F32, kind="ExternalInput").ap()
    f_b1 = DT("f_b1", [128, 32], F32, kind="ExternalInput").ap()
    f_b2 = DT("f_b2", [128, 8], F32, kind="ExternalInput").ap()
    w1bf = DT("w1bf", [DM, 4096], BF16, kind="Internal").ap()
    w2bf = DT("w2bf", [4096, DM], BF16, kind="Internal").ap()
    minus1_in = DT("minus1_in", [1, 128], F32, kind="ExternalInput").ap()
    psel2_in = DT("psel2_in", [128, 128], F32, kind="ExternalInput").ap()
    bsel_in = DT("bsel_in", [128, 128], F32, kind="ExternalInput").ap()
    ident_in = DT("ident_in", [128, 128], F32, kind="ExternalInput").ap()
    a0_wpack = DT("a0_wpack", [DM, 4 * DM], F32, kind="ExternalInput").ap()
    a0_bpack = DT("a0_bpack", [128, 32], F32, kind="ExternalInput").ap()
    a0_bvrow = DT("a0_bvrow", [1, DM], F32, kind="ExternalInput").ap()
    a0_hostm = DT("a0_hostm", [128, 1152], F32, kind="ExternalInput").ap()
    m_wpack = DT("m_wpack", [DM, 4 * DM], F32, kind="ExternalInput").ap()
    m_bpack = DT("m_bpack", [128, 32], F32, kind="ExternalInput").ap()
    m_bvrow = DT("m_bvrow", [1, DM], F32, kind="ExternalInput").ap()
    m_hostm = DT("m_hostm", [128, 1024], F32, kind="ExternalInput").ap()
    m_g1 = DT("m_g1", [128, 8], F32, kind="ExternalInput").ap()
    m_ru = DT("m_ru", [64 if False else 128, T], F32, kind="ExternalInput").ap()
    m_rw = DT("m_rw", [128, T], F32, kind="ExternalInput").ap()
    m_m1 = DT("m_m1", [128, 128], F32, kind="ExternalInput").ap()
    m_m2 = DT("m_m2", [128, 128], F32, kind="ExternalInput").ap()
    r_wr = DT("r_wr", [DM, 8], F32, kind="ExternalInput").ap()
    r_g2 = DT("r_g2", [128, 8], F32, kind="ExternalInput").ap()
    o_w1 = DT("o_w1", [8 * DM, 2048], F32, kind="ExternalInput").ap()
    o_w2 = DT("o_w2", [8 * 2048, DM], F32, kind="ExternalInput").ap()
    o_gsel = DT("o_gsel", [128, 8], F32, kind="ExternalInput").ap()
    o_gsel2 = DT("o_gsel2", [8, 128], F32, kind="ExternalInput").ap()
    o_esel3 = DT("o_esel3", [8, 1024], F32, kind="ExternalInput").ap()
    o_ident8 = DT("o_ident8", [8, 8], F32, kind="ExternalInput").ap()
    o_permsel = DT("o_permsel", [128, 128], F32, kind="ExternalInput").ap()
    a1_wpack = DT("a1_wpack", [DM, 4 * DM], F32, kind="ExternalInput").ap()
    a1_bpack = DT("a1_bpack", [128, 32], F32, kind="ExternalInput").ap()
    a1_bvrow = DT("a1_bvrow", [1, DM], F32, kind="ExternalInput").ap()
    a1_hostm = DT("a1_hostm", [128, 1536], F32, kind="ExternalInput").ap()
    a1_gb = DT("a1_gb", [128, 4], F32, kind="ExternalInput").ap()
    x1 = DT("x1", [DM, T], F32, kind="Internal").ap()
    x2 = DT("x2", [DM, T], F32, kind="Internal").ap()
    x3 = DT("x3", [DM, T], F32, kind="Internal").ap()
    x4 = DT("x4", [DM, T], F32, kind="Internal").ap()
    x5 = DT("x5", [DM, T], F32, kind="Internal").ap()
    s1a = DT("s1a", [DM, T], F32, kind="Internal").ap()
    s1b = DT("s1b", [DM, T], F32, kind="Internal").ap()
    zna = DT("zna", [DM, T], BF16, kind="Internal").ap()
    znb = DT("znb", [DM, T], BF16, kind="Internal").ap()
    za = DT("za", [DM, T], F32, kind="Internal").ap()
    zb = DT("zb", [DM, T], F32, kind="Internal").ap()
    agi_a = DT("agi_a", [128, 128], F32, kind="Internal")
    ago_a = DT("ago_a", [1024, 128], F32, kind="Internal")
    agi_b = DT("agi_b", [128, 128], F32, kind="Internal")
    ago_b = DT("ago_b", [1024, 128], F32, kind="Internal")

    master = contextlib.ExitStack()
    P = Prog(nc)

    _pn = [0]

    def phase(fn, name=None):
        st = contextlib.ExitStack()
        nm = name or "ph{}".format(_pn[0])
        _pn[0] += 1
        P.scope = nm
        fn(st.enter_context)
        st.close()
        P.barrier()
        P.scope = None

    attn_common = dict(minus1_in=minus1_in, psel2_in=psel2_in, bsel_in=bsel_in,
                       ident_in=ident_in)

    phase(lambda E: plan_l0ffn(P, nc, E, dict(
        x_in=x0, x_out=x1, w1t_in=f_w1t, w2t_in=f_w2t, w1bf_out=w1bf, w2bf_out=w2bf,
        b1_in=f_b1, b2_in=f_b2), True), "l0ffn_tz")
    phase(lambda E: plan_attn(P, nc, E, dict(
        attn_common, x_in=x1, x_out=x2, wpack=a0_wpack, bpack=a0_bpack, bvrow=a0_bvrow,
        hostm_in=a0_hostm), "l0"), "l0attn")
    phase(lambda E: plan_l0ffn(P, nc, E, dict(
        x_in=x2, x_out=x3, w1bf_in=w1bf, w2bf_in=w2bf, b1_in=f_b1, b2_in=f_b2), False), "l0ffn2")

    def l1ffn(xin, xout, s1, zn, zz, agi, ago, tg=""):
        phase(lambda E: plan_attn(P, nc, E, dict(
            attn_common, x_in=xin, x_out=s1, wpack=m_wpack, bpack=m_bpack, bvrow=m_bvrow,
            hostm_in=m_hostm, g1_in=m_g1, zbuf=zz, ru=m_ru, rw=m_rw, m1=m_m1, m2=m_m2),
            "l1mha"), "mha" + tg)
        phase(lambda E: plan_router(P, nc, E, dict(
            x_in=s1, wr_in=r_wr, g2_in=r_g2, zn_out=zn, vloc_out=agi.ap())), "router" + tg)

        def moe_phase(E):
            P.cc(lambda e: e.collective_compute(
                    "AllGather", ALU.bypass, replica_groups=[list(range(8))],
                    ins=[agi.ap().opt()], outs=[ago.ap().opt()]),
                 reads=("out:vloc",), writes=("cc_done",))
            plan_moe(P, nc, E, dict(
                x3_in=xin, s1_in=s1, zn_in=zn, vtok_in=agi.ap(),
                w1_in=o_w1, w2_in=o_w2, gsel_in=o_gsel, gsel2_in=o_gsel2, esel3_in=o_esel3,
                ident8_in=o_ident8, identf_in=ident_in, x_out=xout),
                ago=ago.ap(), permsel=o_permsel, idf32=ident_in)

        phase(moe_phase, "moe" + tg)

    l1ffn(x3, x4, s1a, zna, za, agi_a, ago_a, "A")
    phase(lambda E: plan_attn(P, nc, E, dict(
        attn_common, x_in=x4, x_out=x5, wpack=a1_wpack, bpack=a1_bpack, bvrow=a1_bvrow,
        hostm_in=a1_hostm, gb_in=a1_gb), "l1attn"), "l1attn")
    l1ffn(x5, x_out, s1b, znb, zb, agi_b, ago_b, "B")

    block = master.enter_context(nc.Block())
    P.finalize(block, master)
    master.close()
    return nc


def kernel(**inp):
    x = _np(inp["L0_attn_x"]).astype(np.float32, copy=False)
    xt = [np.ascontiguousarray(x[b].T) for b in range(8)]
    C = DM

    im = {}
    w1t, w2t, b1p, b2p = _prep_l0ffn(inp)
    im.update(f_w1t=w1t, f_w2t=w2t, f_b1=b1p, f_b2=b2p)

    def attn_pack(wq, wk, wv, wo, bq, bk, bv, bo, permute=False):
        if permute:
            perm = rope_perm()
            wq = wq[:, perm]
            wk = wk[:, perm]
        wpackh = np.ascontiguousarray(np.concatenate([wq, wk, wv, wo], axis=1).astype(np.float32))
        z8 = np.zeros(DM, np.float32)
        bq = z8 if bq is None else np.asarray(bq, np.float32)
        bk = z8 if bk is None else np.asarray(bk, np.float32)
        bv = z8 if bv is None else np.asarray(bv, np.float32)
        bo = z8 if bo is None else np.asarray(bo, np.float32)
        bpackh = np.zeros((128, 32), np.float32)
        bpackh[:, 0:8] = bq.reshape(8, 128).T
        bpackh[:, 8:16] = bk.reshape(8, 128).T
        bpackh[:, 24:32] = bo.reshape(8, 128).T
        return wpackh, bpackh, np.ascontiguousarray(bv.reshape(1, DM))

    hc0 = attn_host_consts("l0")
    im.update(minus1_in=hc0["minus1_in"], psel2_in=hc0["psel2_in"], bsel_in=hc0["bsel_in"],
              ident_in=hc0["ident_in"], a0_hostm=hc0["hostm_in"])
    wp, bp, bv_ = attn_pack(_np(inp["L0_attn_Wq"]).astype(np.float32),
                            _np(inp["L0_attn_Wk"]).astype(np.float32),
                            _np(inp["L0_attn_Wv"]).astype(np.float32),
                            _np(inp["L0_attn_Wo"]).astype(np.float32),
                            _np(inp["L0_attn_bq"]), _np(inp["L0_attn_bk"]),
                            _np(inp["L0_attn_bv"]), _np(inp["L0_attn_bo"]))
    im.update(a0_wpack=wp, a0_bpack=bp, a0_bvrow=bv_)

    hcm = attn_host_consts("l1mha")
    wqkv = _np(inp["L1_ffn_w_qkv"]).astype(np.float32)
    wp, bp, bv_ = attn_pack(wqkv[:, 0:C].copy(), wqkv[:, C:2 * C].copy(), wqkv[:, 2 * C:].copy(),
                            _np(inp["L1_ffn_w_o"]).astype(np.float32),
                            None, None, None, None, permute=True)
    im.update(m_wpack=wp, m_bpack=bp, m_bvrow=bv_, m_hostm=hcm["hostm_in"],
              m_ru=hcm["ru"], m_rw=hcm["rw"], m_m1=hcm["m1"], m_m2=hcm["m2"],
              m_g1=np.ascontiguousarray(_np(inp["L1_ffn_g1"]).astype(np.float32).reshape(8, 128).T))

    im.update(r_wr=np.ascontiguousarray(_np(inp["L1_ffn_w_router"]).astype(np.float32)),
              r_g2=np.ascontiguousarray(_np(inp["L1_ffn_g2"]).astype(np.float32).reshape(8, 128).T))
    mc = moe_consts()
    im.update(o_w1=np.ascontiguousarray(_np(inp["L1_ffn_w1"]).astype(np.float32).reshape(8 * DM, 2048)),
              o_w2=np.ascontiguousarray(_np(inp["L1_ffn_w2"]).astype(np.float32).reshape(8 * 2048, DM)),
              o_gsel=mc["gsel_in"], o_gsel2=mc["gsel2_in"], o_esel3=mc["esel3_in"],
              o_ident8=mc["ident8_in"], o_permsel=permsel_const())

    hc1 = attn_host_consts("l1attn")
    wqkv1 = _np(inp["L1_attn_W_qkv"]).astype(np.float32)
    bqkv1 = _np(inp["L1_attn_b_qkv"]).astype(np.float32)
    wp, bp, bv_ = attn_pack(wqkv1[:, 0:C].copy(), wqkv1[:, C:2 * C].copy(), wqkv1[:, 2 * C:].copy(),
                            _np(inp["L1_attn_W_out"]).astype(np.float32),
                            bqkv1[0:C], bqkv1[C:2 * C], bqkv1[2 * C:],
                            _np(inp["L1_attn_b_out"]).astype(np.float32))
    gq = _np(inp["L1_attn_q_gamma"]).astype(np.float32)
    gqb = _np(inp["L1_attn_q_beta"]).astype(np.float32)
    gk = _np(inp["L1_attn_k_gamma"]).astype(np.float32)
    gkb = _np(inp["L1_attn_k_beta"]).astype(np.float32)
    gbh = np.zeros((128, 4), np.float32)
    gbh[:, 0] = np.tile(gq, 2)
    gbh[:, 1] = np.tile(gqb, 2)
    gbh[:, 2] = np.tile(gk, 2)
    gbh[:, 3] = np.tile(gkb, 2)
    im.update(a1_wpack=wp, a1_bpack=bp, a1_bvrow=bv_, a1_hostm=hc1["hostm_in"], a1_gb=gbh)

    nc = build_fused()
    in_maps = [dict(im, x0=xt[b]) for b in range(8)]
    res = run_bass_kernel_spmd(nc, in_maps, core_ids=list(range(8)))
    out = np.stack([np.ascontiguousarray(res.results[b]["x_out"].T) for b in range(8)])
    return out
